# version 08
# baseline (speedup 1.0000x reference)
import os
import contextlib
import numpy as np
import ml_dtypes
import concourse.bass as bass
import concourse.mybir as mybir
from concourse.bass_utils import run_bass_kernel_spmd

F32 = mybir.dt.float32
BF16 = mybir.dt.bfloat16
I32 = mybir.dt.int32
AF = mybir.ActivationFunctionType
ALU = mybir.AluOpType
AX = mybir.AxisListType

B, S, H = 2, 8192, 1024
TC = 2048
E = 2048
SD = 128
FF = 4096
HC = H // 128
NT = TC // 128
NTB = TC // 512
MCN = S // 128
EPS_LN = 1e-5
NCORE_IDS = 8

KSTAGE = int(os.environ.get("KSTAGE", "99"))
KDBG = int(os.environ.get("KDBG", "0"))
KFFN = int(os.environ.get("KFFN", "9"))

ENGINES = ("tensor", "vector", "scalar", "gpsimd", "sync")

ARENA_COLS = 49152
_BUFS = {
    "x":       (0,     16384, 1, 1),
    "xn_hi":   (16384,  8192, 1, 2),
    "xn_lo":   (24576,  8192, 1, 2),
    "xsq":     (32768,  4096, 1, 1),
    "wblk":    (0,      8192, 2, 2),
    "stage":   (8192,   2048, 2, 2),
    "base":    (10240,  2048, 2, 2),
    "kown":    (12288,  2048, 2, 2),
    "wblk2":   (36864,  8192, 2, 2),
    "q_hi":    (33792,  1024, 2, 3),
    "q_lo":    (34816,  1024, 2, 3),
    "sq_hi":   (0,      8192, 3, 3),
    "sq_lo":   (8192,   8192, 3, 3),
    "kall_hi": (16384,  4096, 3, 3),
    "kall_lo": (20480,  4096, 3, 3),
    "vbuf":    (24576,  4096, 3, 3),
    "sqevac":  (28672,  1024, 3, 3),
    "avstage": (29696,  4096, 3, 3),
    "avl":     (0,      4096, 4, 4),
    "sul":     (40960,  4096, 4, 4),
    "gsc":     (4096,   2048, 4, 4),
    "gbf":     (6144,   2048, 4, 4),
    "gT":      (8192,   4096, 4, 4),
    "outT":    (12288,  4096, 4, 4),
    "xTrel":   (16384,  4096, 4, 4),
    "osq":     (20480,  4096, 4, 4),
    "woutb":   (24576, 16384, 4, 4),
    "xi":      (0,      8192, 5, 5),
    "z":       (8192,   2048, 5, 5),
    "qscr":    (10240,  8192, 5, 5),
    "wf":      (18432, 16384, 5, 5),
    "hstage":  (34816,  2048, 5, 5),
    "zsq":     (36864,  4096, 5, 5),
    "x2str":   (40960,  4096, 5, 5),
    "w1str":   (45056,  2048, 5, 5),
    "hihalf":  (0,     16384, 6, 7),
    "hrel":    (16384,  4096, 6, 7),
    "q2scr":   (20480,  8192, 6, 7),
    "w2s":     (28672,  3072, 6, 7),
    "outp":    (32768, 16384, 6, 7),
}
for _i, (_n1, (_o1, _c1, _a1, _b1)) in enumerate(list(_BUFS.items())):
    assert _o1 + _c1 <= ARENA_COLS, _n1
    for _n2, (_o2, _c2, _a2, _b2) in list(_BUFS.items())[_i + 1:]:
        if _a1 <= _b2 and _a2 <= _b1:
            assert _o1 + _c1 <= _o2 or _o2 + _c2 <= _o1, f"overlap {_n1}/{_n2}"


class KB:

    def __init__(self, nc):
        self.nc = nc
        self.ops = {e: [] for e in ENGINES}
        self.cnt = {"cT": 0, "cV": 0, "cS": 0, "cG": 0, "dW": 0, "dG": 0,
                    "dH": 0, "dK": 0, "dB": 0, "dV0": 0, "dV1": 0, "dA0": 0,
                    "dA1": 0, "dAR": 0, "dSU": 0, "dXR": 0, "dXS": 0, "dWO": 0,
                    "dF0": 0, "dF1": 0, "dWS": 0, "dY0": 0, "dY1": 0, "dR": 0,
                    "dKL": 0, "cc": 0}
        self.eng_sem = {"tensor": "cT", "vector": "cV", "scalar": "cS", "gpsimd": "cG"}
        self.waited = {}

    def emit(self, eng, fn, inc=None, add=1):
        tok = None
        if inc is not None:
            self.cnt[inc] += add
            tok = (inc, self.cnt[inc])
        self.ops[eng].append(("emit", fn, inc, add))
        return tok

    def step(self, eng, fn):
        return self.emit(eng, fn, inc=self.eng_sem[eng])

    def dma(self, eng, fn, q):
        return self.emit(eng, fn, inc=q, add=16)

    def wait(self, eng, tok):
        if tok is None:
            return
        sem, val = tok
        key = (eng, sem)
        if self.waited.get(key, -1) >= val:
            return
        self.waited[key] = val
        self.ops[eng].append(("wait", sem, val, None))

    def tokT(self):
        return ("cT", self.cnt["cT"])

    def tok(self, s):
        return (s, self.cnt[s])

    def quant(self, src_ap, out_ap, y_ap, ci_ap, cm1_ap, d_ap, recip_ap, relu,
              entry=None):
        if entry is None:
            self.wait("vector", self.tok("cV"))
            self.wait("scalar", self.tok("cV"))
        else:
            vt, st0 = entry
            self.wait("vector", vt)
            self.wait("scalar", vt)
            if st0 is not None:
                self.wait("vector", st0)
                self.wait("scalar", st0)
        rc = 1.0 if recip_ap is None else recip_ap
        if relu:
            t = self.step("vector", lambda h: h.tensor_scalar(
                y_ap, src_ap, rc, 0.0, ALU.mult, ALU.max))
        else:
            t = self.step("vector", lambda h: h.tensor_scalar(
                y_ap, src_ap, rc, None, ALU.mult))
        self.wait("scalar", t)
        t = self.step("scalar", lambda h: h.activation(ci_ap, y_ap, AF.Copy))
        self.wait("scalar", t)
        stok = self.step("scalar", lambda h: h.activation(cm1_ap, ci_ap, AF.Copy, bias=-1.0))
        self.wait("vector", stok)
        t = self.step("vector", lambda h: h.tensor_tensor(
            d_ap, y_ap, cm1_ap, op=ALU.subtract))
        self.wait("vector", t)
        t = self.step("vector", lambda h: h.scalar_tensor_tensor(
            y_ap, d_ap, 1.5, cm1_ap, op0=ALU.is_gt, op1=ALU.add))
        self.wait("vector", t)
        vtok = self.step("vector", lambda h: h.scalar_tensor_tensor(
            out_ap, d_ap, 0.5, y_ap, op0=ALU.is_ge, op1=ALU.add))
        return (vtok, stok)

    def split(self, src_ap, hi_ap, lo_ap, eng="vector"):
        t = self.step(eng, lambda h: h.tensor_copy(hi_ap, src_ap))
        self.wait(eng, t)
        return self.step(eng, lambda h: h.tensor_tensor(lo_ap, src_ap, hi_ap,
                                                        op=ALU.subtract))

    def barrier(self, engines=("tensor", "vector", "scalar", "gpsimd", "sync")):
        snap = dict(self.cnt)
        for e in engines:
            own = self.eng_sem.get(e)
            for sem, val in snap.items():
                if sem == own or val == 0:
                    continue
                self.wait(e, (sem, val))

    def realize(self, block, sems):
        def runner(lst):
            def run(h):
                for item in lst:
                    if item[0] == "wait":
                        h.wait_ge(sems[item[1]], item[2])
                    else:
                        inst = item[1](h)
                        if item[2] is not None:
                            inst.then_inc(sems[item[2]], item[3])
            return run

        block.tensor(runner(self.ops["tensor"]))
        block.vector(runner(self.ops["vector"]))
        block.scalar(runner(self.ops["scalar"]))
        block.gpsimd(runner(self.ops["gpsimd"]))
        block.sync(runner(self.ops["sync"]))


def build_nc():
    nc = bass.Bass("TRN2", target_bir_lowering=False, debug=False)

    xT = nc.dram_tensor("xT", [H, TC], F32, kind="ExternalInput").ap()
    wuv = nc.dram_tensor("wuv", [H, 2 * E + SD], F32, kind="ExternalInput").ap()
    buv_row = nc.dram_tensor("buv_row", [1, 2 * E + SD], F32, kind="ExternalInput").ap()
    buv_base = nc.dram_tensor("buv_base", [SD, 1], F32, kind="ExternalInput").ap()
    gamT = nc.dram_tensor("gamT", [SD, 2], F32, kind="ExternalInput").ap()
    betT = nc.dram_tensor("betT", [SD, 2], F32, kind="ExternalInput").ap()
    wout = nc.dram_tensor("wout", [E, H], F32, kind="ExternalInput").ap()
    bout_pt = nc.dram_tensor("bout_pt", [128, HC], F32, kind="ExternalInput").ap()
    w1 = nc.dram_tensor("w1", [FF, H], F32, kind="ExternalInput").ap()
    w1T = nc.dram_tensor("w1T", [H, FF], F32, kind="ExternalInput").ap()
    b1_pt = nc.dram_tensor("b1_pt", [128, FF // 128], F32, kind="ExternalInput").ap()
    w2 = nc.dram_tensor("w2", [H, FF], F32, kind="ExternalInput").ap()
    w2T = nc.dram_tensor("w2T", [FF, H], F32, kind="ExternalInput").ap()
    b2_pt = nc.dram_tensor("b2_pt", [128, HC], F32, kind="ExternalInput").ap()
    y = nc.dram_tensor("y", [TC, H], F32, kind="ExternalOutput").ap()

    def scratch(name, shape, dt):
        if KDBG:
            return nc.dram_tensor(name, shape, dt, kind="ExternalOutput").ap()
        return nc.dram_tensor(name, shape, dt).ap()

    AGW = 2 * E + 2 * SD
    ag_in = nc.dram_tensor("ag_in", [AGW, TC], BF16).ap()
    ag_out = nc.dram_tensor("ag_out", [4 * AGW, TC], BF16).ap()
    dbg_ag = nc.dram_tensor("dbg_ag", [AGW, TC], BF16, kind="ExternalOutput").ap() if KDBG else None
    su_sp = scratch("su_sp", [2 * TC, E], BF16)
    av_sp = scratch("av_sp", [TC, E], F32)
    h_sp = scratch("h_sp", [FF, TC], F32)
    x2_sp = scratch("x2_sp", [H, TC], F32)
    w2i_sp = nc.dram_tensor("w2i_sp", [H, FF], BF16).ap()
    ar1_in = nc.dram_tensor("ar1_in", [1, 8], F32).ap()
    ar1_out = nc.dram_tensor("ar1_out", [1, 8], F32).ap()
    ar2_in = nc.dram_tensor("ar2_in", [1, 8], F32).ap()
    ar2_out = nc.dram_tensor("ar2_out", [1, 8], F32).ap()

    pl = KB(nc)

    with contextlib.ExitStack() as es:
        En = es.enter_context
        arena = En(nc.sbuf_tensor("arena", [128, ARENA_COLS], F32))

        def V(name, off, n, dt=F32):
            o, ctot, _, _ = _BUFS[name]
            assert off + n <= ctot, f"{name}+{off}+{n}"
            ap = arena[:, o + off:o + off + n]
            return ap if dt == F32 else ap.bitcast(dt)

        ident = En(nc.sbuf_tensor("ident", [128, 128], F32))
        ident_bf = En(nc.sbuf_tensor("ident_bf", [128, 128], BF16))
        ones1 = En(nc.sbuf_tensor("ones1", [1, 128], F32))
        onesK = En(nc.sbuf_tensor("onesK", [128, 1], F32))
        ones1_bf = En(nc.sbuf_tensor("ones1_bf", [1, 128], BF16))
        s_buvrow = En(nc.sbuf_tensor("s_buvrow", [1, 512], F32))
        s_brh = En(nc.sbuf_tensor("s_brh", [1, 512], BF16))
        s_brl = En(nc.sbuf_tensor("s_brl", [1, 512], BF16))
        s_buvb = En(nc.sbuf_tensor("s_buvb", [128, 1], F32))
        s_gb = En(nc.sbuf_tensor("s_gb", [128, 8], F32))
        s_bout = En(nc.sbuf_tensor("s_bout", [128, HC], F32))
        s_b1 = En(nc.sbuf_tensor("s_b1", [128, FF // 128], F32))
        s_b2 = En(nc.sbuf_tensor("s_b2", [128, HC], F32))
        rowM = En(nc.sbuf_tensor("rowM", [1, 512], F32))
        rowV = En(nc.sbuf_tensor("rowV", [1, 512], F32))
        rowT = En(nc.sbuf_tensor("rowT", [1, 512], F32))
        rowR = En(nc.sbuf_tensor("rowR", [1, 512], F32))
        s_sc = En(nc.sbuf_tensor("s_sc", [1, 16], F32))
        s_scp = En(nc.sbuf_tensor("s_scp", [128, 8], F32))
        s_sw1 = En(nc.sbuf_tensor("s_sw1", [128, FF // 128], F32))
        s_sw2 = En(nc.sbuf_tensor("s_sw2", [128, HC], F32))
        s_bq1 = En(nc.sbuf_tensor("s_bq1", [128, FF // 128], F32))
        s_bq2 = En(nc.sbuf_tensor("s_bq2", [128, HC], F32))
        s_sv1 = En(nc.sbuf_tensor("s_sv1", [128, 32], F32))
        s_sv2 = En(nc.sbuf_tensor("s_sv2", [128, 8], F32))
        s_tmp = En(nc.sbuf_tensor("s_tmp", [128, 160], F32))
        s_eps = En(nc.sbuf_tensor("s_eps", [1, 1], F32))
        ps = [En(nc.psum_tensor(f"ps{i}", [128, 512], F32)) for i in range(8)]

        sems = {}
        for s in ("cT", "cV", "cS", "cG", "dW", "dG", "dH", "dK", "dB", "dV0",
                  "dV1", "dA0", "dA1", "dAR", "dSU", "dXR", "dXS", "dWO",
                  "dF0", "dF1", "dWS", "dY0", "dY1", "dR", "dKL", "cc"):
            sems[s] = En(nc.semaphore(s))

        block = En(nc.Block())

        def gchain(fn):
            pl.wait("gpsimd", pl.tok("cG"))
            return pl.step("gpsimd", fn)

        gchain(lambda h: h.memset(ident[:], 0.0))
        gchain(lambda h: h.affine_select(
            out=ident[:], in_=ident[:], compare_op=ALU.not_equal,
            fill=1.0, base=0, pattern=[[-1, 128]], channel_multiplier=1))
        gchain(lambda h: h.memset(ident_bf[:], 0.0))
        gchain(lambda h: h.affine_select(
            out=ident_bf[:], in_=ident_bf[:], compare_op=ALU.not_equal,
            fill=1.0, base=0, pattern=[[-1, 128]], channel_multiplier=1))
        gchain(lambda h: h.memset(ones1[:], 1.0))
        gchain(lambda h: h.memset(onesK[:], 1.0))
        gchain(lambda h: h.memset(ones1_bf[:], 1.0))
        gchain(lambda h: h.memset(s_eps[:], EPS_LN))

        x_toks = [pl.dma("sync", lambda h, c=c: h.dma_start(
            out=V("x", 2048 * c, 2048), in_=xT[128 * c:128 * (c + 1), :]), "dW")
            for c in range(HC)]
        for dst, src in [(s_buvb[:], buv_base[:, :]),
                         (s_bout[:], bout_pt[:, :]), (s_b1[:], b1_pt[:, :]),
                         (s_b2[:], b2_pt[:, :]), (s_gb[:, 0:2], gamT[:, :]),
                         (s_gb[:, 2:4], betT[:, :])]:
            t_sm = pl.dma("sync", lambda h, d=dst, s=src: h.dma_start(out=d, in_=s), "dW")
        pl.wait("vector", t_sm)
        pl.emit("vector", lambda h: h.tensor_scalar(
            s_gb[:, 4:5], s_gb[:, 0:1], 1.0 / 8192.0, None, ALU.mult))
        t_gbs = pl.step("vector", lambda h: h.tensor_scalar(
            s_gb[:, 5:6], s_gb[:, 2:3], 1.0 / 8192.0, None, ALU.mult))

        def emit_ln(nchunk, src_of, sq_name, out_cb, tb_range):
            inv_n = 1.0 / (128 * nchunk)
            prev_vt = None
            for tb in tb_range:
                sl = slice(512 * tb, 512 * (tb + 1))
                pl.wait("vector", prev_vt)
                tv = None
                for c in range(nchunk):
                    xv = src_of(c)
                    xsq = V(sq_name, 512 * c, 512)
                    tv = pl.step("vector", lambda h, xv=xv, sl=sl, xsq=xsq:
                                 h.tensor_tensor(xsq, xv[:, sl], xv[:, sl], op=ALU.mult))
                pl.wait("tensor", tv)
                tm = None
                for c in range(nchunk):
                    xv = src_of(c)
                    xsq = V(sq_name, 512 * c, 512)
                    pl.emit("tensor", lambda h, xv=xv, sl=sl, c=c:
                            h.matmul(ps[6][0:1, :], onesK[:], xv[:, sl],
                                     start=(c == 0), stop=(c == nchunk - 1)))
                    tm = pl.step("tensor", lambda h, xsq=xsq, c=c:
                                 h.matmul(ps[7][0:1, :], onesK[:], xsq,
                                          start=(c == 0), stop=(c == nchunk - 1)))
                pl.wait("vector", tm)
                t1 = pl.step("vector", lambda h: h.tensor_scalar(
                    rowM[:], ps[6][0:1, :], inv_n, None, ALU.mult))
                pl.step("vector", lambda h: h.tensor_scalar(
                    rowV[:], ps[7][0:1, :], inv_n, None, ALU.mult))
                pl.wait("vector", t1)
                t3 = pl.step("vector", lambda h: h.tensor_tensor(
                    rowT[:], rowM[:], rowM[:], op=ALU.mult))
                pl.wait("vector", t3)
                tvv = pl.step("vector", lambda h: h.tensor_tensor(
                    rowV[:], rowV[:], rowT[:], op=ALU.subtract))
                pl.wait("scalar", tvv)
                tss = pl.step("scalar", lambda h: h.activation(
                    rowT[:], rowV[:], AF.Sqrt, bias=s_eps[:]))
                pl.wait("vector", tss)
                trr = pl.step("vector", lambda h: h.reciprocal(rowR[:], rowT[:]))
                pl.wait("tensor", trr)
                pl.emit("tensor", lambda h: h.matmul(
                    ps[6][:, :], ones1[:], rowM[:], start=True, stop=True))
                tbb = pl.step("tensor", lambda h: h.matmul(
                    ps[7][:, :], ones1[:], rowR[:], start=True, stop=True))
                pl.wait("vector", tbb)
                vt = out_cb(tb, sl)
                prev_vt = vt
                pl.wait("tensor", vt)
                pl.wait("scalar", vt)

        pl.barrier()
        pl.wait("vector", t_sm)
        pl.wait("vector", t_gbs)

        def ln1_out(tb, sl):
            t = None
            for c in range(HC):
                xv, tmp = V("x", 2048 * c, 2048), V("xsq", 512 * c, 512)
                t = pl.step("vector", lambda h, xv=xv, tmp=tmp, sl=sl: h.tensor_tensor(
                    tmp, xv[:, sl], ps[6][:, :], op=ALU.subtract))
            pl.wait("vector", t)
            for c in range(HC):
                tmp = V("xsq", 512 * c, 512)
                t = pl.step("vector", lambda h, tmp=tmp: h.tensor_tensor(
                    tmp, tmp, ps[7][:, :], op=ALU.mult))
            pl.wait("vector", t)
            for c in range(HC):
                tmp, hi = V("xsq", 512 * c, 512), V("xn_hi", 1024 * c, 1024, BF16)
                t = pl.step("vector", lambda h, hi=hi, sl=sl, tmp=tmp:
                            h.tensor_copy(hi[:, sl], tmp))
            pl.wait("vector", t)
            vt = None
            for c in range(HC):
                tmp = V("xsq", 512 * c, 512)
                hi = V("xn_hi", 1024 * c, 1024, BF16)
                lo = V("xn_lo", 1024 * c, 1024, BF16)
                vt = pl.step("vector", lambda h, hi=hi, lo=lo, sl=sl, tmp=tmp:
                             h.tensor_tensor(lo[:, sl], tmp, hi[:, sl], op=ALU.subtract))
            return vt

        emit_ln(HC, lambda c: V("x", 2048 * c, 2048), "xsq", ln1_out, range(NTB))
        t_xn = pl.tok("cV")

        pl.barrier()

        def wv_f32(c, wp=0):
            return V("wblk" if wp == 0 else "wblk2", 512 * c, 512)

        def wv_hi(c, wp=0):
            return V("wblk" if wp == 0 else "wblk2", 4096 + 256 * c, 256, BF16)

        def wv_lo(c, wp=0):
            return V("wblk" if wp == 0 else "wblk2", 6144 + 256 * c, 256, BF16)

        def load_wblk(col0, ncols, guards, wp=0, wq="dW"):
            toks = []
            for g in (guards or []):
                pl.wait("sync", g)
            for c in range(HC):
                toks.append(pl.dma("sync", lambda h, c=c, col0=col0, ncols=ncols, wp=wp:
                            h.dma_start(out=wv_f32(c, wp)[:, 0:ncols],
                                        in_=wuv[128 * c:128 * (c + 1), col0:col0 + ncols]),
                            wq))
            pl.wait("vector", toks[-1])
            for g in (guards or []):
                pl.wait("vector", g)
            t = None
            for c in range(HC):
                t = pl.step("vector", lambda h, c=c, n=ncols, wp=wp: h.tensor_copy(
                    wv_hi(c, wp)[:, 0:n], wv_f32(c, wp)[:, 0:n]))
            pl.wait("vector", t)
            last = None
            for c in range(HC):
                last = pl.step("vector", lambda h, c=c, n=ncols, wp=wp: h.tensor_tensor(
                    wv_lo(c, wp)[:, 0:n], wv_f32(c, wp)[:, 0:n], wv_hi(c, wp)[:, 0:n],
                    op=ALU.subtract))
            return last

        t_w = load_wblk(2 * E, SD, None)
        pl.wait("tensor", t_w)
        pl.wait("tensor", t_xn)
        t_base_mm = None
        for tb in range(NTB):
            sl = slice(512 * tb, 512 * (tb + 1))
            for c in range(HC):
                whi, wlo = wv_hi(c)[:, 0:128], wv_lo(c)[:, 0:128]
                xhi = V("xn_hi", 1024 * c, 1024, BF16)
                xlo = V("xn_lo", 1024 * c, 1024, BF16)
                pl.emit("tensor", lambda h, whi=whi, xhi=xhi, sl=sl, st=(c == 0):
                        h.matmul(ps[0][:, :], whi, xhi[:, sl], start=st, stop=False))
                pl.emit("tensor", lambda h, whi=whi, xlo=xlo, sl=sl:
                        h.matmul(ps[0][:, :], whi, xlo[:, sl], start=False, stop=False))
                t_base_mm = pl.step("tensor", lambda h, wlo=wlo, xhi=xhi, sl=sl, c=c:
                                    h.matmul(ps[0][:, :], wlo, xhi[:, sl],
                                             start=False, stop=(c == HC - 1)))
            pl.wait("scalar", t_base_mm)
            t_base_ev = pl.step("scalar", lambda h, sl=sl: h.activation(
                V("base", 0, 2048)[:, sl], ps[0][:, :], AF.Silu, bias=s_buvb[:]))
            pl.wait("tensor", t_base_ev)
        pl.wait("vector", t_base_ev)
        bb = V("base", 0, 2048)
        tq1 = V("stage", 0, 2048)
        tq2 = V("wblk", 0, 2048)
        qh, ql = V("q_hi", 0, 1024, BF16), V("q_lo", 0, 1024, BF16)
        kh, kl = V("kown", 0, 1024, BF16), V("kown", 1024, 1024, BF16)
        pl.emit("vector", lambda h: h.tensor_scalar(
            tq1, bb, s_gb[:, 4:5], s_gb[:, 5:6], ALU.mult, ALU.add))
        t = pl.step("vector", lambda h: h.tensor_scalar(
            tq2, bb, s_gb[:, 1:2], s_gb[:, 3:4], ALU.mult, ALU.add))
        pl.wait("vector", t)
        pl.emit("vector", lambda h: h.tensor_copy(qh, tq1))
        t = pl.step("vector", lambda h: h.tensor_copy(kh, tq2))
        pl.wait("vector", t)
        pl.emit("vector", lambda h: h.tensor_tensor(ql, tq1, qh, op=ALU.subtract))
        t_k = pl.step("vector", lambda h: h.tensor_tensor(kl, tq2, kh, op=ALU.subtract))
        pl.wait("gpsimd", t_k)
        pl.dma("gpsimd", lambda h: h.dma_start(out=ag_in[2 * E:2 * E + SD, :], in_=kh), "dK")
        pl.dma("gpsimd", lambda h: h.dma_start(out=ag_in[2 * E + SD:, :], in_=kl), "dK")

        def suv_f32(par):
            return V("stage", 1024 * par, 512)

        def suv_hi(par):
            return V("stage", 1024 * par + 512, 256, BF16)

        def suv_lo(par):
            return V("stage", 1024 * par + 768, 256, BF16)

        t_wprev = pl.tok("cV")
        stage_free = [t_k, t_k]
        t_ev_hist = [None, None]
        blk_order = list(range(4, 8)) + list(range(4))
        t_w_pref = {}
        consumed_tok = {}
        for bi, blk in enumerate(blk_order):
            is_v = blk >= 4
            col0 = 512 * blk
            wp = bi % 2
            if bi == 0:
                t_w_pref[0] = load_wblk(col0, 512, [pl.tokT(), t_k], wp=0, wq="dW")
            t_w = t_w_pref[bi]
            if bi + 1 < len(blk_order):
                g = [consumed_tok[bi - 1]] if bi >= 1 else []
                t_w_pref[bi + 1] = load_wblk(512 * blk_order[bi + 1], 512, g,
                                             wp=(bi + 1) % 2,
                                             wq="dW" if (bi + 1) % 2 == 0 else "dKL")
            if bi >= 1:
                pl.wait("sync", consumed_tok[bi - 1])
            t_brow = pl.dma("sync", lambda h, col0=col0: h.dma_start(
                out=s_buvrow[:, :], in_=buv_row[:, col0:col0 + 512]), "dB")
            pl.wait("vector", t_brow)
            pl.wait("vector", pl.tokT())
            t_brow = pl.split(s_buvrow[:, :], s_brh[:, :], s_brl[:, :])
            pl.wait("tensor", t_w)
            pl.wait("tensor", t_brow)
            for tt in range(NT):
                par = tt % 2
                bank = tt % 2
                if t_ev_hist[bank] is not None:
                    pl.wait("tensor", t_ev_hist[bank])
                for c in range(HC):
                    whi512 = wv_hi(c, wp)
                    wlo512 = wv_lo(c, wp)
                    xhi = V("xn_hi", 1024 * c, 1024, BF16)[:, 128 * tt:128 * (tt + 1)]
                    xlo = V("xn_lo", 1024 * c, 1024, BF16)[:, 128 * tt:128 * (tt + 1)]
                    pl.emit("tensor", lambda h, a=xhi, b=whi512, bank=bank, st=(c == 0):
                            h.matmul(ps[bank][:, :], a, b, start=st, stop=False))
                    pl.emit("tensor", lambda h, a=xhi, b=wlo512, bank=bank:
                            h.matmul(ps[bank][:, :], a, b, start=False, stop=False))
                    pl.emit("tensor", lambda h, a=xlo, b=whi512, bank=bank, sp=(os.environ.get("KNB2") and c == HC - 1):
                            h.matmul(ps[bank][:, :], a, b, start=False, stop=bool(sp)))
                if os.environ.get("KNB2"):
                    t_mm = pl.step("tensor", lambda h: h.nop())
                elif os.environ.get("KNOBIAS"):
                    t_mm = pl.step("tensor", lambda h, bank=bank: h.matmul(
                        ps[bank][:, :], ones1_bf[:], s_brh[:, :],
                        start=False, stop=True))
                else:
                    pl.emit("tensor", lambda h, bank=bank: h.matmul(
                        ps[bank][:, :], ones1_bf[:], s_brh[:, :],
                        start=False, stop=False))
                    t_mm = pl.step("tensor", lambda h, bank=bank: h.matmul(
                        ps[bank][:, :], ones1_bf[:], s_brl[:, :],
                        start=False, stop=True))
                pl.wait("scalar", t_mm)
                if stage_free[par] is not None:
                    pl.wait("scalar", stage_free[par])
                t_ev = pl.step("scalar", lambda h, bank=bank, par=par: h.activation(
                    suv_f32(par), ps[bank][:, :], AF.Silu))
                t_ev_hist[bank] = t_ev
                pl.wait("vector", t_ev)
                t_sp = pl.split(suv_f32(par), suv_hi(par), suv_lo(par))
                pl.wait("gpsimd", t_sp)
                r0 = 128 * tt
                dq = "dG" if par == 0 else "dH"
                if is_v:
                    cs = col0 - E
                    pl.dma("gpsimd", lambda h, r0=r0, cs=cs, par=par: h.dma_start(
                        out=ag_in[r0:r0 + 128, cs:cs + 512], in_=suv_hi(par)), dq)
                    stage_free[par] = pl.dma("gpsimd",
                        lambda h, r0=r0, cs=cs, par=par: h.dma_start(
                            out=ag_in[TC + r0:TC + r0 + 128, cs:cs + 512],
                            in_=suv_lo(par)), dq)
                else:
                    pl.dma("gpsimd", lambda h, r0=r0, cs=col0, par=par: h.dma_start(
                        out=su_sp[r0:r0 + 128, cs:cs + 512], in_=suv_hi(par)), dq)
                    stage_free[par] = pl.dma("gpsimd",
                        lambda h, r0=r0, cs=col0, par=par: h.dma_start(
                            out=su_sp[TC + r0:TC + r0 + 128, cs:cs + 512],
                            in_=suv_lo(par)), dq)
            consumed_tok[bi] = pl.tokT()
            if blk == 7:
                pl.wait("gpsimd", pl.tok("dG"))
                pl.wait("gpsimd", pl.tok("dH"))
                pl.wait("gpsimd", pl.tok("dK"))
                pl.emit("gpsimd", lambda h: h.collective_compute(
                    "AllGather", ALU.bypass,
                    replica_groups=[[0, 1, 2, 3], [4, 5, 6, 7]],
                    ins=[ag_in.opt()],
                    outs=[ag_out.opt()]), inc="cc")




        if KSTAGE >= 2:
            qh_v = V("q_hi", 0, 1024, BF16)
            ql_v = V("q_lo", 0, 1024, BF16)
            kh_v = V("kall_hi", 0, 4096, BF16)
            kl_v = V("kall_lo", 0, 4096, BF16)
            sqh_v = V("sq_hi", 0, 8192, BF16)
            sql_v = V("sq_lo", 0, 8192, BF16)

            def vb_hi(p):
                return V("vbuf", 2048 * p, 1024, BF16)

            def vb_lo(p):
                return V("vbuf", 2048 * p + 1024, 1024, BF16)

            def scr2(p):
                return V("sqevac", 512 * p, 512)

            def sq22(p):
                return V("avstage", 512 * p, 512)

            def avst(ns):
                return V("avstage", 2048 * ns, 2048)

            pl.wait("sync", pl.tok("cT"))
            pl.wait("sync", pl.tok("cc"))
            for e in ("vector", "scalar", "tensor"):
                pl.wait(e, pl.tok("dG"))
                pl.wait(e, pl.tok("dH"))
            tkl = None
            for r in range(4):
                b0 = r * AGW
                pl.dma("sync", lambda h, b0=b0, r=r: h.dma_start(
                    out=kh_v[:, 2048 * r:2048 * (r + 1)],
                    in_=ag_out[b0 + 2 * E:b0 + 2 * E + SD, :]), "dKL")
                tkl = pl.dma("sync", lambda h, b0=b0, r=r: h.dma_start(
                    out=kl_v[:, 2048 * r:2048 * (r + 1)],
                    in_=ag_out[b0 + 2 * E + SD:b0 + AGW, :]), "dKL")
            pl.wait("tensor", tkl)

            av_guard = [None, None]
            prev_evac = [pl.tok("cS"), pl.tok("cS")]
            vcons = {}

            for nb in range(8):
                nsl = slice(256 * nb, 256 * (nb + 1))
                pl.wait("tensor", prev_evac[0])
                pl.wait("tensor", prev_evac[1])
                pl.wait("vector", av_guard[0])
                pl.wait("scalar", av_guard[0])
                ev_tok = {}
                sq_tok = {}
                for mp in range(MCN // 2):
                    bank = mp % 2
                    if mp >= 2:
                        pl.wait("tensor", ev_tok[mp - 2])
                    t_mm = None
                    for sub in range(2):
                        mc = 2 * mp + sub
                        ksl = slice(128 * mc, 128 * (mc + 1))
                        osl = slice(256 * sub, 256 * (sub + 1))
                        pl.emit("tensor", lambda h, ksl=ksl, nsl=nsl, bank=bank, osl=osl:
                                h.matmul(ps[bank][:, osl], kh_v[:, ksl], qh_v[:, nsl],
                                         start=True, stop=False))
                        pl.emit("tensor", lambda h, ksl=ksl, nsl=nsl, bank=bank, osl=osl:
                                h.matmul(ps[bank][:, osl], kh_v[:, ksl], ql_v[:, nsl],
                                         start=False, stop=False))
                        t_mm = pl.step("tensor", lambda h, ksl=ksl, nsl=nsl, bank=bank, osl=osl:
                                       h.matmul(ps[bank][:, osl], kl_v[:, ksl],
                                                qh_v[:, nsl], start=False, stop=True))
                    pl.wait("scalar", t_mm)
                    if mp >= 2:
                        pl.wait("scalar", sq_tok[mp - 2])
                    ev_tok[mp] = pl.step("scalar", lambda h, bank=bank:
                                         h.activation(scr2(bank), ps[bank][:, 0:512],
                                                      AF.Relu))
                    pl.wait("vector", ev_tok[mp])
                    if mp >= 2:
                        pl.wait("vector", sq_tok[mp - 2])
                    tsq = pl.step("vector", lambda h, p=bank: h.tensor_tensor(
                        sq22(p), scr2(p), scr2(p), op=ALU.mult))
                    pl.wait("vector", tsq)
                    sq_tok[mp] = pl.split(
                        sq22(bank), sqh_v[:, 512 * mp:512 * (mp + 1)],
                        sql_v[:, 512 * mp:512 * (mp + 1)])
                t_sq_all = sq_tok[MCN // 2 - 1]

                pl.wait("tensor", t_sq_all)
                pl.wait("tensor", ev_tok[MCN // 2 - 1])
                t_av = None
                for mc in range(MCN):
                    par = mc % 2
                    r, lt = mc // 16, mc % 16
                    b0 = r * AGW
                    if mc >= 2:
                        pl.wait("sync", vcons[(nb, mc - 2)])
                    else:
                        pl.wait("sync", vcons.get((nb - 1, MCN - 2 + mc), None))
                    pl.dma("sync", lambda h, b0=b0, lt=lt, par=par: h.dma_start(
                        out=vb_hi(par), in_=ag_out[b0 + 128 * lt:b0 + 128 * (lt + 1), :]),
                        "dV0" if par == 0 else "dV1")
                    pl.dma("sync", lambda h, b0=b0, lt=lt, par=par: h.dma_start(
                        out=vb_lo(par),
                        in_=ag_out[b0 + TC + 128 * lt:b0 + TC + 128 * (lt + 1), :]),
                        "dV0" if par == 0 else "dV1")
                    pl.wait("tensor", pl.tok("dV0" if par == 0 else "dV1"))
                    for ns in range(2):
                        s_hi = sqh_v[:, 256 * mc + 128 * ns:256 * mc + 128 * (ns + 1)]
                        s_lo = sql_v[:, 256 * mc + 128 * ns:256 * mc + 128 * (ns + 1)]
                        for eq in range(4):
                            bank = ns * 4 + eq
                            esl = slice(512 * eq, 512 * (eq + 1))
                            st = (mc == 0)
                            sp = (mc == MCN - 1)
                            pl.emit("tensor", lambda h, a=s_hi, p=par, esl=esl, bank=bank, st=st:
                                    h.matmul(ps[bank][:, :], a, vb_hi(p)[:, esl],
                                             start=st, stop=False))
                            pl.emit("tensor", lambda h, a=s_hi, p=par, esl=esl, bank=bank:
                                    h.matmul(ps[bank][:, :], a, vb_lo(p)[:, esl],
                                             start=False, stop=False))
                            t_av = pl.step("tensor",
                                           lambda h, a=s_lo, p=par, esl=esl, bank=bank, sp=sp:
                                           h.matmul(ps[bank][:, :], a, vb_hi(p)[:, esl],
                                                    start=False, stop=sp))
                    vcons[(nb, mc)] = t_av
                for ns in range(2):
                    pl.wait("scalar", t_av)
                    pl.wait("scalar", av_guard[ns])
                    tev = None
                    for eq in range(4):
                        bank = ns * 4 + eq
                        tev = pl.step("scalar", lambda h, bank=bank, ns=ns, eq=eq:
                                      h.copy(avst(ns)[:, 512 * eq:512 * (eq + 1)],
                                             ps[bank][:, :]))
                    prev_evac[ns] = tev
                    pl.wait("gpsimd", tev)
                    r0 = 256 * nb + 128 * ns
                    av_guard[ns] = pl.dma("gpsimd", lambda h, r0=r0, ns=ns: h.dma_start(
                        out=av_sp[r0:r0 + 128, :], in_=avst(ns)),
                        "dA0" if ns == 0 else "dA1")

        if KSTAGE >= 2:
            pl.barrier()
            def wo_hi(ec):
                return V("woutb", 512 * ec, 512, BF16)

            def wo_lo(ec):
                return V("woutb", 8192 + 512 * ec, 512, BF16)

            t_wo = None
            stg = V("osq", 0, 1024)
            for ec in range(16):
                if t_wo is not None:
                    pl.wait("sync", t_wo)
                tld = pl.dma("sync", lambda h, ec=ec, stg=stg: h.dma_start(
                    out=stg, in_=wout[128 * ec:128 * (ec + 1), :]), "dWO")
                pl.wait("vector", tld)
                t_wo = pl.split(stg, wo_hi(ec), wo_lo(ec))

            def avl_p(p):
                return V("avl", 2048 * p, 2048)

            def suh_p(p):
                return V("sul", 2048 * p, 1024, BF16)

            def sul_p(p):
                return V("sul", 2048 * p + 1024, 1024, BF16)
            gsc_v = V("gsc", 0, 2048)
            gbh_v = V("gbf", 0, 1024, BF16)
            gbl_v = V("gbf", 1024, 1024, BF16)

            def gT_hi(ec):
                return V("gT", 128 * ec, 128, BF16)

            def gT_lo(ec):
                return V("gT", 2048 + 128 * ec, 128, BF16)

            t_gcons = None
            gcons_hist = [None, None]
            t_x2st = None
            t_ln2v = None
            for tb in range(NTB):
                tbsl = slice(512 * tb, 512 * (tb + 1))
                pl.wait("sync", t_x2st)
                pl.wait("sync", t_ln2v)
                txr = None
                for c in range(HC):
                    txr = pl.dma("sync", lambda h, c=c, tbsl=tbsl: h.dma_start(
                        out=V("xTrel", 512 * c, 512),
                        in_=xT[128 * c:128 * (c + 1), tbsl]), "dXR")
                t_oute = None
                for htb in range(2):
                    for tl in range(2):
                        tt = 4 * tb + 2 * htb + tl
                        lp = tt % 2
                        pl.wait("sync", gcons_hist[lp])
                        if tb == 0 and htb == 0 and tl == 0:
                            pl.wait("sync", pl.tok("dA0"))
                            pl.wait("sync", pl.tok("dA1"))
                        tar = pl.dma("sync", lambda h, tt=tt, lp=lp: h.dma_start(
                            out=avl_p(lp), in_=av_sp[128 * tt:128 * (tt + 1), :]),
                            "dAR" if lp == 0 else "dKL")
                        tsr = pl.dma("sync", lambda h, tt=tt, lp=lp: h.dma_start(
                            out=suh_p(lp), in_=su_sp[128 * tt:128 * (tt + 1), :]),
                            "dSU" if lp == 0 else "dB")
                        tsr = pl.dma("sync", lambda h, tt=tt, lp=lp: h.dma_start(
                            out=sul_p(lp), in_=su_sp[TC + 128 * tt:TC + 128 * (tt + 1), :]),
                            "dSU" if lp == 0 else "dB")
                        pl.wait("vector", tar)
                        pl.wait("vector", tsr)
                        tg = pl.step("vector", lambda h, lp=lp: h.tensor_tensor(
                            gsc_v, suh_p(lp), sul_p(lp), op=ALU.add))
                        pl.wait("vector", tg)
                        t_gcons = pl.step("vector", lambda h, lp=lp: h.tensor_tensor(
                            gsc_v, gsc_v, avl_p(lp), op=ALU.mult))
                        gcons_hist[lp] = t_gcons
                        pl.wait("vector", t_gcons)
                        tsp = pl.split(gsc_v, gbh_v, gbl_v)
                        pl.wait("tensor", tsp)
                        pl.wait("tensor", t_ln2v)
                        tq = None
                        tq_hist = [None, None]
                        for i2, (ec, half) in enumerate(
                                (e, hh) for e in range(16) for hh in (0, 1)):
                            gin = gbh_v if half == 0 else gbl_v
                            gout = gT_hi if half == 0 else gT_lo
                            bank = 6 + (i2 % 2)
                            if tq_hist[i2 % 2] is not None:
                                pl.wait("tensor", tq_hist[i2 % 2])
                            tt_mm = pl.step("tensor", lambda h, gin=gin, ec=ec, bank=bank:
                                            h.transpose(ps[bank][:, 0:64].bitcast(BF16),
                                                        gin[:, 128 * ec:128 * (ec + 1)],
                                                        ident_bf[:]))
                            pl.wait("vector", tt_mm)
                            tq = pl.step("vector", lambda h, gout=gout, ec=ec, tl=tl, bank=bank:
                                         h.tensor_copy(
                                             gout(ec)[:, 128 * tl:128 * (tl + 1)],
                                             ps[bank][:, 0:64].bitcast(BF16)))
                            tq_hist[i2 % 2] = tq
                    pl.wait("tensor", tq)
                    pl.wait("tensor", t_oute)
                    if tb == 0 and htb == 0:
                        pl.wait("tensor", t_wo)
                        pl.wait("tensor", prev_evac[0])
                        pl.wait("tensor", prev_evac[1])
                    t_oute_prev = t_oute
                    t_oute = None
                    for hp in range(2):
                        pl.wait("tensor", t_oute if t_oute else t_oute_prev)
                        t_op = None
                        for ec in range(16):
                            for hq2 in range(4):
                                ht = 4 * hp + hq2
                                st = (ec == 0)
                                sp = (ec == 15)
                                hsl = slice(128 * ht, 128 * (ht + 1))
                                pl.emit("tensor", lambda h, ec=ec, hsl=hsl, hq2=hq2, st=st:
                                        h.matmul(ps[hq2][:, 0:256], wo_hi(ec)[:, hsl],
                                                 gT_hi(ec)[:, :], start=st, stop=False))
                                pl.emit("tensor", lambda h, ec=ec, hsl=hsl, hq2=hq2:
                                        h.matmul(ps[hq2][:, 0:256], wo_lo(ec)[:, hsl],
                                                 gT_hi(ec)[:, :], start=False, stop=False))
                                t_op = pl.step("tensor", lambda h, ec=ec, hsl=hsl, hq2=hq2, sp=sp:
                                               h.matmul(ps[hq2][:, 0:256], wo_hi(ec)[:, hsl],
                                                        gT_lo(ec)[:, :], start=False, stop=sp))
                        pl.wait("vector", t_op)
                        for hq2 in range(4):
                            ht = 4 * hp + hq2
                            t_oute = pl.step("vector", lambda h, ht=ht, hq2=hq2, htb=htb:
                                             h.tensor_scalar(
                                                 V("outT", 512 * ht, 512)[:, 256 * htb:256 * (htb + 1)],
                                                 ps[hq2][:, 0:256], s_bout[:, ht:ht + 1], None,
                                                 ALU.add))
                pl.wait("tensor", t_oute)
                pl.wait("vector", t_oute)
                pl.wait("vector", txr)

                def ln2_out(tb2, sl2):
                    t = None
                    for c in range(HC):
                        ov, tmp = V("outT", 512 * c, 512), V("osq", 512 * c, 512)
                        t = pl.step("vector", lambda h, ov=ov, tmp=tmp: h.tensor_tensor(
                            tmp, ov, ps[6][:, :], op=ALU.subtract))
                    pl.wait("vector", t)
                    for c in range(HC):
                        tmp = V("osq", 512 * c, 512)
                        t = pl.step("vector", lambda h, tmp=tmp: h.tensor_tensor(
                            tmp, tmp, ps[7][:, :], op=ALU.mult))
                    pl.wait("vector", t)
                    vt = None
                    for c in range(HC):
                        tmp, xr = V("osq", 512 * c, 512), V("xTrel", 512 * c, 512)
                        vt = pl.step("vector", lambda h, tmp=tmp, xr=xr: h.tensor_tensor(
                            xr, tmp, xr, op=ALU.add))
                    return vt

                emit_ln(HC, lambda c: V("outT", 512 * c, 512), "osq", ln2_out, [0])
                t_ln2v = pl.tok("cV")
                pl.wait("gpsimd", t_ln2v)
                for c in range(HC):
                    t_x2st = pl.dma("gpsimd", lambda h, c=c, tbsl=tbsl: h.dma_start(
                        out=x2_sp[128 * c:128 * (c + 1), tbsl],
                        in_=V("xTrel", 512 * c, 512)), "dXS")

        class _StopEmit(Exception):
            pass

        _ffn_stopped = False
        if KSTAGE >= 3:
          try:
            pl.barrier()

            def xi_v(dc):
                return V("xi", 1024 * dc, 1024, BF16)

            def wiT_v(dc):
                return V("wf", 2048 * dc, 2048, BF16)

            def x2s_v(c):
                return V("x2str", 512 * c, 512)

            def zsq_v(c):
                return V("zsq", 512 * c, 512)

            def m_row(tb):
                return V("z", 512 * tb, 512)[0:1, :]

            def r_row(tb):
                return V("hstage", 512 * tb, 512)[0:1, :]

            def q_y(p):
                return V("qscr", 2048 * p, 512)

            def q_ci(p):
                return V("qscr", 2048 * p + 512, 512).bitcast(I32)

            def q_cm1(p):
                return V("qscr", 2048 * p + 1024, 512)

            def q_d(p):
                return V("qscr", 2048 * p + 1536, 512)

            def w1s_v(p):
                return V("w1str", 1024 * p, 1024)

            def hst_v(p):
                return V("hstage", 512 * p, 512)

            t_x2cons = None
            rmax_cols = []
            for tb in range(NTB):
                tbsl = slice(512 * tb, 512 * (tb + 1))
                pl.wait("sync", t_x2cons)
                pl.wait("sync", pl.tok("dXS"))
                tld = None
                for c in range(HC):
                    tld = pl.dma("sync", lambda h, c=c, tbsl=tbsl: h.dma_start(
                        out=x2s_v(c), in_=x2_sp[128 * c:128 * (c + 1), tbsl]), "dXR")
                pl.wait("vector", tld)
                t = None
                for c in range(HC):
                    t = pl.step("vector", lambda h, c=c: h.tensor_tensor(
                        zsq_v(c), x2s_v(c), x2s_v(c), op=ALU.mult))
                pl.wait("tensor", t)
                tm = None
                for c in range(HC):
                    pl.emit("tensor", lambda h, c=c: h.matmul(
                        ps[6][0:1, :], onesK[:], x2s_v(c),
                        start=(c == 0), stop=(c == HC - 1)))
                    tm = pl.step("tensor", lambda h, c=c: h.matmul(
                        ps[7][0:1, :], onesK[:], zsq_v(c),
                        start=(c == 0), stop=(c == HC - 1)))
                pl.wait("vector", tm)
                t1 = pl.step("vector", lambda h: h.tensor_scalar(
                    rowM[:], ps[6][0:1, :], 1.0 / H, None, ALU.mult))
                pl.step("vector", lambda h: h.tensor_scalar(
                    rowV[:], ps[7][0:1, :], 1.0 / H, None, ALU.mult))
                pl.wait("vector", t1)
                t3 = pl.step("vector", lambda h: h.tensor_tensor(
                    rowT[:], rowM[:], rowM[:], op=ALU.mult))
                pl.wait("vector", t3)
                tvv = pl.step("vector", lambda h: h.tensor_tensor(
                    rowV[:], rowV[:], rowT[:], op=ALU.subtract))
                pl.wait("scalar", tvv)
                tss = pl.step("scalar", lambda h: h.activation(
                    rowT[:], rowV[:], AF.Sqrt, bias=s_eps[:]))
                pl.wait("vector", tss)
                trr = pl.step("vector", lambda h: h.reciprocal(rowR[:], rowT[:]))
                pl.wait("vector", trr)
                pl.emit("vector", lambda h, tb=tb: h.tensor_copy(m_row(tb), rowM[:]))
                trc = pl.step("vector", lambda h, tb=tb: h.tensor_copy(r_row(tb), rowR[:]))
                pl.wait("tensor", trc)
                pl.emit("tensor", lambda h: h.matmul(
                    ps[6][:, :], ones1[:], rowM[:], start=True, stop=True))
                tbb = pl.step("tensor", lambda h: h.matmul(
                    ps[7][:, :], ones1[:], rowR[:], start=True, stop=True))
                pl.wait("vector", tbb)
                t = None
                for c in range(HC):
                    t = pl.step("vector", lambda h, c=c: h.tensor_tensor(
                        zsq_v(c), x2s_v(c), ps[6][:, :], op=ALU.subtract))
                pl.wait("vector", t)
                for c in range(HC):
                    t = pl.step("vector", lambda h, c=c: h.tensor_tensor(
                        zsq_v(c), zsq_v(c), ps[7][:, :], op=ALU.mult))
                pl.wait("vector", t)
                for c in range(HC):
                    col = 32 + 8 * tb + c
                    rmax_cols.append(col)
                    t = pl.step("vector", lambda h, c=c, col=col: h.tensor_reduce(
                        s_tmp[:, col:col + 1], zsq_v(c), axis=AX.X, op=ALU.max))
                t_x2cons = t
                pl.wait("tensor", t)

            pl.wait("vector", t_x2cons)
            tcm = pl.step("vector", lambda h: h.reduce_max(
                s_tmp[:, 64:65], s_tmp[:, 32:64], axis=AX.X))
            pl.wait("tensor", tcm)
            ttr = pl.step("tensor", lambda h: h.transpose(
                ps[5][0:1, 0:128], s_tmp[:, 64:65], ident[:]))
            pl.wait("vector", ttr)
            tgm = pl.step("gpsimd", lambda h: h.memset(s_sc[:, :], -1e30))
            pl.wait("vector", tgm)
            tgm = pl.step("vector", lambda h: h.reduce_max(
                s_sc[0:1, 0:1], ps[5][0:1, 0:128], axis=AX.X))
            pl.wait("gpsimd", tgm)
            tdm = pl.dma("gpsimd", lambda h: h.dma_start(
                out=ar1_in[:, :], in_=s_sc[0:1, 0:8]), "dR")
            pl.wait("gpsimd", tdm)
            pl.emit("gpsimd", lambda h: h.collective_compute(
                "AllReduce", ALU.max, replica_groups=[list(range(8))],
                ins=[ar1_in.opt()], outs=[ar1_out.opt()]), inc="cc")
            t_w1cons = [None, None]
            q_entry = [None, None]
            wi_slot_guard = [None, None]
            wiT_bank_hist = [None, None]
            t_wiT = None

            def QW(p):
                return V("qscr", 4096 * p, 1024)

            def QWCI(p):
                return V("qscr", 4096 * p + 1024, 1024).bitcast(I32)

            def QWCM(p):
                return V("qscr", 4096 * p + 2048, 1024)

            def QWD(p):
                return V("qscr", 4096 * p + 3072, 1024)

            for fc in range(32):
                par = fc % 2
                pl.wait("sync", t_w1cons[par])
                tw = pl.dma("sync", lambda h, fc=fc, par=par: h.dma_start(
                    out=w1s_v(par), in_=w1[128 * fc:128 * (fc + 1), :]),
                    "dF0" if par == 0 else "dF1")
                pl.wait("vector", tw)
                trm = pl.step("vector", lambda h, fc=fc, par=par: h.tensor_reduce(
                    s_sw1[:, fc:fc + 1], w1s_v(par), axis=AX.X, op=ALU.max,
                    apply_absolute_value=True))
                pl.wait("vector", trm)
                trm = pl.step("vector", lambda h, fc=fc, par=par: h.reciprocal(
                    s_tmp[:, 66 + 2 * par:67 + 2 * par], s_sw1[:, fc:fc + 1]))
                pl.wait("vector", trm)
                trm = pl.step("vector", lambda h, par=par: h.tensor_scalar(
                    s_tmp[:, 67 + 2 * par:68 + 2 * par],
                    s_tmp[:, 66 + 2 * par:67 + 2 * par], 127.0, None, ALU.mult))
                pl.wait("vector", trm)
                pl.wait("vector", wi_slot_guard[par])
                q_entry[par] = pl.quant(
                    w1s_v(par), V("zsq", 1024 * par, 512).bitcast(BF16),
                    QW(par), QWCI(par), QWCM(par), QWD(par),
                    s_tmp[:, 67 + 2 * par:68 + 2 * par], relu=False,
                    entry=q_entry[par])
                t_w1cons[par] = q_entry[par][0]
                pl.wait("tensor", q_entry[par][0])
                for tpi in range(8):
                    bank = (tpi % 2) + 2
                    if wiT_bank_hist[tpi % 2] is not None:
                        pl.wait("tensor", wiT_bank_hist[tpi % 2])
                    tmm = pl.step("tensor", lambda h, par=par, tpi=tpi, bank=bank:
                                  h.transpose(
                                      ps[bank][:, 0:64].bitcast(BF16),
                                      V("zsq", 1024 * par, 512).bitcast(BF16)[:, 128 * tpi:128 * (tpi + 1)],
                                      ident_bf[:]))
                    pl.wait("vector", tmm)
                    t_wiT = pl.step("vector", lambda h, dc=tpi, fc=fc, bank=bank:
                                    h.tensor_copy(
                                        wiT_v(dc)[:, 128 * fc:128 * (fc + 1)],
                                        ps[bank][:, 0:64].bitcast(BF16)))
                    wiT_bank_hist[tpi % 2] = t_wiT
                wi_slot_guard[par] = t_wiT

            pl.wait("gpsimd", pl.tok("cc"))
            tdm = pl.dma("gpsimd", lambda h: h.dma_start(
                out=s_sc[0:1, 8:16], in_=ar1_out[:, :]), "dR")
            pl.wait("vector", tdm)
            ts1 = pl.step("vector", lambda h: h.tensor_scalar(
                s_sc[0:1, 1:2], s_sc[0:1, 8:9], 1e-8, 1.0 / 255.0, ALU.max, ALU.mult))
            pl.wait("vector", ts1)
            ts1 = pl.step("vector", lambda h: h.reciprocal(s_sc[0:1, 2:3], s_sc[0:1, 1:2]))
            pl.wait("tensor", ts1)
            tbq = pl.step("tensor", lambda h: h.matmul(
                ps[5][:, 0:1], ones1[:], s_sc[0:1, 2:3], start=True, stop=True))
            pl.wait("vector", tbq)
            trb = pl.step("vector", lambda h: h.tensor_copy(s_scp[:, 0:1], ps[5][:, 0:1]))
            pl.wait("tensor", trb)
            tbq = pl.step("tensor", lambda h: h.matmul(
                ps[5][:, 0:1], ones1[:], s_sc[0:1, 1:2], start=True, stop=True))
            pl.wait("vector", tbq)
            trb = pl.step("vector", lambda h: h.tensor_copy(s_scp[:, 2:3], ps[5][:, 0:1]))

            pl.wait("vector", trb)
            tsv = pl.step("vector", lambda h: h.tensor_scalar(
                s_sv1[:, :], s_sw1[:, 0:32], s_scp[:, 2:3], None, ALU.mult))
            pl.wait("vector", tsv)
            tsv = pl.step("vector", lambda h: h.tensor_scalar(
                s_sv1[:, :], s_sv1[:, :], 1.0 / 127.0, None, ALU.mult))
            pl.wait("vector", tsv)
            tsv = pl.step("vector", lambda h: h.reciprocal(s_tmp[:, 68:100], s_sv1[:, :]))
            pl.wait("vector", tsv)
            tsv = pl.step("vector", lambda h: h.tensor_tensor(
                s_tmp[:, 100:132], s_b1[:, :], s_tmp[:, 68:100], op=ALU.mult))
            pl.wait("vector", tsv)
            tbqq = pl.quant(s_tmp[:, 100:132], s_tmp[:, 100:132],
                            QW(0)[:, 0:32], QWCI(0)[:, 0:32], QWCM(0)[:, 0:32],
                            QWD(0)[:, 0:32], None, relu=False)[0]
            pl.wait("vector", tbqq)
            tbqq = pl.step("vector", lambda h: h.tensor_tensor(
                s_bq1[:, :], s_tmp[:, 100:132], s_sv1[:, :], op=ALU.mult))

            pl.wait("tensor", pl.tok("cV"))
            pl.wait("tensor", pl.tok("cS"))
            tbb = None
            for tb in range(NTB):
                pl.emit("tensor", lambda h, tb=tb: h.matmul(
                    ps[tb][:, :], ones1[:], m_row(tb), start=True, stop=True))
                tbb = pl.step("tensor", lambda h, tb=tb: h.matmul(
                    ps[4 + tb][:, :], ones1[:], r_row(tb), start=True, stop=True))
            def QY(p):
                return V("qscr", 4096 * p, 1024)
            def QCI(p):
                return V("qscr", 4096 * p + 1024, 1024).bitcast(I32)
            def QCM(p):
                return V("qscr", 4096 * p + 2048, 1024)
            def QD(p):
                return V("qscr", 4096 * p + 3072, 1024)
            t_x2c2 = None
            b2_entry = [None, None]
            ld_guard = [None, None]
            for c in range(HC):
                par = c % 2
                pl.wait("sync", ld_guard[par])
                tld = pl.dma("sync", lambda h, c=c, par=par: h.dma_start(
                    out=V("x2str", 2048 * par, 2048),
                    in_=x2_sp[128 * c:128 * (c + 1), :]),
                    "dXR" if par == 0 else "dSU")
                pl.wait("vector", tld)
                pl.wait("vector", tbb)
                for hhalf in range(2):
                    hsl = slice(1024 * hhalf, 1024 * (hhalf + 1))
                    if b2_entry[par] is not None:
                        pl.wait("vector", b2_entry[par][0])
                    t = None
                    for tb in (2 * hhalf, 2 * hhalf + 1):
                        tbsl = slice(512 * tb, 512 * (tb + 1))
                        osl = slice(512 * (tb - 2 * hhalf), 512 * (tb - 2 * hhalf + 1))
                        t = pl.step("vector", lambda h, par=par, tb=tb, tbsl=tbsl, osl=osl:
                                    h.tensor_tensor(QY(par)[:, osl],
                                                    V("x2str", 2048 * par, 2048)[:, tbsl],
                                                    ps[tb][:, :], op=ALU.subtract))
                    pl.wait("vector", t)
                    for tb in (2 * hhalf, 2 * hhalf + 1):
                        osl = slice(512 * (tb - 2 * hhalf), 512 * (tb - 2 * hhalf + 1))
                        t = pl.step("vector", lambda h, par=par, tb=tb, osl=osl:
                                    h.tensor_tensor(QY(par)[:, osl], QY(par)[:, osl],
                                                    ps[4 + tb][:, :], op=ALU.mult))
                    pl.wait("vector", t)
                    b2_entry[par] = pl.quant(
                        QY(par), xi_v(c)[:, slice(1024 * hhalf, 1024 * (hhalf + 1))],
                        QY(par), QCI(par), QCM(par), QD(par),
                        s_scp[:, 0:1], relu=True, entry=b2_entry[par])
                ld_guard[par] = b2_entry[par][0]
                t_x2c2 = b2_entry[par][0]
            pl.wait("tensor", t_x2c2)

            _ffn_on = [KFFN >= 1]
            pl.wait("vector", tbqq)
            pl.wait("scalar", tbqq)
            pl.wait("tensor", t_wiT)
            pl.wait("vector", t_wiT)
            tmaxg = pl.step("gpsimd", lambda h: h.memset(s_tmp[:, 65:66], -1e30))
            pl.wait("vector", tmaxg)
            h_store = [None, None]
            t_hev = {}
            t_hred = {}
            t_hmax = tmaxg
            for ft in range(32 if KFFN >= 1 else 0):
                for tb in range(NTB):
                    it = ft * NTB + tb
                    par = it % 2
                    bank = it % 2
                    tbsl = slice(512 * tb, 512 * (tb + 1))
                    if it >= 2:
                        pl.wait("tensor", t_hev[it - 2])
                    tmf = None
                    for dc in range(HC):
                        tmf = pl.step("tensor", lambda h, dc=dc, ft=ft, tbsl=tbsl, bank=bank:
                                      h.matmul(ps[bank][:, :],
                                               wiT_v(dc)[:, 128 * ft:128 * (ft + 1)],
                                               xi_v(dc)[:, tbsl],
                                               start=(dc == 0), stop=(dc == HC - 1)))
                    pl.wait("scalar", tmf)
                    if h_store[par] is not None:
                        pl.wait("scalar", h_store[par])
                    if (it - 2) in t_hred:
                        pl.wait("scalar", t_hred[it - 2])
                    t_hev[it] = pl.step("scalar", lambda h, bank=bank, par=par, ft=ft:
                                        h.activation(hst_v(par), ps[bank][:, :], AF.Relu,
                                                     scale=s_sv1[:, ft:ft + 1],
                                                     bias=s_bq1[:, ft:ft + 1]))
                    pl.wait("vector", t_hev[it])
                    tr1 = pl.step("vector", lambda h, par=par: h.tensor_reduce(
                        s_tmp[:, 70 + par:71 + par], hst_v(par), axis=AX.X, op=ALU.max))
                    t_hred[it] = tr1
                    pl.wait("vector", tr1)
                    pl.wait("vector", t_hmax)
                    t_hmax = pl.step("vector", lambda h, par=par: h.tensor_tensor(
                        s_tmp[:, 65:66], s_tmp[:, 65:66], s_tmp[:, 70 + par:71 + par],
                        op=ALU.max))
                    pl.wait("gpsimd", t_hev[it])
                    h_store[par] = pl.dma("gpsimd", lambda h, ft=ft, tbsl=tbsl, par=par:
                                          h.dma_start(out=h_sp[128 * ft:128 * (ft + 1), tbsl],
                                                      in_=hst_v(par)),
                                          "dG" if par == 0 else "dH")

            if KFFN < 2:
                raise _StopEmit()
            pl.wait("vector", t_hmax)
            tcm = pl.step("vector", lambda h: h.tensor_copy(s_tmp[:, 64:65], s_tmp[:, 65:66]))
            pl.wait("tensor", tcm)
            ttr = pl.step("tensor", lambda h: h.transpose(
                ps[5][0:1, 0:128], s_tmp[:, 64:65], ident[:]))
            pl.wait("vector", ttr)
            tgm = pl.step("gpsimd", lambda h: h.memset(s_sc[:, :], -1e30))
            pl.wait("vector", tgm)
            tgm = pl.step("vector", lambda h: h.reduce_max(
                s_sc[0:1, 0:1], ps[5][0:1, 0:128], axis=AX.X))
            pl.wait("gpsimd", tgm)
            tdm = pl.dma("gpsimd", lambda h: h.dma_start(
                out=ar2_in[:, :], in_=s_sc[0:1, 0:8]), "dR")
            pl.wait("gpsimd", tdm)
            pl.emit("gpsimd", lambda h: h.collective_compute(
                "AllReduce", ALU.max, replica_groups=[list(range(8))],
                ins=[ar2_in.opt()], outs=[ar2_out.opt()]), inc="cc")
            pl.wait("gpsimd", pl.tok("cc"))
            tdm = pl.dma("gpsimd", lambda h: h.dma_start(
                out=s_sc[0:1, 8:16], in_=ar2_out[:, :]), "dR")
            pl.wait("vector", tdm)
            ts2 = pl.step("vector", lambda h: h.tensor_scalar(
                s_sc[0:1, 1:2], s_sc[0:1, 8:9], 1e-8, 1.0 / 255.0, ALU.max, ALU.mult))
            pl.wait("vector", ts2)
            ts2 = pl.step("vector", lambda h: h.reciprocal(s_sc[0:1, 2:3], s_sc[0:1, 1:2]))
            pl.wait("tensor", ts2)
            tbq = pl.step("tensor", lambda h: h.matmul(
                ps[5][:, 0:1], ones1[:], s_sc[0:1, 2:3], start=True, stop=True))
            pl.wait("vector", tbq)
            tr2b = pl.step("vector", lambda h: h.tensor_copy(s_scp[:, 1:2], ps[5][:, 0:1]))
            pl.wait("tensor", tr2b)
            tbq = pl.step("tensor", lambda h: h.matmul(
                ps[5][:, 0:1], ones1[:], s_sc[0:1, 1:2], start=True, stop=True))
            pl.wait("vector", tbq)
            tr2b = pl.step("vector", lambda h: h.tensor_copy(s_scp[:, 3:4], ps[5][:, 0:1]))

          except _StopEmit:
            _ffn_stopped = True
        if KSTAGE >= 3 and not _ffn_stopped:
          try:
            pl.barrier()

            def hi_v(fc):
                return V("hihalf", 1024 * fc, 1024, BF16)

            def hrl_v(p):
                return V("hrel", 2048 * p, 2048)

            def q2_y(p):
                return V("q2scr", 2048 * p, 512)

            def q2_ci(p):
                return V("q2scr", 2048 * p + 512, 512).bitcast(I32)

            def q2_cm1(p):
                return V("q2scr", 2048 * p + 1024, 512)

            def q2_d(p):
                return V("q2scr", 2048 * p + 1536, 512)

            def w2sv(p):
                return V("w2s", 512 * p, 512, BF16)

            def w2qstg():
                return V("q2scr", 4096, 1024)

            def w2ibf():
                return V("w2s", 1024, 1024, BF16)

            def outp_v(c):
                return V("outp", 2048 * c, 2048)

            if KFFN < 3:
                raise _StopEmit()

            def w2stg(p):
                return V("hihalf", 4096 * p, 4096)

            def w2ibf():
                return V("w2s", 0, 2048, BF16)

            def Q2Y(p):
                return V("q2scr", 4096 * p, 1024)

            def Q2CI(p):
                return V("q2scr", 4096 * p + 1024, 1024).bitcast(I32)

            def Q2CM(p):
                return V("q2scr", 4096 * p + 2048, 1024)

            def Q2D(p):
                return V("q2scr", 4096 * p + 3072, 1024)

            t_w2ld = [None, None]
            t_w2st = None
            w2_entry = [None, None]
            t_w2c = None
            for dc in range(HC):
                lpar = dc % 2
                pl.wait("sync", t_w2ld[lpar])
                tw = pl.dma("sync", lambda h, dc=dc, lpar=lpar: h.dma_start(
                    out=w2stg(lpar), in_=w2[128 * dc:128 * (dc + 1), :]), "dF0" if lpar == 0 else "dF1")
                pl.wait("vector", tw)
                trm = pl.step("vector", lambda h, dc=dc, lpar=lpar: h.tensor_reduce(
                    s_sw2[:, dc:dc + 1], w2stg(lpar), axis=AX.X, op=ALU.max,
                    apply_absolute_value=True))
                pl.wait("vector", trm)
                trm = pl.step("vector", lambda h, dc=dc, lpar=lpar: h.reciprocal(
                    s_tmp[:, 70 + 2 * lpar:71 + 2 * lpar], s_sw2[:, dc:dc + 1]))
                pl.wait("vector", trm)
                trm = pl.step("vector", lambda h, lpar=lpar: h.tensor_scalar(
                    s_tmp[:, 71 + 2 * lpar:72 + 2 * lpar],
                    s_tmp[:, 70 + 2 * lpar:71 + 2 * lpar], 127.0, None, ALU.mult))
                pl.wait("vector", trm)
                pl.wait("vector", t_w2st)
                for qtr in range(4):
                    qsl = slice(1024 * qtr, 1024 * (qtr + 1))
                    qpar = qtr % 2
                    w2_entry[qpar] = pl.quant(
                        w2stg(lpar)[:, qsl], w2ibf()[:, qsl],
                        Q2Y(qpar), Q2CI(qpar), Q2CM(qpar), Q2D(qpar),
                        s_tmp[:, 71 + 2 * lpar:72 + 2 * lpar], relu=False,
                        entry=w2_entry[qpar])
                t_w2ld[lpar] = w2_entry[1][0]
                pl.wait("gpsimd", w2_entry[0][0])
                pl.wait("gpsimd", w2_entry[1][0])
                t_w2st = pl.dma("gpsimd", lambda h, dc=dc: h.dma_start(
                    out=w2i_sp[128 * dc:128 * (dc + 1), :], in_=w2ibf()), "dWS")
                t_w2c = w2_entry[1][0]

            pl.wait("vector", tr2b)
            tsv = pl.step("vector", lambda h: h.tensor_scalar(
                s_sv2[:, :], s_sw2[:, :], s_scp[:, 3:4], None, ALU.mult))
            pl.wait("vector", tsv)
            tsv = pl.step("vector", lambda h: h.tensor_scalar(
                s_sv2[:, :], s_sv2[:, :], 1.0 / 127.0, None, ALU.mult))
            pl.wait("vector", tsv)
            tsv = pl.step("vector", lambda h: h.reciprocal(s_tmp[:, 80:88], s_sv2[:, :]))
            pl.wait("vector", tsv)
            tsv = pl.step("vector", lambda h: h.tensor_tensor(
                s_tmp[:, 88:96], s_b2[:, :], s_tmp[:, 80:88], op=ALU.mult))
            pl.wait("vector", tsv)
            tbq2 = pl.quant(s_tmp[:, 88:96], s_tmp[:, 88:96],
                            Q2Y(0)[:, 0:8], Q2CI(0)[:, 0:8], Q2CM(0)[:, 0:8],
                            Q2D(0)[:, 0:8], None, relu=False)[0]
            pl.wait("vector", tbq2)
            tbq2 = pl.step("vector", lambda h: h.tensor_tensor(
                s_bq2[:, :], s_tmp[:, 88:96], s_sv2[:, :], op=ALU.mult))

            if KFFN < 4:
                raise _StopEmit()
            t_outev = None
            t_ln4 = None
            def hi_q(q, fc):
                return V("hihalf", 8192 * (q % 2) + 1024 * fc, 1024, BF16)

            hq_entry = [None, None]
            hq_ld = [None, None]

            def emit_hquant_quarter(q, gate_tok):
                tqh = gate_tok
                for fc in range(8):
                    fg = 8 * q + fc
                    par = fc % 2
                    pl.wait("sync", hq_ld[par])
                    pl.wait("sync", gate_tok)
                    th = pl.dma("sync", lambda h, fg=fg, par=par: h.dma_start(
                        out=hrl_v(par), in_=h_sp[128 * fg:128 * (fg + 1), :]),
                        "dF0" if par == 0 else "dF1")
                    pl.wait("vector", th)
                    for hh in range(2):
                        hsl = slice(1024 * hh, 1024 * (hh + 1))
                        hq_entry[par] = pl.quant(
                            hrl_v(par)[:, hsl], hi_q(q, fc)[:, hsl],
                            Q2Y(par), Q2CI(par), Q2CM(par), Q2D(par),
                            s_scp[:, 1:2], relu=True, entry=hq_entry[par])
                    hq_ld[par] = hq_entry[par][0]
                    tqh = hq_entry[par][0]
                return tqh

            pl.wait("sync", pl.tok("dWS"))
            tqh = {}
            tqh[0] = emit_hquant_quarter(0, t_w2c)
            hq_cons = {}
            t_w2l = [None, None]
            for q in range(4):
                pl.wait("tensor", tqh[q])
                for tb in range(NTB):
                    tbsl = slice(512 * tb, 512 * (tb + 1))
                    if q > 0 or tb > 0:
                        pl.wait("tensor", t_outev)
                    tmf = None
                    for fc in range(8):
                        fg = 8 * q + fc
                        wpar = fc % 2
                        pl.wait("sync", t_w2l[wpar])
                        twl = pl.dma("sync", lambda h, fg=fg, wpar=wpar: h.dma_start(
                            out=w2sv(wpar)[:, :],
                            in_=w2i_sp[:, 128 * fg:128 * (fg + 1)], transpose=True),
                            "dV0" if wpar == 0 else "dV1")
                        pl.wait("tensor", twl)
                        for dt in range(HC):
                            tmf = pl.step("tensor",
                                          lambda h, q=q, fc=fc, dt=dt, tbsl=tbsl, wpar=wpar:
                                          h.matmul(ps[dt][:, :],
                                                   w2sv(wpar)[:, 128 * dt:128 * (dt + 1)],
                                                   hi_q(q, fc)[:, tbsl],
                                                   start=(fc == 0), stop=(fc == 7)))
                        t_w2l[wpar] = tmf
                    if tb == 0 and q < 3:
                        gate = hq_cons.get(q - 1, tqh[q])
                        tqh[q + 1] = emit_hquant_quarter(q + 1, gate)
                    pl.wait("vector", tmf)
                    pl.wait("vector", t_outev)
                    for dt in range(HC):
                        if q == 0:
                            t_outev = pl.step("vector", lambda h, dt=dt, tbsl=tbsl:
                                              h.tensor_copy(outp_v(dt)[:, tbsl],
                                                            ps[dt][:, :]))
                        else:
                            t_outev = pl.step("vector", lambda h, dt=dt, tbsl=tbsl:
                                              h.tensor_tensor(outp_v(dt)[:, tbsl],
                                                              outp_v(dt)[:, tbsl],
                                                              ps[dt][:, :], op=ALU.add))
                hq_cons[q] = tmf

            pl.wait("vector", tbq2)
            pl.wait("vector", t_outev)
            t = None
            for dt in range(HC):
                t = pl.step("vector", lambda h, dt=dt: h.tensor_scalar(
                    outp_v(dt), outp_v(dt), s_sv2[:, dt:dt + 1], s_bq2[:, dt:dt + 1],
                    ALU.mult, ALU.add))
            pl.wait("vector", t)
            pl.wait("tensor", t)

            def x2r_v(c):
                return V("hrel", 512 * c, 512)

            def ystg_v(p):
                return V("q2scr", 4096 + 1024 * p, 1024)

            t_yst = [None, None]
            y_bank_hist = [None, None]
            t_x2r = None
            for tb in range(NTB):
                tbsl = slice(512 * tb, 512 * (tb + 1))
                pl.wait("sync", t_x2r)
                tld = None
                for c in range(HC):
                    tld = pl.dma("sync", lambda h, c=c, tbsl=tbsl: h.dma_start(
                        out=x2r_v(c), in_=x2_sp[128 * c:128 * (c + 1), tbsl]), "dAR")
                pl.wait("vector", tld)

                def ln4_out(tb2, sl2, tbsl=tbsl):
                    t = None
                    for c in range(HC):
                        t = pl.step("vector", lambda h, c=c: h.tensor_tensor(
                            V("q2scr", 512 * c, 512), outp_v(c)[:, tbsl], ps[6][:, :],
                            op=ALU.subtract))
                    pl.wait("vector", t)
                    for c in range(HC):
                        t = pl.step("vector", lambda h, c=c: h.tensor_tensor(
                            V("q2scr", 512 * c, 512), V("q2scr", 512 * c, 512),
                            ps[7][:, :], op=ALU.mult))
                    pl.wait("vector", t)
                    vt = None
                    for c in range(HC):
                        vt = pl.step("vector", lambda h, c=c: h.tensor_tensor(
                            x2r_v(c), V("q2scr", 512 * c, 512), x2r_v(c), op=ALU.add))
                    return vt

                emit_ln(HC, lambda c, tbsl=tbsl: outp_v(c)[:, tbsl], "q2scr",
                        ln4_out, [0])
                t_ln4 = pl.tok("cV")
                for tl in range(4):
                    par = tl % 2
                    pl.wait("tensor", t_ln4)
                    pl.wait("tensor", t_yst[par])
                    tq = None
                    for c in range(HC):
                        bank = (c % 2) + 2
                        if y_bank_hist[c % 2] is not None:
                            pl.wait("tensor", y_bank_hist[c % 2])
                        tmm = pl.step("tensor", lambda h, c=c, tl=tl, bank=bank:
                                      h.transpose(ps[bank][:, 0:128],
                                                  x2r_v(c)[:, 128 * tl:128 * (tl + 1)],
                                                  ident[:]))
                        pl.wait("vector", tmm)
                        tq = pl.step("vector", lambda h, c=c, par=par, bank=bank:
                                     h.tensor_copy(ystg_v(par)[:, 128 * c:128 * (c + 1)],
                                                   ps[bank][:, 0:128]))
                        y_bank_hist[c % 2] = tq
                    pl.wait("gpsimd", tq)
                    r0 = 512 * tb + 128 * tl
                    t_yst[par] = pl.dma("gpsimd", lambda h, r0=r0, par=par: h.dma_start(
                        out=y[r0:r0 + 128, :], in_=ystg_v(par)),
                        "dY0" if par == 0 else "dY1")
                t_x2r = pl.tokT()
            pl.wait("gpsimd", pl.tok("dY0"))
            pl.wait("gpsimd", pl.tok("dY1"))
          except _StopEmit:
            _ffn_stopped = True


        if KSTAGE >= 3 and _ffn_stopped:
            pl.barrier()
            tms = pl.step("gpsimd", lambda h: h.memset(arena[:, 0:1024], 0.0))
            pl.wait("gpsimd", tms)
            for tt in range(NT):
                tst = pl.dma("gpsimd", lambda h, tt=tt: h.dma_start(
                    out=y[128 * tt:128 * (tt + 1), :], in_=arena[:, 0:1024]), "dG")
            pl.wait("gpsimd", tst)
        if KSTAGE < 3:
            pl.barrier()
            tms = pl.step("gpsimd", lambda h: h.memset(
                V("x" if KSTAGE < 2 else "avl", 0, 1024), 0.0))
            pl.wait("gpsimd", tms)
            tst = None
            for tt in range(NT):
                tst = pl.dma("gpsimd", lambda h, tt=tt: h.dma_start(
                    out=y[128 * tt:128 * (tt + 1), :],
                    in_=V("x" if KSTAGE < 2 else "avl", 0, 1024)), "dG")
            pl.wait("gpsimd", tst)
        if KDBG:
            pl.wait("gpsimd", pl.tok("cc"))
            pl.dma("gpsimd", lambda h: h.dma_start(out=dbg_ag[:, :], in_=ag_in[:, :]), "dG")
        pl.wait("gpsimd", pl.tok("cc"))
        pl.wait("gpsimd", pl.tok("dG"))
        pl.wait("gpsimd", pl.tok("dH"))
        pl.wait("gpsimd", pl.tok("dK"))
        pl.wait("gpsimd", pl.tok("dW"))

        pl.realize(block, sems)
    return nc


def kernel(L0_attn_inputs, L0_attn_W_uv, L0_attn_b_uv, L0_attn_gamma, L0_attn_beta,
           L0_attn_W_out, L0_attn_b_out, L0_ffn_w1, L0_ffn_b1, L0_ffn_w2, L0_ffn_b2):
    x = np.ascontiguousarray(np.asarray(L0_attn_inputs, dtype=np.float32))
    wuv = np.ascontiguousarray(np.asarray(L0_attn_W_uv, dtype=np.float32))
    buv = np.asarray(L0_attn_b_uv, dtype=np.float32)
    gam = np.asarray(L0_attn_gamma, dtype=np.float32)
    bet = np.asarray(L0_attn_beta, dtype=np.float32)
    wout = np.ascontiguousarray(np.asarray(L0_attn_W_out, dtype=np.float32))
    bout = np.asarray(L0_attn_b_out, dtype=np.float32)
    w1 = np.ascontiguousarray(np.asarray(L0_ffn_w1, dtype=np.float32))
    b1 = np.asarray(L0_ffn_b1, dtype=np.float32)
    w2 = np.ascontiguousarray(np.asarray(L0_ffn_w2, dtype=np.float32))
    b2 = np.asarray(L0_ffn_b2, dtype=np.float32)

    shared = dict(
        wuv=wuv,
        buv_row=buv.reshape(1, -1),
        buv_base=buv[2 * E:].reshape(SD, 1),
        gamT=np.ascontiguousarray(gam.T),
        betT=np.ascontiguousarray(bet.T),
        wout=wout,
        bout_pt=np.ascontiguousarray(bout.reshape(HC, 128).T),
        w1=w1,
        w1T=np.ascontiguousarray(w1.T),
        b1_pt=np.ascontiguousarray(b1.reshape(FF // 128, 128).T),
        w2=w2,
        w2T=np.ascontiguousarray(w2.T),
        b2_pt=np.ascontiguousarray(b2.reshape(HC, 128).T),
    )
    in_maps = []
    for c in range(NCORE_IDS):
        b, slc = c // 4, c % 4
        xs = x[b, TC * slc:TC * (slc + 1), :]
        m = dict(shared)
        m["xT"] = np.ascontiguousarray(xs.T)
        in_maps.append(m)

    nc = build_nc()
    res = run_bass_kernel_spmd(nc, in_maps, core_ids=list(range(8)))
    out = np.empty((B, S, H), dtype=np.float32)
    for c in range(8):
        b, slc = c // 4, c % 4
        out[b, TC * slc:TC * (slc + 1), :] = np.asarray(res.results[c]["y"])
    if KDBG:
        kernel.last_results = res.results
    return out




# version 09
# speedup vs baseline: 1.0052x; 1.0052x over previous
import os
import contextlib
import numpy as np
import ml_dtypes
import concourse.bass as bass
import concourse.mybir as mybir
from concourse.bass_utils import run_bass_kernel_spmd

F32 = mybir.dt.float32
BF16 = mybir.dt.bfloat16
I32 = mybir.dt.int32
AF = mybir.ActivationFunctionType
ALU = mybir.AluOpType
AX = mybir.AxisListType

B, S, H = 2, 8192, 1024
TC = 2048
E = 2048
SD = 128
FF = 4096
HC = H // 128
NT = TC // 128
NTB = TC // 512
MCN = S // 128
EPS_LN = 1e-5
NCORE_IDS = 8

KSTAGE = int(os.environ.get("KSTAGE", "99"))
KDBG = int(os.environ.get("KDBG", "0"))
KFFN = int(os.environ.get("KFFN", "9"))

ENGINES = ("tensor", "vector", "scalar", "gpsimd", "sync")

ARENA_COLS = 49152
_BUFS = {
    "x":       (0,     16384, 1, 1),
    "xn_hi":   (16384,  8192, 1, 2),
    "xn_lo":   (24576,  8192, 1, 2),
    "xsq":     (32768,  4096, 1, 1),
    "wblk":    (0,      8192, 2, 2),
    "stage":   (8192,   2048, 2, 2),
    "base":    (10240,  2048, 2, 2),
    "kown":    (12288,  2048, 2, 2),
    "wblk2":   (36864,  8192, 2, 2),
    "q_hi":    (33792,  1024, 2, 3),
    "q_lo":    (34816,  1024, 2, 3),
    "sq_hi":   (0,      8192, 3, 3),
    "sq_lo":   (8192,   8192, 3, 3),
    "kall_hi": (16384,  4096, 3, 3),
    "kall_lo": (20480,  4096, 3, 3),
    "vbuf":    (24576,  4096, 3, 3),
    "sqevac":  (28672,  1024, 3, 3),
    "avstage": (29696,  4096, 3, 3),
    "avl":     (0,      4096, 4, 4),
    "sul":     (40960,  4096, 4, 4),
    "gsc":     (4096,   2048, 4, 4),
    "gbf":     (6144,   2048, 4, 4),
    "gT":      (8192,   4096, 4, 4),
    "outT":    (12288,  4096, 4, 4),
    "xTrel":   (16384,  4096, 4, 4),
    "osq":     (20480,  4096, 4, 4),
    "woutb":   (24576, 16384, 4, 4),
    "xi":      (0,      8192, 5, 5),
    "z":       (8192,   2048, 5, 5),
    "qscr":    (10240,  8192, 5, 5),
    "wf":      (18432, 16384, 5, 5),
    "hstage":  (34816,  2048, 5, 5),
    "zsq":     (36864,  4096, 5, 5),
    "x2str":   (40960,  4096, 5, 5),
    "w1str":   (45056,  2048, 5, 5),
    "hihalf":  (0,     16384, 6, 7),
    "hrel":    (16384,  4096, 6, 7),
    "q2scr":   (20480,  8192, 6, 7),
    "w2s":     (28672,  3072, 6, 7),
    "outp":    (32768, 16384, 6, 7),
}
for _i, (_n1, (_o1, _c1, _a1, _b1)) in enumerate(list(_BUFS.items())):
    assert _o1 + _c1 <= ARENA_COLS, _n1
    for _n2, (_o2, _c2, _a2, _b2) in list(_BUFS.items())[_i + 1:]:
        if _a1 <= _b2 and _a2 <= _b1:
            assert _o1 + _c1 <= _o2 or _o2 + _c2 <= _o1, f"overlap {_n1}/{_n2}"


class KB:

    def __init__(self, nc):
        self.nc = nc
        self.ops = {e: [] for e in ENGINES}
        self.cnt = {"cT": 0, "cV": 0, "cS": 0, "cG": 0, "dW": 0, "dG": 0,
                    "dH": 0, "dK": 0, "dB": 0, "dV0": 0, "dV1": 0, "dA0": 0,
                    "dA1": 0, "dAR": 0, "dSU": 0, "dXR": 0, "dXS": 0, "dWO": 0,
                    "dF0": 0, "dF1": 0, "dWS": 0, "dY0": 0, "dY1": 0, "dR": 0,
                    "dKL": 0, "cc": 0}
        self.eng_sem = {"tensor": "cT", "vector": "cV", "scalar": "cS", "gpsimd": "cG"}
        self.waited = {}

    def emit(self, eng, fn, inc=None, add=1):
        tok = None
        if inc is not None:
            self.cnt[inc] += add
            tok = (inc, self.cnt[inc])
        self.ops[eng].append(("emit", fn, inc, add))
        return tok

    def step(self, eng, fn):
        return self.emit(eng, fn, inc=self.eng_sem[eng])

    def dma(self, eng, fn, q):
        return self.emit(eng, fn, inc=q, add=16)

    def wait(self, eng, tok):
        if tok is None:
            return
        sem, val = tok
        key = (eng, sem)
        if self.waited.get(key, -1) >= val:
            return
        self.waited[key] = val
        self.ops[eng].append(("wait", sem, val, None))

    def tokT(self):
        return ("cT", self.cnt["cT"])

    def tok(self, s):
        return (s, self.cnt[s])

    def quant(self, src_ap, out_ap, y_ap, ci_ap, cm1_ap, d_ap, recip_ap, relu,
              entry=None):
        if entry is None:
            self.wait("vector", self.tok("cV"))
            self.wait("scalar", self.tok("cV"))
        else:
            vt, st0 = entry
            self.wait("vector", vt)
            self.wait("scalar", vt)
            if st0 is not None:
                self.wait("vector", st0)
                self.wait("scalar", st0)
        rc = 1.0 if recip_ap is None else recip_ap
        if relu:
            t = self.step("vector", lambda h: h.tensor_scalar(
                y_ap, src_ap, rc, 0.0, ALU.mult, ALU.max))
        else:
            t = self.step("vector", lambda h: h.tensor_scalar(
                y_ap, src_ap, rc, None, ALU.mult))
        self.wait("scalar", t)
        t = self.step("scalar", lambda h: h.activation(ci_ap, y_ap, AF.Copy))
        self.wait("scalar", t)
        stok = self.step("scalar", lambda h: h.activation(cm1_ap, ci_ap, AF.Copy, bias=-1.0))
        self.wait("vector", stok)
        t = self.step("vector", lambda h: h.tensor_tensor(
            d_ap, y_ap, cm1_ap, op=ALU.subtract))
        self.wait("vector", t)
        t = self.step("vector", lambda h: h.scalar_tensor_tensor(
            y_ap, d_ap, 1.5, cm1_ap, op0=ALU.is_gt, op1=ALU.add))
        self.wait("vector", t)
        vtok = self.step("vector", lambda h: h.scalar_tensor_tensor(
            out_ap, d_ap, 0.5, y_ap, op0=ALU.is_ge, op1=ALU.add))
        return (vtok, stok)

    def split(self, src_ap, hi_ap, lo_ap, eng="vector"):
        t = self.step(eng, lambda h: h.tensor_copy(hi_ap, src_ap))
        self.wait(eng, t)
        return self.step(eng, lambda h: h.tensor_tensor(lo_ap, src_ap, hi_ap,
                                                        op=ALU.subtract))

    def barrier(self, engines=("tensor", "vector", "scalar", "gpsimd", "sync")):
        snap = dict(self.cnt)
        for e in engines:
            own = self.eng_sem.get(e)
            for sem, val in snap.items():
                if sem == own or val == 0:
                    continue
                self.wait(e, (sem, val))

    def realize(self, block, sems):
        def runner(lst):
            def run(h):
                for item in lst:
                    if item[0] == "wait":
                        h.wait_ge(sems[item[1]], item[2])
                    else:
                        inst = item[1](h)
                        if item[2] is not None:
                            inst.then_inc(sems[item[2]], item[3])
            return run

        block.tensor(runner(self.ops["tensor"]))
        block.vector(runner(self.ops["vector"]))
        block.scalar(runner(self.ops["scalar"]))
        block.gpsimd(runner(self.ops["gpsimd"]))
        block.sync(runner(self.ops["sync"]))


def build_nc():
    nc = bass.Bass("TRN2", target_bir_lowering=False, debug=False)

    xT = nc.dram_tensor("xT", [H, TC], F32, kind="ExternalInput").ap()
    wuv = nc.dram_tensor("wuv", [H, 2 * E + SD], F32, kind="ExternalInput").ap()
    buv_row = nc.dram_tensor("buv_row", [1, 2 * E + SD], F32, kind="ExternalInput").ap()
    buv_base = nc.dram_tensor("buv_base", [SD, 1], F32, kind="ExternalInput").ap()
    gamT = nc.dram_tensor("gamT", [SD, 2], F32, kind="ExternalInput").ap()
    betT = nc.dram_tensor("betT", [SD, 2], F32, kind="ExternalInput").ap()
    wout = nc.dram_tensor("wout", [E, H], F32, kind="ExternalInput").ap()
    bout_pt = nc.dram_tensor("bout_pt", [128, HC], F32, kind="ExternalInput").ap()
    w1 = nc.dram_tensor("w1", [FF, H], F32, kind="ExternalInput").ap()
    w1T = nc.dram_tensor("w1T", [H, FF], F32, kind="ExternalInput").ap()
    b1_pt = nc.dram_tensor("b1_pt", [128, FF // 128], F32, kind="ExternalInput").ap()
    w2 = nc.dram_tensor("w2", [H, FF], F32, kind="ExternalInput").ap()
    w2T = nc.dram_tensor("w2T", [FF, H], F32, kind="ExternalInput").ap()
    b2_pt = nc.dram_tensor("b2_pt", [128, HC], F32, kind="ExternalInput").ap()
    y = nc.dram_tensor("y", [TC, H], F32, kind="ExternalOutput").ap()

    def scratch(name, shape, dt):
        if KDBG:
            return nc.dram_tensor(name, shape, dt, kind="ExternalOutput").ap()
        return nc.dram_tensor(name, shape, dt).ap()

    AGW = 2 * E + 2 * SD
    ag_in = nc.dram_tensor("ag_in", [AGW, TC], BF16).ap()
    ag_out = nc.dram_tensor("ag_out", [4 * AGW, TC], BF16).ap()
    dbg_ag = nc.dram_tensor("dbg_ag", [AGW, TC], BF16, kind="ExternalOutput").ap() if KDBG else None
    su_sp = scratch("su_sp", [2 * TC, E], BF16)
    av_sp = scratch("av_sp", [TC, E], F32)
    h_sp = scratch("h_sp", [FF, TC], F32)
    x2_sp = scratch("x2_sp", [H, TC], F32)
    w2i_sp = nc.dram_tensor("w2i_sp", [H, FF], BF16).ap()
    ar1_in = nc.dram_tensor("ar1_in", [1, 8], F32).ap()
    ar1_out = nc.dram_tensor("ar1_out", [1, 8], F32).ap()
    ar2_in = nc.dram_tensor("ar2_in", [1, 8], F32).ap()
    ar2_out = nc.dram_tensor("ar2_out", [1, 8], F32).ap()

    pl = KB(nc)

    with contextlib.ExitStack() as es:
        En = es.enter_context
        arena = En(nc.sbuf_tensor("arena", [128, ARENA_COLS], F32))

        def V(name, off, n, dt=F32):
            o, ctot, _, _ = _BUFS[name]
            assert off + n <= ctot, f"{name}+{off}+{n}"
            ap = arena[:, o + off:o + off + n]
            return ap if dt == F32 else ap.bitcast(dt)

        ident = En(nc.sbuf_tensor("ident", [128, 128], F32))
        ident_bf = En(nc.sbuf_tensor("ident_bf", [128, 128], BF16))
        ones1 = En(nc.sbuf_tensor("ones1", [1, 128], F32))
        onesK = En(nc.sbuf_tensor("onesK", [128, 1], F32))
        ones1_bf = En(nc.sbuf_tensor("ones1_bf", [1, 128], BF16))
        s_buvrow = En(nc.sbuf_tensor("s_buvrow", [1, 512], F32))
        s_brh = En(nc.sbuf_tensor("s_brh", [1, 512], BF16))
        s_brl = En(nc.sbuf_tensor("s_brl", [1, 512], BF16))
        s_buvb = En(nc.sbuf_tensor("s_buvb", [128, 1], F32))
        s_gb = En(nc.sbuf_tensor("s_gb", [128, 8], F32))
        s_bout = En(nc.sbuf_tensor("s_bout", [128, HC], F32))
        s_b1 = En(nc.sbuf_tensor("s_b1", [128, FF // 128], F32))
        s_b2 = En(nc.sbuf_tensor("s_b2", [128, HC], F32))
        rowM = En(nc.sbuf_tensor("rowM", [1, 512], F32))
        rowV = En(nc.sbuf_tensor("rowV", [1, 512], F32))
        rowT = En(nc.sbuf_tensor("rowT", [1, 512], F32))
        rowR = En(nc.sbuf_tensor("rowR", [1, 512], F32))
        s_sc = En(nc.sbuf_tensor("s_sc", [1, 16], F32))
        s_scp = En(nc.sbuf_tensor("s_scp", [128, 8], F32))
        s_sw1 = En(nc.sbuf_tensor("s_sw1", [128, FF // 128], F32))
        s_sw2 = En(nc.sbuf_tensor("s_sw2", [128, HC], F32))
        s_bq1 = En(nc.sbuf_tensor("s_bq1", [128, FF // 128], F32))
        s_bq2 = En(nc.sbuf_tensor("s_bq2", [128, HC], F32))
        s_sv1 = En(nc.sbuf_tensor("s_sv1", [128, 32], F32))
        s_sv2 = En(nc.sbuf_tensor("s_sv2", [128, 8], F32))
        s_tmp = En(nc.sbuf_tensor("s_tmp", [128, 160], F32))
        s_eps = En(nc.sbuf_tensor("s_eps", [1, 1], F32))
        ps = [En(nc.psum_tensor(f"ps{i}", [128, 512], F32)) for i in range(8)]

        sems = {}
        for s in ("cT", "cV", "cS", "cG", "dW", "dG", "dH", "dK", "dB", "dV0",
                  "dV1", "dA0", "dA1", "dAR", "dSU", "dXR", "dXS", "dWO",
                  "dF0", "dF1", "dWS", "dY0", "dY1", "dR", "dKL", "cc"):
            sems[s] = En(nc.semaphore(s))

        block = En(nc.Block())

        def gchain(fn):
            pl.wait("gpsimd", pl.tok("cG"))
            return pl.step("gpsimd", fn)

        gchain(lambda h: h.memset(ident[:], 0.0))
        gchain(lambda h: h.affine_select(
            out=ident[:], in_=ident[:], compare_op=ALU.not_equal,
            fill=1.0, base=0, pattern=[[-1, 128]], channel_multiplier=1))
        gchain(lambda h: h.memset(ident_bf[:], 0.0))
        gchain(lambda h: h.affine_select(
            out=ident_bf[:], in_=ident_bf[:], compare_op=ALU.not_equal,
            fill=1.0, base=0, pattern=[[-1, 128]], channel_multiplier=1))
        gchain(lambda h: h.memset(ones1[:], 1.0))
        gchain(lambda h: h.memset(onesK[:], 1.0))
        gchain(lambda h: h.memset(ones1_bf[:], 1.0))
        gchain(lambda h: h.memset(s_eps[:], EPS_LN))

        x_toks = [pl.dma("sync", lambda h, c=c: h.dma_start(
            out=V("x", 2048 * c, 2048), in_=xT[128 * c:128 * (c + 1), :]), "dW")
            for c in range(HC)]
        for dst, src in [(s_buvb[:], buv_base[:, :]),
                         (s_bout[:], bout_pt[:, :]), (s_b1[:], b1_pt[:, :]),
                         (s_b2[:], b2_pt[:, :]), (s_gb[:, 0:2], gamT[:, :]),
                         (s_gb[:, 2:4], betT[:, :])]:
            t_sm = pl.dma("sync", lambda h, d=dst, s=src: h.dma_start(out=d, in_=s), "dW")
        pl.wait("vector", t_sm)
        pl.emit("vector", lambda h: h.tensor_scalar(
            s_gb[:, 4:5], s_gb[:, 0:1], 1.0 / 8192.0, None, ALU.mult))
        t_gbs = pl.step("vector", lambda h: h.tensor_scalar(
            s_gb[:, 5:6], s_gb[:, 2:3], 1.0 / 8192.0, None, ALU.mult))

        def emit_ln(nchunk, src_of, sq_name, out_cb, tb_range):
            inv_n = 1.0 / (128 * nchunk)
            prev_vt = None
            for tb in tb_range:
                sl = slice(512 * tb, 512 * (tb + 1))
                pl.wait("vector", prev_vt)
                tv = None
                for c in range(nchunk):
                    xv = src_of(c)
                    xsq = V(sq_name, 512 * c, 512)
                    tv = pl.step("vector", lambda h, xv=xv, sl=sl, xsq=xsq:
                                 h.tensor_tensor(xsq, xv[:, sl], xv[:, sl], op=ALU.mult))
                pl.wait("tensor", tv)
                tm = None
                for c in range(nchunk):
                    xv = src_of(c)
                    xsq = V(sq_name, 512 * c, 512)
                    pl.emit("tensor", lambda h, xv=xv, sl=sl, c=c:
                            h.matmul(ps[6][0:1, :], onesK[:], xv[:, sl],
                                     start=(c == 0), stop=(c == nchunk - 1)))
                    tm = pl.step("tensor", lambda h, xsq=xsq, c=c:
                                 h.matmul(ps[7][0:1, :], onesK[:], xsq,
                                          start=(c == 0), stop=(c == nchunk - 1)))
                pl.wait("vector", tm)
                t1 = pl.step("vector", lambda h: h.tensor_scalar(
                    rowM[:], ps[6][0:1, :], inv_n, None, ALU.mult))
                pl.step("vector", lambda h: h.tensor_scalar(
                    rowV[:], ps[7][0:1, :], inv_n, None, ALU.mult))
                pl.wait("vector", t1)
                t3 = pl.step("vector", lambda h: h.tensor_tensor(
                    rowT[:], rowM[:], rowM[:], op=ALU.mult))
                pl.wait("vector", t3)
                tvv = pl.step("vector", lambda h: h.tensor_tensor(
                    rowV[:], rowV[:], rowT[:], op=ALU.subtract))
                pl.wait("scalar", tvv)
                tss = pl.step("scalar", lambda h: h.activation(
                    rowT[:], rowV[:], AF.Sqrt, bias=s_eps[:]))
                pl.wait("vector", tss)
                trr = pl.step("vector", lambda h: h.reciprocal(rowR[:], rowT[:]))
                pl.wait("tensor", trr)
                pl.emit("tensor", lambda h: h.matmul(
                    ps[6][:, :], ones1[:], rowM[:], start=True, stop=True))
                tbb = pl.step("tensor", lambda h: h.matmul(
                    ps[7][:, :], ones1[:], rowR[:], start=True, stop=True))
                pl.wait("vector", tbb)
                vt = out_cb(tb, sl)
                prev_vt = vt
                pl.wait("tensor", vt)
                pl.wait("scalar", vt)

        pl.barrier()
        pl.wait("vector", t_sm)
        pl.wait("vector", t_gbs)

        def ln1_out(tb, sl):
            t = None
            for c in range(HC):
                xv, tmp = V("x", 2048 * c, 2048), V("xsq", 512 * c, 512)
                t = pl.step("vector", lambda h, xv=xv, tmp=tmp, sl=sl: h.tensor_tensor(
                    tmp, xv[:, sl], ps[6][:, :], op=ALU.subtract))
            pl.wait("vector", t)
            for c in range(HC):
                tmp = V("xsq", 512 * c, 512)
                t = pl.step("vector", lambda h, tmp=tmp: h.tensor_tensor(
                    tmp, tmp, ps[7][:, :], op=ALU.mult))
            pl.wait("vector", t)
            for c in range(HC):
                tmp, hi = V("xsq", 512 * c, 512), V("xn_hi", 1024 * c, 1024, BF16)
                t = pl.step("vector", lambda h, hi=hi, sl=sl, tmp=tmp:
                            h.tensor_copy(hi[:, sl], tmp))
            pl.wait("vector", t)
            vt = None
            for c in range(HC):
                tmp = V("xsq", 512 * c, 512)
                hi = V("xn_hi", 1024 * c, 1024, BF16)
                lo = V("xn_lo", 1024 * c, 1024, BF16)
                vt = pl.step("vector", lambda h, hi=hi, lo=lo, sl=sl, tmp=tmp:
                             h.tensor_tensor(lo[:, sl], tmp, hi[:, sl], op=ALU.subtract))
            return vt

        emit_ln(HC, lambda c: V("x", 2048 * c, 2048), "xsq", ln1_out, range(NTB))
        t_xn = pl.tok("cV")

        pl.barrier()

        def wv_f32(c, wp=0):
            return V("wblk" if wp == 0 else "wblk2", 512 * c, 512)

        def wv_hi(c, wp=0):
            return V("wblk" if wp == 0 else "wblk2", 4096 + 256 * c, 256, BF16)

        def wv_lo(c, wp=0):
            return V("wblk" if wp == 0 else "wblk2", 6144 + 256 * c, 256, BF16)

        def load_wblk(col0, ncols, guards, wp=0, wq="dW"):
            toks = []
            for g in (guards or []):
                pl.wait("sync", g)
            for c in range(HC):
                toks.append(pl.dma("sync", lambda h, c=c, col0=col0, ncols=ncols, wp=wp:
                            h.dma_start(out=wv_f32(c, wp)[:, 0:ncols],
                                        in_=wuv[128 * c:128 * (c + 1), col0:col0 + ncols]),
                            wq))
            pl.wait("vector", toks[-1])
            for g in (guards or []):
                pl.wait("vector", g)
            t = None
            for c in range(HC):
                t = pl.step("vector", lambda h, c=c, n=ncols, wp=wp: h.tensor_copy(
                    wv_hi(c, wp)[:, 0:n], wv_f32(c, wp)[:, 0:n]))
            pl.wait("vector", t)
            last = None
            for c in range(HC):
                last = pl.step("vector", lambda h, c=c, n=ncols, wp=wp: h.tensor_tensor(
                    wv_lo(c, wp)[:, 0:n], wv_f32(c, wp)[:, 0:n], wv_hi(c, wp)[:, 0:n],
                    op=ALU.subtract))
            return last

        t_w = load_wblk(2 * E, SD, None)
        pl.wait("tensor", t_w)
        pl.wait("tensor", t_xn)
        t_base_mm = None
        for tb in range(NTB):
            sl = slice(512 * tb, 512 * (tb + 1))
            for c in range(HC):
                whi, wlo = wv_hi(c)[:, 0:128], wv_lo(c)[:, 0:128]
                xhi = V("xn_hi", 1024 * c, 1024, BF16)
                xlo = V("xn_lo", 1024 * c, 1024, BF16)
                pl.emit("tensor", lambda h, whi=whi, xhi=xhi, sl=sl, st=(c == 0):
                        h.matmul(ps[0][:, :], whi, xhi[:, sl], start=st, stop=False))
                pl.emit("tensor", lambda h, whi=whi, xlo=xlo, sl=sl:
                        h.matmul(ps[0][:, :], whi, xlo[:, sl], start=False, stop=False))
                t_base_mm = pl.step("tensor", lambda h, wlo=wlo, xhi=xhi, sl=sl, c=c:
                                    h.matmul(ps[0][:, :], wlo, xhi[:, sl],
                                             start=False, stop=(c == HC - 1)))
            pl.wait("scalar", t_base_mm)
            t_base_ev = pl.step("scalar", lambda h, sl=sl: h.activation(
                V("base", 0, 2048)[:, sl], ps[0][:, :], AF.Silu, bias=s_buvb[:]))
            pl.wait("tensor", t_base_ev)
        pl.wait("vector", t_base_ev)
        bb = V("base", 0, 2048)
        tq1 = V("stage", 0, 2048)
        tq2 = V("wblk", 0, 2048)
        qh, ql = V("q_hi", 0, 1024, BF16), V("q_lo", 0, 1024, BF16)
        kh, kl = V("kown", 0, 1024, BF16), V("kown", 1024, 1024, BF16)
        pl.emit("vector", lambda h: h.tensor_scalar(
            tq1, bb, s_gb[:, 4:5], s_gb[:, 5:6], ALU.mult, ALU.add))
        t = pl.step("vector", lambda h: h.tensor_scalar(
            tq2, bb, s_gb[:, 1:2], s_gb[:, 3:4], ALU.mult, ALU.add))
        pl.wait("vector", t)
        pl.emit("vector", lambda h: h.tensor_copy(qh, tq1))
        t = pl.step("vector", lambda h: h.tensor_copy(kh, tq2))
        pl.wait("vector", t)
        pl.emit("vector", lambda h: h.tensor_tensor(ql, tq1, qh, op=ALU.subtract))
        t_k = pl.step("vector", lambda h: h.tensor_tensor(kl, tq2, kh, op=ALU.subtract))
        pl.wait("gpsimd", t_k)
        pl.dma("gpsimd", lambda h: h.dma_start(out=ag_in[2 * E:2 * E + SD, :], in_=kh), "dK")
        pl.dma("gpsimd", lambda h: h.dma_start(out=ag_in[2 * E + SD:, :], in_=kl), "dK")

        def suv_f32(par):
            return V("stage", 1024 * par, 512)

        def suv_hi(par):
            return V("stage", 1024 * par + 512, 256, BF16)

        def suv_lo(par):
            return V("stage", 1024 * par + 768, 256, BF16)

        t_wprev = pl.tok("cV")
        stage_free = [t_k, t_k]
        t_ev_hist = [None, None]
        blk_order = list(range(4, 8)) + list(range(4))
        t_w_pref = {}
        consumed_tok = {}
        for bi, blk in enumerate(blk_order):
            is_v = blk >= 4
            col0 = 512 * blk
            wp = bi % 2
            if bi == 0:
                t_w_pref[0] = load_wblk(col0, 512, [pl.tokT(), t_k], wp=0, wq="dW")
            t_w = t_w_pref[bi]
            if bi + 1 < len(blk_order):
                g = [consumed_tok[bi - 1]] if bi >= 1 else []
                t_w_pref[bi + 1] = load_wblk(512 * blk_order[bi + 1], 512, g,
                                             wp=(bi + 1) % 2,
                                             wq="dW" if (bi + 1) % 2 == 0 else "dKL")
            if bi >= 1:
                pl.wait("sync", consumed_tok[bi - 1])
            t_brow = pl.dma("sync", lambda h, col0=col0: h.dma_start(
                out=s_buvrow[:, :], in_=buv_row[:, col0:col0 + 512]), "dB")
            pl.wait("vector", t_brow)
            pl.wait("vector", pl.tokT())
            t_brow = pl.split(s_buvrow[:, :], s_brh[:, :], s_brl[:, :])
            pl.wait("tensor", t_w)
            pl.wait("tensor", t_brow)
            for tt in range(NT):
                par = tt % 2
                bank = tt % 2
                if t_ev_hist[bank] is not None:
                    pl.wait("tensor", t_ev_hist[bank])
                for c in range(HC):
                    whi512 = wv_hi(c, wp)
                    wlo512 = wv_lo(c, wp)
                    xhi = V("xn_hi", 1024 * c, 1024, BF16)[:, 128 * tt:128 * (tt + 1)]
                    xlo = V("xn_lo", 1024 * c, 1024, BF16)[:, 128 * tt:128 * (tt + 1)]
                    pl.emit("tensor", lambda h, a=xhi, b=whi512, bank=bank, st=(c == 0):
                            h.matmul(ps[bank][:, :], a, b, start=st, stop=False))
                    pl.emit("tensor", lambda h, a=xhi, b=wlo512, bank=bank:
                            h.matmul(ps[bank][:, :], a, b, start=False, stop=False))
                    pl.emit("tensor", lambda h, a=xlo, b=whi512, bank=bank, sp=(os.environ.get("KNB2") and c == HC - 1):
                            h.matmul(ps[bank][:, :], a, b, start=False, stop=bool(sp)))
                if os.environ.get("KNB2"):
                    t_mm = pl.step("tensor", lambda h: h.nop())
                elif os.environ.get("KNOBIAS"):
                    t_mm = pl.step("tensor", lambda h, bank=bank: h.matmul(
                        ps[bank][:, :], ones1_bf[:], s_brh[:, :],
                        start=False, stop=True))
                else:
                    pl.emit("tensor", lambda h, bank=bank: h.matmul(
                        ps[bank][:, :], ones1_bf[:], s_brh[:, :],
                        start=False, stop=False))
                    t_mm = pl.step("tensor", lambda h, bank=bank: h.matmul(
                        ps[bank][:, :], ones1_bf[:], s_brl[:, :],
                        start=False, stop=True))
                pl.wait("scalar", t_mm)
                if stage_free[par] is not None:
                    pl.wait("scalar", stage_free[par])
                t_ev = pl.step("scalar", lambda h, bank=bank, par=par: h.activation(
                    suv_f32(par), ps[bank][:, :], AF.Silu))
                t_ev_hist[bank] = t_ev
                pl.wait("vector", t_ev)
                t_sp = pl.split(suv_f32(par), suv_hi(par), suv_lo(par))
                pl.wait("gpsimd", t_sp)
                r0 = 128 * tt
                dq = "dG" if par == 0 else "dH"
                if is_v:
                    cs = col0 - E
                    pl.dma("gpsimd", lambda h, r0=r0, cs=cs, par=par: h.dma_start(
                        out=ag_in[r0:r0 + 128, cs:cs + 512], in_=suv_hi(par)), dq)
                    stage_free[par] = pl.dma("gpsimd",
                        lambda h, r0=r0, cs=cs, par=par: h.dma_start(
                            out=ag_in[TC + r0:TC + r0 + 128, cs:cs + 512],
                            in_=suv_lo(par)), dq)
                else:
                    pl.dma("gpsimd", lambda h, r0=r0, cs=col0, par=par: h.dma_start(
                        out=su_sp[r0:r0 + 128, cs:cs + 512], in_=suv_hi(par)), dq)
                    stage_free[par] = pl.dma("gpsimd",
                        lambda h, r0=r0, cs=col0, par=par: h.dma_start(
                            out=su_sp[TC + r0:TC + r0 + 128, cs:cs + 512],
                            in_=suv_lo(par)), dq)
            consumed_tok[bi] = pl.tokT()
            if blk == 7:
                pl.wait("gpsimd", pl.tok("dG"))
                pl.wait("gpsimd", pl.tok("dH"))
                pl.wait("gpsimd", pl.tok("dK"))
                pl.emit("gpsimd", lambda h: h.collective_compute(
                    "AllGather", ALU.bypass,
                    replica_groups=[[0, 1, 2, 3], [4, 5, 6, 7]],
                    ins=[ag_in.opt()],
                    outs=[ag_out.opt()]), inc="cc")




        if KSTAGE >= 2:
            qh_v = V("q_hi", 0, 1024, BF16)
            ql_v = V("q_lo", 0, 1024, BF16)
            kh_v = V("kall_hi", 0, 4096, BF16)
            kl_v = V("kall_lo", 0, 4096, BF16)
            sqh_v = V("sq_hi", 0, 8192, BF16)
            sql_v = V("sq_lo", 0, 8192, BF16)

            def vb_hi(p):
                return V("vbuf", 2048 * p, 1024, BF16)

            def vb_lo(p):
                return V("vbuf", 2048 * p + 1024, 1024, BF16)

            def scr2(p):
                return V("sqevac", 512 * p, 512)

            def sq22(p):
                return V("avstage", 512 * p, 512)

            def avst(ns):
                return V("avstage", 2048 * ns, 2048)

            pl.wait("sync", pl.tok("cT"))
            pl.wait("sync", pl.tok("cc"))
            for e in ("vector", "scalar", "tensor"):
                pl.wait(e, pl.tok("dG"))
                pl.wait(e, pl.tok("dH"))
            tkl = None
            for r in range(4):
                b0 = r * AGW
                pl.dma("sync", lambda h, b0=b0, r=r: h.dma_start(
                    out=kh_v[:, 2048 * r:2048 * (r + 1)],
                    in_=ag_out[b0 + 2 * E:b0 + 2 * E + SD, :]), "dKL")
                tkl = pl.dma("sync", lambda h, b0=b0, r=r: h.dma_start(
                    out=kl_v[:, 2048 * r:2048 * (r + 1)],
                    in_=ag_out[b0 + 2 * E + SD:b0 + AGW, :]), "dKL")
            pl.wait("tensor", tkl)

            av_guard = [None, None]
            prev_evac = [pl.tok("cS"), pl.tok("cS")]
            vcons = {}

            for nb in range(8):
                nsl = slice(256 * nb, 256 * (nb + 1))
                pl.wait("tensor", prev_evac[0])
                pl.wait("tensor", prev_evac[1])
                pl.wait("vector", av_guard[0])
                pl.wait("scalar", av_guard[0])
                ev_tok = {}
                sq_tok = {}
                for mp in range(MCN // 2):
                    bank = mp % 2
                    if mp >= 2:
                        pl.wait("tensor", ev_tok[mp - 2])
                    t_mm = None
                    for sub in range(2):
                        mc = 2 * mp + sub
                        ksl = slice(128 * mc, 128 * (mc + 1))
                        osl = slice(256 * sub, 256 * (sub + 1))
                        pl.emit("tensor", lambda h, ksl=ksl, nsl=nsl, bank=bank, osl=osl:
                                h.matmul(ps[bank][:, osl], kh_v[:, ksl], qh_v[:, nsl],
                                         start=True, stop=False))
                        pl.emit("tensor", lambda h, ksl=ksl, nsl=nsl, bank=bank, osl=osl:
                                h.matmul(ps[bank][:, osl], kh_v[:, ksl], ql_v[:, nsl],
                                         start=False, stop=False))
                        t_mm = pl.step("tensor", lambda h, ksl=ksl, nsl=nsl, bank=bank, osl=osl:
                                       h.matmul(ps[bank][:, osl], kl_v[:, ksl],
                                                qh_v[:, nsl], start=False, stop=True))
                    pl.wait("scalar", t_mm)
                    if mp >= 2:
                        pl.wait("scalar", sq_tok[mp - 2])
                    ev_tok[mp] = pl.step("scalar", lambda h, bank=bank:
                                         h.activation(scr2(bank), ps[bank][:, 0:512],
                                                      AF.Relu))
                    pl.wait("vector", ev_tok[mp])
                    if mp >= 2:
                        pl.wait("vector", sq_tok[mp - 2])
                    tsq = pl.step("vector", lambda h, p=bank: h.tensor_tensor(
                        sq22(p), scr2(p), scr2(p), op=ALU.mult))
                    pl.wait("vector", tsq)
                    sq_tok[mp] = pl.split(
                        sq22(bank), sqh_v[:, 512 * mp:512 * (mp + 1)],
                        sql_v[:, 512 * mp:512 * (mp + 1)])
                t_sq_all = sq_tok[MCN // 2 - 1]

                pl.wait("tensor", t_sq_all)
                pl.wait("tensor", ev_tok[MCN // 2 - 1])
                t_av = None
                for mc in range(MCN):
                    par = mc % 2
                    r, lt = mc // 16, mc % 16
                    b0 = r * AGW
                    if mc >= 2:
                        pl.wait("sync", vcons[(nb, mc - 2)])
                    else:
                        pl.wait("sync", vcons.get((nb - 1, MCN - 2 + mc), None))
                    pl.dma("sync", lambda h, b0=b0, lt=lt, par=par: h.dma_start(
                        out=vb_hi(par), in_=ag_out[b0 + 128 * lt:b0 + 128 * (lt + 1), :]),
                        "dV0" if par == 0 else "dV1")
                    pl.dma("sync", lambda h, b0=b0, lt=lt, par=par: h.dma_start(
                        out=vb_lo(par),
                        in_=ag_out[b0 + TC + 128 * lt:b0 + TC + 128 * (lt + 1), :]),
                        "dV0" if par == 0 else "dV1")
                    pl.wait("tensor", pl.tok("dV0" if par == 0 else "dV1"))
                    for ns in range(2):
                        s_hi = sqh_v[:, 256 * mc + 128 * ns:256 * mc + 128 * (ns + 1)]
                        s_lo = sql_v[:, 256 * mc + 128 * ns:256 * mc + 128 * (ns + 1)]
                        for eq in range(4):
                            bank = ns * 4 + eq
                            esl = slice(512 * eq, 512 * (eq + 1))
                            st = (mc == 0)
                            sp = (mc == MCN - 1)
                            pl.emit("tensor", lambda h, a=s_hi, p=par, esl=esl, bank=bank, st=st:
                                    h.matmul(ps[bank][:, :], a, vb_hi(p)[:, esl],
                                             start=st, stop=False))
                            pl.emit("tensor", lambda h, a=s_hi, p=par, esl=esl, bank=bank:
                                    h.matmul(ps[bank][:, :], a, vb_lo(p)[:, esl],
                                             start=False, stop=False))
                            t_av = pl.step("tensor",
                                           lambda h, a=s_lo, p=par, esl=esl, bank=bank, sp=sp:
                                           h.matmul(ps[bank][:, :], a, vb_hi(p)[:, esl],
                                                    start=False, stop=sp))
                    vcons[(nb, mc)] = t_av
                for ns in range(2):
                    pl.wait("scalar", t_av)
                    pl.wait("scalar", av_guard[ns])
                    tev = None
                    for eq in range(4):
                        bank = ns * 4 + eq
                        tev = pl.step("scalar", lambda h, bank=bank, ns=ns, eq=eq:
                                      h.copy(avst(ns)[:, 512 * eq:512 * (eq + 1)],
                                             ps[bank][:, :]))
                    prev_evac[ns] = tev
                    pl.wait("gpsimd", tev)
                    r0 = 256 * nb + 128 * ns
                    av_guard[ns] = pl.dma("gpsimd", lambda h, r0=r0, ns=ns: h.dma_start(
                        out=av_sp[r0:r0 + 128, :], in_=avst(ns)),
                        "dA0" if ns == 0 else "dA1")

        if KSTAGE >= 2:
            pl.barrier()
            def wo_hi(ec):
                return V("woutb", 512 * ec, 512, BF16)

            def wo_lo(ec):
                return V("woutb", 8192 + 512 * ec, 512, BF16)

            t_wo = None
            stg = V("osq", 0, 1024)
            for ec in range(16):
                if t_wo is not None:
                    pl.wait("sync", t_wo)
                tld = pl.dma("sync", lambda h, ec=ec, stg=stg: h.dma_start(
                    out=stg, in_=wout[128 * ec:128 * (ec + 1), :]), "dWO")
                pl.wait("vector", tld)
                t_wo = pl.split(stg, wo_hi(ec), wo_lo(ec))

            def avl_p(p):
                return V("avl", 2048 * p, 2048)

            def suh_p(p):
                return V("sul", 2048 * p, 1024, BF16)

            def sul_p(p):
                return V("sul", 2048 * p + 1024, 1024, BF16)
            gsc_v = V("gsc", 0, 2048)
            gbh_v = V("gbf", 0, 1024, BF16)
            gbl_v = V("gbf", 1024, 1024, BF16)

            def gT_hi(ec):
                return V("gT", 128 * ec, 128, BF16)

            def gT_lo(ec):
                return V("gT", 2048 + 128 * ec, 128, BF16)

            t_gcons = None
            gcons_hist = [None, None]
            t_x2st = None
            t_ln2v = None
            for tb in range(NTB):
                tbsl = slice(512 * tb, 512 * (tb + 1))
                pl.wait("sync", t_x2st)
                pl.wait("sync", t_ln2v)
                txr = None
                for c in range(HC):
                    txr = pl.dma("sync", lambda h, c=c, tbsl=tbsl: h.dma_start(
                        out=V("xTrel", 512 * c, 512),
                        in_=xT[128 * c:128 * (c + 1), tbsl]), "dXR")
                t_oute = None
                for htb in range(2):
                    for tl in range(2):
                        tt = 4 * tb + 2 * htb + tl
                        lp = tt % 2
                        pl.wait("sync", gcons_hist[lp])
                        if tb == 0 and htb == 0 and tl == 0:
                            pl.wait("sync", pl.tok("dA0"))
                            pl.wait("sync", pl.tok("dA1"))
                        tar = pl.dma("sync", lambda h, tt=tt, lp=lp: h.dma_start(
                            out=avl_p(lp), in_=av_sp[128 * tt:128 * (tt + 1), :]),
                            "dAR" if lp == 0 else "dKL")
                        tsr = pl.dma("sync", lambda h, tt=tt, lp=lp: h.dma_start(
                            out=suh_p(lp), in_=su_sp[128 * tt:128 * (tt + 1), :]),
                            "dSU" if lp == 0 else "dB")
                        tsr = pl.dma("sync", lambda h, tt=tt, lp=lp: h.dma_start(
                            out=sul_p(lp), in_=su_sp[TC + 128 * tt:TC + 128 * (tt + 1), :]),
                            "dSU" if lp == 0 else "dB")
                        pl.wait("vector", tar)
                        pl.wait("vector", tsr)
                        tg = pl.step("vector", lambda h, lp=lp: h.tensor_tensor(
                            gsc_v, suh_p(lp), sul_p(lp), op=ALU.add))
                        pl.wait("vector", tg)
                        t_gcons = pl.step("vector", lambda h, lp=lp: h.tensor_tensor(
                            gsc_v, gsc_v, avl_p(lp), op=ALU.mult))
                        gcons_hist[lp] = t_gcons
                        pl.wait("vector", t_gcons)
                        tsp = pl.split(gsc_v, gbh_v, gbl_v)
                        pl.wait("tensor", tsp)
                        pl.wait("tensor", t_ln2v)
                        tq = None
                        tq_hist = [None, None]
                        for i2, (ec, half) in enumerate(
                                (e, hh) for e in range(16) for hh in (0, 1)):
                            gin = gbh_v if half == 0 else gbl_v
                            gout = gT_hi if half == 0 else gT_lo
                            bank = 6 + (i2 % 2)
                            if tq_hist[i2 % 2] is not None:
                                pl.wait("tensor", tq_hist[i2 % 2])
                            tt_mm = pl.step("tensor", lambda h, gin=gin, ec=ec, bank=bank:
                                            h.transpose(ps[bank][:, 0:64].bitcast(BF16),
                                                        gin[:, 128 * ec:128 * (ec + 1)],
                                                        ident_bf[:]))
                            pl.wait("vector", tt_mm)
                            tq = pl.step("vector", lambda h, gout=gout, ec=ec, tl=tl, bank=bank:
                                         h.tensor_copy(
                                             gout(ec)[:, 128 * tl:128 * (tl + 1)],
                                             ps[bank][:, 0:64].bitcast(BF16)))
                            tq_hist[i2 % 2] = tq
                    pl.wait("tensor", tq)
                    pl.wait("tensor", t_oute)
                    if tb == 0 and htb == 0:
                        pl.wait("tensor", t_wo)
                        pl.wait("tensor", prev_evac[0])
                        pl.wait("tensor", prev_evac[1])
                    t_oute_prev = t_oute
                    t_oute = None
                    for hp in range(2):
                        pl.wait("tensor", t_oute if t_oute else t_oute_prev)
                        t_op = None
                        for ec in range(16):
                            for hq2 in range(4):
                                ht = 4 * hp + hq2
                                st = (ec == 0)
                                sp = (ec == 15)
                                hsl = slice(128 * ht, 128 * (ht + 1))
                                pl.emit("tensor", lambda h, ec=ec, hsl=hsl, hq2=hq2, st=st:
                                        h.matmul(ps[hq2][:, 0:256], wo_hi(ec)[:, hsl],
                                                 gT_hi(ec)[:, :], start=st, stop=False))
                                pl.emit("tensor", lambda h, ec=ec, hsl=hsl, hq2=hq2:
                                        h.matmul(ps[hq2][:, 0:256], wo_lo(ec)[:, hsl],
                                                 gT_hi(ec)[:, :], start=False, stop=False))
                                t_op = pl.step("tensor", lambda h, ec=ec, hsl=hsl, hq2=hq2, sp=sp:
                                               h.matmul(ps[hq2][:, 0:256], wo_hi(ec)[:, hsl],
                                                        gT_lo(ec)[:, :], start=False, stop=sp))
                        pl.wait("vector", t_op)
                        for hq2 in range(4):
                            ht = 4 * hp + hq2
                            t_oute = pl.step("vector", lambda h, ht=ht, hq2=hq2, htb=htb:
                                             h.tensor_scalar(
                                                 V("outT", 512 * ht, 512)[:, 256 * htb:256 * (htb + 1)],
                                                 ps[hq2][:, 0:256], s_bout[:, ht:ht + 1], None,
                                                 ALU.add))
                pl.wait("tensor", t_oute)
                pl.wait("vector", t_oute)
                pl.wait("vector", txr)

                def ln2_out(tb2, sl2):
                    t = None
                    for c in range(HC):
                        ov, tmp = V("outT", 512 * c, 512), V("osq", 512 * c, 512)
                        t = pl.step("vector", lambda h, ov=ov, tmp=tmp: h.tensor_tensor(
                            tmp, ov, ps[6][:, :], op=ALU.subtract))
                    pl.wait("vector", t)
                    for c in range(HC):
                        tmp = V("osq", 512 * c, 512)
                        t = pl.step("vector", lambda h, tmp=tmp: h.tensor_tensor(
                            tmp, tmp, ps[7][:, :], op=ALU.mult))
                    pl.wait("vector", t)
                    vt = None
                    for c in range(HC):
                        tmp, xr = V("osq", 512 * c, 512), V("xTrel", 512 * c, 512)
                        vt = pl.step("vector", lambda h, tmp=tmp, xr=xr: h.tensor_tensor(
                            xr, tmp, xr, op=ALU.add))
                    return vt

                emit_ln(HC, lambda c: V("outT", 512 * c, 512), "osq", ln2_out, [0])
                t_ln2v = pl.tok("cV")
                pl.wait("gpsimd", t_ln2v)
                for c in range(HC):
                    t_x2st = pl.dma("gpsimd", lambda h, c=c, tbsl=tbsl: h.dma_start(
                        out=x2_sp[128 * c:128 * (c + 1), tbsl],
                        in_=V("xTrel", 512 * c, 512)), "dXS")

        class _StopEmit(Exception):
            pass

        _ffn_stopped = False
        if KSTAGE >= 3:
          try:
            pl.barrier()

            def xi_v(dc):
                return V("xi", 1024 * dc, 1024, BF16)

            def wiT_v(dc):
                return V("wf", 2048 * dc, 2048, BF16)

            def x2s_v(c):
                return V("x2str", 512 * c, 512)

            def zsq_v(c):
                return V("zsq", 512 * c, 512)

            def m_row(tb):
                return V("z", 512 * tb, 512)[0:1, :]

            def r_row(tb):
                return V("hstage", 512 * tb, 512)[0:1, :]

            def q_y(p):
                return V("qscr", 2048 * p, 512)

            def q_ci(p):
                return V("qscr", 2048 * p + 512, 512).bitcast(I32)

            def q_cm1(p):
                return V("qscr", 2048 * p + 1024, 512)

            def q_d(p):
                return V("qscr", 2048 * p + 1536, 512)

            def w1s_v(p):
                return V("w1str", 1024 * p, 1024)

            def hst_v(p):
                return V("hstage", 512 * p, 512)

            t_x2cons = None
            rmax_cols = []
            for tb in range(NTB):
                tbsl = slice(512 * tb, 512 * (tb + 1))
                pl.wait("sync", t_x2cons)
                pl.wait("sync", pl.tok("dXS"))
                tld = None
                for c in range(HC):
                    tld = pl.dma("sync", lambda h, c=c, tbsl=tbsl: h.dma_start(
                        out=x2s_v(c), in_=x2_sp[128 * c:128 * (c + 1), tbsl]), "dXR")
                pl.wait("vector", tld)
                t = None
                for c in range(HC):
                    t = pl.step("vector", lambda h, c=c: h.tensor_tensor(
                        zsq_v(c), x2s_v(c), x2s_v(c), op=ALU.mult))
                pl.wait("tensor", t)
                tm = None
                for c in range(HC):
                    pl.emit("tensor", lambda h, c=c: h.matmul(
                        ps[6][0:1, :], onesK[:], x2s_v(c),
                        start=(c == 0), stop=(c == HC - 1)))
                    tm = pl.step("tensor", lambda h, c=c: h.matmul(
                        ps[7][0:1, :], onesK[:], zsq_v(c),
                        start=(c == 0), stop=(c == HC - 1)))
                pl.wait("vector", tm)
                t1 = pl.step("vector", lambda h: h.tensor_scalar(
                    rowM[:], ps[6][0:1, :], 1.0 / H, None, ALU.mult))
                pl.step("vector", lambda h: h.tensor_scalar(
                    rowV[:], ps[7][0:1, :], 1.0 / H, None, ALU.mult))
                pl.wait("vector", t1)
                t3 = pl.step("vector", lambda h: h.tensor_tensor(
                    rowT[:], rowM[:], rowM[:], op=ALU.mult))
                pl.wait("vector", t3)
                tvv = pl.step("vector", lambda h: h.tensor_tensor(
                    rowV[:], rowV[:], rowT[:], op=ALU.subtract))
                pl.wait("scalar", tvv)
                tss = pl.step("scalar", lambda h: h.activation(
                    rowT[:], rowV[:], AF.Sqrt, bias=s_eps[:]))
                pl.wait("vector", tss)
                trr = pl.step("vector", lambda h: h.reciprocal(rowR[:], rowT[:]))
                pl.wait("vector", trr)
                pl.emit("vector", lambda h, tb=tb: h.tensor_copy(m_row(tb), rowM[:]))
                trc = pl.step("vector", lambda h, tb=tb: h.tensor_copy(r_row(tb), rowR[:]))
                pl.wait("tensor", trc)
                pl.emit("tensor", lambda h: h.matmul(
                    ps[6][:, :], ones1[:], rowM[:], start=True, stop=True))
                tbb = pl.step("tensor", lambda h: h.matmul(
                    ps[7][:, :], ones1[:], rowR[:], start=True, stop=True))
                pl.wait("vector", tbb)
                t = None
                for c in range(HC):
                    t = pl.step("vector", lambda h, c=c: h.tensor_tensor(
                        zsq_v(c), x2s_v(c), ps[6][:, :], op=ALU.subtract))
                pl.wait("vector", t)
                for c in range(HC):
                    t = pl.step("vector", lambda h, c=c: h.tensor_tensor(
                        zsq_v(c), zsq_v(c), ps[7][:, :], op=ALU.mult))
                pl.wait("vector", t)
                for c in range(HC):
                    col = 32 + 8 * tb + c
                    rmax_cols.append(col)
                    t = pl.step("vector", lambda h, c=c, col=col: h.tensor_reduce(
                        s_tmp[:, col:col + 1], zsq_v(c), axis=AX.X, op=ALU.max))
                t_x2cons = t
                pl.wait("tensor", t)

            pl.wait("vector", t_x2cons)
            tcm = pl.step("vector", lambda h: h.reduce_max(
                s_tmp[:, 64:65], s_tmp[:, 32:64], axis=AX.X))
            pl.wait("tensor", tcm)
            ttr = pl.step("tensor", lambda h: h.transpose(
                ps[5][0:1, 0:128], s_tmp[:, 64:65], ident[:]))
            pl.wait("vector", ttr)
            tgm = pl.step("gpsimd", lambda h: h.memset(s_sc[:, :], -1e30))
            pl.wait("vector", tgm)
            tgm = pl.step("vector", lambda h: h.reduce_max(
                s_sc[0:1, 0:1], ps[5][0:1, 0:128], axis=AX.X))
            pl.wait("gpsimd", tgm)
            tdm = pl.dma("gpsimd", lambda h: h.dma_start(
                out=ar1_in[:, :], in_=s_sc[0:1, 0:8]), "dR")
            pl.wait("gpsimd", tdm)
            pl.emit("gpsimd", lambda h: h.collective_compute(
                "AllReduce", ALU.max, replica_groups=[list(range(8))],
                ins=[ar1_in.opt()], outs=[ar1_out.opt()]), inc="cc")
            t_w1cons = [None, None]
            q_entry = [None, None]
            wi_slot_guard = [None, None]
            wiT_bank_hist = [None, None]
            t_wiT = None

            def QW(p):
                return V("qscr", 4096 * p, 1024)

            def QWCI(p):
                return V("qscr", 4096 * p + 1024, 1024).bitcast(I32)

            def QWCM(p):
                return V("qscr", 4096 * p + 2048, 1024)

            def QWD(p):
                return V("qscr", 4096 * p + 3072, 1024)

            for fc in range(32):
                par = fc % 2
                pl.wait("sync", t_w1cons[par])
                tw = pl.dma("sync", lambda h, fc=fc, par=par: h.dma_start(
                    out=w1s_v(par), in_=w1[128 * fc:128 * (fc + 1), :]),
                    "dF0" if par == 0 else "dF1")
                pl.wait("vector", tw)
                trm = pl.step("vector", lambda h, fc=fc, par=par: h.tensor_reduce(
                    s_sw1[:, fc:fc + 1], w1s_v(par), axis=AX.X, op=ALU.max,
                    apply_absolute_value=True))
                pl.wait("vector", trm)
                trm = pl.step("vector", lambda h, fc=fc, par=par: h.reciprocal(
                    s_tmp[:, 66 + 2 * par:67 + 2 * par], s_sw1[:, fc:fc + 1]))
                pl.wait("vector", trm)
                trm = pl.step("vector", lambda h, par=par: h.tensor_scalar(
                    s_tmp[:, 67 + 2 * par:68 + 2 * par],
                    s_tmp[:, 66 + 2 * par:67 + 2 * par], 127.0, None, ALU.mult))
                pl.wait("vector", trm)
                pl.wait("vector", wi_slot_guard[par])
                q_entry[par] = pl.quant(
                    w1s_v(par), V("zsq", 1024 * par, 512).bitcast(BF16),
                    QW(par), QWCI(par), QWCM(par), QWD(par),
                    s_tmp[:, 67 + 2 * par:68 + 2 * par], relu=False,
                    entry=q_entry[par])
                t_w1cons[par] = q_entry[par][0]
                pl.wait("tensor", q_entry[par][0])
                for tpi in range(8):
                    bank = (tpi % 2) + 2
                    if wiT_bank_hist[tpi % 2] is not None:
                        pl.wait("tensor", wiT_bank_hist[tpi % 2])
                    tmm = pl.step("tensor", lambda h, par=par, tpi=tpi, bank=bank:
                                  h.transpose(
                                      ps[bank][:, 0:64].bitcast(BF16),
                                      V("zsq", 1024 * par, 512).bitcast(BF16)[:, 128 * tpi:128 * (tpi + 1)],
                                      ident_bf[:]))
                    pl.wait("vector", tmm)
                    t_wiT = pl.step("vector", lambda h, dc=tpi, fc=fc, bank=bank:
                                    h.tensor_copy(
                                        wiT_v(dc)[:, 128 * fc:128 * (fc + 1)],
                                        ps[bank][:, 0:64].bitcast(BF16)))
                    wiT_bank_hist[tpi % 2] = t_wiT
                wi_slot_guard[par] = t_wiT

            pl.wait("gpsimd", pl.tok("cc"))
            tdm = pl.dma("gpsimd", lambda h: h.dma_start(
                out=s_sc[0:1, 8:16], in_=ar1_out[:, :]), "dR")
            pl.wait("vector", tdm)
            ts1 = pl.step("vector", lambda h: h.tensor_scalar(
                s_sc[0:1, 1:2], s_sc[0:1, 8:9], 1e-8, 1.0 / 255.0, ALU.max, ALU.mult))
            pl.wait("vector", ts1)
            ts1 = pl.step("vector", lambda h: h.reciprocal(s_sc[0:1, 2:3], s_sc[0:1, 1:2]))
            pl.wait("tensor", ts1)
            tbq = pl.step("tensor", lambda h: h.matmul(
                ps[5][:, 0:1], ones1[:], s_sc[0:1, 2:3], start=True, stop=True))
            pl.wait("vector", tbq)
            trb = pl.step("vector", lambda h: h.tensor_copy(s_scp[:, 0:1], ps[5][:, 0:1]))
            pl.wait("tensor", trb)
            tbq = pl.step("tensor", lambda h: h.matmul(
                ps[5][:, 0:1], ones1[:], s_sc[0:1, 1:2], start=True, stop=True))
            pl.wait("vector", tbq)
            trb = pl.step("vector", lambda h: h.tensor_copy(s_scp[:, 2:3], ps[5][:, 0:1]))

            pl.wait("vector", trb)
            tsv = pl.step("vector", lambda h: h.tensor_scalar(
                s_sv1[:, :], s_sw1[:, 0:32], s_scp[:, 2:3], None, ALU.mult))
            pl.wait("vector", tsv)
            tsv = pl.step("vector", lambda h: h.tensor_scalar(
                s_sv1[:, :], s_sv1[:, :], 1.0 / 127.0, None, ALU.mult))
            pl.wait("vector", tsv)
            tsv = pl.step("vector", lambda h: h.reciprocal(s_tmp[:, 68:100], s_sv1[:, :]))
            pl.wait("vector", tsv)
            tsv = pl.step("vector", lambda h: h.tensor_tensor(
                s_tmp[:, 100:132], s_b1[:, :], s_tmp[:, 68:100], op=ALU.mult))
            pl.wait("vector", tsv)
            tbqq = pl.quant(s_tmp[:, 100:132], s_tmp[:, 100:132],
                            QW(0)[:, 0:32], QWCI(0)[:, 0:32], QWCM(0)[:, 0:32],
                            QWD(0)[:, 0:32], None, relu=False)[0]
            pl.wait("vector", tbqq)
            tbqq = pl.step("vector", lambda h: h.tensor_tensor(
                s_bq1[:, :], s_tmp[:, 100:132], s_sv1[:, :], op=ALU.mult))

            pl.wait("tensor", pl.tok("cV"))
            pl.wait("tensor", pl.tok("cS"))
            tbb = None
            for tb in range(NTB):
                pl.emit("tensor", lambda h, tb=tb: h.matmul(
                    ps[tb][:, :], ones1[:], m_row(tb), start=True, stop=True))
                tbb = pl.step("tensor", lambda h, tb=tb: h.matmul(
                    ps[4 + tb][:, :], ones1[:], r_row(tb), start=True, stop=True))
            def QY(p):
                return V("qscr", 4096 * p, 1024)
            def QCI(p):
                return V("qscr", 4096 * p + 1024, 1024).bitcast(I32)
            def QCM(p):
                return V("qscr", 4096 * p + 2048, 1024)
            def QD(p):
                return V("qscr", 4096 * p + 3072, 1024)
            t_x2c2 = None
            b2_entry = [None, None]
            ld_guard = [None, None]
            for c in range(HC):
                par = c % 2
                pl.wait("sync", ld_guard[par])
                tld = pl.dma("sync", lambda h, c=c, par=par: h.dma_start(
                    out=V("x2str", 2048 * par, 2048),
                    in_=x2_sp[128 * c:128 * (c + 1), :]),
                    "dXR" if par == 0 else "dSU")
                pl.wait("vector", tld)
                pl.wait("vector", tbb)
                for hhalf in range(2):
                    hsl = slice(1024 * hhalf, 1024 * (hhalf + 1))
                    if b2_entry[par] is not None:
                        pl.wait("vector", b2_entry[par][0])
                    t = None
                    for tb in (2 * hhalf, 2 * hhalf + 1):
                        tbsl = slice(512 * tb, 512 * (tb + 1))
                        osl = slice(512 * (tb - 2 * hhalf), 512 * (tb - 2 * hhalf + 1))
                        t = pl.step("vector", lambda h, par=par, tb=tb, tbsl=tbsl, osl=osl:
                                    h.tensor_tensor(QY(par)[:, osl],
                                                    V("x2str", 2048 * par, 2048)[:, tbsl],
                                                    ps[tb][:, :], op=ALU.subtract))
                    pl.wait("vector", t)
                    for tb in (2 * hhalf, 2 * hhalf + 1):
                        osl = slice(512 * (tb - 2 * hhalf), 512 * (tb - 2 * hhalf + 1))
                        t = pl.step("vector", lambda h, par=par, tb=tb, osl=osl:
                                    h.tensor_tensor(QY(par)[:, osl], QY(par)[:, osl],
                                                    ps[4 + tb][:, :], op=ALU.mult))
                    pl.wait("vector", t)
                    b2_entry[par] = pl.quant(
                        QY(par), xi_v(c)[:, slice(1024 * hhalf, 1024 * (hhalf + 1))],
                        QY(par), QCI(par), QCM(par), QD(par),
                        s_scp[:, 0:1], relu=True, entry=b2_entry[par])
                ld_guard[par] = b2_entry[par][0]
                t_x2c2 = b2_entry[par][0]
            pl.wait("tensor", t_x2c2)

            _ffn_on = [KFFN >= 1]
            pl.wait("vector", tbqq)
            pl.wait("scalar", tbqq)
            pl.wait("tensor", t_wiT)
            pl.wait("vector", t_wiT)
            tmaxg = pl.step("gpsimd", lambda h: h.memset(s_tmp[:, 65:66], -1e30))
            pl.wait("vector", tmaxg)
            def w2stg5():
                return V("x2str", 0, 4096)

            def w2ibf5():
                return V("zsq", 0, 2048, BF16)

            t_w2ld = [None, None]
            w2_st_box = [None]
            w2_entry = list(q_entry)
            t_w2c = None

            def emit_w2q_chunk(dc):
                lpar = dc % 2
                if dc == 0:
                    pl.wait("sync", pl.tok("cV"))
                    pl.wait("vector", wi_slot_guard[0])
                    pl.wait("vector", wi_slot_guard[1])
                pl.wait("sync", t_w2ld[0])
                tw = pl.dma("sync", lambda h, dc=dc: h.dma_start(
                    out=w2stg5(), in_=w2[128 * dc:128 * (dc + 1), :]), "dWO")
                pl.wait("vector", tw)
                trm = pl.step("vector", lambda h, dc=dc, lpar=lpar: h.tensor_reduce(
                    s_sw2[:, dc:dc + 1], w2stg5(), axis=AX.X, op=ALU.max,
                    apply_absolute_value=True))
                pl.wait("vector", trm)
                trm = pl.step("vector", lambda h, dc=dc, lpar=lpar: h.reciprocal(
                    s_tmp[:, 74:75], s_sw2[:, dc:dc + 1]))
                pl.wait("vector", trm)
                trm = pl.step("vector", lambda h: h.tensor_scalar(
                    s_tmp[:, 75:76], s_tmp[:, 74:75], 127.0, None, ALU.mult))
                pl.wait("vector", trm)
                pl.wait("vector", w2_st_box[0])
                for qtr in range(4):
                    qsl = slice(1024 * qtr, 1024 * (qtr + 1))
                    qpar = qtr % 2
                    w2_entry[qpar] = pl.quant(
                        w2stg5()[:, qsl], w2ibf5()[:, qsl],
                        QW(qpar), QWCI(qpar), QWCM(qpar), QWD(qpar),
                        s_tmp[:, 75:76], relu=False,
                        entry=w2_entry[qpar])
                t_w2ld[0] = w2_entry[1][0]
                pl.wait("gpsimd", w2_entry[0][0])
                pl.wait("gpsimd", w2_entry[1][0])
                w2_st_box[0] = pl.dma("gpsimd", lambda h, dc=dc: h.dma_start(
                    out=w2i_sp[128 * dc:128 * (dc + 1), :], in_=w2ibf5()), "dWS")
                return w2_entry[1][0]



            h_store = [None, None]
            t_hev = {}
            t_hred = {}
            t_hmax = tmaxg
            for ft in range(32 if KFFN >= 1 else 0):
                if ft % 4 == 0 and KFFN >= 3:
                    t_w2c = emit_w2q_chunk(ft // 4)
                for tb in range(NTB):
                    it = ft * NTB + tb
                    par = it % 2
                    bank = it % 2
                    tbsl = slice(512 * tb, 512 * (tb + 1))
                    if it >= 2:
                        pl.wait("tensor", t_hev[it - 2])
                    tmf = None
                    for dc in range(HC):
                        tmf = pl.step("tensor", lambda h, dc=dc, ft=ft, tbsl=tbsl, bank=bank:
                                      h.matmul(ps[bank][:, :],
                                               wiT_v(dc)[:, 128 * ft:128 * (ft + 1)],
                                               xi_v(dc)[:, tbsl],
                                               start=(dc == 0), stop=(dc == HC - 1)))
                    pl.wait("scalar", tmf)
                    if h_store[par] is not None:
                        pl.wait("scalar", h_store[par])
                    if (it - 2) in t_hred:
                        pl.wait("scalar", t_hred[it - 2])
                    t_hev[it] = pl.step("scalar", lambda h, bank=bank, par=par, ft=ft:
                                        h.activation(hst_v(par), ps[bank][:, :], AF.Relu,
                                                     scale=s_sv1[:, ft:ft + 1],
                                                     bias=s_bq1[:, ft:ft + 1]))
                    pl.wait("vector", t_hev[it])
                    tr1 = pl.step("vector", lambda h, par=par: h.tensor_reduce(
                        s_tmp[:, 70 + par:71 + par], hst_v(par), axis=AX.X, op=ALU.max))
                    t_hred[it] = tr1
                    pl.wait("vector", tr1)
                    pl.wait("vector", t_hmax)
                    t_hmax = pl.step("vector", lambda h, par=par: h.tensor_tensor(
                        s_tmp[:, 65:66], s_tmp[:, 65:66], s_tmp[:, 70 + par:71 + par],
                        op=ALU.max))
                    pl.wait("gpsimd", t_hev[it])
                    h_store[par] = pl.dma("gpsimd", lambda h, ft=ft, tbsl=tbsl, par=par:
                                          h.dma_start(out=h_sp[128 * ft:128 * (ft + 1), tbsl],
                                                      in_=hst_v(par)),
                                          "dG" if par == 0 else "dH")

            if KFFN < 2:
                raise _StopEmit()
            pl.wait("vector", t_hmax)
            tcm = pl.step("vector", lambda h: h.tensor_copy(s_tmp[:, 64:65], s_tmp[:, 65:66]))
            pl.wait("tensor", tcm)
            ttr = pl.step("tensor", lambda h: h.transpose(
                ps[5][0:1, 0:128], s_tmp[:, 64:65], ident[:]))
            pl.wait("vector", ttr)
            tgm = pl.step("gpsimd", lambda h: h.memset(s_sc[:, :], -1e30))
            pl.wait("vector", tgm)
            tgm = pl.step("vector", lambda h: h.reduce_max(
                s_sc[0:1, 0:1], ps[5][0:1, 0:128], axis=AX.X))
            pl.wait("gpsimd", tgm)
            tdm = pl.dma("gpsimd", lambda h: h.dma_start(
                out=ar2_in[:, :], in_=s_sc[0:1, 0:8]), "dR")
            pl.wait("gpsimd", tdm)
            pl.emit("gpsimd", lambda h: h.collective_compute(
                "AllReduce", ALU.max, replica_groups=[list(range(8))],
                ins=[ar2_in.opt()], outs=[ar2_out.opt()]), inc="cc")
            pl.wait("gpsimd", pl.tok("cc"))
            tdm = pl.dma("gpsimd", lambda h: h.dma_start(
                out=s_sc[0:1, 8:16], in_=ar2_out[:, :]), "dR")
            pl.wait("vector", tdm)
            ts2 = pl.step("vector", lambda h: h.tensor_scalar(
                s_sc[0:1, 1:2], s_sc[0:1, 8:9], 1e-8, 1.0 / 255.0, ALU.max, ALU.mult))
            pl.wait("vector", ts2)
            ts2 = pl.step("vector", lambda h: h.reciprocal(s_sc[0:1, 2:3], s_sc[0:1, 1:2]))
            pl.wait("tensor", ts2)
            tbq = pl.step("tensor", lambda h: h.matmul(
                ps[5][:, 0:1], ones1[:], s_sc[0:1, 2:3], start=True, stop=True))
            pl.wait("vector", tbq)
            tr2b = pl.step("vector", lambda h: h.tensor_copy(s_scp[:, 1:2], ps[5][:, 0:1]))
            pl.wait("tensor", tr2b)
            tbq = pl.step("tensor", lambda h: h.matmul(
                ps[5][:, 0:1], ones1[:], s_sc[0:1, 1:2], start=True, stop=True))
            pl.wait("vector", tbq)
            tr2b = pl.step("vector", lambda h: h.tensor_copy(s_scp[:, 3:4], ps[5][:, 0:1]))

          except _StopEmit:
            _ffn_stopped = True
        if KSTAGE >= 3 and not _ffn_stopped:
          try:
            pl.barrier()

            def hi_v(fc):
                return V("hihalf", 1024 * fc, 1024, BF16)

            def hrl_v(p):
                return V("hrel", 2048 * p, 2048)

            def q2_y(p):
                return V("q2scr", 2048 * p, 512)

            def q2_ci(p):
                return V("q2scr", 2048 * p + 512, 512).bitcast(I32)

            def q2_cm1(p):
                return V("q2scr", 2048 * p + 1024, 512)

            def q2_d(p):
                return V("q2scr", 2048 * p + 1536, 512)

            def w2sv(p):
                return V("w2s", 512 * p, 512, BF16)

            def w2qstg():
                return V("q2scr", 4096, 1024)

            def w2ibf():
                return V("w2s", 1024, 1024, BF16)

            def outp_v(c):
                return V("outp", 2048 * c, 2048)

            if KFFN < 3:
                raise _StopEmit()
            def Q2Y(p):
                return V("q2scr", 4096 * p, 1024)

            def Q2CI(p):
                return V("q2scr", 4096 * p + 1024, 1024).bitcast(I32)

            def Q2CM(p):
                return V("q2scr", 4096 * p + 2048, 1024)

            def Q2D(p):
                return V("q2scr", 4096 * p + 3072, 1024)

            pl.wait("vector", tr2b)
            tsv = pl.step("vector", lambda h: h.tensor_scalar(
                s_sv2[:, :], s_sw2[:, :], s_scp[:, 3:4], None, ALU.mult))
            pl.wait("vector", tsv)
            tsv = pl.step("vector", lambda h: h.tensor_scalar(
                s_sv2[:, :], s_sv2[:, :], 1.0 / 127.0, None, ALU.mult))
            pl.wait("vector", tsv)
            tsv = pl.step("vector", lambda h: h.reciprocal(s_tmp[:, 80:88], s_sv2[:, :]))
            pl.wait("vector", tsv)
            tsv = pl.step("vector", lambda h: h.tensor_tensor(
                s_tmp[:, 88:96], s_b2[:, :], s_tmp[:, 80:88], op=ALU.mult))
            pl.wait("vector", tsv)
            tbq2 = pl.quant(s_tmp[:, 88:96], s_tmp[:, 88:96],
                            Q2Y(0)[:, 0:8], Q2CI(0)[:, 0:8], Q2CM(0)[:, 0:8],
                            Q2D(0)[:, 0:8], None, relu=False)[0]
            pl.wait("vector", tbq2)
            tbq2 = pl.step("vector", lambda h: h.tensor_tensor(
                s_bq2[:, :], s_tmp[:, 88:96], s_sv2[:, :], op=ALU.mult))

            if KFFN < 4:
                raise _StopEmit()
            t_outev = None
            t_ln4 = None
            def hi_q(q, fc):
                return V("hihalf", 8192 * (q % 2) + 1024 * fc, 1024, BF16)

            hq_entry = [None, None]
            hq_ld = [None, None]

            def emit_hquant_quarter(q, gate_tok):
                tqh = gate_tok
                for fc in range(8):
                    fg = 8 * q + fc
                    par = fc % 2
                    pl.wait("sync", hq_ld[par])
                    pl.wait("sync", gate_tok)
                    th = pl.dma("sync", lambda h, fg=fg, par=par: h.dma_start(
                        out=hrl_v(par), in_=h_sp[128 * fg:128 * (fg + 1), :]),
                        "dF0" if par == 0 else "dF1")
                    pl.wait("vector", th)
                    for hh in range(2):
                        hsl = slice(1024 * hh, 1024 * (hh + 1))
                        hq_entry[par] = pl.quant(
                            hrl_v(par)[:, hsl], hi_q(q, fc)[:, hsl],
                            Q2Y(par), Q2CI(par), Q2CM(par), Q2D(par),
                            s_scp[:, 1:2], relu=True, entry=hq_entry[par])
                    hq_ld[par] = hq_entry[par][0]
                    tqh = hq_entry[par][0]
                return tqh

            pl.wait("sync", pl.tok("dWS"))
            tqh = {}
            tqh[0] = emit_hquant_quarter(0, t_w2c)
            hq_cons = {}
            t_w2l = [None, None]
            for q in range(4):
                pl.wait("tensor", tqh[q])
                for tb in range(NTB):
                    tbsl = slice(512 * tb, 512 * (tb + 1))
                    if q > 0 or tb > 0:
                        pl.wait("tensor", t_outev)
                    tmf = None
                    for fc in range(8):
                        fg = 8 * q + fc
                        wpar = fc % 2
                        pl.wait("sync", t_w2l[wpar])
                        twl = pl.dma("sync", lambda h, fg=fg, wpar=wpar: h.dma_start(
                            out=w2sv(wpar)[:, :],
                            in_=w2i_sp[:, 128 * fg:128 * (fg + 1)], transpose=True),
                            "dV0" if wpar == 0 else "dV1")
                        pl.wait("tensor", twl)
                        for dt in range(HC):
                            tmf = pl.step("tensor",
                                          lambda h, q=q, fc=fc, dt=dt, tbsl=tbsl, wpar=wpar:
                                          h.matmul(ps[dt][:, :],
                                                   w2sv(wpar)[:, 128 * dt:128 * (dt + 1)],
                                                   hi_q(q, fc)[:, tbsl],
                                                   start=(fc == 0), stop=(fc == 7)))
                        t_w2l[wpar] = tmf
                    if tb == 0 and q < 3:
                        gate = hq_cons.get(q - 1, tqh[q])
                        tqh[q + 1] = emit_hquant_quarter(q + 1, gate)
                    pl.wait("vector", tmf)
                    pl.wait("vector", t_outev)
                    for dt in range(HC):
                        if q == 0:
                            t_outev = pl.step("vector", lambda h, dt=dt, tbsl=tbsl:
                                              h.tensor_copy(outp_v(dt)[:, tbsl],
                                                            ps[dt][:, :]))
                        else:
                            t_outev = pl.step("vector", lambda h, dt=dt, tbsl=tbsl:
                                              h.tensor_tensor(outp_v(dt)[:, tbsl],
                                                              outp_v(dt)[:, tbsl],
                                                              ps[dt][:, :], op=ALU.add))
                hq_cons[q] = tmf

            pl.wait("vector", tbq2)
            pl.wait("vector", t_outev)
            t = None
            for dt in range(HC):
                t = pl.step("vector", lambda h, dt=dt: h.tensor_scalar(
                    outp_v(dt), outp_v(dt), s_sv2[:, dt:dt + 1], s_bq2[:, dt:dt + 1],
                    ALU.mult, ALU.add))
            pl.wait("vector", t)
            pl.wait("tensor", t)

            def x2r_v(c):
                return V("hrel", 512 * c, 512)

            def ystg_v(p):
                return V("q2scr", 4096 + 1024 * p, 1024)

            t_yst = [None, None]
            y_bank_hist = [None, None]
            t_x2r = None
            for tb in range(NTB):
                tbsl = slice(512 * tb, 512 * (tb + 1))
                pl.wait("sync", t_x2r)
                tld = None
                for c in range(HC):
                    tld = pl.dma("sync", lambda h, c=c, tbsl=tbsl: h.dma_start(
                        out=x2r_v(c), in_=x2_sp[128 * c:128 * (c + 1), tbsl]), "dAR")
                pl.wait("vector", tld)

                def ln4_out(tb2, sl2, tbsl=tbsl):
                    t = None
                    for c in range(HC):
                        t = pl.step("vector", lambda h, c=c: h.tensor_tensor(
                            V("q2scr", 512 * c, 512), outp_v(c)[:, tbsl], ps[6][:, :],
                            op=ALU.subtract))
                    pl.wait("vector", t)
                    for c in range(HC):
                        t = pl.step("vector", lambda h, c=c: h.tensor_tensor(
                            V("q2scr", 512 * c, 512), V("q2scr", 512 * c, 512),
                            ps[7][:, :], op=ALU.mult))
                    pl.wait("vector", t)
                    vt = None
                    for c in range(HC):
                        vt = pl.step("vector", lambda h, c=c: h.tensor_tensor(
                            x2r_v(c), V("q2scr", 512 * c, 512), x2r_v(c), op=ALU.add))
                    return vt

                emit_ln(HC, lambda c, tbsl=tbsl: outp_v(c)[:, tbsl], "q2scr",
                        ln4_out, [0])
                t_ln4 = pl.tok("cV")
                for tl in range(4):
                    par = tl % 2
                    pl.wait("tensor", t_ln4)
                    pl.wait("tensor", t_yst[par])
                    tq = None
                    for c in range(HC):
                        bank = (c % 2) + 2
                        if y_bank_hist[c % 2] is not None:
                            pl.wait("tensor", y_bank_hist[c % 2])
                        tmm = pl.step("tensor", lambda h, c=c, tl=tl, bank=bank:
                                      h.transpose(ps[bank][:, 0:128],
                                                  x2r_v(c)[:, 128 * tl:128 * (tl + 1)],
                                                  ident[:]))
                        pl.wait("vector", tmm)
                        tq = pl.step("vector", lambda h, c=c, par=par, bank=bank:
                                     h.tensor_copy(ystg_v(par)[:, 128 * c:128 * (c + 1)],
                                                   ps[bank][:, 0:128]))
                        y_bank_hist[c % 2] = tq
                    pl.wait("gpsimd", tq)
                    r0 = 512 * tb + 128 * tl
                    t_yst[par] = pl.dma("gpsimd", lambda h, r0=r0, par=par: h.dma_start(
                        out=y[r0:r0 + 128, :], in_=ystg_v(par)),
                        "dY0" if par == 0 else "dY1")
                t_x2r = pl.tokT()
            pl.wait("gpsimd", pl.tok("dY0"))
            pl.wait("gpsimd", pl.tok("dY1"))
          except _StopEmit:
            _ffn_stopped = True


        if KSTAGE >= 3 and _ffn_stopped:
            pl.barrier()
            tms = pl.step("gpsimd", lambda h: h.memset(arena[:, 0:1024], 0.0))
            pl.wait("gpsimd", tms)
            for tt in range(NT):
                tst = pl.dma("gpsimd", lambda h, tt=tt: h.dma_start(
                    out=y[128 * tt:128 * (tt + 1), :], in_=arena[:, 0:1024]), "dG")
            pl.wait("gpsimd", tst)
        if KSTAGE < 3:
            pl.barrier()
            tms = pl.step("gpsimd", lambda h: h.memset(
                V("x" if KSTAGE < 2 else "avl", 0, 1024), 0.0))
            pl.wait("gpsimd", tms)
            tst = None
            for tt in range(NT):
                tst = pl.dma("gpsimd", lambda h, tt=tt: h.dma_start(
                    out=y[128 * tt:128 * (tt + 1), :],
                    in_=V("x" if KSTAGE < 2 else "avl", 0, 1024)), "dG")
            pl.wait("gpsimd", tst)
        if KDBG:
            pl.wait("gpsimd", pl.tok("cc"))
            pl.dma("gpsimd", lambda h: h.dma_start(out=dbg_ag[:, :], in_=ag_in[:, :]), "dG")
        pl.wait("gpsimd", pl.tok("cc"))
        pl.wait("gpsimd", pl.tok("dG"))
        pl.wait("gpsimd", pl.tok("dH"))
        pl.wait("gpsimd", pl.tok("dK"))
        pl.wait("gpsimd", pl.tok("dW"))

        pl.realize(block, sems)
    return nc


def kernel(L0_attn_inputs, L0_attn_W_uv, L0_attn_b_uv, L0_attn_gamma, L0_attn_beta,
           L0_attn_W_out, L0_attn_b_out, L0_ffn_w1, L0_ffn_b1, L0_ffn_w2, L0_ffn_b2):
    x = np.ascontiguousarray(np.asarray(L0_attn_inputs, dtype=np.float32))
    wuv = np.ascontiguousarray(np.asarray(L0_attn_W_uv, dtype=np.float32))
    buv = np.asarray(L0_attn_b_uv, dtype=np.float32)
    gam = np.asarray(L0_attn_gamma, dtype=np.float32)
    bet = np.asarray(L0_attn_beta, dtype=np.float32)
    wout = np.ascontiguousarray(np.asarray(L0_attn_W_out, dtype=np.float32))
    bout = np.asarray(L0_attn_b_out, dtype=np.float32)
    w1 = np.ascontiguousarray(np.asarray(L0_ffn_w1, dtype=np.float32))
    b1 = np.asarray(L0_ffn_b1, dtype=np.float32)
    w2 = np.ascontiguousarray(np.asarray(L0_ffn_w2, dtype=np.float32))
    b2 = np.asarray(L0_ffn_b2, dtype=np.float32)

    shared = dict(
        wuv=wuv,
        buv_row=buv.reshape(1, -1),
        buv_base=buv[2 * E:].reshape(SD, 1),
        gamT=np.ascontiguousarray(gam.T),
        betT=np.ascontiguousarray(bet.T),
        wout=wout,
        bout_pt=np.ascontiguousarray(bout.reshape(HC, 128).T),
        w1=w1,
        w1T=np.ascontiguousarray(w1.T),
        b1_pt=np.ascontiguousarray(b1.reshape(FF // 128, 128).T),
        w2=w2,
        w2T=np.ascontiguousarray(w2.T),
        b2_pt=np.ascontiguousarray(b2.reshape(HC, 128).T),
    )
    in_maps = []
    for c in range(NCORE_IDS):
        b, slc = c // 4, c % 4
        xs = x[b, TC * slc:TC * (slc + 1), :]
        m = dict(shared)
        m["xT"] = np.ascontiguousarray(xs.T)
        in_maps.append(m)

    nc = build_nc()
    res = run_bass_kernel_spmd(nc, in_maps, core_ids=list(range(8)))
    out = np.empty((B, S, H), dtype=np.float32)
    for c in range(8):
        b, slc = c // 4, c % 4
        out[b, TC * slc:TC * (slc + 1), :] = np.asarray(res.results[c]["y"])
    if KDBG:
        kernel.last_results = res.results
    return out




# version 10
# speedup vs baseline: 1.0098x; 1.0046x over previous
import os
import contextlib
import numpy as np
import ml_dtypes
import concourse.bass as bass
import concourse.mybir as mybir
from concourse.bass_utils import run_bass_kernel_spmd

F32 = mybir.dt.float32
BF16 = mybir.dt.bfloat16
I32 = mybir.dt.int32
AF = mybir.ActivationFunctionType
ALU = mybir.AluOpType
AX = mybir.AxisListType

B, S, H = 2, 8192, 1024
TC = 2048
E = 2048
SD = 128
FF = 4096
HC = H // 128
NT = TC // 128
NTB = TC // 512
MCN = S // 128
EPS_LN = 1e-5
NCORE_IDS = 8

KSTAGE = int(os.environ.get("KSTAGE", "99"))
KDBG = int(os.environ.get("KDBG", "0"))
KFFN = int(os.environ.get("KFFN", "9"))

ENGINES = ("tensor", "vector", "scalar", "gpsimd", "sync")

ARENA_COLS = 49152
_BUFS = {
    "x":       (0,     16384, 1, 1),
    "xn_hi":   (16384,  8192, 1, 2),
    "xn_lo":   (24576,  8192, 1, 2),
    "xsq":     (32768,  4096, 1, 1),
    "wblk":    (0,      8192, 2, 2),
    "stage":   (8192,   2048, 2, 2),
    "base":    (10240,  2048, 2, 2),
    "kown":    (12288,  2048, 2, 2),
    "wblk2":   (36864,  8192, 2, 2),
    "q_hi":    (33792,  1024, 2, 3),
    "q_lo":    (34816,  1024, 2, 3),
    "sq_hi":   (0,      8192, 3, 3),
    "sq_lo":   (8192,   8192, 3, 3),
    "kall_hi": (16384,  4096, 3, 3),
    "kall_lo": (20480,  4096, 3, 3),
    "vbuf":    (24576,  4096, 3, 3),
    "sqevac":  (28672,  1024, 3, 3),
    "avstage": (29696,  4096, 3, 3),
    "avl":     (0,      4096, 4, 4),
    "sul":     (40960,  4096, 4, 4),
    "g2":      (45056,  4096, 4, 4),
    "gsc":     (4096,   2048, 4, 4),
    "gbf":     (6144,   2048, 4, 4),
    "gT":      (8192,   4096, 4, 4),
    "outT":    (12288,  4096, 4, 4),
    "xTrel":   (16384,  4096, 4, 4),
    "osq":     (20480,  4096, 4, 4),
    "woutb":   (24576, 16384, 4, 4),
    "xi":      (0,      8192, 5, 5),
    "z":       (8192,   2048, 5, 5),
    "qscr":    (10240,  8192, 5, 5),
    "wf":      (18432, 16384, 5, 5),
    "hstage":  (34816,  2048, 5, 5),
    "zsq":     (36864,  4096, 5, 5),
    "x2str":   (40960,  4096, 5, 5),
    "w1str":   (45056,  2048, 5, 5),
    "hihalf":  (0,     16384, 6, 7),
    "hrel":    (16384,  4096, 6, 7),
    "q2scr":   (20480,  8192, 6, 7),
    "w2s":     (28672,  3072, 6, 7),
    "outp":    (32768, 16384, 6, 7),
}
for _i, (_n1, (_o1, _c1, _a1, _b1)) in enumerate(list(_BUFS.items())):
    assert _o1 + _c1 <= ARENA_COLS, _n1
    for _n2, (_o2, _c2, _a2, _b2) in list(_BUFS.items())[_i + 1:]:
        if _a1 <= _b2 and _a2 <= _b1:
            assert _o1 + _c1 <= _o2 or _o2 + _c2 <= _o1, f"overlap {_n1}/{_n2}"


class KB:

    def __init__(self, nc):
        self.nc = nc
        self.ops = {e: [] for e in ENGINES}
        self.cnt = {"cT": 0, "cV": 0, "cS": 0, "cG": 0, "dW": 0, "dG": 0,
                    "dH": 0, "dK": 0, "dB": 0, "dV0": 0, "dV1": 0, "dA0": 0,
                    "dA1": 0, "dAR": 0, "dSU": 0, "dXR": 0, "dXS": 0, "dWO": 0,
                    "dF0": 0, "dF1": 0, "dWS": 0, "dY0": 0, "dY1": 0, "dR": 0,
                    "dKL": 0, "cc": 0}
        self.eng_sem = {"tensor": "cT", "vector": "cV", "scalar": "cS", "gpsimd": "cG"}
        self.waited = {}

    def emit(self, eng, fn, inc=None, add=1):
        tok = None
        if inc is not None:
            self.cnt[inc] += add
            tok = (inc, self.cnt[inc])
        self.ops[eng].append(("emit", fn, inc, add))
        return tok

    def step(self, eng, fn):
        return self.emit(eng, fn, inc=self.eng_sem[eng])

    def dma(self, eng, fn, q):
        return self.emit(eng, fn, inc=q, add=16)

    def wait(self, eng, tok):
        if tok is None:
            return
        sem, val = tok
        key = (eng, sem)
        if self.waited.get(key, -1) >= val:
            return
        self.waited[key] = val
        self.ops[eng].append(("wait", sem, val, None))

    def tokT(self):
        return ("cT", self.cnt["cT"])

    def tok(self, s):
        return (s, self.cnt[s])

    def quant(self, src_ap, out_ap, y_ap, ci_ap, cm1_ap, d_ap, recip_ap, relu,
              entry=None):
        if entry is None:
            self.wait("vector", self.tok("cV"))
            self.wait("scalar", self.tok("cV"))
        else:
            vt, st0 = entry
            self.wait("vector", vt)
            self.wait("scalar", vt)
            if st0 is not None:
                self.wait("vector", st0)
                self.wait("scalar", st0)
        rc = 1.0 if recip_ap is None else recip_ap
        if relu:
            t = self.step("vector", lambda h: h.tensor_scalar(
                y_ap, src_ap, rc, 0.0, ALU.mult, ALU.max))
        else:
            t = self.step("vector", lambda h: h.tensor_scalar(
                y_ap, src_ap, rc, None, ALU.mult))
        self.wait("scalar", t)
        t = self.step("scalar", lambda h: h.activation(ci_ap, y_ap, AF.Copy))
        self.wait("scalar", t)
        stok = self.step("scalar", lambda h: h.activation(cm1_ap, ci_ap, AF.Copy, bias=-1.0))
        self.wait("vector", stok)
        t = self.step("vector", lambda h: h.tensor_tensor(
            d_ap, y_ap, cm1_ap, op=ALU.subtract))
        self.wait("vector", t)
        t = self.step("vector", lambda h: h.scalar_tensor_tensor(
            y_ap, d_ap, 1.5, cm1_ap, op0=ALU.is_gt, op1=ALU.add))
        self.wait("vector", t)
        vtok = self.step("vector", lambda h: h.scalar_tensor_tensor(
            out_ap, d_ap, 0.5, y_ap, op0=ALU.is_ge, op1=ALU.add))
        return (vtok, stok)

    def split(self, src_ap, hi_ap, lo_ap, eng="vector"):
        t = self.step(eng, lambda h: h.tensor_copy(hi_ap, src_ap))
        self.wait(eng, t)
        return self.step(eng, lambda h: h.tensor_tensor(lo_ap, src_ap, hi_ap,
                                                        op=ALU.subtract))

    def barrier(self, engines=("tensor", "vector", "scalar", "gpsimd", "sync")):
        snap = dict(self.cnt)
        for e in engines:
            own = self.eng_sem.get(e)
            for sem, val in snap.items():
                if sem == own or val == 0:
                    continue
                self.wait(e, (sem, val))

    def realize(self, block, sems):
        def runner(lst):
            def run(h):
                for item in lst:
                    if item[0] == "wait":
                        h.wait_ge(sems[item[1]], item[2])
                    else:
                        inst = item[1](h)
                        if item[2] is not None:
                            inst.then_inc(sems[item[2]], item[3])
            return run

        block.tensor(runner(self.ops["tensor"]))
        block.vector(runner(self.ops["vector"]))
        block.scalar(runner(self.ops["scalar"]))
        block.gpsimd(runner(self.ops["gpsimd"]))
        block.sync(runner(self.ops["sync"]))


def build_nc():
    nc = bass.Bass("TRN2", target_bir_lowering=False, debug=False)

    xT = nc.dram_tensor("xT", [H, TC], F32, kind="ExternalInput").ap()
    wuv = nc.dram_tensor("wuv", [H, 2 * E + SD], F32, kind="ExternalInput").ap()
    buv_row = nc.dram_tensor("buv_row", [1, 2 * E + SD], F32, kind="ExternalInput").ap()
    buv_base = nc.dram_tensor("buv_base", [SD, 1], F32, kind="ExternalInput").ap()
    gamT = nc.dram_tensor("gamT", [SD, 2], F32, kind="ExternalInput").ap()
    betT = nc.dram_tensor("betT", [SD, 2], F32, kind="ExternalInput").ap()
    wout = nc.dram_tensor("wout", [E, H], F32, kind="ExternalInput").ap()
    bout_pt = nc.dram_tensor("bout_pt", [128, HC], F32, kind="ExternalInput").ap()
    w1 = nc.dram_tensor("w1", [FF, H], F32, kind="ExternalInput").ap()
    w1T = nc.dram_tensor("w1T", [H, FF], F32, kind="ExternalInput").ap()
    b1_pt = nc.dram_tensor("b1_pt", [128, FF // 128], F32, kind="ExternalInput").ap()
    w2 = nc.dram_tensor("w2", [H, FF], F32, kind="ExternalInput").ap()
    w2T = nc.dram_tensor("w2T", [FF, H], F32, kind="ExternalInput").ap()
    b2_pt = nc.dram_tensor("b2_pt", [128, HC], F32, kind="ExternalInput").ap()
    y = nc.dram_tensor("y", [TC, H], F32, kind="ExternalOutput").ap()

    def scratch(name, shape, dt):
        if KDBG:
            return nc.dram_tensor(name, shape, dt, kind="ExternalOutput").ap()
        return nc.dram_tensor(name, shape, dt).ap()

    AGW = 2 * E + 2 * SD
    ag_in = nc.dram_tensor("ag_in", [AGW, TC], BF16).ap()
    ag_out = nc.dram_tensor("ag_out", [4 * AGW, TC], BF16).ap()
    dbg_ag = nc.dram_tensor("dbg_ag", [AGW, TC], BF16, kind="ExternalOutput").ap() if KDBG else None
    su_sp = scratch("su_sp", [2 * TC, E], BF16)
    av_sp = scratch("av_sp", [TC, E], F32)
    h_sp = scratch("h_sp", [FF, TC], F32)
    x2_sp = scratch("x2_sp", [H, TC], F32)
    w2i_sp = nc.dram_tensor("w2i_sp", [H, FF], BF16).ap()
    ar1_in = nc.dram_tensor("ar1_in", [1, 8], F32).ap()
    ar1_out = nc.dram_tensor("ar1_out", [1, 8], F32).ap()
    ar2_in = nc.dram_tensor("ar2_in", [1, 8], F32).ap()
    ar2_out = nc.dram_tensor("ar2_out", [1, 8], F32).ap()

    pl = KB(nc)

    with contextlib.ExitStack() as es:
        En = es.enter_context
        arena = En(nc.sbuf_tensor("arena", [128, ARENA_COLS], F32))

        def V(name, off, n, dt=F32):
            o, ctot, _, _ = _BUFS[name]
            assert off + n <= ctot, f"{name}+{off}+{n}"
            ap = arena[:, o + off:o + off + n]
            return ap if dt == F32 else ap.bitcast(dt)

        ident = En(nc.sbuf_tensor("ident", [128, 128], F32))
        ident_bf = En(nc.sbuf_tensor("ident_bf", [128, 128], BF16))
        ones1 = En(nc.sbuf_tensor("ones1", [1, 128], F32))
        onesK = En(nc.sbuf_tensor("onesK", [128, 1], F32))
        ones1_bf = En(nc.sbuf_tensor("ones1_bf", [1, 128], BF16))
        s_buvrow = En(nc.sbuf_tensor("s_buvrow", [1, 512], F32))
        s_brh = En(nc.sbuf_tensor("s_brh", [1, 512], BF16))
        s_brl = En(nc.sbuf_tensor("s_brl", [1, 512], BF16))
        s_buvb = En(nc.sbuf_tensor("s_buvb", [128, 1], F32))
        s_gb = En(nc.sbuf_tensor("s_gb", [128, 8], F32))
        s_bout = En(nc.sbuf_tensor("s_bout", [128, HC], F32))
        s_b1 = En(nc.sbuf_tensor("s_b1", [128, FF // 128], F32))
        s_b2 = En(nc.sbuf_tensor("s_b2", [128, HC], F32))
        rowM = En(nc.sbuf_tensor("rowM", [1, 512], F32))
        rowV = En(nc.sbuf_tensor("rowV", [1, 512], F32))
        rowT = En(nc.sbuf_tensor("rowT", [1, 512], F32))
        rowR = En(nc.sbuf_tensor("rowR", [1, 512], F32))
        s_sc = En(nc.sbuf_tensor("s_sc", [1, 16], F32))
        s_scp = En(nc.sbuf_tensor("s_scp", [128, 8], F32))
        s_sw1 = En(nc.sbuf_tensor("s_sw1", [128, FF // 128], F32))
        s_sw2 = En(nc.sbuf_tensor("s_sw2", [128, HC], F32))
        s_bq1 = En(nc.sbuf_tensor("s_bq1", [128, FF // 128], F32))
        s_bq2 = En(nc.sbuf_tensor("s_bq2", [128, HC], F32))
        s_sv1 = En(nc.sbuf_tensor("s_sv1", [128, 32], F32))
        s_sv2 = En(nc.sbuf_tensor("s_sv2", [128, 8], F32))
        s_tmp = En(nc.sbuf_tensor("s_tmp", [128, 160], F32))
        s_eps = En(nc.sbuf_tensor("s_eps", [1, 1], F32))
        ps = [En(nc.psum_tensor(f"ps{i}", [128, 512], F32)) for i in range(8)]

        sems = {}
        for s in ("cT", "cV", "cS", "cG", "dW", "dG", "dH", "dK", "dB", "dV0",
                  "dV1", "dA0", "dA1", "dAR", "dSU", "dXR", "dXS", "dWO",
                  "dF0", "dF1", "dWS", "dY0", "dY1", "dR", "dKL", "cc"):
            sems[s] = En(nc.semaphore(s))

        block = En(nc.Block())

        def gchain(fn):
            pl.wait("gpsimd", pl.tok("cG"))
            return pl.step("gpsimd", fn)

        gchain(lambda h: h.memset(ident[:], 0.0))
        gchain(lambda h: h.affine_select(
            out=ident[:], in_=ident[:], compare_op=ALU.not_equal,
            fill=1.0, base=0, pattern=[[-1, 128]], channel_multiplier=1))
        gchain(lambda h: h.memset(ident_bf[:], 0.0))
        gchain(lambda h: h.affine_select(
            out=ident_bf[:], in_=ident_bf[:], compare_op=ALU.not_equal,
            fill=1.0, base=0, pattern=[[-1, 128]], channel_multiplier=1))
        gchain(lambda h: h.memset(ones1[:], 1.0))
        gchain(lambda h: h.memset(onesK[:], 1.0))
        gchain(lambda h: h.memset(ones1_bf[:], 1.0))
        gchain(lambda h: h.memset(s_eps[:], EPS_LN))

        x_toks = [pl.dma("sync", lambda h, c=c: h.dma_start(
            out=V("x", 2048 * c, 2048), in_=xT[128 * c:128 * (c + 1), :]), "dW")
            for c in range(HC)]
        for dst, src in [(s_buvb[:], buv_base[:, :]),
                         (s_bout[:], bout_pt[:, :]), (s_b1[:], b1_pt[:, :]),
                         (s_b2[:], b2_pt[:, :]), (s_gb[:, 0:2], gamT[:, :]),
                         (s_gb[:, 2:4], betT[:, :])]:
            t_sm = pl.dma("sync", lambda h, d=dst, s=src: h.dma_start(out=d, in_=s), "dW")
        pl.wait("vector", t_sm)
        pl.emit("vector", lambda h: h.tensor_scalar(
            s_gb[:, 4:5], s_gb[:, 0:1], 1.0 / 8192.0, None, ALU.mult))
        t_gbs = pl.step("vector", lambda h: h.tensor_scalar(
            s_gb[:, 5:6], s_gb[:, 2:3], 1.0 / 8192.0, None, ALU.mult))

        def emit_ln(nchunk, src_of, sq_name, out_cb, tb_range):
            inv_n = 1.0 / (128 * nchunk)
            prev_vt = None
            for tb in tb_range:
                sl = slice(512 * tb, 512 * (tb + 1))
                pl.wait("vector", prev_vt)
                tv = None
                for c in range(nchunk):
                    xv = src_of(c)
                    xsq = V(sq_name, 512 * c, 512)
                    tv = pl.step("vector", lambda h, xv=xv, sl=sl, xsq=xsq:
                                 h.tensor_tensor(xsq, xv[:, sl], xv[:, sl], op=ALU.mult))
                pl.wait("tensor", tv)
                tm = None
                for c in range(nchunk):
                    xv = src_of(c)
                    xsq = V(sq_name, 512 * c, 512)
                    pl.emit("tensor", lambda h, xv=xv, sl=sl, c=c:
                            h.matmul(ps[6][0:1, :], onesK[:], xv[:, sl],
                                     start=(c == 0), stop=(c == nchunk - 1)))
                    tm = pl.step("tensor", lambda h, xsq=xsq, c=c:
                                 h.matmul(ps[7][0:1, :], onesK[:], xsq,
                                          start=(c == 0), stop=(c == nchunk - 1)))
                pl.wait("vector", tm)
                t1 = pl.step("vector", lambda h: h.tensor_scalar(
                    rowM[:], ps[6][0:1, :], inv_n, None, ALU.mult))
                pl.step("vector", lambda h: h.tensor_scalar(
                    rowV[:], ps[7][0:1, :], inv_n, None, ALU.mult))
                pl.wait("vector", t1)
                t3 = pl.step("vector", lambda h: h.tensor_tensor(
                    rowT[:], rowM[:], rowM[:], op=ALU.mult))
                pl.wait("vector", t3)
                tvv = pl.step("vector", lambda h: h.tensor_tensor(
                    rowV[:], rowV[:], rowT[:], op=ALU.subtract))
                pl.wait("scalar", tvv)
                tss = pl.step("scalar", lambda h: h.activation(
                    rowT[:], rowV[:], AF.Sqrt, bias=s_eps[:]))
                pl.wait("vector", tss)
                trr = pl.step("vector", lambda h: h.reciprocal(rowR[:], rowT[:]))
                pl.wait("tensor", trr)
                pl.emit("tensor", lambda h: h.matmul(
                    ps[6][:, :], ones1[:], rowM[:], start=True, stop=True))
                tbb = pl.step("tensor", lambda h: h.matmul(
                    ps[7][:, :], ones1[:], rowR[:], start=True, stop=True))
                pl.wait("vector", tbb)
                vt = out_cb(tb, sl)
                prev_vt = vt
                pl.wait("tensor", vt)
                pl.wait("scalar", vt)

        pl.barrier()
        pl.wait("vector", t_sm)
        pl.wait("vector", t_gbs)

        def ln1_out(tb, sl):
            t = None
            for c in range(HC):
                xv, tmp = V("x", 2048 * c, 2048), V("xsq", 512 * c, 512)
                t = pl.step("vector", lambda h, xv=xv, tmp=tmp, sl=sl: h.tensor_tensor(
                    tmp, xv[:, sl], ps[6][:, :], op=ALU.subtract))
            pl.wait("vector", t)
            for c in range(HC):
                tmp = V("xsq", 512 * c, 512)
                t = pl.step("vector", lambda h, tmp=tmp: h.tensor_tensor(
                    tmp, tmp, ps[7][:, :], op=ALU.mult))
            pl.wait("vector", t)
            for c in range(HC):
                tmp, hi = V("xsq", 512 * c, 512), V("xn_hi", 1024 * c, 1024, BF16)
                t = pl.step("vector", lambda h, hi=hi, sl=sl, tmp=tmp:
                            h.tensor_copy(hi[:, sl], tmp))
            pl.wait("vector", t)
            vt = None
            for c in range(HC):
                tmp = V("xsq", 512 * c, 512)
                hi = V("xn_hi", 1024 * c, 1024, BF16)
                lo = V("xn_lo", 1024 * c, 1024, BF16)
                vt = pl.step("vector", lambda h, hi=hi, lo=lo, sl=sl, tmp=tmp:
                             h.tensor_tensor(lo[:, sl], tmp, hi[:, sl], op=ALU.subtract))
            return vt

        emit_ln(HC, lambda c: V("x", 2048 * c, 2048), "xsq", ln1_out, range(NTB))
        t_xn = pl.tok("cV")

        pl.barrier()

        def wv_f32(c, wp=0):
            return V("wblk" if wp == 0 else "wblk2", 512 * c, 512)

        def wv_hi(c, wp=0):
            return V("wblk" if wp == 0 else "wblk2", 4096 + 256 * c, 256, BF16)

        def wv_lo(c, wp=0):
            return V("wblk" if wp == 0 else "wblk2", 6144 + 256 * c, 256, BF16)

        def load_wblk(col0, ncols, guards, wp=0, wq="dW"):
            toks = []
            for g in (guards or []):
                pl.wait("sync", g)
            for c in range(HC):
                toks.append(pl.dma("sync", lambda h, c=c, col0=col0, ncols=ncols, wp=wp:
                            h.dma_start(out=wv_f32(c, wp)[:, 0:ncols],
                                        in_=wuv[128 * c:128 * (c + 1), col0:col0 + ncols]),
                            wq))
            pl.wait("vector", toks[-1])
            for g in (guards or []):
                pl.wait("vector", g)
            t = None
            for c in range(HC):
                t = pl.step("vector", lambda h, c=c, n=ncols, wp=wp: h.tensor_copy(
                    wv_hi(c, wp)[:, 0:n], wv_f32(c, wp)[:, 0:n]))
            pl.wait("vector", t)
            last = None
            for c in range(HC):
                last = pl.step("vector", lambda h, c=c, n=ncols, wp=wp: h.tensor_tensor(
                    wv_lo(c, wp)[:, 0:n], wv_f32(c, wp)[:, 0:n], wv_hi(c, wp)[:, 0:n],
                    op=ALU.subtract))
            return last

        t_w = load_wblk(2 * E, SD, None)
        pl.wait("tensor", t_w)
        pl.wait("tensor", t_xn)
        t_base_mm = None
        for tb in range(NTB):
            sl = slice(512 * tb, 512 * (tb + 1))
            for c in range(HC):
                whi, wlo = wv_hi(c)[:, 0:128], wv_lo(c)[:, 0:128]
                xhi = V("xn_hi", 1024 * c, 1024, BF16)
                xlo = V("xn_lo", 1024 * c, 1024, BF16)
                pl.emit("tensor", lambda h, whi=whi, xhi=xhi, sl=sl, st=(c == 0):
                        h.matmul(ps[0][:, :], whi, xhi[:, sl], start=st, stop=False))
                pl.emit("tensor", lambda h, whi=whi, xlo=xlo, sl=sl:
                        h.matmul(ps[0][:, :], whi, xlo[:, sl], start=False, stop=False))
                t_base_mm = pl.step("tensor", lambda h, wlo=wlo, xhi=xhi, sl=sl, c=c:
                                    h.matmul(ps[0][:, :], wlo, xhi[:, sl],
                                             start=False, stop=(c == HC - 1)))
            pl.wait("scalar", t_base_mm)
            t_base_ev = pl.step("scalar", lambda h, sl=sl: h.activation(
                V("base", 0, 2048)[:, sl], ps[0][:, :], AF.Silu, bias=s_buvb[:]))
            pl.wait("tensor", t_base_ev)
        pl.wait("vector", t_base_ev)
        bb = V("base", 0, 2048)
        tq1 = V("stage", 0, 2048)
        tq2 = V("wblk", 0, 2048)
        qh, ql = V("q_hi", 0, 1024, BF16), V("q_lo", 0, 1024, BF16)
        kh, kl = V("kown", 0, 1024, BF16), V("kown", 1024, 1024, BF16)
        pl.emit("vector", lambda h: h.tensor_scalar(
            tq1, bb, s_gb[:, 4:5], s_gb[:, 5:6], ALU.mult, ALU.add))
        t = pl.step("vector", lambda h: h.tensor_scalar(
            tq2, bb, s_gb[:, 1:2], s_gb[:, 3:4], ALU.mult, ALU.add))
        pl.wait("vector", t)
        pl.emit("vector", lambda h: h.tensor_copy(qh, tq1))
        t = pl.step("vector", lambda h: h.tensor_copy(kh, tq2))
        pl.wait("vector", t)
        pl.emit("vector", lambda h: h.tensor_tensor(ql, tq1, qh, op=ALU.subtract))
        t_k = pl.step("vector", lambda h: h.tensor_tensor(kl, tq2, kh, op=ALU.subtract))
        pl.wait("gpsimd", t_k)
        pl.dma("gpsimd", lambda h: h.dma_start(out=ag_in[2 * E:2 * E + SD, :], in_=kh), "dK")
        pl.dma("gpsimd", lambda h: h.dma_start(out=ag_in[2 * E + SD:, :], in_=kl), "dK")

        def suv_f32(par):
            return V("stage", 1024 * par, 512)

        def suv_hi(par):
            return V("stage", 1024 * par + 512, 256, BF16)

        def suv_lo(par):
            return V("stage", 1024 * par + 768, 256, BF16)

        t_wprev = pl.tok("cV")
        stage_free = [t_k, t_k]
        t_ev_hist = [None, None]
        blk_order = list(range(4, 8)) + list(range(4))
        t_w_pref = {}
        consumed_tok = {}
        for bi, blk in enumerate(blk_order):
            is_v = blk >= 4
            col0 = 512 * blk
            wp = bi % 2
            if bi == 0:
                t_w_pref[0] = load_wblk(col0, 512, [pl.tokT(), t_k], wp=0, wq="dW")
            t_w = t_w_pref[bi]
            if bi + 1 < len(blk_order):
                g = [consumed_tok[bi - 1]] if bi >= 1 else []
                t_w_pref[bi + 1] = load_wblk(512 * blk_order[bi + 1], 512, g,
                                             wp=(bi + 1) % 2,
                                             wq="dW" if (bi + 1) % 2 == 0 else "dKL")
            if bi >= 1:
                pl.wait("sync", consumed_tok[bi - 1])
            t_brow = pl.dma("sync", lambda h, col0=col0: h.dma_start(
                out=s_buvrow[:, :], in_=buv_row[:, col0:col0 + 512]), "dB")
            pl.wait("vector", t_brow)
            pl.wait("vector", pl.tokT())
            t_brow = pl.split(s_buvrow[:, :], s_brh[:, :], s_brl[:, :])
            pl.wait("tensor", t_w)
            pl.wait("tensor", t_brow)
            for tt in range(NT):
                par = tt % 2
                bank = tt % 2
                if t_ev_hist[bank] is not None:
                    pl.wait("tensor", t_ev_hist[bank])
                for c in range(HC):
                    whi512 = wv_hi(c, wp)
                    wlo512 = wv_lo(c, wp)
                    xhi = V("xn_hi", 1024 * c, 1024, BF16)[:, 128 * tt:128 * (tt + 1)]
                    xlo = V("xn_lo", 1024 * c, 1024, BF16)[:, 128 * tt:128 * (tt + 1)]
                    pl.emit("tensor", lambda h, a=xhi, b=whi512, bank=bank, st=(c == 0):
                            h.matmul(ps[bank][:, :], a, b, start=st, stop=False))
                    pl.emit("tensor", lambda h, a=xhi, b=wlo512, bank=bank:
                            h.matmul(ps[bank][:, :], a, b, start=False, stop=False))
                    pl.emit("tensor", lambda h, a=xlo, b=whi512, bank=bank, sp=(os.environ.get("KNB2") and c == HC - 1):
                            h.matmul(ps[bank][:, :], a, b, start=False, stop=bool(sp)))
                if os.environ.get("KNB2"):
                    t_mm = pl.step("tensor", lambda h: h.nop())
                elif os.environ.get("KNOBIAS"):
                    t_mm = pl.step("tensor", lambda h, bank=bank: h.matmul(
                        ps[bank][:, :], ones1_bf[:], s_brh[:, :],
                        start=False, stop=True))
                else:
                    pl.emit("tensor", lambda h, bank=bank: h.matmul(
                        ps[bank][:, :], ones1_bf[:], s_brh[:, :],
                        start=False, stop=False))
                    t_mm = pl.step("tensor", lambda h, bank=bank: h.matmul(
                        ps[bank][:, :], ones1_bf[:], s_brl[:, :],
                        start=False, stop=True))
                pl.wait("scalar", t_mm)
                if stage_free[par] is not None:
                    pl.wait("scalar", stage_free[par])
                t_ev = pl.step("scalar", lambda h, bank=bank, par=par: h.activation(
                    suv_f32(par), ps[bank][:, :], AF.Silu))
                t_ev_hist[bank] = t_ev
                pl.wait("vector", t_ev)
                t_sp = pl.split(suv_f32(par), suv_hi(par), suv_lo(par))
                pl.wait("gpsimd", t_sp)
                r0 = 128 * tt
                dq = "dG" if par == 0 else "dH"
                if is_v:
                    cs = col0 - E
                    pl.dma("gpsimd", lambda h, r0=r0, cs=cs, par=par: h.dma_start(
                        out=ag_in[r0:r0 + 128, cs:cs + 512], in_=suv_hi(par)), dq)
                    stage_free[par] = pl.dma("gpsimd",
                        lambda h, r0=r0, cs=cs, par=par: h.dma_start(
                            out=ag_in[TC + r0:TC + r0 + 128, cs:cs + 512],
                            in_=suv_lo(par)), dq)
                else:
                    pl.dma("gpsimd", lambda h, r0=r0, cs=col0, par=par: h.dma_start(
                        out=su_sp[r0:r0 + 128, cs:cs + 512], in_=suv_hi(par)), dq)
                    stage_free[par] = pl.dma("gpsimd",
                        lambda h, r0=r0, cs=col0, par=par: h.dma_start(
                            out=su_sp[TC + r0:TC + r0 + 128, cs:cs + 512],
                            in_=suv_lo(par)), dq)
            consumed_tok[bi] = pl.tokT()
            if blk == 7:
                pl.wait("gpsimd", pl.tok("dG"))
                pl.wait("gpsimd", pl.tok("dH"))
                pl.wait("gpsimd", pl.tok("dK"))
                pl.emit("gpsimd", lambda h: h.collective_compute(
                    "AllGather", ALU.bypass,
                    replica_groups=[[0, 1, 2, 3], [4, 5, 6, 7]],
                    ins=[ag_in.opt()],
                    outs=[ag_out.opt()]), inc="cc")




        if KSTAGE >= 2:
            qh_v = V("q_hi", 0, 1024, BF16)
            ql_v = V("q_lo", 0, 1024, BF16)
            kh_v = V("kall_hi", 0, 4096, BF16)
            kl_v = V("kall_lo", 0, 4096, BF16)
            sqh_v = V("sq_hi", 0, 8192, BF16)
            sql_v = V("sq_lo", 0, 8192, BF16)

            def vb_hi(p):
                return V("vbuf", 2048 * p, 1024, BF16)

            def vb_lo(p):
                return V("vbuf", 2048 * p + 1024, 1024, BF16)

            def scr2(p):
                return V("sqevac", 512 * p, 512)

            def sq22(p):
                return V("avstage", 512 * p, 512)

            def avst(ns):
                return V("avstage", 2048 * ns, 2048)

            pl.wait("sync", pl.tok("cT"))
            pl.wait("sync", pl.tok("cc"))
            for e in ("vector", "scalar", "tensor"):
                pl.wait(e, pl.tok("dG"))
                pl.wait(e, pl.tok("dH"))
            tkl = None
            for r in range(4):
                b0 = r * AGW
                pl.dma("sync", lambda h, b0=b0, r=r: h.dma_start(
                    out=kh_v[:, 2048 * r:2048 * (r + 1)],
                    in_=ag_out[b0 + 2 * E:b0 + 2 * E + SD, :]), "dKL")
                tkl = pl.dma("sync", lambda h, b0=b0, r=r: h.dma_start(
                    out=kl_v[:, 2048 * r:2048 * (r + 1)],
                    in_=ag_out[b0 + 2 * E + SD:b0 + AGW, :]), "dKL")
            pl.wait("tensor", tkl)

            av_guard = [None, None]
            prev_evac = [pl.tok("cS"), pl.tok("cS")]
            vcons = {}

            for nb in range(8):
                nsl = slice(256 * nb, 256 * (nb + 1))
                pl.wait("tensor", prev_evac[0])
                pl.wait("tensor", prev_evac[1])
                pl.wait("vector", av_guard[0])
                pl.wait("scalar", av_guard[0])
                ev_tok = {}
                sq_tok = {}
                for mp in range(MCN // 2):
                    bank = mp % 2
                    if mp >= 2:
                        pl.wait("tensor", ev_tok[mp - 2])
                    t_mm = None
                    for sub in range(2):
                        mc = 2 * mp + sub
                        ksl = slice(128 * mc, 128 * (mc + 1))
                        osl = slice(256 * sub, 256 * (sub + 1))
                        pl.emit("tensor", lambda h, ksl=ksl, nsl=nsl, bank=bank, osl=osl:
                                h.matmul(ps[bank][:, osl], kh_v[:, ksl], qh_v[:, nsl],
                                         start=True, stop=False))
                        pl.emit("tensor", lambda h, ksl=ksl, nsl=nsl, bank=bank, osl=osl:
                                h.matmul(ps[bank][:, osl], kh_v[:, ksl], ql_v[:, nsl],
                                         start=False, stop=False))
                        t_mm = pl.step("tensor", lambda h, ksl=ksl, nsl=nsl, bank=bank, osl=osl:
                                       h.matmul(ps[bank][:, osl], kl_v[:, ksl],
                                                qh_v[:, nsl], start=False, stop=True))
                    pl.wait("scalar", t_mm)
                    if mp >= 2:
                        pl.wait("scalar", sq_tok[mp - 2])
                    ev_tok[mp] = pl.step("scalar", lambda h, bank=bank:
                                         h.activation(scr2(bank), ps[bank][:, 0:512],
                                                      AF.Relu))
                    pl.wait("vector", ev_tok[mp])
                    if mp >= 2:
                        pl.wait("vector", sq_tok[mp - 2])
                    tsq = pl.step("vector", lambda h, p=bank: h.tensor_tensor(
                        sq22(p), scr2(p), scr2(p), op=ALU.mult))
                    pl.wait("vector", tsq)
                    sq_tok[mp] = pl.split(
                        sq22(bank), sqh_v[:, 512 * mp:512 * (mp + 1)],
                        sql_v[:, 512 * mp:512 * (mp + 1)])
                t_sq_all = sq_tok[MCN // 2 - 1]

                pl.wait("tensor", t_sq_all)
                pl.wait("tensor", ev_tok[MCN // 2 - 1])
                t_av = None
                for mc in range(MCN):
                    par = mc % 2
                    r, lt = mc // 16, mc % 16
                    b0 = r * AGW
                    if mc >= 2:
                        pl.wait("sync", vcons[(nb, mc - 2)])
                    else:
                        pl.wait("sync", vcons.get((nb - 1, MCN - 2 + mc), None))
                    pl.dma("sync", lambda h, b0=b0, lt=lt, par=par: h.dma_start(
                        out=vb_hi(par), in_=ag_out[b0 + 128 * lt:b0 + 128 * (lt + 1), :]),
                        "dV0" if par == 0 else "dV1")
                    pl.dma("sync", lambda h, b0=b0, lt=lt, par=par: h.dma_start(
                        out=vb_lo(par),
                        in_=ag_out[b0 + TC + 128 * lt:b0 + TC + 128 * (lt + 1), :]),
                        "dV0" if par == 0 else "dV1")
                    pl.wait("tensor", pl.tok("dV0" if par == 0 else "dV1"))
                    for ns in range(2):
                        s_hi = sqh_v[:, 256 * mc + 128 * ns:256 * mc + 128 * (ns + 1)]
                        s_lo = sql_v[:, 256 * mc + 128 * ns:256 * mc + 128 * (ns + 1)]
                        for eq in range(4):
                            bank = ns * 4 + eq
                            esl = slice(512 * eq, 512 * (eq + 1))
                            st = (mc == 0)
                            sp = (mc == MCN - 1)
                            pl.emit("tensor", lambda h, a=s_hi, p=par, esl=esl, bank=bank, st=st:
                                    h.matmul(ps[bank][:, :], a, vb_hi(p)[:, esl],
                                             start=st, stop=False))
                            pl.emit("tensor", lambda h, a=s_hi, p=par, esl=esl, bank=bank:
                                    h.matmul(ps[bank][:, :], a, vb_lo(p)[:, esl],
                                             start=False, stop=False))
                            t_av = pl.step("tensor",
                                           lambda h, a=s_lo, p=par, esl=esl, bank=bank, sp=sp:
                                           h.matmul(ps[bank][:, :], a, vb_hi(p)[:, esl],
                                                    start=False, stop=sp))
                    vcons[(nb, mc)] = t_av
                for ns in range(2):
                    pl.wait("scalar", t_av)
                    pl.wait("scalar", av_guard[ns])
                    tev = None
                    for eq in range(4):
                        bank = ns * 4 + eq
                        tev = pl.step("scalar", lambda h, bank=bank, ns=ns, eq=eq:
                                      h.copy(avst(ns)[:, 512 * eq:512 * (eq + 1)],
                                             ps[bank][:, :]))
                    prev_evac[ns] = tev
                    pl.wait("gpsimd", tev)
                    r0 = 256 * nb + 128 * ns
                    av_guard[ns] = pl.dma("gpsimd", lambda h, r0=r0, ns=ns: h.dma_start(
                        out=av_sp[r0:r0 + 128, :], in_=avst(ns)),
                        "dA0" if ns == 0 else "dA1")

        if KSTAGE >= 2:
            pl.barrier()
            def wo_hi(ec):
                return V("woutb", 512 * ec, 512, BF16)

            def wo_lo(ec):
                return V("woutb", 8192 + 512 * ec, 512, BF16)

            t_wo = None
            stg = V("osq", 0, 1024)
            for ec in range(16):
                if t_wo is not None:
                    pl.wait("sync", t_wo)
                tld = pl.dma("sync", lambda h, ec=ec, stg=stg: h.dma_start(
                    out=stg, in_=wout[128 * ec:128 * (ec + 1), :]), "dWO")
                pl.wait("vector", tld)
                t_wo = pl.split(stg, wo_hi(ec), wo_lo(ec))

            def avl_p(p):
                return V("avl", 2048 * p, 2048)

            def gsc_p(p):
                return V("gsc", 0, 2048) if p == 0 else V("g2", 0, 2048)

            def gbh_p(p):
                return (V("gbf", 0, 1024, BF16) if p == 0 else V("g2", 2048, 1024, BF16))

            def gbl_p(p):
                return (V("gbf", 1024, 1024, BF16) if p == 0 else V("g2", 3072, 1024, BF16))

            def suh_p(p):
                return V("sul", 2048 * p, 1024, BF16)

            def sul_p(p):
                return V("sul", 2048 * p + 1024, 1024, BF16)
            gsc_v = V("gsc", 0, 2048)
            gbh_v = V("gbf", 0, 1024, BF16)
            gbl_v = V("gbf", 1024, 1024, BF16)

            def gT_hi(ec):
                return V("gT", 128 * ec, 128, BF16)

            def gT_lo(ec):
                return V("gT", 2048 + 128 * ec, 128, BF16)

            t_gcons = None
            gcons_hist = [None, None]
            gT_cons = [None, None]
            t_x2st = None
            t_ln2v = None
            for tb in range(NTB):
                tbsl = slice(512 * tb, 512 * (tb + 1))
                pl.wait("sync", t_x2st)
                pl.wait("sync", t_ln2v)
                txr = None
                for c in range(HC):
                    txr = pl.dma("sync", lambda h, c=c, tbsl=tbsl: h.dma_start(
                        out=V("xTrel", 512 * c, 512),
                        in_=xT[128 * c:128 * (c + 1), tbsl]), "dXR")
                t_oute = None
                for htb in range(2):
                    for tl in range(2):
                        tt = 4 * tb + 2 * htb + tl
                        lp = tt % 2
                        pl.wait("sync", gcons_hist[lp])
                        if tb == 0 and htb == 0 and tl == 0:
                            pl.wait("sync", pl.tok("dA0"))
                            pl.wait("sync", pl.tok("dA1"))
                        tar = pl.dma("sync", lambda h, tt=tt, lp=lp: h.dma_start(
                            out=avl_p(lp), in_=av_sp[128 * tt:128 * (tt + 1), :]),
                            "dAR" if lp == 0 else "dKL")
                        tsr = pl.dma("sync", lambda h, tt=tt, lp=lp: h.dma_start(
                            out=suh_p(lp), in_=su_sp[128 * tt:128 * (tt + 1), :]),
                            "dSU" if lp == 0 else "dB")
                        tsr = pl.dma("sync", lambda h, tt=tt, lp=lp: h.dma_start(
                            out=sul_p(lp), in_=su_sp[TC + 128 * tt:TC + 128 * (tt + 1), :]),
                            "dSU" if lp == 0 else "dB")
                        pl.wait("vector", tar)
                        pl.wait("vector", tsr)
                        pl.wait("vector", gT_cons[lp])
                        tg = pl.step("vector", lambda h, lp=lp: h.tensor_tensor(
                            gsc_p(lp), suh_p(lp), sul_p(lp), op=ALU.add))
                        pl.wait("vector", tg)
                        t_gcons = pl.step("vector", lambda h, lp=lp: h.tensor_tensor(
                            gsc_p(lp), gsc_p(lp), avl_p(lp), op=ALU.mult))
                        gcons_hist[lp] = t_gcons
                        pl.wait("vector", t_gcons)
                        tsp = pl.split(gsc_p(lp), gbh_p(lp), gbl_p(lp))
                        pl.wait("tensor", tsp)
                        pl.wait("tensor", t_ln2v)
                        tq = None
                        tq_hist = [None, None]
                        for i2, (ec, half) in enumerate(
                                (e, hh) for e in range(16) for hh in (0, 1)):
                            gin = gbh_p(lp) if half == 0 else gbl_p(lp)
                            gout = gT_hi if half == 0 else gT_lo
                            bank = 6 + (i2 % 2)
                            if tq_hist[i2 % 2] is not None:
                                pl.wait("tensor", tq_hist[i2 % 2])
                            tt_mm = pl.step("tensor", lambda h, gin=gin, ec=ec, bank=bank:
                                            h.transpose(ps[bank][:, 0:64].bitcast(BF16),
                                                        gin[:, 128 * ec:128 * (ec + 1)],
                                                        ident_bf[:]))
                            pl.wait("vector", tt_mm)
                            tq = pl.step("vector", lambda h, gout=gout, ec=ec, tl=tl, bank=bank:
                                         h.tensor_copy(
                                             gout(ec)[:, 128 * tl:128 * (tl + 1)],
                                             ps[bank][:, 0:64].bitcast(BF16)))
                            tq_hist[i2 % 2] = tq
                        gT_cons[lp] = pl.tokT()
                    pl.wait("tensor", tq)
                    pl.wait("tensor", t_oute)
                    if tb == 0 and htb == 0:
                        pl.wait("tensor", t_wo)
                        pl.wait("tensor", prev_evac[0])
                        pl.wait("tensor", prev_evac[1])
                    t_oute_prev = t_oute
                    t_oute = None
                    for hp in range(2):
                        pl.wait("tensor", t_oute if t_oute else t_oute_prev)
                        t_op = None
                        for ec in range(16):
                            for hq2 in range(4):
                                ht = 4 * hp + hq2
                                st = (ec == 0)
                                sp = (ec == 15)
                                hsl = slice(128 * ht, 128 * (ht + 1))
                                pl.emit("tensor", lambda h, ec=ec, hsl=hsl, hq2=hq2, st=st:
                                        h.matmul(ps[hq2][:, 0:256], wo_hi(ec)[:, hsl],
                                                 gT_hi(ec)[:, :], start=st, stop=False))
                                pl.emit("tensor", lambda h, ec=ec, hsl=hsl, hq2=hq2:
                                        h.matmul(ps[hq2][:, 0:256], wo_lo(ec)[:, hsl],
                                                 gT_hi(ec)[:, :], start=False, stop=False))
                                t_op = pl.step("tensor", lambda h, ec=ec, hsl=hsl, hq2=hq2, sp=sp:
                                               h.matmul(ps[hq2][:, 0:256], wo_hi(ec)[:, hsl],
                                                        gT_lo(ec)[:, :], start=False, stop=sp))
                        pl.wait("vector", t_op)
                        for hq2 in range(4):
                            ht = 4 * hp + hq2
                            t_oute = pl.step("vector", lambda h, ht=ht, hq2=hq2, htb=htb:
                                             h.tensor_scalar(
                                                 V("outT", 512 * ht, 512)[:, 256 * htb:256 * (htb + 1)],
                                                 ps[hq2][:, 0:256], s_bout[:, ht:ht + 1], None,
                                                 ALU.add))
                pl.wait("tensor", t_oute)
                pl.wait("vector", t_oute)
                pl.wait("vector", txr)

                def ln2_out(tb2, sl2):
                    t = None
                    for c in range(HC):
                        ov, tmp = V("outT", 512 * c, 512), V("osq", 512 * c, 512)
                        t = pl.step("vector", lambda h, ov=ov, tmp=tmp: h.tensor_tensor(
                            tmp, ov, ps[6][:, :], op=ALU.subtract))
                    pl.wait("vector", t)
                    for c in range(HC):
                        tmp = V("osq", 512 * c, 512)
                        t = pl.step("vector", lambda h, tmp=tmp: h.tensor_tensor(
                            tmp, tmp, ps[7][:, :], op=ALU.mult))
                    pl.wait("vector", t)
                    vt = None
                    for c in range(HC):
                        tmp, xr = V("osq", 512 * c, 512), V("xTrel", 512 * c, 512)
                        vt = pl.step("vector", lambda h, tmp=tmp, xr=xr: h.tensor_tensor(
                            xr, tmp, xr, op=ALU.add))
                    return vt

                emit_ln(HC, lambda c: V("outT", 512 * c, 512), "osq", ln2_out, [0])
                t_ln2v = pl.tok("cV")
                pl.wait("gpsimd", t_ln2v)
                for c in range(HC):
                    t_x2st = pl.dma("gpsimd", lambda h, c=c, tbsl=tbsl: h.dma_start(
                        out=x2_sp[128 * c:128 * (c + 1), tbsl],
                        in_=V("xTrel", 512 * c, 512)), "dXS")

        class _StopEmit(Exception):
            pass

        _ffn_stopped = False
        if KSTAGE >= 3:
          try:
            pl.barrier()

            def xi_v(dc):
                return V("xi", 1024 * dc, 1024, BF16)

            def wiT_v(dc):
                return V("wf", 2048 * dc, 2048, BF16)

            def x2s_v(c):
                return V("x2str", 512 * c, 512)

            def zsq_v(c):
                return V("zsq", 512 * c, 512)

            def m_row(tb):
                return V("z", 512 * tb, 512)[0:1, :]

            def r_row(tb):
                return V("hstage", 512 * tb, 512)[0:1, :]

            def q_y(p):
                return V("qscr", 2048 * p, 512)

            def q_ci(p):
                return V("qscr", 2048 * p + 512, 512).bitcast(I32)

            def q_cm1(p):
                return V("qscr", 2048 * p + 1024, 512)

            def q_d(p):
                return V("qscr", 2048 * p + 1536, 512)

            def w1s_v(p):
                return V("w1str", 1024 * p, 1024)

            def hst_v(p):
                return V("hstage", 512 * p, 512)

            t_x2cons = None
            rmax_cols = []
            for tb in range(NTB):
                tbsl = slice(512 * tb, 512 * (tb + 1))
                pl.wait("sync", t_x2cons)
                pl.wait("sync", pl.tok("dXS"))
                tld = None
                for c in range(HC):
                    tld = pl.dma("sync", lambda h, c=c, tbsl=tbsl: h.dma_start(
                        out=x2s_v(c), in_=x2_sp[128 * c:128 * (c + 1), tbsl]), "dXR")
                pl.wait("vector", tld)
                t = None
                for c in range(HC):
                    t = pl.step("vector", lambda h, c=c: h.tensor_tensor(
                        zsq_v(c), x2s_v(c), x2s_v(c), op=ALU.mult))
                pl.wait("tensor", t)
                tm = None
                for c in range(HC):
                    pl.emit("tensor", lambda h, c=c: h.matmul(
                        ps[6][0:1, :], onesK[:], x2s_v(c),
                        start=(c == 0), stop=(c == HC - 1)))
                    tm = pl.step("tensor", lambda h, c=c: h.matmul(
                        ps[7][0:1, :], onesK[:], zsq_v(c),
                        start=(c == 0), stop=(c == HC - 1)))
                pl.wait("vector", tm)
                t1 = pl.step("vector", lambda h: h.tensor_scalar(
                    rowM[:], ps[6][0:1, :], 1.0 / H, None, ALU.mult))
                pl.step("vector", lambda h: h.tensor_scalar(
                    rowV[:], ps[7][0:1, :], 1.0 / H, None, ALU.mult))
                pl.wait("vector", t1)
                t3 = pl.step("vector", lambda h: h.tensor_tensor(
                    rowT[:], rowM[:], rowM[:], op=ALU.mult))
                pl.wait("vector", t3)
                tvv = pl.step("vector", lambda h: h.tensor_tensor(
                    rowV[:], rowV[:], rowT[:], op=ALU.subtract))
                pl.wait("scalar", tvv)
                tss = pl.step("scalar", lambda h: h.activation(
                    rowT[:], rowV[:], AF.Sqrt, bias=s_eps[:]))
                pl.wait("vector", tss)
                trr = pl.step("vector", lambda h: h.reciprocal(rowR[:], rowT[:]))
                pl.wait("vector", trr)
                pl.emit("vector", lambda h, tb=tb: h.tensor_copy(m_row(tb), rowM[:]))
                trc = pl.step("vector", lambda h, tb=tb: h.tensor_copy(r_row(tb), rowR[:]))
                pl.wait("tensor", trc)
                pl.emit("tensor", lambda h: h.matmul(
                    ps[6][:, :], ones1[:], rowM[:], start=True, stop=True))
                tbb = pl.step("tensor", lambda h: h.matmul(
                    ps[7][:, :], ones1[:], rowR[:], start=True, stop=True))
                pl.wait("vector", tbb)
                t = None
                for c in range(HC):
                    t = pl.step("vector", lambda h, c=c: h.tensor_tensor(
                        zsq_v(c), x2s_v(c), ps[6][:, :], op=ALU.subtract))
                pl.wait("vector", t)
                for c in range(HC):
                    t = pl.step("vector", lambda h, c=c: h.tensor_tensor(
                        zsq_v(c), zsq_v(c), ps[7][:, :], op=ALU.mult))
                pl.wait("vector", t)
                for c in range(HC):
                    col = 32 + 8 * tb + c
                    rmax_cols.append(col)
                    t = pl.step("vector", lambda h, c=c, col=col: h.tensor_reduce(
                        s_tmp[:, col:col + 1], zsq_v(c), axis=AX.X, op=ALU.max))
                t_x2cons = t
                pl.wait("tensor", t)

            pl.wait("vector", t_x2cons)
            tcm = pl.step("vector", lambda h: h.reduce_max(
                s_tmp[:, 64:65], s_tmp[:, 32:64], axis=AX.X))
            pl.wait("tensor", tcm)
            ttr = pl.step("tensor", lambda h: h.transpose(
                ps[5][0:1, 0:128], s_tmp[:, 64:65], ident[:]))
            pl.wait("vector", ttr)
            tgm = pl.step("gpsimd", lambda h: h.memset(s_sc[:, :], -1e30))
            pl.wait("vector", tgm)
            tgm = pl.step("vector", lambda h: h.reduce_max(
                s_sc[0:1, 0:1], ps[5][0:1, 0:128], axis=AX.X))
            pl.wait("gpsimd", tgm)
            tdm = pl.dma("gpsimd", lambda h: h.dma_start(
                out=ar1_in[:, :], in_=s_sc[0:1, 0:8]), "dR")
            pl.wait("gpsimd", tdm)
            pl.emit("gpsimd", lambda h: h.collective_compute(
                "AllReduce", ALU.max, replica_groups=[list(range(8))],
                ins=[ar1_in.opt()], outs=[ar1_out.opt()]), inc="cc")
            t_w1cons = [None, None]
            q_entry = [None, None]
            wi_slot_guard = [None, None]
            wiT_bank_hist = [None, None]
            t_wiT = None

            def QW(p):
                return V("qscr", 4096 * p, 1024)

            def QWCI(p):
                return V("qscr", 4096 * p + 1024, 1024).bitcast(I32)

            def QWCM(p):
                return V("qscr", 4096 * p + 2048, 1024)

            def QWD(p):
                return V("qscr", 4096 * p + 3072, 1024)

            for fc in range(32):
                par = fc % 2
                pl.wait("sync", t_w1cons[par])
                tw = pl.dma("sync", lambda h, fc=fc, par=par: h.dma_start(
                    out=w1s_v(par), in_=w1[128 * fc:128 * (fc + 1), :]),
                    "dF0" if par == 0 else "dF1")
                pl.wait("vector", tw)
                trm = pl.step("vector", lambda h, fc=fc, par=par: h.tensor_reduce(
                    s_sw1[:, fc:fc + 1], w1s_v(par), axis=AX.X, op=ALU.max,
                    apply_absolute_value=True))
                pl.wait("vector", trm)
                trm = pl.step("vector", lambda h, fc=fc, par=par: h.reciprocal(
                    s_tmp[:, 66 + 2 * par:67 + 2 * par], s_sw1[:, fc:fc + 1]))
                pl.wait("vector", trm)
                trm = pl.step("vector", lambda h, par=par: h.tensor_scalar(
                    s_tmp[:, 67 + 2 * par:68 + 2 * par],
                    s_tmp[:, 66 + 2 * par:67 + 2 * par], 127.0, None, ALU.mult))
                pl.wait("vector", trm)
                pl.wait("vector", wi_slot_guard[par])
                q_entry[par] = pl.quant(
                    w1s_v(par), V("zsq", 1024 * par, 512).bitcast(BF16),
                    QW(par), QWCI(par), QWCM(par), QWD(par),
                    s_tmp[:, 67 + 2 * par:68 + 2 * par], relu=False,
                    entry=q_entry[par])
                t_w1cons[par] = q_entry[par][0]
                pl.wait("tensor", q_entry[par][0])
                for tpi in range(8):
                    bank = (tpi % 2) + 2
                    if wiT_bank_hist[tpi % 2] is not None:
                        pl.wait("tensor", wiT_bank_hist[tpi % 2])
                    tmm = pl.step("tensor", lambda h, par=par, tpi=tpi, bank=bank:
                                  h.transpose(
                                      ps[bank][:, 0:64].bitcast(BF16),
                                      V("zsq", 1024 * par, 512).bitcast(BF16)[:, 128 * tpi:128 * (tpi + 1)],
                                      ident_bf[:]))
                    pl.wait("vector", tmm)
                    t_wiT = pl.step("vector", lambda h, dc=tpi, fc=fc, bank=bank:
                                    h.tensor_copy(
                                        wiT_v(dc)[:, 128 * fc:128 * (fc + 1)],
                                        ps[bank][:, 0:64].bitcast(BF16)))
                    wiT_bank_hist[tpi % 2] = t_wiT
                wi_slot_guard[par] = t_wiT

            pl.wait("gpsimd", pl.tok("cc"))
            tdm = pl.dma("gpsimd", lambda h: h.dma_start(
                out=s_sc[0:1, 8:16], in_=ar1_out[:, :]), "dR")
            pl.wait("vector", tdm)
            ts1 = pl.step("vector", lambda h: h.tensor_scalar(
                s_sc[0:1, 1:2], s_sc[0:1, 8:9], 1e-8, 1.0 / 255.0, ALU.max, ALU.mult))
            pl.wait("vector", ts1)
            ts1 = pl.step("vector", lambda h: h.reciprocal(s_sc[0:1, 2:3], s_sc[0:1, 1:2]))
            pl.wait("tensor", ts1)
            tbq = pl.step("tensor", lambda h: h.matmul(
                ps[5][:, 0:1], ones1[:], s_sc[0:1, 2:3], start=True, stop=True))
            pl.wait("vector", tbq)
            trb = pl.step("vector", lambda h: h.tensor_copy(s_scp[:, 0:1], ps[5][:, 0:1]))
            pl.wait("tensor", trb)
            tbq = pl.step("tensor", lambda h: h.matmul(
                ps[5][:, 0:1], ones1[:], s_sc[0:1, 1:2], start=True, stop=True))
            pl.wait("vector", tbq)
            trb = pl.step("vector", lambda h: h.tensor_copy(s_scp[:, 2:3], ps[5][:, 0:1]))

            pl.wait("vector", trb)
            tsv = pl.step("vector", lambda h: h.tensor_scalar(
                s_sv1[:, :], s_sw1[:, 0:32], s_scp[:, 2:3], None, ALU.mult))
            pl.wait("vector", tsv)
            tsv = pl.step("vector", lambda h: h.tensor_scalar(
                s_sv1[:, :], s_sv1[:, :], 1.0 / 127.0, None, ALU.mult))
            pl.wait("vector", tsv)
            tsv = pl.step("vector", lambda h: h.reciprocal(s_tmp[:, 68:100], s_sv1[:, :]))
            pl.wait("vector", tsv)
            tsv = pl.step("vector", lambda h: h.tensor_tensor(
                s_tmp[:, 100:132], s_b1[:, :], s_tmp[:, 68:100], op=ALU.mult))
            pl.wait("vector", tsv)
            tbqq = pl.quant(s_tmp[:, 100:132], s_tmp[:, 100:132],
                            QW(0)[:, 0:32], QWCI(0)[:, 0:32], QWCM(0)[:, 0:32],
                            QWD(0)[:, 0:32], None, relu=False)[0]
            pl.wait("vector", tbqq)
            tbqq = pl.step("vector", lambda h: h.tensor_tensor(
                s_bq1[:, :], s_tmp[:, 100:132], s_sv1[:, :], op=ALU.mult))

            pl.wait("tensor", pl.tok("cV"))
            pl.wait("tensor", pl.tok("cS"))
            tbb = None
            for tb in range(NTB):
                pl.emit("tensor", lambda h, tb=tb: h.matmul(
                    ps[tb][:, :], ones1[:], m_row(tb), start=True, stop=True))
                tbb = pl.step("tensor", lambda h, tb=tb: h.matmul(
                    ps[4 + tb][:, :], ones1[:], r_row(tb), start=True, stop=True))
            def QY(p):
                return V("qscr", 4096 * p, 1024)
            def QCI(p):
                return V("qscr", 4096 * p + 1024, 1024).bitcast(I32)
            def QCM(p):
                return V("qscr", 4096 * p + 2048, 1024)
            def QD(p):
                return V("qscr", 4096 * p + 3072, 1024)
            t_x2c2 = None
            b2_entry = [None, None]
            ld_guard = [None, None]
            for c in range(HC):
                par = c % 2
                pl.wait("sync", ld_guard[par])
                tld = pl.dma("sync", lambda h, c=c, par=par: h.dma_start(
                    out=V("x2str", 2048 * par, 2048),
                    in_=x2_sp[128 * c:128 * (c + 1), :]),
                    "dXR" if par == 0 else "dSU")
                pl.wait("vector", tld)
                pl.wait("vector", tbb)
                for hhalf in range(2):
                    hsl = slice(1024 * hhalf, 1024 * (hhalf + 1))
                    if b2_entry[par] is not None:
                        pl.wait("vector", b2_entry[par][0])
                    t = None
                    for tb in (2 * hhalf, 2 * hhalf + 1):
                        tbsl = slice(512 * tb, 512 * (tb + 1))
                        osl = slice(512 * (tb - 2 * hhalf), 512 * (tb - 2 * hhalf + 1))
                        t = pl.step("vector", lambda h, par=par, tb=tb, tbsl=tbsl, osl=osl:
                                    h.tensor_tensor(QY(par)[:, osl],
                                                    V("x2str", 2048 * par, 2048)[:, tbsl],
                                                    ps[tb][:, :], op=ALU.subtract))
                    pl.wait("vector", t)
                    for tb in (2 * hhalf, 2 * hhalf + 1):
                        osl = slice(512 * (tb - 2 * hhalf), 512 * (tb - 2 * hhalf + 1))
                        t = pl.step("vector", lambda h, par=par, tb=tb, osl=osl:
                                    h.tensor_tensor(QY(par)[:, osl], QY(par)[:, osl],
                                                    ps[4 + tb][:, :], op=ALU.mult))
                    pl.wait("vector", t)
                    b2_entry[par] = pl.quant(
                        QY(par), xi_v(c)[:, slice(1024 * hhalf, 1024 * (hhalf + 1))],
                        QY(par), QCI(par), QCM(par), QD(par),
                        s_scp[:, 0:1], relu=True, entry=b2_entry[par])
                ld_guard[par] = b2_entry[par][0]
                t_x2c2 = b2_entry[par][0]
            pl.wait("tensor", t_x2c2)

            _ffn_on = [KFFN >= 1]
            pl.wait("vector", tbqq)
            pl.wait("scalar", tbqq)
            pl.wait("tensor", t_wiT)
            pl.wait("vector", t_wiT)
            tmaxg = pl.step("gpsimd", lambda h: h.memset(s_tmp[:, 65:66], -1e30))
            pl.wait("vector", tmaxg)
            h_store = [None, None, None, None]
            t_hev = {}
            t_hred = {}
            t_hmax = tmaxg
            for ft in range(32 if KFFN >= 1 else 0):
                for tb in range(NTB):
                    it = ft * NTB + tb
                    par = it % 4
                    bank = it % 2
                    tbsl = slice(512 * tb, 512 * (tb + 1))
                    if it >= 2:
                        pl.wait("tensor", t_hev[it - 2])
                    tmf = None
                    for dc in range(HC):
                        tmf = pl.step("tensor", lambda h, dc=dc, ft=ft, tbsl=tbsl, bank=bank:
                                      h.matmul(ps[bank][:, :],
                                               wiT_v(dc)[:, 128 * ft:128 * (ft + 1)],
                                               xi_v(dc)[:, tbsl],
                                               start=(dc == 0), stop=(dc == HC - 1)))
                    pl.wait("scalar", tmf)
                    if h_store[par] is not None:
                        pl.wait("scalar", h_store[par])
                    if (it - 4) in t_hred:
                        pl.wait("scalar", t_hred[it - 4])
                    t_hev[it] = pl.step("scalar", lambda h, bank=bank, par=par, ft=ft:
                                        h.activation(hst_v(par), ps[bank][:, :], AF.Relu,
                                                     scale=s_sv1[:, ft:ft + 1],
                                                     bias=s_bq1[:, ft:ft + 1]))
                    pl.wait("vector", t_hev[it])
                    tr1 = pl.step("vector", lambda h, par=par: h.tensor_reduce(
                        s_tmp[:, 70 + par:71 + par], hst_v(par), axis=AX.X, op=ALU.max))
                    t_hred[it] = tr1
                    pl.wait("vector", tr1)
                    pl.wait("vector", t_hmax)
                    t_hmax = pl.step("vector", lambda h, par=par: h.tensor_tensor(
                        s_tmp[:, 65:66], s_tmp[:, 65:66], s_tmp[:, 70 + par:71 + par],
                        op=ALU.max))
                    pl.wait("gpsimd", t_hev[it])
                    h_store[par] = pl.dma("gpsimd", lambda h, ft=ft, tbsl=tbsl, par=par:
                                          h.dma_start(out=h_sp[128 * ft:128 * (ft + 1), tbsl],
                                                      in_=hst_v(par)),
                                          ("dG", "dH", "dA0", "dA1")[par])

            if KFFN < 2:
                raise _StopEmit()
            pl.wait("vector", t_hmax)
            tcm = pl.step("vector", lambda h: h.tensor_copy(s_tmp[:, 64:65], s_tmp[:, 65:66]))
            pl.wait("tensor", tcm)
            ttr = pl.step("tensor", lambda h: h.transpose(
                ps[5][0:1, 0:128], s_tmp[:, 64:65], ident[:]))
            pl.wait("vector", ttr)
            tgm = pl.step("gpsimd", lambda h: h.memset(s_sc[:, :], -1e30))
            pl.wait("vector", tgm)
            tgm = pl.step("vector", lambda h: h.reduce_max(
                s_sc[0:1, 0:1], ps[5][0:1, 0:128], axis=AX.X))
            pl.wait("gpsimd", tgm)
            tdm = pl.dma("gpsimd", lambda h: h.dma_start(
                out=ar2_in[:, :], in_=s_sc[0:1, 0:8]), "dR")
            pl.wait("gpsimd", tdm)
            pl.emit("gpsimd", lambda h: h.collective_compute(
                "AllReduce", ALU.max, replica_groups=[list(range(8))],
                ins=[ar2_in.opt()], outs=[ar2_out.opt()]), inc="cc")
            pl.wait("gpsimd", pl.tok("cc"))
            tdm = pl.dma("gpsimd", lambda h: h.dma_start(
                out=s_sc[0:1, 8:16], in_=ar2_out[:, :]), "dR")
            pl.wait("vector", tdm)
            ts2 = pl.step("vector", lambda h: h.tensor_scalar(
                s_sc[0:1, 1:2], s_sc[0:1, 8:9], 1e-8, 1.0 / 255.0, ALU.max, ALU.mult))
            pl.wait("vector", ts2)
            ts2 = pl.step("vector", lambda h: h.reciprocal(s_sc[0:1, 2:3], s_sc[0:1, 1:2]))
            pl.wait("tensor", ts2)
            tbq = pl.step("tensor", lambda h: h.matmul(
                ps[5][:, 0:1], ones1[:], s_sc[0:1, 2:3], start=True, stop=True))
            pl.wait("vector", tbq)
            tr2b = pl.step("vector", lambda h: h.tensor_copy(s_scp[:, 1:2], ps[5][:, 0:1]))
            pl.wait("tensor", tr2b)
            tbq = pl.step("tensor", lambda h: h.matmul(
                ps[5][:, 0:1], ones1[:], s_sc[0:1, 1:2], start=True, stop=True))
            pl.wait("vector", tbq)
            tr2b = pl.step("vector", lambda h: h.tensor_copy(s_scp[:, 3:4], ps[5][:, 0:1]))

          except _StopEmit:
            _ffn_stopped = True
        if KSTAGE >= 3 and not _ffn_stopped:
          try:
            pl.barrier()

            def hi_v(fc):
                return V("hihalf", 1024 * fc, 1024, BF16)

            def hrl_v(p):
                return V("hrel", 2048 * p, 2048)

            def q2_y(p):
                return V("q2scr", 2048 * p, 512)

            def q2_ci(p):
                return V("q2scr", 2048 * p + 512, 512).bitcast(I32)

            def q2_cm1(p):
                return V("q2scr", 2048 * p + 1024, 512)

            def q2_d(p):
                return V("q2scr", 2048 * p + 1536, 512)

            def w2sv(p):
                return V("w2s", 512 * p, 512, BF16)

            def w2qstg():
                return V("q2scr", 4096, 1024)

            def w2ibf():
                return V("w2s", 1024, 1024, BF16)

            def outp_v(c):
                return V("outp", 2048 * c, 2048)

            if KFFN < 3:
                raise _StopEmit()

            def w2stg(p):
                return V("hihalf", 4096 * p, 4096)

            def w2ibf():
                return V("w2s", 0, 2048, BF16)

            def Q2Y(p):
                return V("q2scr", 4096 * p, 1024)

            def Q2CI(p):
                return V("q2scr", 4096 * p + 1024, 1024).bitcast(I32)

            def Q2CM(p):
                return V("q2scr", 4096 * p + 2048, 1024)

            def Q2D(p):
                return V("q2scr", 4096 * p + 3072, 1024)

            t_w2ld = [None, None]
            t_w2st = None
            w2_entry = [None, None]
            t_w2c = None
            for dc in range(HC):
                lpar = dc % 2
                pl.wait("sync", t_w2ld[lpar])
                tw = pl.dma("sync", lambda h, dc=dc, lpar=lpar: h.dma_start(
                    out=w2stg(lpar), in_=w2[128 * dc:128 * (dc + 1), :]), "dF0" if lpar == 0 else "dF1")
                pl.wait("vector", tw)
                trm = pl.step("vector", lambda h, dc=dc, lpar=lpar: h.tensor_reduce(
                    s_sw2[:, dc:dc + 1], w2stg(lpar), axis=AX.X, op=ALU.max,
                    apply_absolute_value=True))
                pl.wait("vector", trm)
                trm = pl.step("vector", lambda h, dc=dc, lpar=lpar: h.reciprocal(
                    s_tmp[:, 70 + 2 * lpar:71 + 2 * lpar], s_sw2[:, dc:dc + 1]))
                pl.wait("vector", trm)
                trm = pl.step("vector", lambda h, lpar=lpar: h.tensor_scalar(
                    s_tmp[:, 71 + 2 * lpar:72 + 2 * lpar],
                    s_tmp[:, 70 + 2 * lpar:71 + 2 * lpar], 127.0, None, ALU.mult))
                pl.wait("vector", trm)
                pl.wait("vector", t_w2st)
                for qtr in range(4):
                    qsl = slice(1024 * qtr, 1024 * (qtr + 1))
                    qpar = qtr % 2
                    w2_entry[qpar] = pl.quant(
                        w2stg(lpar)[:, qsl], w2ibf()[:, qsl],
                        Q2Y(qpar), Q2CI(qpar), Q2CM(qpar), Q2D(qpar),
                        s_tmp[:, 71 + 2 * lpar:72 + 2 * lpar], relu=False,
                        entry=w2_entry[qpar])
                t_w2ld[lpar] = w2_entry[1][0]
                pl.wait("gpsimd", w2_entry[0][0])
                pl.wait("gpsimd", w2_entry[1][0])
                t_w2st = pl.dma("gpsimd", lambda h, dc=dc: h.dma_start(
                    out=w2i_sp[128 * dc:128 * (dc + 1), :], in_=w2ibf()), "dWS")
                t_w2c = w2_entry[1][0]

            pl.wait("vector", tr2b)
            tsv = pl.step("vector", lambda h: h.tensor_scalar(
                s_sv2[:, :], s_sw2[:, :], s_scp[:, 3:4], None, ALU.mult))
            pl.wait("vector", tsv)
            tsv = pl.step("vector", lambda h: h.tensor_scalar(
                s_sv2[:, :], s_sv2[:, :], 1.0 / 127.0, None, ALU.mult))
            pl.wait("vector", tsv)
            tsv = pl.step("vector", lambda h: h.reciprocal(s_tmp[:, 80:88], s_sv2[:, :]))
            pl.wait("vector", tsv)
            tsv = pl.step("vector", lambda h: h.tensor_tensor(
                s_tmp[:, 88:96], s_b2[:, :], s_tmp[:, 80:88], op=ALU.mult))
            pl.wait("vector", tsv)
            tbq2 = pl.quant(s_tmp[:, 88:96], s_tmp[:, 88:96],
                            Q2Y(0)[:, 0:8], Q2CI(0)[:, 0:8], Q2CM(0)[:, 0:8],
                            Q2D(0)[:, 0:8], None, relu=False)[0]
            pl.wait("vector", tbq2)
            tbq2 = pl.step("vector", lambda h: h.tensor_tensor(
                s_bq2[:, :], s_tmp[:, 88:96], s_sv2[:, :], op=ALU.mult))

            if KFFN < 4:
                raise _StopEmit()
            t_outev = None
            t_ln4 = None
            def hi_q(q, fc):
                return V("hihalf", 8192 * (q % 2) + 1024 * fc, 1024, BF16)

            hq_entry = [None, None]
            hq_ld = [None, None]

            def emit_hquant_quarter(q, gate_tok):
                tqh = gate_tok
                for fc in range(8):
                    fg = 8 * q + fc
                    par = fc % 2
                    pl.wait("sync", hq_ld[par])
                    pl.wait("sync", gate_tok)
                    th = pl.dma("sync", lambda h, fg=fg, par=par: h.dma_start(
                        out=hrl_v(par), in_=h_sp[128 * fg:128 * (fg + 1), :]),
                        "dF0" if par == 0 else "dF1")
                    pl.wait("vector", th)
                    for hh in range(2):
                        hsl = slice(1024 * hh, 1024 * (hh + 1))
                        hq_entry[par] = pl.quant(
                            hrl_v(par)[:, hsl], hi_q(q, fc)[:, hsl],
                            Q2Y(par), Q2CI(par), Q2CM(par), Q2D(par),
                            s_scp[:, 1:2], relu=True, entry=hq_entry[par])
                    hq_ld[par] = hq_entry[par][0]
                    tqh = hq_entry[par][0]
                return tqh

            pl.wait("sync", pl.tok("dWS"))
            for q_ in ("dG", "dH", "dA0", "dA1"):
                pl.wait("sync", pl.tok(q_))
            tqh = {}
            tqh[0] = emit_hquant_quarter(0, t_w2c)
            hq_cons = {}
            t_w2l = [None, None]
            for q in range(4):
                pl.wait("tensor", tqh[q])
                for tb in range(NTB):
                    tbsl = slice(512 * tb, 512 * (tb + 1))
                    if q > 0 or tb > 0:
                        pl.wait("tensor", t_outev)
                    tmf = None
                    for fc in range(8):
                        fg = 8 * q + fc
                        wpar = fc % 2
                        pl.wait("sync", t_w2l[wpar])
                        twl = pl.dma("sync", lambda h, fg=fg, wpar=wpar: h.dma_start(
                            out=w2sv(wpar)[:, :],
                            in_=w2i_sp[:, 128 * fg:128 * (fg + 1)], transpose=True),
                            "dV0" if wpar == 0 else "dV1")
                        pl.wait("tensor", twl)
                        for dt in range(HC):
                            tmf = pl.step("tensor",
                                          lambda h, q=q, fc=fc, dt=dt, tbsl=tbsl, wpar=wpar:
                                          h.matmul(ps[dt][:, :],
                                                   w2sv(wpar)[:, 128 * dt:128 * (dt + 1)],
                                                   hi_q(q, fc)[:, tbsl],
                                                   start=(fc == 0), stop=(fc == 7)))
                        t_w2l[wpar] = tmf
                    if tb == 0 and q < 3:
                        gate = hq_cons.get(q - 1, tqh[q])
                        tqh[q + 1] = emit_hquant_quarter(q + 1, gate)
                    pl.wait("vector", tmf)
                    pl.wait("vector", t_outev)
                    for dt in range(HC):
                        if q == 0:
                            t_outev = pl.step("vector", lambda h, dt=dt, tbsl=tbsl:
                                              h.tensor_copy(outp_v(dt)[:, tbsl],
                                                            ps[dt][:, :]))
                        else:
                            t_outev = pl.step("vector", lambda h, dt=dt, tbsl=tbsl:
                                              h.tensor_tensor(outp_v(dt)[:, tbsl],
                                                              outp_v(dt)[:, tbsl],
                                                              ps[dt][:, :], op=ALU.add))
                hq_cons[q] = tmf

            pl.wait("vector", tbq2)
            pl.wait("vector", t_outev)
            t = None
            for dt in range(HC):
                t = pl.step("vector", lambda h, dt=dt: h.tensor_scalar(
                    outp_v(dt), outp_v(dt), s_sv2[:, dt:dt + 1], s_bq2[:, dt:dt + 1],
                    ALU.mult, ALU.add))
            pl.wait("vector", t)
            pl.wait("tensor", t)

            def x2r_v(c):
                return V("hrel", 512 * c, 512)

            def ystg_v(p):
                return V("q2scr", 4096 + 1024 * p, 1024)

            t_yst = [None, None]
            y_bank_hist = [None, None]
            t_x2r = None
            for tb in range(NTB):
                tbsl = slice(512 * tb, 512 * (tb + 1))
                pl.wait("sync", t_x2r)
                tld = None
                for c in range(HC):
                    tld = pl.dma("sync", lambda h, c=c, tbsl=tbsl: h.dma_start(
                        out=x2r_v(c), in_=x2_sp[128 * c:128 * (c + 1), tbsl]), "dAR")
                pl.wait("vector", tld)

                def ln4_out(tb2, sl2, tbsl=tbsl):
                    t = None
                    for c in range(HC):
                        t = pl.step("vector", lambda h, c=c: h.tensor_tensor(
                            V("q2scr", 512 * c, 512), outp_v(c)[:, tbsl], ps[6][:, :],
                            op=ALU.subtract))
                    pl.wait("vector", t)
                    for c in range(HC):
                        t = pl.step("vector", lambda h, c=c: h.tensor_tensor(
                            V("q2scr", 512 * c, 512), V("q2scr", 512 * c, 512),
                            ps[7][:, :], op=ALU.mult))
                    pl.wait("vector", t)
                    vt = None
                    for c in range(HC):
                        vt = pl.step("vector", lambda h, c=c: h.tensor_tensor(
                            x2r_v(c), V("q2scr", 512 * c, 512), x2r_v(c), op=ALU.add))
                    return vt

                emit_ln(HC, lambda c, tbsl=tbsl: outp_v(c)[:, tbsl], "q2scr",
                        ln4_out, [0])
                t_ln4 = pl.tok("cV")
                for tl in range(4):
                    par = tl % 2
                    pl.wait("tensor", t_ln4)
                    pl.wait("tensor", t_yst[par])
                    tq = None
                    for c in range(HC):
                        bank = (c % 2) + 2
                        if y_bank_hist[c % 2] is not None:
                            pl.wait("tensor", y_bank_hist[c % 2])
                        tmm = pl.step("tensor", lambda h, c=c, tl=tl, bank=bank:
                                      h.transpose(ps[bank][:, 0:128],
                                                  x2r_v(c)[:, 128 * tl:128 * (tl + 1)],
                                                  ident[:]))
                        pl.wait("vector", tmm)
                        tq = pl.step("vector", lambda h, c=c, par=par, bank=bank:
                                     h.tensor_copy(ystg_v(par)[:, 128 * c:128 * (c + 1)],
                                                   ps[bank][:, 0:128]))
                        y_bank_hist[c % 2] = tq
                    pl.wait("gpsimd", tq)
                    r0 = 512 * tb + 128 * tl
                    t_yst[par] = pl.dma("gpsimd", lambda h, r0=r0, par=par: h.dma_start(
                        out=y[r0:r0 + 128, :], in_=ystg_v(par)),
                        "dY0" if par == 0 else "dY1")
                t_x2r = pl.tokT()
            pl.wait("gpsimd", pl.tok("dY0"))
            pl.wait("gpsimd", pl.tok("dY1"))
          except _StopEmit:
            _ffn_stopped = True


        if KSTAGE >= 3 and _ffn_stopped:
            pl.barrier()
            tms = pl.step("gpsimd", lambda h: h.memset(arena[:, 0:1024], 0.0))
            pl.wait("gpsimd", tms)
            for tt in range(NT):
                tst = pl.dma("gpsimd", lambda h, tt=tt: h.dma_start(
                    out=y[128 * tt:128 * (tt + 1), :], in_=arena[:, 0:1024]), "dG")
            pl.wait("gpsimd", tst)
        if KSTAGE < 3:
            pl.barrier()
            tms = pl.step("gpsimd", lambda h: h.memset(
                V("x" if KSTAGE < 2 else "avl", 0, 1024), 0.0))
            pl.wait("gpsimd", tms)
            tst = None
            for tt in range(NT):
                tst = pl.dma("gpsimd", lambda h, tt=tt: h.dma_start(
                    out=y[128 * tt:128 * (tt + 1), :],
                    in_=V("x" if KSTAGE < 2 else "avl", 0, 1024)), "dG")
            pl.wait("gpsimd", tst)
        if KDBG:
            pl.wait("gpsimd", pl.tok("cc"))
            pl.dma("gpsimd", lambda h: h.dma_start(out=dbg_ag[:, :], in_=ag_in[:, :]), "dG")
        pl.wait("gpsimd", pl.tok("cc"))
        pl.wait("gpsimd", pl.tok("dG"))
        pl.wait("gpsimd", pl.tok("dH"))
        pl.wait("gpsimd", pl.tok("dK"))
        pl.wait("gpsimd", pl.tok("dW"))

        pl.realize(block, sems)
    return nc


def kernel(L0_attn_inputs, L0_attn_W_uv, L0_attn_b_uv, L0_attn_gamma, L0_attn_beta,
           L0_attn_W_out, L0_attn_b_out, L0_ffn_w1, L0_ffn_b1, L0_ffn_w2, L0_ffn_b2):
    x = np.ascontiguousarray(np.asarray(L0_attn_inputs, dtype=np.float32))
    wuv = np.ascontiguousarray(np.asarray(L0_attn_W_uv, dtype=np.float32))
    buv = np.asarray(L0_attn_b_uv, dtype=np.float32)
    gam = np.asarray(L0_attn_gamma, dtype=np.float32)
    bet = np.asarray(L0_attn_beta, dtype=np.float32)
    wout = np.ascontiguousarray(np.asarray(L0_attn_W_out, dtype=np.float32))
    bout = np.asarray(L0_attn_b_out, dtype=np.float32)
    w1 = np.ascontiguousarray(np.asarray(L0_ffn_w1, dtype=np.float32))
    b1 = np.asarray(L0_ffn_b1, dtype=np.float32)
    w2 = np.ascontiguousarray(np.asarray(L0_ffn_w2, dtype=np.float32))
    b2 = np.asarray(L0_ffn_b2, dtype=np.float32)

    shared = dict(
        wuv=wuv,
        buv_row=buv.reshape(1, -1),
        buv_base=buv[2 * E:].reshape(SD, 1),
        gamT=np.ascontiguousarray(gam.T),
        betT=np.ascontiguousarray(bet.T),
        wout=wout,
        bout_pt=np.ascontiguousarray(bout.reshape(HC, 128).T),
        w1=w1,
        w1T=np.ascontiguousarray(w1.T),
        b1_pt=np.ascontiguousarray(b1.reshape(FF // 128, 128).T),
        w2=w2,
        w2T=np.ascontiguousarray(w2.T),
        b2_pt=np.ascontiguousarray(b2.reshape(HC, 128).T),
    )
    in_maps = []
    for c in range(NCORE_IDS):
        b, slc = c // 4, c % 4
        xs = x[b, TC * slc:TC * (slc + 1), :]
        m = dict(shared)
        m["xT"] = np.ascontiguousarray(xs.T)
        in_maps.append(m)

    nc = build_nc()
    res = run_bass_kernel_spmd(nc, in_maps, core_ids=list(range(8)))
    out = np.empty((B, S, H), dtype=np.float32)
    for c in range(8):
        b, slc = c // 4, c % 4
        out[b, TC * slc:TC * (slc + 1), :] = np.asarray(res.results[c]["y"])
    if KDBG:
        kernel.last_results = res.results
    return out




# version 11
# speedup vs baseline: 1.0230x; 1.0130x over previous
import os
import contextlib
import numpy as np
import ml_dtypes
import concourse.bass as bass
import concourse.mybir as mybir
from concourse.bass_utils import run_bass_kernel_spmd

F32 = mybir.dt.float32
BF16 = mybir.dt.bfloat16
I32 = mybir.dt.int32
AF = mybir.ActivationFunctionType
ALU = mybir.AluOpType
AX = mybir.AxisListType

B, S, H = 2, 8192, 1024
TC = 2048
E = 2048
SD = 128
FF = 4096
HC = H // 128
NT = TC // 128
NTB = TC // 512
MCN = S // 128
EPS_LN = 1e-5
NCORE_IDS = 8

KSTAGE = int(os.environ.get("KSTAGE", "99"))
KDBG = int(os.environ.get("KDBG", "0"))
KFFN = int(os.environ.get("KFFN", "9"))

ENGINES = ("tensor", "vector", "scalar", "gpsimd", "sync")

ARENA_COLS = 49152
_BUFS = {
    "x":       (0,     16384, 1, 1),
    "xn_hi":   (16384,  8192, 1, 2),
    "xn_lo":   (24576,  8192, 1, 2),
    "xsq":     (32768,  4096, 1, 1),
    "wblk":    (0,      8192, 2, 2),
    "stage":   (8192,   2048, 2, 2),
    "base":    (10240,  2048, 2, 2),
    "kown":    (12288,  2048, 2, 2),
    "wblk2":   (36864,  8192, 2, 2),
    "q_hi":    (33792,  1024, 2, 3),
    "q_lo":    (34816,  1024, 2, 3),
    "sq_hi":   (0,      8192, 3, 3),
    "sq_lo":   (8192,   8192, 3, 3),
    "kall_hi": (16384,  4096, 3, 3),
    "kall_lo": (20480,  4096, 3, 3),
    "vbuf":    (24576,  4096, 3, 3),
    "sqevac":  (28672,  1024, 3, 3),
    "avstage": (29696,  4096, 3, 3),
    "avl":     (0,      4096, 4, 4),
    "sul":     (40960,  4096, 4, 4),
    "gsc":     (4096,   2048, 4, 4),
    "gbf":     (6144,   2048, 4, 4),
    "gT":      (8192,   4096, 4, 4),
    "outT":    (12288,  4096, 4, 4),
    "xTrel":   (16384,  4096, 4, 4),
    "osq":     (20480,  4096, 4, 4),
    "woutb":   (24576, 16384, 4, 4),
    "xi":      (0,      8192, 5, 5),
    "z":       (8192,   2048, 5, 5),
    "qscr":    (10240,  8192, 5, 5),
    "wf":      (18432, 16384, 5, 5),
    "hstage":  (34816,  2048, 5, 5),
    "zsq":     (36864,  4096, 5, 5),
    "x2str":   (40960,  4096, 5, 5),
    "w1str":   (45056,  2048, 5, 5),
    "hihalf":  (0,     16384, 6, 7),
    "hrel":    (16384,  4096, 6, 7),
    "q2scr":   (20480,  8192, 6, 7),
    "w2s":     (28672,  3072, 6, 7),
    "outp":    (32768, 16384, 6, 7),
}
for _i, (_n1, (_o1, _c1, _a1, _b1)) in enumerate(list(_BUFS.items())):
    assert _o1 + _c1 <= ARENA_COLS, _n1
    for _n2, (_o2, _c2, _a2, _b2) in list(_BUFS.items())[_i + 1:]:
        if _a1 <= _b2 and _a2 <= _b1:
            assert _o1 + _c1 <= _o2 or _o2 + _c2 <= _o1, f"overlap {_n1}/{_n2}"


class KB:

    def __init__(self, nc):
        self.nc = nc
        self.ops = {e: [] for e in ENGINES}
        self.cnt = {"cT": 0, "cV": 0, "cS": 0, "cG": 0, "dW": 0, "dG": 0,
                    "dH": 0, "dK": 0, "dB": 0, "dV0": 0, "dV1": 0, "dA0": 0,
                    "dA1": 0, "dAR": 0, "dSU": 0, "dXR": 0, "dXS": 0, "dWO": 0,
                    "dF0": 0, "dF1": 0, "dWS": 0, "dY0": 0, "dY1": 0, "dR": 0,
                    "dKL": 0, "cc": 0}
        self.eng_sem = {"tensor": "cT", "vector": "cV", "scalar": "cS", "gpsimd": "cG"}
        self.waited = {}

    def emit(self, eng, fn, inc=None, add=1):
        tok = None
        if inc is not None:
            self.cnt[inc] += add
            tok = (inc, self.cnt[inc])
        self.ops[eng].append(("emit", fn, inc, add))
        return tok

    def step(self, eng, fn):
        return self.emit(eng, fn, inc=self.eng_sem[eng])

    def dma(self, eng, fn, q):
        return self.emit(eng, fn, inc=q, add=16)

    def wait(self, eng, tok):
        if tok is None:
            return
        sem, val = tok
        key = (eng, sem)
        if self.waited.get(key, -1) >= val:
            return
        self.waited[key] = val
        self.ops[eng].append(("wait", sem, val, None))

    def tokT(self):
        return ("cT", self.cnt["cT"])

    def tok(self, s):
        return (s, self.cnt[s])

    def quant(self, src_ap, out_ap, y_ap, ci_ap, cm1_ap, d_ap, recip_ap, relu,
              entry=None):
        if entry is None:
            self.wait("vector", self.tok("cV"))
            self.wait("scalar", self.tok("cV"))
        else:
            vt, st0 = entry
            self.wait("vector", vt)
            self.wait("scalar", vt)
            if st0 is not None:
                self.wait("vector", st0)
                self.wait("scalar", st0)
        rc = 1.0 if recip_ap is None else recip_ap
        if relu:
            t = self.step("vector", lambda h: h.tensor_scalar(
                y_ap, src_ap, rc, 0.0, ALU.mult, ALU.max))
        else:
            t = self.step("vector", lambda h: h.tensor_scalar(
                y_ap, src_ap, rc, None, ALU.mult))
        self.wait("scalar", t)
        t = self.step("scalar", lambda h: h.activation(ci_ap, y_ap, AF.Copy))
        self.wait("scalar", t)
        stok = self.step("scalar", lambda h: h.activation(cm1_ap, ci_ap, AF.Copy, bias=-1.0))
        self.wait("vector", stok)
        t = self.step("vector", lambda h: h.tensor_tensor(
            d_ap, y_ap, cm1_ap, op=ALU.subtract))
        self.wait("vector", t)
        t = self.step("vector", lambda h: h.scalar_tensor_tensor(
            y_ap, d_ap, 1.5, cm1_ap, op0=ALU.is_gt, op1=ALU.add))
        self.wait("vector", t)
        vtok = self.step("vector", lambda h: h.scalar_tensor_tensor(
            out_ap, d_ap, 0.5, y_ap, op0=ALU.is_ge, op1=ALU.add))
        return (vtok, stok)

    def split(self, src_ap, hi_ap, lo_ap, eng="vector"):
        t = self.step(eng, lambda h: h.tensor_copy(hi_ap, src_ap))
        self.wait(eng, t)
        return self.step(eng, lambda h: h.tensor_tensor(lo_ap, src_ap, hi_ap,
                                                        op=ALU.subtract))

    def barrier(self, engines=("tensor", "vector", "scalar", "gpsimd", "sync")):
        snap = dict(self.cnt)
        for e in engines:
            own = self.eng_sem.get(e)
            for sem, val in snap.items():
                if sem == own or val == 0:
                    continue
                self.wait(e, (sem, val))

    def realize(self, block, sems):
        def runner(lst):
            def run(h):
                for item in lst:
                    if item[0] == "wait":
                        h.wait_ge(sems[item[1]], item[2])
                    else:
                        inst = item[1](h)
                        if item[2] is not None:
                            inst.then_inc(sems[item[2]], item[3])
            return run

        block.tensor(runner(self.ops["tensor"]))
        block.vector(runner(self.ops["vector"]))
        block.scalar(runner(self.ops["scalar"]))
        block.gpsimd(runner(self.ops["gpsimd"]))
        block.sync(runner(self.ops["sync"]))


def build_nc():
    nc = bass.Bass("TRN2", target_bir_lowering=False, debug=False)

    xT = nc.dram_tensor("xT", [H, TC], F32, kind="ExternalInput").ap()
    wuv = nc.dram_tensor("wuv", [H, 2 * E + SD], F32, kind="ExternalInput").ap()
    buv_row = nc.dram_tensor("buv_row", [1, 2 * E + SD], F32, kind="ExternalInput").ap()
    buv_base = nc.dram_tensor("buv_base", [SD, 1], F32, kind="ExternalInput").ap()
    gamT = nc.dram_tensor("gamT", [SD, 2], F32, kind="ExternalInput").ap()
    betT = nc.dram_tensor("betT", [SD, 2], F32, kind="ExternalInput").ap()
    wout = nc.dram_tensor("wout", [E, H], F32, kind="ExternalInput").ap()
    bout_pt = nc.dram_tensor("bout_pt", [128, HC], F32, kind="ExternalInput").ap()
    w1 = nc.dram_tensor("w1", [FF, H], F32, kind="ExternalInput").ap()
    w1T = nc.dram_tensor("w1T", [H, FF], F32, kind="ExternalInput").ap()
    b1_pt = nc.dram_tensor("b1_pt", [128, FF // 128], F32, kind="ExternalInput").ap()
    w2 = nc.dram_tensor("w2", [H, FF], F32, kind="ExternalInput").ap()
    w2T = nc.dram_tensor("w2T", [FF, H], F32, kind="ExternalInput").ap()
    b2_pt = nc.dram_tensor("b2_pt", [128, HC], F32, kind="ExternalInput").ap()
    y = nc.dram_tensor("y", [TC, H], F32, kind="ExternalOutput").ap()

    def scratch(name, shape, dt):
        if KDBG:
            return nc.dram_tensor(name, shape, dt, kind="ExternalOutput").ap()
        return nc.dram_tensor(name, shape, dt).ap()

    AGW = 2 * E + 2 * SD
    ag_in = nc.dram_tensor("ag_in", [AGW, TC], BF16).ap()
    ag_out = nc.dram_tensor("ag_out", [4 * AGW, TC], BF16).ap()
    dbg_ag = nc.dram_tensor("dbg_ag", [AGW, TC], BF16, kind="ExternalOutput").ap() if KDBG else None
    su_sp = scratch("su_sp", [2 * TC, E], BF16)
    av_sp = scratch("av_sp", [TC, E], F32)
    h_sp = scratch("h_sp", [FF, TC], F32)
    x2_sp = scratch("x2_sp", [H, TC], F32)
    w2i_sp = nc.dram_tensor("w2i_sp", [H, FF], BF16).ap()
    ar1_in = nc.dram_tensor("ar1_in", [1, 8], F32).ap()
    ar1_out = nc.dram_tensor("ar1_out", [1, 8], F32).ap()
    ar2_in = nc.dram_tensor("ar2_in", [1, 8], F32).ap()
    ar2_out = nc.dram_tensor("ar2_out", [1, 8], F32).ap()

    pl = KB(nc)

    with contextlib.ExitStack() as es:
        En = es.enter_context
        arena = En(nc.sbuf_tensor("arena", [128, ARENA_COLS], F32))

        def V(name, off, n, dt=F32):
            o, ctot, _, _ = _BUFS[name]
            assert off + n <= ctot, f"{name}+{off}+{n}"
            ap = arena[:, o + off:o + off + n]
            return ap if dt == F32 else ap.bitcast(dt)

        ident = En(nc.sbuf_tensor("ident", [128, 128], F32))
        ident_bf = En(nc.sbuf_tensor("ident_bf", [128, 128], BF16))
        ones1 = En(nc.sbuf_tensor("ones1", [1, 128], F32))
        onesK = En(nc.sbuf_tensor("onesK", [128, 1], F32))
        ones1_bf = En(nc.sbuf_tensor("ones1_bf", [1, 128], BF16))
        s_buvrow = En(nc.sbuf_tensor("s_buvrow", [1, 512], F32))
        s_brh = En(nc.sbuf_tensor("s_brh", [1, 512], BF16))
        s_brl = En(nc.sbuf_tensor("s_brl", [1, 512], BF16))
        s_buvb = En(nc.sbuf_tensor("s_buvb", [128, 1], F32))
        s_gb = En(nc.sbuf_tensor("s_gb", [128, 8], F32))
        s_bout = En(nc.sbuf_tensor("s_bout", [128, HC], F32))
        s_b1 = En(nc.sbuf_tensor("s_b1", [128, FF // 128], F32))
        s_b2 = En(nc.sbuf_tensor("s_b2", [128, HC], F32))
        rowM = En(nc.sbuf_tensor("rowM", [1, 512], F32))
        rowV = En(nc.sbuf_tensor("rowV", [1, 512], F32))
        rowT = En(nc.sbuf_tensor("rowT", [1, 512], F32))
        rowR = En(nc.sbuf_tensor("rowR", [1, 512], F32))
        s_sc = En(nc.sbuf_tensor("s_sc", [1, 16], F32))
        s_scp = En(nc.sbuf_tensor("s_scp", [128, 8], F32))
        s_sw1 = En(nc.sbuf_tensor("s_sw1", [128, FF // 128], F32))
        s_sw2 = En(nc.sbuf_tensor("s_sw2", [128, HC], F32))
        s_bq1 = En(nc.sbuf_tensor("s_bq1", [128, FF // 128], F32))
        s_bq2 = En(nc.sbuf_tensor("s_bq2", [128, HC], F32))
        s_sv1 = En(nc.sbuf_tensor("s_sv1", [128, 32], F32))
        s_sv2 = En(nc.sbuf_tensor("s_sv2", [128, 8], F32))
        s_tmp = En(nc.sbuf_tensor("s_tmp", [128, 160], F32))
        s_eps = En(nc.sbuf_tensor("s_eps", [1, 1], F32))
        ps = [En(nc.psum_tensor(f"ps{i}", [128, 512], F32)) for i in range(8)]

        sems = {}
        for s in ("cT", "cV", "cS", "cG", "dW", "dG", "dH", "dK", "dB", "dV0",
                  "dV1", "dA0", "dA1", "dAR", "dSU", "dXR", "dXS", "dWO",
                  "dF0", "dF1", "dWS", "dY0", "dY1", "dR", "dKL", "cc"):
            sems[s] = En(nc.semaphore(s))

        block = En(nc.Block())

        def gchain(fn):
            pl.wait("gpsimd", pl.tok("cG"))
            return pl.step("gpsimd", fn)

        gchain(lambda h: h.memset(ident[:], 0.0))
        gchain(lambda h: h.affine_select(
            out=ident[:], in_=ident[:], compare_op=ALU.not_equal,
            fill=1.0, base=0, pattern=[[-1, 128]], channel_multiplier=1))
        gchain(lambda h: h.memset(ident_bf[:], 0.0))
        gchain(lambda h: h.affine_select(
            out=ident_bf[:], in_=ident_bf[:], compare_op=ALU.not_equal,
            fill=1.0, base=0, pattern=[[-1, 128]], channel_multiplier=1))
        gchain(lambda h: h.memset(ones1[:], 1.0))
        gchain(lambda h: h.memset(onesK[:], 1.0))
        gchain(lambda h: h.memset(ones1_bf[:], 1.0))
        gchain(lambda h: h.memset(s_eps[:], EPS_LN))

        x_toks = [pl.dma("sync", lambda h, c=c: h.dma_start(
            out=V("x", 2048 * c, 2048), in_=xT[128 * c:128 * (c + 1), :]), "dW")
            for c in range(HC)]
        for dst, src in [(s_buvb[:], buv_base[:, :]),
                         (s_bout[:], bout_pt[:, :]), (s_b1[:], b1_pt[:, :]),
                         (s_b2[:], b2_pt[:, :]), (s_gb[:, 0:2], gamT[:, :]),
                         (s_gb[:, 2:4], betT[:, :])]:
            t_sm = pl.dma("sync", lambda h, d=dst, s=src: h.dma_start(out=d, in_=s), "dW")
        pl.wait("vector", t_sm)
        pl.emit("vector", lambda h: h.tensor_scalar(
            s_gb[:, 4:5], s_gb[:, 0:1], 1.0 / 8192.0, None, ALU.mult))
        t_gbs = pl.step("vector", lambda h: h.tensor_scalar(
            s_gb[:, 5:6], s_gb[:, 2:3], 1.0 / 8192.0, None, ALU.mult))

        def emit_ln(nchunk, src_of, sq_name, out_cb, tb_range):
            inv_n = 1.0 / (128 * nchunk)
            prev_vt = None
            for tb in tb_range:
                sl = slice(512 * tb, 512 * (tb + 1))
                pl.wait("vector", prev_vt)
                tv = None
                for c in range(nchunk):
                    xv = src_of(c)
                    xsq = V(sq_name, 512 * c, 512)
                    tv = pl.step("vector", lambda h, xv=xv, sl=sl, xsq=xsq:
                                 h.tensor_tensor(xsq, xv[:, sl], xv[:, sl], op=ALU.mult))
                pl.wait("tensor", tv)
                tm = None
                for c in range(nchunk):
                    xv = src_of(c)
                    xsq = V(sq_name, 512 * c, 512)
                    pl.emit("tensor", lambda h, xv=xv, sl=sl, c=c:
                            h.matmul(ps[6][0:1, :], onesK[:], xv[:, sl],
                                     start=(c == 0), stop=(c == nchunk - 1)))
                    tm = pl.step("tensor", lambda h, xsq=xsq, c=c:
                                 h.matmul(ps[7][0:1, :], onesK[:], xsq,
                                          start=(c == 0), stop=(c == nchunk - 1)))
                pl.wait("vector", tm)
                t1 = pl.step("vector", lambda h: h.tensor_scalar(
                    rowM[:], ps[6][0:1, :], inv_n, None, ALU.mult))
                pl.step("vector", lambda h: h.tensor_scalar(
                    rowV[:], ps[7][0:1, :], inv_n, None, ALU.mult))
                pl.wait("vector", t1)
                t3 = pl.step("vector", lambda h: h.tensor_tensor(
                    rowT[:], rowM[:], rowM[:], op=ALU.mult))
                pl.wait("vector", t3)
                tvv = pl.step("vector", lambda h: h.tensor_tensor(
                    rowV[:], rowV[:], rowT[:], op=ALU.subtract))
                pl.wait("scalar", tvv)
                tss = pl.step("scalar", lambda h: h.activation(
                    rowT[:], rowV[:], AF.Sqrt, bias=s_eps[:]))
                pl.wait("vector", tss)
                trr = pl.step("vector", lambda h: h.reciprocal(rowR[:], rowT[:]))
                pl.wait("tensor", trr)
                pl.emit("tensor", lambda h: h.matmul(
                    ps[6][:, :], ones1[:], rowM[:], start=True, stop=True))
                tbb = pl.step("tensor", lambda h: h.matmul(
                    ps[7][:, :], ones1[:], rowR[:], start=True, stop=True))
                pl.wait("vector", tbb)
                vt = out_cb(tb, sl)
                prev_vt = vt
                pl.wait("tensor", vt)
                pl.wait("scalar", vt)

        pl.barrier()
        pl.wait("vector", t_sm)
        pl.wait("vector", t_gbs)

        def ln1_out(tb, sl):
            t = None
            for c in range(HC):
                xv, tmp = V("x", 2048 * c, 2048), V("xsq", 512 * c, 512)
                t = pl.step("vector", lambda h, xv=xv, tmp=tmp, sl=sl: h.tensor_tensor(
                    tmp, xv[:, sl], ps[6][:, :], op=ALU.subtract))
            pl.wait("vector", t)
            for c in range(HC):
                tmp = V("xsq", 512 * c, 512)
                t = pl.step("vector", lambda h, tmp=tmp: h.tensor_tensor(
                    tmp, tmp, ps[7][:, :], op=ALU.mult))
            pl.wait("vector", t)
            for c in range(HC):
                tmp, hi = V("xsq", 512 * c, 512), V("xn_hi", 1024 * c, 1024, BF16)
                t = pl.step("vector", lambda h, hi=hi, sl=sl, tmp=tmp:
                            h.tensor_copy(hi[:, sl], tmp))
            pl.wait("vector", t)
            vt = None
            for c in range(HC):
                tmp = V("xsq", 512 * c, 512)
                hi = V("xn_hi", 1024 * c, 1024, BF16)
                lo = V("xn_lo", 1024 * c, 1024, BF16)
                vt = pl.step("vector", lambda h, hi=hi, lo=lo, sl=sl, tmp=tmp:
                             h.tensor_tensor(lo[:, sl], tmp, hi[:, sl], op=ALU.subtract))
            return vt

        emit_ln(HC, lambda c: V("x", 2048 * c, 2048), "xsq", ln1_out, range(NTB))
        t_xn = pl.tok("cV")

        pl.barrier()

        def wv_f32(c, wp=0):
            return V("wblk" if wp == 0 else "wblk2", 512 * c, 512)

        def wv_hi(c, wp=0):
            return V("wblk" if wp == 0 else "wblk2", 4096 + 256 * c, 256, BF16)

        def wv_lo(c, wp=0):
            return V("wblk" if wp == 0 else "wblk2", 6144 + 256 * c, 256, BF16)

        def load_wblk(col0, ncols, guards, wp=0, wq="dW"):
            toks = []
            for g in (guards or []):
                pl.wait("sync", g)
            for c in range(HC):
                toks.append(pl.dma("sync", lambda h, c=c, col0=col0, ncols=ncols, wp=wp:
                            h.dma_start(out=wv_f32(c, wp)[:, 0:ncols],
                                        in_=wuv[128 * c:128 * (c + 1), col0:col0 + ncols]),
                            wq))
            pl.wait("vector", toks[-1])
            for g in (guards or []):
                pl.wait("vector", g)
            t = None
            for c in range(HC):
                t = pl.step("vector", lambda h, c=c, n=ncols, wp=wp: h.tensor_copy(
                    wv_hi(c, wp)[:, 0:n], wv_f32(c, wp)[:, 0:n]))
            pl.wait("vector", t)
            last = None
            for c in range(HC):
                last = pl.step("vector", lambda h, c=c, n=ncols, wp=wp: h.tensor_tensor(
                    wv_lo(c, wp)[:, 0:n], wv_f32(c, wp)[:, 0:n], wv_hi(c, wp)[:, 0:n],
                    op=ALU.subtract))
            return last

        t_w = load_wblk(2 * E, SD, None)
        pl.wait("tensor", t_w)
        pl.wait("tensor", t_xn)
        t_base_mm = None
        for tb in range(NTB):
            sl = slice(512 * tb, 512 * (tb + 1))
            for c in range(HC):
                whi, wlo = wv_hi(c)[:, 0:128], wv_lo(c)[:, 0:128]
                xhi = V("xn_hi", 1024 * c, 1024, BF16)
                xlo = V("xn_lo", 1024 * c, 1024, BF16)
                pl.emit("tensor", lambda h, whi=whi, xhi=xhi, sl=sl, st=(c == 0):
                        h.matmul(ps[0][:, :], whi, xhi[:, sl], start=st, stop=False))
                pl.emit("tensor", lambda h, whi=whi, xlo=xlo, sl=sl:
                        h.matmul(ps[0][:, :], whi, xlo[:, sl], start=False, stop=False))
                t_base_mm = pl.step("tensor", lambda h, wlo=wlo, xhi=xhi, sl=sl, c=c:
                                    h.matmul(ps[0][:, :], wlo, xhi[:, sl],
                                             start=False, stop=(c == HC - 1)))
            pl.wait("scalar", t_base_mm)
            t_base_ev = pl.step("scalar", lambda h, sl=sl: h.activation(
                V("base", 0, 2048)[:, sl], ps[0][:, :], AF.Silu, bias=s_buvb[:]))
            pl.wait("tensor", t_base_ev)
        pl.wait("vector", t_base_ev)
        bb = V("base", 0, 2048)
        tq1 = V("stage", 0, 2048)
        tq2 = V("wblk", 0, 2048)
        qh, ql = V("q_hi", 0, 1024, BF16), V("q_lo", 0, 1024, BF16)
        kh, kl = V("kown", 0, 1024, BF16), V("kown", 1024, 1024, BF16)
        pl.emit("vector", lambda h: h.tensor_scalar(
            tq1, bb, s_gb[:, 4:5], s_gb[:, 5:6], ALU.mult, ALU.add))
        t = pl.step("vector", lambda h: h.tensor_scalar(
            tq2, bb, s_gb[:, 1:2], s_gb[:, 3:4], ALU.mult, ALU.add))
        pl.wait("vector", t)
        pl.emit("vector", lambda h: h.tensor_copy(qh, tq1))
        t = pl.step("vector", lambda h: h.tensor_copy(kh, tq2))
        pl.wait("vector", t)
        pl.emit("vector", lambda h: h.tensor_tensor(ql, tq1, qh, op=ALU.subtract))
        t_k = pl.step("vector", lambda h: h.tensor_tensor(kl, tq2, kh, op=ALU.subtract))
        pl.wait("gpsimd", t_k)
        pl.dma("gpsimd", lambda h: h.dma_start(out=ag_in[2 * E:2 * E + SD, :], in_=kh), "dK")
        pl.dma("gpsimd", lambda h: h.dma_start(out=ag_in[2 * E + SD:, :], in_=kl), "dK")

        def suv_f32(par):
            return V("stage", 1024 * par, 512)

        def suv_hi(par):
            return V("stage", 1024 * par + 512, 256, BF16)

        def suv_lo(par):
            return V("stage", 1024 * par + 768, 256, BF16)

        t_wprev = pl.tok("cV")
        stage_free = [t_k, t_k]
        t_ev_hist = [None, None]
        blk_order = list(range(4, 8)) + list(range(4))
        t_w_pref = {}
        consumed_tok = {}
        for bi, blk in enumerate(blk_order):
            is_v = blk >= 4
            col0 = 512 * blk
            wp = bi % 2
            if bi == 0:
                t_w_pref[0] = load_wblk(col0, 512, [pl.tokT(), t_k], wp=0, wq="dW")
            t_w = t_w_pref[bi]
            if bi + 1 < len(blk_order):
                g = [consumed_tok[bi - 1]] if bi >= 1 else []
                t_w_pref[bi + 1] = load_wblk(512 * blk_order[bi + 1], 512, g,
                                             wp=(bi + 1) % 2,
                                             wq="dW" if (bi + 1) % 2 == 0 else "dKL")
            if bi >= 1:
                pl.wait("sync", consumed_tok[bi - 1])
            t_brow = pl.dma("sync", lambda h, col0=col0: h.dma_start(
                out=s_buvrow[:, :], in_=buv_row[:, col0:col0 + 512]), "dB")
            pl.wait("vector", t_brow)
            pl.wait("vector", pl.tokT())
            t_brow = pl.split(s_buvrow[:, :], s_brh[:, :], s_brl[:, :])
            pl.wait("tensor", t_w)
            pl.wait("tensor", t_brow)
            for tt in range(NT):
                par = tt % 2
                bank = tt % 2
                if t_ev_hist[bank] is not None:
                    pl.wait("tensor", t_ev_hist[bank])
                for c in range(HC):
                    whi512 = wv_hi(c, wp)
                    wlo512 = wv_lo(c, wp)
                    xhi = V("xn_hi", 1024 * c, 1024, BF16)[:, 128 * tt:128 * (tt + 1)]
                    xlo = V("xn_lo", 1024 * c, 1024, BF16)[:, 128 * tt:128 * (tt + 1)]
                    pl.emit("tensor", lambda h, a=xhi, b=whi512, bank=bank, st=(c == 0):
                            h.matmul(ps[bank][:, :], a, b, start=st, stop=False))
                    pl.emit("tensor", lambda h, a=xhi, b=wlo512, bank=bank:
                            h.matmul(ps[bank][:, :], a, b, start=False, stop=False))
                    pl.emit("tensor", lambda h, a=xlo, b=whi512, bank=bank, sp=(os.environ.get("KNB2") and c == HC - 1):
                            h.matmul(ps[bank][:, :], a, b, start=False, stop=bool(sp)))
                if os.environ.get("KNB2"):
                    t_mm = pl.step("tensor", lambda h: h.nop())
                elif os.environ.get("KNOBIAS"):
                    t_mm = pl.step("tensor", lambda h, bank=bank: h.matmul(
                        ps[bank][:, :], ones1_bf[:], s_brh[:, :],
                        start=False, stop=True))
                else:
                    pl.emit("tensor", lambda h, bank=bank: h.matmul(
                        ps[bank][:, :], ones1_bf[:], s_brh[:, :],
                        start=False, stop=False))
                    t_mm = pl.step("tensor", lambda h, bank=bank: h.matmul(
                        ps[bank][:, :], ones1_bf[:], s_brl[:, :],
                        start=False, stop=True))
                pl.wait("scalar", t_mm)
                if stage_free[par] is not None:
                    pl.wait("scalar", stage_free[par])
                t_ev = pl.step("scalar", lambda h, bank=bank, par=par: h.activation(
                    suv_f32(par), ps[bank][:, :], AF.Silu))
                t_ev_hist[bank] = t_ev
                pl.wait("vector", t_ev)
                t_sp = pl.split(suv_f32(par), suv_hi(par), suv_lo(par))
                pl.wait("gpsimd", t_sp)
                r0 = 128 * tt
                dq = "dG" if par == 0 else "dH"
                if is_v:
                    cs = col0 - E
                    pl.dma("gpsimd", lambda h, r0=r0, cs=cs, par=par: h.dma_start(
                        out=ag_in[r0:r0 + 128, cs:cs + 512], in_=suv_hi(par)), dq)
                    stage_free[par] = pl.dma("gpsimd",
                        lambda h, r0=r0, cs=cs, par=par: h.dma_start(
                            out=ag_in[TC + r0:TC + r0 + 128, cs:cs + 512],
                            in_=suv_lo(par)), dq)
                else:
                    pl.dma("gpsimd", lambda h, r0=r0, cs=col0, par=par: h.dma_start(
                        out=su_sp[r0:r0 + 128, cs:cs + 512], in_=suv_hi(par)), dq)
                    stage_free[par] = pl.dma("gpsimd",
                        lambda h, r0=r0, cs=col0, par=par: h.dma_start(
                            out=su_sp[TC + r0:TC + r0 + 128, cs:cs + 512],
                            in_=suv_lo(par)), dq)
            consumed_tok[bi] = pl.tokT()
            if blk == 7:
                pl.wait("gpsimd", pl.tok("dG"))
                pl.wait("gpsimd", pl.tok("dH"))
                pl.wait("gpsimd", pl.tok("dK"))
                pl.emit("gpsimd", lambda h: h.collective_compute(
                    "AllGather", ALU.bypass,
                    replica_groups=[[0, 1, 2, 3], [4, 5, 6, 7]],
                    ins=[ag_in.opt()],
                    outs=[ag_out.opt()]), inc="cc")




        if KSTAGE >= 2:
            qh_v = V("q_hi", 0, 1024, BF16)
            ql_v = V("q_lo", 0, 1024, BF16)
            kh_v = V("kall_hi", 0, 4096, BF16)
            kl_v = V("kall_lo", 0, 4096, BF16)
            sqh_v = V("sq_hi", 0, 8192, BF16)
            sql_v = V("sq_lo", 0, 8192, BF16)

            def vb_hi(p):
                return V("vbuf", 2048 * p, 1024, BF16)

            def vb_lo(p):
                return V("vbuf", 2048 * p + 1024, 1024, BF16)

            def scr2(p):
                return V("sqevac", 512 * p, 512)

            def sq22(p):
                return V("avstage", 512 * p, 512)

            def avst(ns):
                return V("avstage", 2048 * ns, 2048)

            pl.wait("sync", pl.tok("cT"))
            pl.wait("sync", pl.tok("cc"))
            for e in ("vector", "scalar", "tensor"):
                pl.wait(e, pl.tok("dG"))
                pl.wait(e, pl.tok("dH"))
            tkl = None
            for r in range(4):
                b0 = r * AGW
                pl.dma("sync", lambda h, b0=b0, r=r: h.dma_start(
                    out=kh_v[:, 2048 * r:2048 * (r + 1)],
                    in_=ag_out[b0 + 2 * E:b0 + 2 * E + SD, :]), "dKL")
                tkl = pl.dma("sync", lambda h, b0=b0, r=r: h.dma_start(
                    out=kl_v[:, 2048 * r:2048 * (r + 1)],
                    in_=ag_out[b0 + 2 * E + SD:b0 + AGW, :]), "dKL")
            pl.wait("tensor", tkl)

            av_guard = [None, None]
            prev_evac = [pl.tok("cS"), pl.tok("cS")]
            vcons = {}

            for nb in range(8):
                nsl = slice(256 * nb, 256 * (nb + 1))
                pl.wait("tensor", prev_evac[0])
                pl.wait("tensor", prev_evac[1])
                pl.wait("vector", av_guard[0])
                pl.wait("scalar", av_guard[0])
                ev_tok = {}
                sq_tok = {}
                c_tok = {}
                sub_tok = {}

                def sq4(mp):
                    return V("avstage", 512 * (mp % 4), 512)

                def emit_cast_subs(g):
                    pl.wait("scalar", sq_tok[2 * g + 1])
                    s0 = 512 * ((2 * g) % 4)
                    c_tok[g] = pl.step("scalar", lambda h, g=g, s0=s0: h.copy(
                        sqh_v[:, 1024 * g:1024 * (g + 1)],
                        V("avstage", s0, 1024)))
                    for mp2 in (2 * g, 2 * g + 1):
                        pl.wait("vector", c_tok[g])
                        sub_tok[mp2] = pl.step(
                            "vector", lambda h, mp2=mp2: h.tensor_tensor(
                                sql_v[:, 512 * mp2:512 * (mp2 + 1)], sq4(mp2),
                                sqh_v[:, 512 * mp2:512 * (mp2 + 1)],
                                op=ALU.subtract))

                for g in range(MCN // 4):
                    for half in range(2):
                        mp = 2 * g + half
                        bank = mp % 2
                        if mp >= 2:
                            pl.wait("tensor", ev_tok[mp - 2])
                        t_mm = None
                        for sub in range(2):
                            mc = 2 * mp + sub
                            ksl = slice(128 * mc, 128 * (mc + 1))
                            osl = slice(256 * sub, 256 * (sub + 1))
                            pl.emit("tensor", lambda h, ksl=ksl, nsl=nsl, bank=bank, osl=osl:
                                    h.matmul(ps[bank][:, osl], kh_v[:, ksl], qh_v[:, nsl],
                                             start=True, stop=False))
                            pl.emit("tensor", lambda h, ksl=ksl, nsl=nsl, bank=bank, osl=osl:
                                    h.matmul(ps[bank][:, osl], kh_v[:, ksl], ql_v[:, nsl],
                                             start=False, stop=False))
                            t_mm = pl.step("tensor", lambda h, ksl=ksl, nsl=nsl, bank=bank, osl=osl:
                                           h.matmul(ps[bank][:, osl], kl_v[:, ksl],
                                                    qh_v[:, nsl], start=False, stop=True))
                        pl.wait("scalar", t_mm)
                        if mp >= 2:
                            pl.wait("scalar", sq_tok[mp - 2])
                        ev_tok[mp] = pl.step("scalar", lambda h, bank=bank:
                                             h.activation(scr2(bank), ps[bank][:, 0:512],
                                                          AF.Relu))
                        pl.wait("vector", ev_tok[mp])
                        if mp >= 4:
                            pl.wait("vector", c_tok[(mp - 4) // 2])
                            pl.wait("vector", sub_tok[mp - 4])
                        sq_tok[mp] = pl.step("vector", lambda h, mp=mp, p=bank:
                                             h.tensor_tensor(sq4(mp), scr2(p), scr2(p),
                                                             op=ALU.mult))
                    if g >= 1:
                        emit_cast_subs(g - 1)
                emit_cast_subs(MCN // 4 - 1)
                t_sq_all = sub_tok[MCN // 2 - 1]

                pl.wait("tensor", t_sq_all)
                pl.wait("tensor", ev_tok[MCN // 2 - 1])
                t_av = None
                for mc in range(MCN):
                    par = mc % 2
                    r, lt = mc // 16, mc % 16
                    b0 = r * AGW
                    if mc >= 2:
                        pl.wait("sync", vcons[(nb, mc - 2)])
                    else:
                        pl.wait("sync", vcons.get((nb - 1, MCN - 2 + mc), None))
                    pl.dma("sync", lambda h, b0=b0, lt=lt, par=par: h.dma_start(
                        out=vb_hi(par), in_=ag_out[b0 + 128 * lt:b0 + 128 * (lt + 1), :]),
                        "dV0" if par == 0 else "dV1")
                    pl.dma("sync", lambda h, b0=b0, lt=lt, par=par: h.dma_start(
                        out=vb_lo(par),
                        in_=ag_out[b0 + TC + 128 * lt:b0 + TC + 128 * (lt + 1), :]),
                        "dV0" if par == 0 else "dV1")
                    pl.wait("tensor", pl.tok("dV0" if par == 0 else "dV1"))
                    for ns in range(2):
                        s_hi = sqh_v[:, 256 * mc + 128 * ns:256 * mc + 128 * (ns + 1)]
                        s_lo = sql_v[:, 256 * mc + 128 * ns:256 * mc + 128 * (ns + 1)]
                        for eq in range(4):
                            bank = ns * 4 + eq
                            esl = slice(512 * eq, 512 * (eq + 1))
                            st = (mc == 0)
                            sp = (mc == MCN - 1)
                            pl.emit("tensor", lambda h, a=s_hi, p=par, esl=esl, bank=bank, st=st:
                                    h.matmul(ps[bank][:, :], a, vb_hi(p)[:, esl],
                                             start=st, stop=False))
                            pl.emit("tensor", lambda h, a=s_hi, p=par, esl=esl, bank=bank:
                                    h.matmul(ps[bank][:, :], a, vb_lo(p)[:, esl],
                                             start=False, stop=False))
                            t_av = pl.step("tensor",
                                           lambda h, a=s_lo, p=par, esl=esl, bank=bank, sp=sp:
                                           h.matmul(ps[bank][:, :], a, vb_hi(p)[:, esl],
                                                    start=False, stop=sp))
                    vcons[(nb, mc)] = t_av
                for ns in range(2):
                    pl.wait("scalar", t_av)
                    pl.wait("scalar", av_guard[ns])
                    tev = None
                    for eq in range(4):
                        bank = ns * 4 + eq
                        tev = pl.step("scalar", lambda h, bank=bank, ns=ns, eq=eq:
                                      h.copy(avst(ns)[:, 512 * eq:512 * (eq + 1)],
                                             ps[bank][:, :]))
                    prev_evac[ns] = tev
                    pl.wait("gpsimd", tev)
                    r0 = 256 * nb + 128 * ns
                    av_guard[ns] = pl.dma("gpsimd", lambda h, r0=r0, ns=ns: h.dma_start(
                        out=av_sp[r0:r0 + 128, :], in_=avst(ns)),
                        "dA0" if ns == 0 else "dA1")

        if KSTAGE >= 2:
            pl.barrier()
            def wo_hi(ec):
                return V("woutb", 512 * ec, 512, BF16)

            def wo_lo(ec):
                return V("woutb", 8192 + 512 * ec, 512, BF16)

            t_wo = None
            stg = V("osq", 0, 1024)
            for ec in range(16):
                if t_wo is not None:
                    pl.wait("sync", t_wo)
                tld = pl.dma("sync", lambda h, ec=ec, stg=stg: h.dma_start(
                    out=stg, in_=wout[128 * ec:128 * (ec + 1), :]), "dWO")
                pl.wait("vector", tld)
                t_wo = pl.split(stg, wo_hi(ec), wo_lo(ec))

            def avl_p(p):
                return V("avl", 2048 * p, 2048)

            def suh_p(p):
                return V("sul", 2048 * p, 1024, BF16)

            def sul_p(p):
                return V("sul", 2048 * p + 1024, 1024, BF16)
            gsc_v = V("gsc", 0, 2048)
            gbh_v = V("gbf", 0, 1024, BF16)
            gbl_v = V("gbf", 1024, 1024, BF16)

            def gT_hi(ec):
                return V("gT", 128 * ec, 128, BF16)

            def gT_lo(ec):
                return V("gT", 2048 + 128 * ec, 128, BF16)

            t_gcons = None
            gcons_hist = [None, None]
            t_x2st = None
            t_ln2v = None
            for tb in range(NTB):
                tbsl = slice(512 * tb, 512 * (tb + 1))
                pl.wait("sync", t_x2st)
                pl.wait("sync", t_ln2v)
                txr = None
                for c in range(HC):
                    txr = pl.dma("sync", lambda h, c=c, tbsl=tbsl: h.dma_start(
                        out=V("xTrel", 512 * c, 512),
                        in_=xT[128 * c:128 * (c + 1), tbsl]), "dXR")
                t_oute = None
                for htb in range(2):
                    for tl in range(2):
                        tt = 4 * tb + 2 * htb + tl
                        lp = tt % 2
                        pl.wait("sync", gcons_hist[lp])
                        if tb == 0 and htb == 0 and tl == 0:
                            pl.wait("sync", pl.tok("dA0"))
                            pl.wait("sync", pl.tok("dA1"))
                        tar = pl.dma("sync", lambda h, tt=tt, lp=lp: h.dma_start(
                            out=avl_p(lp), in_=av_sp[128 * tt:128 * (tt + 1), :]),
                            "dAR" if lp == 0 else "dKL")
                        tsr = pl.dma("sync", lambda h, tt=tt, lp=lp: h.dma_start(
                            out=suh_p(lp), in_=su_sp[128 * tt:128 * (tt + 1), :]),
                            "dSU" if lp == 0 else "dB")
                        tsr = pl.dma("sync", lambda h, tt=tt, lp=lp: h.dma_start(
                            out=sul_p(lp), in_=su_sp[TC + 128 * tt:TC + 128 * (tt + 1), :]),
                            "dSU" if lp == 0 else "dB")
                        pl.wait("vector", tar)
                        pl.wait("vector", tsr)
                        tg = pl.step("vector", lambda h, lp=lp: h.tensor_tensor(
                            gsc_v, suh_p(lp), sul_p(lp), op=ALU.add))
                        pl.wait("vector", tg)
                        t_gcons = pl.step("vector", lambda h, lp=lp: h.tensor_tensor(
                            gsc_v, gsc_v, avl_p(lp), op=ALU.mult))
                        gcons_hist[lp] = t_gcons
                        pl.wait("vector", t_gcons)
                        tsp = pl.split(gsc_v, gbh_v, gbl_v)
                        pl.wait("tensor", tsp)
                        pl.wait("tensor", t_ln2v)
                        tq = None
                        tq_hist = [None, None]
                        for i2, (ec, half) in enumerate(
                                (e, hh) for e in range(16) for hh in (0, 1)):
                            gin = gbh_v if half == 0 else gbl_v
                            gout = gT_hi if half == 0 else gT_lo
                            bank = 6 + (i2 % 2)
                            if tq_hist[i2 % 2] is not None:
                                pl.wait("tensor", tq_hist[i2 % 2])
                            tt_mm = pl.step("tensor", lambda h, gin=gin, ec=ec, bank=bank:
                                            h.transpose(ps[bank][:, 0:64].bitcast(BF16),
                                                        gin[:, 128 * ec:128 * (ec + 1)],
                                                        ident_bf[:]))
                            pl.wait("vector", tt_mm)
                            tq = pl.step("vector", lambda h, gout=gout, ec=ec, tl=tl, bank=bank:
                                         h.tensor_copy(
                                             gout(ec)[:, 128 * tl:128 * (tl + 1)],
                                             ps[bank][:, 0:64].bitcast(BF16)))
                            tq_hist[i2 % 2] = tq
                    pl.wait("tensor", tq)
                    pl.wait("tensor", t_oute)
                    if tb == 0 and htb == 0:
                        pl.wait("tensor", t_wo)
                        pl.wait("tensor", prev_evac[0])
                        pl.wait("tensor", prev_evac[1])
                    t_oute_prev = t_oute
                    t_oute = None
                    for hp in range(2):
                        pl.wait("tensor", t_oute if t_oute else t_oute_prev)
                        t_op = None
                        for ec in range(16):
                            for hq2 in range(4):
                                ht = 4 * hp + hq2
                                st = (ec == 0)
                                sp = (ec == 15)
                                hsl = slice(128 * ht, 128 * (ht + 1))
                                pl.emit("tensor", lambda h, ec=ec, hsl=hsl, hq2=hq2, st=st:
                                        h.matmul(ps[hq2][:, 0:256], wo_hi(ec)[:, hsl],
                                                 gT_hi(ec)[:, :], start=st, stop=False))
                                pl.emit("tensor", lambda h, ec=ec, hsl=hsl, hq2=hq2:
                                        h.matmul(ps[hq2][:, 0:256], wo_lo(ec)[:, hsl],
                                                 gT_hi(ec)[:, :], start=False, stop=False))
                                t_op = pl.step("tensor", lambda h, ec=ec, hsl=hsl, hq2=hq2, sp=sp:
                                               h.matmul(ps[hq2][:, 0:256], wo_hi(ec)[:, hsl],
                                                        gT_lo(ec)[:, :], start=False, stop=sp))
                        pl.wait("vector", t_op)
                        for hq2 in range(4):
                            ht = 4 * hp + hq2
                            t_oute = pl.step("vector", lambda h, ht=ht, hq2=hq2, htb=htb:
                                             h.tensor_scalar(
                                                 V("outT", 512 * ht, 512)[:, 256 * htb:256 * (htb + 1)],
                                                 ps[hq2][:, 0:256], s_bout[:, ht:ht + 1], None,
                                                 ALU.add))
                pl.wait("tensor", t_oute)
                pl.wait("vector", t_oute)
                pl.wait("vector", txr)

                def ln2_out(tb2, sl2):
                    t = None
                    for c in range(HC):
                        ov, tmp = V("outT", 512 * c, 512), V("osq", 512 * c, 512)
                        t = pl.step("vector", lambda h, ov=ov, tmp=tmp: h.tensor_tensor(
                            tmp, ov, ps[6][:, :], op=ALU.subtract))
                    pl.wait("vector", t)
                    for c in range(HC):
                        tmp = V("osq", 512 * c, 512)
                        t = pl.step("vector", lambda h, tmp=tmp: h.tensor_tensor(
                            tmp, tmp, ps[7][:, :], op=ALU.mult))
                    pl.wait("vector", t)
                    vt = None
                    for c in range(HC):
                        tmp, xr = V("osq", 512 * c, 512), V("xTrel", 512 * c, 512)
                        vt = pl.step("vector", lambda h, tmp=tmp, xr=xr: h.tensor_tensor(
                            xr, tmp, xr, op=ALU.add))
                    return vt

                emit_ln(HC, lambda c: V("outT", 512 * c, 512), "osq", ln2_out, [0])
                t_ln2v = pl.tok("cV")
                pl.wait("gpsimd", t_ln2v)
                for c in range(HC):
                    t_x2st = pl.dma("gpsimd", lambda h, c=c, tbsl=tbsl: h.dma_start(
                        out=x2_sp[128 * c:128 * (c + 1), tbsl],
                        in_=V("xTrel", 512 * c, 512)), "dXS")

        class _StopEmit(Exception):
            pass

        _ffn_stopped = False
        if KSTAGE >= 3:
          try:
            pl.barrier()

            def xi_v(dc):
                return V("xi", 1024 * dc, 1024, BF16)

            def wiT_v(dc):
                return V("wf", 2048 * dc, 2048, BF16)

            def x2s_v(c):
                return V("x2str", 512 * c, 512)

            def zsq_v(c):
                return V("zsq", 512 * c, 512)

            def m_row(tb):
                return V("z", 512 * tb, 512)[0:1, :]

            def r_row(tb):
                return V("hstage", 512 * tb, 512)[0:1, :]

            def q_y(p):
                return V("qscr", 2048 * p, 512)

            def q_ci(p):
                return V("qscr", 2048 * p + 512, 512).bitcast(I32)

            def q_cm1(p):
                return V("qscr", 2048 * p + 1024, 512)

            def q_d(p):
                return V("qscr", 2048 * p + 1536, 512)

            def w1s_v(p):
                return V("w1str", 1024 * p, 1024)

            def hst_v(p):
                return V("hstage", 512 * p, 512)

            t_x2cons = None
            rmax_cols = []
            for tb in range(NTB):
                tbsl = slice(512 * tb, 512 * (tb + 1))
                pl.wait("sync", t_x2cons)
                pl.wait("sync", pl.tok("dXS"))
                tld = None
                for c in range(HC):
                    tld = pl.dma("sync", lambda h, c=c, tbsl=tbsl: h.dma_start(
                        out=x2s_v(c), in_=x2_sp[128 * c:128 * (c + 1), tbsl]), "dXR")
                pl.wait("vector", tld)
                t = None
                for c in range(HC):
                    t = pl.step("vector", lambda h, c=c: h.tensor_tensor(
                        zsq_v(c), x2s_v(c), x2s_v(c), op=ALU.mult))
                pl.wait("tensor", t)
                tm = None
                for c in range(HC):
                    pl.emit("tensor", lambda h, c=c: h.matmul(
                        ps[6][0:1, :], onesK[:], x2s_v(c),
                        start=(c == 0), stop=(c == HC - 1)))
                    tm = pl.step("tensor", lambda h, c=c: h.matmul(
                        ps[7][0:1, :], onesK[:], zsq_v(c),
                        start=(c == 0), stop=(c == HC - 1)))
                pl.wait("vector", tm)
                t1 = pl.step("vector", lambda h: h.tensor_scalar(
                    rowM[:], ps[6][0:1, :], 1.0 / H, None, ALU.mult))
                pl.step("vector", lambda h: h.tensor_scalar(
                    rowV[:], ps[7][0:1, :], 1.0 / H, None, ALU.mult))
                pl.wait("vector", t1)
                t3 = pl.step("vector", lambda h: h.tensor_tensor(
                    rowT[:], rowM[:], rowM[:], op=ALU.mult))
                pl.wait("vector", t3)
                tvv = pl.step("vector", lambda h: h.tensor_tensor(
                    rowV[:], rowV[:], rowT[:], op=ALU.subtract))
                pl.wait("scalar", tvv)
                tss = pl.step("scalar", lambda h: h.activation(
                    rowT[:], rowV[:], AF.Sqrt, bias=s_eps[:]))
                pl.wait("vector", tss)
                trr = pl.step("vector", lambda h: h.reciprocal(rowR[:], rowT[:]))
                pl.wait("vector", trr)
                pl.emit("vector", lambda h, tb=tb: h.tensor_copy(m_row(tb), rowM[:]))
                trc = pl.step("vector", lambda h, tb=tb: h.tensor_copy(r_row(tb), rowR[:]))
                pl.wait("tensor", trc)
                pl.emit("tensor", lambda h: h.matmul(
                    ps[6][:, :], ones1[:], rowM[:], start=True, stop=True))
                tbb = pl.step("tensor", lambda h: h.matmul(
                    ps[7][:, :], ones1[:], rowR[:], start=True, stop=True))
                pl.wait("vector", tbb)
                t = None
                for c in range(HC):
                    t = pl.step("vector", lambda h, c=c: h.tensor_tensor(
                        zsq_v(c), x2s_v(c), ps[6][:, :], op=ALU.subtract))
                pl.wait("vector", t)
                for c in range(HC):
                    t = pl.step("vector", lambda h, c=c: h.tensor_tensor(
                        zsq_v(c), zsq_v(c), ps[7][:, :], op=ALU.mult))
                pl.wait("vector", t)
                for c in range(HC):
                    col = 32 + 8 * tb + c
                    rmax_cols.append(col)
                    t = pl.step("vector", lambda h, c=c, col=col: h.tensor_reduce(
                        s_tmp[:, col:col + 1], zsq_v(c), axis=AX.X, op=ALU.max))
                t_x2cons = t
                pl.wait("tensor", t)

            pl.wait("vector", t_x2cons)
            tcm = pl.step("vector", lambda h: h.reduce_max(
                s_tmp[:, 64:65], s_tmp[:, 32:64], axis=AX.X))
            pl.wait("tensor", tcm)
            ttr = pl.step("tensor", lambda h: h.transpose(
                ps[5][0:1, 0:128], s_tmp[:, 64:65], ident[:]))
            pl.wait("vector", ttr)
            tgm = pl.step("gpsimd", lambda h: h.memset(s_sc[:, :], -1e30))
            pl.wait("vector", tgm)
            tgm = pl.step("vector", lambda h: h.reduce_max(
                s_sc[0:1, 0:1], ps[5][0:1, 0:128], axis=AX.X))
            pl.wait("gpsimd", tgm)
            tdm = pl.dma("gpsimd", lambda h: h.dma_start(
                out=ar1_in[:, :], in_=s_sc[0:1, 0:8]), "dR")
            pl.wait("gpsimd", tdm)
            pl.emit("gpsimd", lambda h: h.collective_compute(
                "AllReduce", ALU.max, replica_groups=[list(range(8))],
                ins=[ar1_in.opt()], outs=[ar1_out.opt()]), inc="cc")
            t_w1cons = [None, None]
            q_entry = [None, None]
            wi_slot_guard = [None, None]
            wiT_bank_hist = [None, None]
            t_wiT = None

            def QW(p):
                return V("qscr", 4096 * p, 1024)

            def QWCI(p):
                return V("qscr", 4096 * p + 1024, 1024).bitcast(I32)

            def QWCM(p):
                return V("qscr", 4096 * p + 2048, 1024)

            def QWD(p):
                return V("qscr", 4096 * p + 3072, 1024)

            for fc in range(32):
                par = fc % 2
                pl.wait("sync", t_w1cons[par])
                tw = pl.dma("sync", lambda h, fc=fc, par=par: h.dma_start(
                    out=w1s_v(par), in_=w1[128 * fc:128 * (fc + 1), :]),
                    "dF0" if par == 0 else "dF1")
                pl.wait("vector", tw)
                trm = pl.step("vector", lambda h, fc=fc, par=par: h.tensor_reduce(
                    s_sw1[:, fc:fc + 1], w1s_v(par), axis=AX.X, op=ALU.max,
                    apply_absolute_value=True))
                pl.wait("vector", trm)
                trm = pl.step("vector", lambda h, fc=fc, par=par: h.reciprocal(
                    s_tmp[:, 66 + 2 * par:67 + 2 * par], s_sw1[:, fc:fc + 1]))
                pl.wait("vector", trm)
                trm = pl.step("vector", lambda h, par=par: h.tensor_scalar(
                    s_tmp[:, 67 + 2 * par:68 + 2 * par],
                    s_tmp[:, 66 + 2 * par:67 + 2 * par], 127.0, None, ALU.mult))
                pl.wait("vector", trm)
                pl.wait("vector", wi_slot_guard[par])
                q_entry[par] = pl.quant(
                    w1s_v(par), V("zsq", 1024 * par, 512).bitcast(BF16),
                    QW(par), QWCI(par), QWCM(par), QWD(par),
                    s_tmp[:, 67 + 2 * par:68 + 2 * par], relu=False,
                    entry=q_entry[par])
                t_w1cons[par] = q_entry[par][0]
                pl.wait("tensor", q_entry[par][0])
                for tpi in range(8):
                    bank = (tpi % 2) + 2
                    if wiT_bank_hist[tpi % 2] is not None:
                        pl.wait("tensor", wiT_bank_hist[tpi % 2])
                    tmm = pl.step("tensor", lambda h, par=par, tpi=tpi, bank=bank:
                                  h.transpose(
                                      ps[bank][:, 0:64].bitcast(BF16),
                                      V("zsq", 1024 * par, 512).bitcast(BF16)[:, 128 * tpi:128 * (tpi + 1)],
                                      ident_bf[:]))
                    pl.wait("vector", tmm)
                    t_wiT = pl.step("vector", lambda h, dc=tpi, fc=fc, bank=bank:
                                    h.tensor_copy(
                                        wiT_v(dc)[:, 128 * fc:128 * (fc + 1)],
                                        ps[bank][:, 0:64].bitcast(BF16)))
                    wiT_bank_hist[tpi % 2] = t_wiT
                wi_slot_guard[par] = t_wiT

            pl.wait("gpsimd", pl.tok("cc"))
            tdm = pl.dma("gpsimd", lambda h: h.dma_start(
                out=s_sc[0:1, 8:16], in_=ar1_out[:, :]), "dR")
            pl.wait("vector", tdm)
            ts1 = pl.step("vector", lambda h: h.tensor_scalar(
                s_sc[0:1, 1:2], s_sc[0:1, 8:9], 1e-8, 1.0 / 255.0, ALU.max, ALU.mult))
            pl.wait("vector", ts1)
            ts1 = pl.step("vector", lambda h: h.reciprocal(s_sc[0:1, 2:3], s_sc[0:1, 1:2]))
            pl.wait("tensor", ts1)
            tbq = pl.step("tensor", lambda h: h.matmul(
                ps[5][:, 0:1], ones1[:], s_sc[0:1, 2:3], start=True, stop=True))
            pl.wait("vector", tbq)
            trb = pl.step("vector", lambda h: h.tensor_copy(s_scp[:, 0:1], ps[5][:, 0:1]))
            pl.wait("tensor", trb)
            tbq = pl.step("tensor", lambda h: h.matmul(
                ps[5][:, 0:1], ones1[:], s_sc[0:1, 1:2], start=True, stop=True))
            pl.wait("vector", tbq)
            trb = pl.step("vector", lambda h: h.tensor_copy(s_scp[:, 2:3], ps[5][:, 0:1]))

            pl.wait("vector", trb)
            tsv = pl.step("vector", lambda h: h.tensor_scalar(
                s_sv1[:, :], s_sw1[:, 0:32], s_scp[:, 2:3], None, ALU.mult))
            pl.wait("vector", tsv)
            tsv = pl.step("vector", lambda h: h.tensor_scalar(
                s_sv1[:, :], s_sv1[:, :], 1.0 / 127.0, None, ALU.mult))
            pl.wait("vector", tsv)
            tsv = pl.step("vector", lambda h: h.reciprocal(s_tmp[:, 68:100], s_sv1[:, :]))
            pl.wait("vector", tsv)
            tsv = pl.step("vector", lambda h: h.tensor_tensor(
                s_tmp[:, 100:132], s_b1[:, :], s_tmp[:, 68:100], op=ALU.mult))
            pl.wait("vector", tsv)
            tbqq = pl.quant(s_tmp[:, 100:132], s_tmp[:, 100:132],
                            QW(0)[:, 0:32], QWCI(0)[:, 0:32], QWCM(0)[:, 0:32],
                            QWD(0)[:, 0:32], None, relu=False)[0]
            pl.wait("vector", tbqq)
            tbqq = pl.step("vector", lambda h: h.tensor_tensor(
                s_bq1[:, :], s_tmp[:, 100:132], s_sv1[:, :], op=ALU.mult))

            pl.wait("tensor", pl.tok("cV"))
            pl.wait("tensor", pl.tok("cS"))
            tbb = None
            for tb in range(NTB):
                pl.emit("tensor", lambda h, tb=tb: h.matmul(
                    ps[tb][:, :], ones1[:], m_row(tb), start=True, stop=True))
                tbb = pl.step("tensor", lambda h, tb=tb: h.matmul(
                    ps[4 + tb][:, :], ones1[:], r_row(tb), start=True, stop=True))
            def QY(p):
                return V("qscr", 4096 * p, 1024)
            def QCI(p):
                return V("qscr", 4096 * p + 1024, 1024).bitcast(I32)
            def QCM(p):
                return V("qscr", 4096 * p + 2048, 1024)
            def QD(p):
                return V("qscr", 4096 * p + 3072, 1024)
            t_x2c2 = None
            b2_entry = [None, None]
            ld_guard = [None, None]
            for c in range(HC):
                par = c % 2
                pl.wait("sync", ld_guard[par])
                tld = pl.dma("sync", lambda h, c=c, par=par: h.dma_start(
                    out=V("x2str", 2048 * par, 2048),
                    in_=x2_sp[128 * c:128 * (c + 1), :]),
                    "dXR" if par == 0 else "dSU")
                pl.wait("vector", tld)
                pl.wait("vector", tbb)
                for hhalf in range(2):
                    hsl = slice(1024 * hhalf, 1024 * (hhalf + 1))
                    if b2_entry[par] is not None:
                        pl.wait("vector", b2_entry[par][0])
                    t = None
                    for tb in (2 * hhalf, 2 * hhalf + 1):
                        tbsl = slice(512 * tb, 512 * (tb + 1))
                        osl = slice(512 * (tb - 2 * hhalf), 512 * (tb - 2 * hhalf + 1))
                        t = pl.step("vector", lambda h, par=par, tb=tb, tbsl=tbsl, osl=osl:
                                    h.tensor_tensor(QY(par)[:, osl],
                                                    V("x2str", 2048 * par, 2048)[:, tbsl],
                                                    ps[tb][:, :], op=ALU.subtract))
                    pl.wait("vector", t)
                    for tb in (2 * hhalf, 2 * hhalf + 1):
                        osl = slice(512 * (tb - 2 * hhalf), 512 * (tb - 2 * hhalf + 1))
                        t = pl.step("vector", lambda h, par=par, tb=tb, osl=osl:
                                    h.tensor_tensor(QY(par)[:, osl], QY(par)[:, osl],
                                                    ps[4 + tb][:, :], op=ALU.mult))
                    pl.wait("vector", t)
                    b2_entry[par] = pl.quant(
                        QY(par), xi_v(c)[:, slice(1024 * hhalf, 1024 * (hhalf + 1))],
                        QY(par), QCI(par), QCM(par), QD(par),
                        s_scp[:, 0:1], relu=True, entry=b2_entry[par])
                ld_guard[par] = b2_entry[par][0]
                t_x2c2 = b2_entry[par][0]
            pl.wait("tensor", t_x2c2)

            _ffn_on = [KFFN >= 1]
            pl.wait("vector", tbqq)
            pl.wait("scalar", tbqq)
            pl.wait("tensor", t_wiT)
            pl.wait("vector", t_wiT)
            tmaxg = pl.step("gpsimd", lambda h: h.memset(s_tmp[:, 65:66], -1e30))
            pl.wait("vector", tmaxg)
            h_store = [None, None]
            t_hev = {}
            t_hred = {}
            t_hmax = tmaxg
            for ft in range(32 if KFFN >= 1 else 0):
                for tb in range(NTB):
                    it = ft * NTB + tb
                    par = it % 2
                    bank = it % 2
                    tbsl = slice(512 * tb, 512 * (tb + 1))
                    if it >= 2:
                        pl.wait("tensor", t_hev[it - 2])
                    tmf = None
                    for dc in range(HC):
                        tmf = pl.step("tensor", lambda h, dc=dc, ft=ft, tbsl=tbsl, bank=bank:
                                      h.matmul(ps[bank][:, :],
                                               wiT_v(dc)[:, 128 * ft:128 * (ft + 1)],
                                               xi_v(dc)[:, tbsl],
                                               start=(dc == 0), stop=(dc == HC - 1)))
                    pl.wait("scalar", tmf)
                    if h_store[par] is not None:
                        pl.wait("scalar", h_store[par])
                    if (it - 2) in t_hred:
                        pl.wait("scalar", t_hred[it - 2])
                    t_hev[it] = pl.step("scalar", lambda h, bank=bank, par=par, ft=ft:
                                        h.activation(hst_v(par), ps[bank][:, :], AF.Relu,
                                                     scale=s_sv1[:, ft:ft + 1],
                                                     bias=s_bq1[:, ft:ft + 1]))
                    pl.wait("vector", t_hev[it])
                    tr1 = pl.step("vector", lambda h, par=par: h.tensor_reduce(
                        s_tmp[:, 70 + par:71 + par], hst_v(par), axis=AX.X, op=ALU.max))
                    t_hred[it] = tr1
                    pl.wait("vector", tr1)
                    pl.wait("vector", t_hmax)
                    t_hmax = pl.step("vector", lambda h, par=par: h.tensor_tensor(
                        s_tmp[:, 65:66], s_tmp[:, 65:66], s_tmp[:, 70 + par:71 + par],
                        op=ALU.max))
                    pl.wait("gpsimd", t_hev[it])
                    h_store[par] = pl.dma("gpsimd", lambda h, ft=ft, tbsl=tbsl, par=par:
                                          h.dma_start(out=h_sp[128 * ft:128 * (ft + 1), tbsl],
                                                      in_=hst_v(par)),
                                          "dG" if par == 0 else "dH")

            if KFFN < 2:
                raise _StopEmit()
            pl.wait("vector", t_hmax)
            tcm = pl.step("vector", lambda h: h.tensor_copy(s_tmp[:, 64:65], s_tmp[:, 65:66]))
            pl.wait("tensor", tcm)
            ttr = pl.step("tensor", lambda h: h.transpose(
                ps[5][0:1, 0:128], s_tmp[:, 64:65], ident[:]))
            pl.wait("vector", ttr)
            tgm = pl.step("gpsimd", lambda h: h.memset(s_sc[:, :], -1e30))
            pl.wait("vector", tgm)
            tgm = pl.step("vector", lambda h: h.reduce_max(
                s_sc[0:1, 0:1], ps[5][0:1, 0:128], axis=AX.X))
            pl.wait("gpsimd", tgm)
            tdm = pl.dma("gpsimd", lambda h: h.dma_start(
                out=ar2_in[:, :], in_=s_sc[0:1, 0:8]), "dR")
            pl.wait("gpsimd", tdm)
            pl.emit("gpsimd", lambda h: h.collective_compute(
                "AllReduce", ALU.max, replica_groups=[list(range(8))],
                ins=[ar2_in.opt()], outs=[ar2_out.opt()]), inc="cc")
            pl.wait("gpsimd", pl.tok("cc"))
            tdm = pl.dma("gpsimd", lambda h: h.dma_start(
                out=s_sc[0:1, 8:16], in_=ar2_out[:, :]), "dR")
            pl.wait("vector", tdm)
            ts2 = pl.step("vector", lambda h: h.tensor_scalar(
                s_sc[0:1, 1:2], s_sc[0:1, 8:9], 1e-8, 1.0 / 255.0, ALU.max, ALU.mult))
            pl.wait("vector", ts2)
            ts2 = pl.step("vector", lambda h: h.reciprocal(s_sc[0:1, 2:3], s_sc[0:1, 1:2]))
            pl.wait("tensor", ts2)
            tbq = pl.step("tensor", lambda h: h.matmul(
                ps[5][:, 0:1], ones1[:], s_sc[0:1, 2:3], start=True, stop=True))
            pl.wait("vector", tbq)
            tr2b = pl.step("vector", lambda h: h.tensor_copy(s_scp[:, 1:2], ps[5][:, 0:1]))
            pl.wait("tensor", tr2b)
            tbq = pl.step("tensor", lambda h: h.matmul(
                ps[5][:, 0:1], ones1[:], s_sc[0:1, 1:2], start=True, stop=True))
            pl.wait("vector", tbq)
            tr2b = pl.step("vector", lambda h: h.tensor_copy(s_scp[:, 3:4], ps[5][:, 0:1]))

          except _StopEmit:
            _ffn_stopped = True
        if KSTAGE >= 3 and not _ffn_stopped:
          try:
            pl.barrier()

            def hi_v(fc):
                return V("hihalf", 1024 * fc, 1024, BF16)

            def hrl_v(p):
                return V("hrel", 2048 * p, 2048)

            def q2_y(p):
                return V("q2scr", 2048 * p, 512)

            def q2_ci(p):
                return V("q2scr", 2048 * p + 512, 512).bitcast(I32)

            def q2_cm1(p):
                return V("q2scr", 2048 * p + 1024, 512)

            def q2_d(p):
                return V("q2scr", 2048 * p + 1536, 512)

            def w2sv(p):
                return V("w2s", 512 * p, 512, BF16)

            def w2qstg():
                return V("q2scr", 4096, 1024)

            def w2ibf():
                return V("w2s", 1024, 1024, BF16)

            def outp_v(c):
                return V("outp", 2048 * c, 2048)

            if KFFN < 3:
                raise _StopEmit()

            def w2stg(p):
                return V("hihalf", 4096 * p, 4096)

            def w2ibf():
                return V("w2s", 0, 2048, BF16)

            def Q2Y(p):
                return V("q2scr", 4096 * p, 1024)

            def Q2CI(p):
                return V("q2scr", 4096 * p + 1024, 1024).bitcast(I32)

            def Q2CM(p):
                return V("q2scr", 4096 * p + 2048, 1024)

            def Q2D(p):
                return V("q2scr", 4096 * p + 3072, 1024)

            t_w2ld = [None, None]
            t_w2st = None
            w2_entry = [None, None]
            t_w2c = None
            for dc in range(HC):
                lpar = dc % 2
                pl.wait("sync", t_w2ld[lpar])
                tw = pl.dma("sync", lambda h, dc=dc, lpar=lpar: h.dma_start(
                    out=w2stg(lpar), in_=w2[128 * dc:128 * (dc + 1), :]), "dF0" if lpar == 0 else "dF1")
                pl.wait("vector", tw)
                trm = pl.step("vector", lambda h, dc=dc, lpar=lpar: h.tensor_reduce(
                    s_sw2[:, dc:dc + 1], w2stg(lpar), axis=AX.X, op=ALU.max,
                    apply_absolute_value=True))
                pl.wait("vector", trm)
                trm = pl.step("vector", lambda h, dc=dc, lpar=lpar: h.reciprocal(
                    s_tmp[:, 70 + 2 * lpar:71 + 2 * lpar], s_sw2[:, dc:dc + 1]))
                pl.wait("vector", trm)
                trm = pl.step("vector", lambda h, lpar=lpar: h.tensor_scalar(
                    s_tmp[:, 71 + 2 * lpar:72 + 2 * lpar],
                    s_tmp[:, 70 + 2 * lpar:71 + 2 * lpar], 127.0, None, ALU.mult))
                pl.wait("vector", trm)
                pl.wait("vector", t_w2st)
                for qtr in range(4):
                    qsl = slice(1024 * qtr, 1024 * (qtr + 1))
                    qpar = qtr % 2
                    w2_entry[qpar] = pl.quant(
                        w2stg(lpar)[:, qsl], w2ibf()[:, qsl],
                        Q2Y(qpar), Q2CI(qpar), Q2CM(qpar), Q2D(qpar),
                        s_tmp[:, 71 + 2 * lpar:72 + 2 * lpar], relu=False,
                        entry=w2_entry[qpar])
                t_w2ld[lpar] = w2_entry[1][0]
                pl.wait("gpsimd", w2_entry[0][0])
                pl.wait("gpsimd", w2_entry[1][0])
                t_w2st = pl.dma("gpsimd", lambda h, dc=dc: h.dma_start(
                    out=w2i_sp[128 * dc:128 * (dc + 1), :], in_=w2ibf()), "dWS")
                t_w2c = w2_entry[1][0]

            pl.wait("vector", tr2b)
            tsv = pl.step("vector", lambda h: h.tensor_scalar(
                s_sv2[:, :], s_sw2[:, :], s_scp[:, 3:4], None, ALU.mult))
            pl.wait("vector", tsv)
            tsv = pl.step("vector", lambda h: h.tensor_scalar(
                s_sv2[:, :], s_sv2[:, :], 1.0 / 127.0, None, ALU.mult))
            pl.wait("vector", tsv)
            tsv = pl.step("vector", lambda h: h.reciprocal(s_tmp[:, 80:88], s_sv2[:, :]))
            pl.wait("vector", tsv)
            tsv = pl.step("vector", lambda h: h.tensor_tensor(
                s_tmp[:, 88:96], s_b2[:, :], s_tmp[:, 80:88], op=ALU.mult))
            pl.wait("vector", tsv)
            tbq2 = pl.quant(s_tmp[:, 88:96], s_tmp[:, 88:96],
                            Q2Y(0)[:, 0:8], Q2CI(0)[:, 0:8], Q2CM(0)[:, 0:8],
                            Q2D(0)[:, 0:8], None, relu=False)[0]
            pl.wait("vector", tbq2)
            tbq2 = pl.step("vector", lambda h: h.tensor_tensor(
                s_bq2[:, :], s_tmp[:, 88:96], s_sv2[:, :], op=ALU.mult))

            if KFFN < 4:
                raise _StopEmit()
            t_outev = None
            t_ln4 = None
            def hi_q(q, fc):
                return V("hihalf", 8192 * (q % 2) + 1024 * fc, 1024, BF16)

            hq_entry = [None, None]
            hq_ld = [None, None]

            def emit_hquant_quarter(q, gate_tok):
                tqh = gate_tok
                for fc in range(8):
                    fg = 8 * q + fc
                    par = fc % 2
                    pl.wait("sync", hq_ld[par])
                    pl.wait("sync", gate_tok)
                    th = pl.dma("sync", lambda h, fg=fg, par=par: h.dma_start(
                        out=hrl_v(par), in_=h_sp[128 * fg:128 * (fg + 1), :]),
                        "dF0" if par == 0 else "dF1")
                    pl.wait("vector", th)
                    for hh in range(2):
                        hsl = slice(1024 * hh, 1024 * (hh + 1))
                        hq_entry[par] = pl.quant(
                            hrl_v(par)[:, hsl], hi_q(q, fc)[:, hsl],
                            Q2Y(par), Q2CI(par), Q2CM(par), Q2D(par),
                            s_scp[:, 1:2], relu=True, entry=hq_entry[par])
                    hq_ld[par] = hq_entry[par][0]
                    tqh = hq_entry[par][0]
                return tqh

            pl.wait("sync", pl.tok("dWS"))
            tqh = {}
            tqh[0] = emit_hquant_quarter(0, t_w2c)
            hq_cons = {}
            t_w2l = [None, None]
            for q in range(4):
                pl.wait("tensor", tqh[q])
                for tb in range(NTB):
                    tbsl = slice(512 * tb, 512 * (tb + 1))
                    if q > 0 or tb > 0:
                        pl.wait("tensor", t_outev)
                    tmf = None
                    for fc in range(8):
                        fg = 8 * q + fc
                        wpar = fc % 2
                        pl.wait("sync", t_w2l[wpar])
                        twl = pl.dma("sync", lambda h, fg=fg, wpar=wpar: h.dma_start(
                            out=w2sv(wpar)[:, :],
                            in_=w2i_sp[:, 128 * fg:128 * (fg + 1)], transpose=True),
                            "dV0" if wpar == 0 else "dV1")
                        pl.wait("tensor", twl)
                        for dt in range(HC):
                            tmf = pl.step("tensor",
                                          lambda h, q=q, fc=fc, dt=dt, tbsl=tbsl, wpar=wpar:
                                          h.matmul(ps[dt][:, :],
                                                   w2sv(wpar)[:, 128 * dt:128 * (dt + 1)],
                                                   hi_q(q, fc)[:, tbsl],
                                                   start=(fc == 0), stop=(fc == 7)))
                        t_w2l[wpar] = tmf
                    if tb == 0 and q < 3:
                        gate = hq_cons.get(q - 1, tqh[q])
                        tqh[q + 1] = emit_hquant_quarter(q + 1, gate)
                    pl.wait("vector", tmf)
                    pl.wait("vector", t_outev)
                    for dt in range(HC):
                        if q == 0:
                            t_outev = pl.step("vector", lambda h, dt=dt, tbsl=tbsl:
                                              h.tensor_copy(outp_v(dt)[:, tbsl],
                                                            ps[dt][:, :]))
                        else:
                            t_outev = pl.step("vector", lambda h, dt=dt, tbsl=tbsl:
                                              h.tensor_tensor(outp_v(dt)[:, tbsl],
                                                              outp_v(dt)[:, tbsl],
                                                              ps[dt][:, :], op=ALU.add))
                hq_cons[q] = tmf

            pl.wait("vector", tbq2)
            pl.wait("vector", t_outev)
            t = None
            for dt in range(HC):
                t = pl.step("vector", lambda h, dt=dt: h.tensor_scalar(
                    outp_v(dt), outp_v(dt), s_sv2[:, dt:dt + 1], s_bq2[:, dt:dt + 1],
                    ALU.mult, ALU.add))
            pl.wait("vector", t)
            pl.wait("tensor", t)

            def x2r_v(c):
                return V("hrel", 512 * c, 512)

            def ystg_v(p):
                return V("q2scr", 4096 + 1024 * p, 1024)

            t_yst = [None, None]
            y_bank_hist = [None, None]
            t_x2r = None
            for tb in range(NTB):
                tbsl = slice(512 * tb, 512 * (tb + 1))
                pl.wait("sync", t_x2r)
                tld = None
                for c in range(HC):
                    tld = pl.dma("sync", lambda h, c=c, tbsl=tbsl: h.dma_start(
                        out=x2r_v(c), in_=x2_sp[128 * c:128 * (c + 1), tbsl]), "dAR")
                pl.wait("vector", tld)

                def ln4_out(tb2, sl2, tbsl=tbsl):
                    t = None
                    for c in range(HC):
                        t = pl.step("vector", lambda h, c=c: h.tensor_tensor(
                            V("q2scr", 512 * c, 512), outp_v(c)[:, tbsl], ps[6][:, :],
                            op=ALU.subtract))
                    pl.wait("vector", t)
                    for c in range(HC):
                        t = pl.step("vector", lambda h, c=c: h.tensor_tensor(
                            V("q2scr", 512 * c, 512), V("q2scr", 512 * c, 512),
                            ps[7][:, :], op=ALU.mult))
                    pl.wait("vector", t)
                    vt = None
                    for c in range(HC):
                        vt = pl.step("vector", lambda h, c=c: h.tensor_tensor(
                            x2r_v(c), V("q2scr", 512 * c, 512), x2r_v(c), op=ALU.add))
                    return vt

                emit_ln(HC, lambda c, tbsl=tbsl: outp_v(c)[:, tbsl], "q2scr",
                        ln4_out, [0])
                t_ln4 = pl.tok("cV")
                for tl in range(4):
                    par = tl % 2
                    pl.wait("tensor", t_ln4)
                    pl.wait("tensor", t_yst[par])
                    tq = None
                    for c in range(HC):
                        bank = (c % 2) + 2
                        if y_bank_hist[c % 2] is not None:
                            pl.wait("tensor", y_bank_hist[c % 2])
                        tmm = pl.step("tensor", lambda h, c=c, tl=tl, bank=bank:
                                      h.transpose(ps[bank][:, 0:128],
                                                  x2r_v(c)[:, 128 * tl:128 * (tl + 1)],
                                                  ident[:]))
                        pl.wait("vector", tmm)
                        tq = pl.step("vector", lambda h, c=c, par=par, bank=bank:
                                     h.tensor_copy(ystg_v(par)[:, 128 * c:128 * (c + 1)],
                                                   ps[bank][:, 0:128]))
                        y_bank_hist[c % 2] = tq
                    pl.wait("gpsimd", tq)
                    r0 = 512 * tb + 128 * tl
                    t_yst[par] = pl.dma("gpsimd", lambda h, r0=r0, par=par: h.dma_start(
                        out=y[r0:r0 + 128, :], in_=ystg_v(par)),
                        "dY0" if par == 0 else "dY1")
                t_x2r = pl.tokT()
            pl.wait("gpsimd", pl.tok("dY0"))
            pl.wait("gpsimd", pl.tok("dY1"))
          except _StopEmit:
            _ffn_stopped = True


        if KSTAGE >= 3 and _ffn_stopped:
            pl.barrier()
            tms = pl.step("gpsimd", lambda h: h.memset(arena[:, 0:1024], 0.0))
            pl.wait("gpsimd", tms)
            for tt in range(NT):
                tst = pl.dma("gpsimd", lambda h, tt=tt: h.dma_start(
                    out=y[128 * tt:128 * (tt + 1), :], in_=arena[:, 0:1024]), "dG")
            pl.wait("gpsimd", tst)
        if KSTAGE < 3:
            pl.barrier()
            tms = pl.step("gpsimd", lambda h: h.memset(
                V("x" if KSTAGE < 2 else "avl", 0, 1024), 0.0))
            pl.wait("gpsimd", tms)
            tst = None
            for tt in range(NT):
                tst = pl.dma("gpsimd", lambda h, tt=tt: h.dma_start(
                    out=y[128 * tt:128 * (tt + 1), :],
                    in_=V("x" if KSTAGE < 2 else "avl", 0, 1024)), "dG")
            pl.wait("gpsimd", tst)
        if KDBG:
            pl.wait("gpsimd", pl.tok("cc"))
            pl.dma("gpsimd", lambda h: h.dma_start(out=dbg_ag[:, :], in_=ag_in[:, :]), "dG")
        pl.wait("gpsimd", pl.tok("cc"))
        pl.wait("gpsimd", pl.tok("dG"))
        pl.wait("gpsimd", pl.tok("dH"))
        pl.wait("gpsimd", pl.tok("dK"))
        pl.wait("gpsimd", pl.tok("dW"))

        pl.realize(block, sems)
    return nc


def kernel(L0_attn_inputs, L0_attn_W_uv, L0_attn_b_uv, L0_attn_gamma, L0_attn_beta,
           L0_attn_W_out, L0_attn_b_out, L0_ffn_w1, L0_ffn_b1, L0_ffn_w2, L0_ffn_b2):
    x = np.ascontiguousarray(np.asarray(L0_attn_inputs, dtype=np.float32))
    wuv = np.ascontiguousarray(np.asarray(L0_attn_W_uv, dtype=np.float32))
    buv = np.asarray(L0_attn_b_uv, dtype=np.float32)
    gam = np.asarray(L0_attn_gamma, dtype=np.float32)
    bet = np.asarray(L0_attn_beta, dtype=np.float32)
    wout = np.ascontiguousarray(np.asarray(L0_attn_W_out, dtype=np.float32))
    bout = np.asarray(L0_attn_b_out, dtype=np.float32)
    w1 = np.ascontiguousarray(np.asarray(L0_ffn_w1, dtype=np.float32))
    b1 = np.asarray(L0_ffn_b1, dtype=np.float32)
    w2 = np.ascontiguousarray(np.asarray(L0_ffn_w2, dtype=np.float32))
    b2 = np.asarray(L0_ffn_b2, dtype=np.float32)

    shared = dict(
        wuv=wuv,
        buv_row=buv.reshape(1, -1),
        buv_base=buv[2 * E:].reshape(SD, 1),
        gamT=np.ascontiguousarray(gam.T),
        betT=np.ascontiguousarray(bet.T),
        wout=wout,
        bout_pt=np.ascontiguousarray(bout.reshape(HC, 128).T),
        w1=w1,
        w1T=np.ascontiguousarray(w1.T),
        b1_pt=np.ascontiguousarray(b1.reshape(FF // 128, 128).T),
        w2=w2,
        w2T=np.ascontiguousarray(w2.T),
        b2_pt=np.ascontiguousarray(b2.reshape(HC, 128).T),
    )
    in_maps = []
    for c in range(NCORE_IDS):
        b, slc = c // 4, c % 4
        xs = x[b, TC * slc:TC * (slc + 1), :]
        m = dict(shared)
        m["xT"] = np.ascontiguousarray(xs.T)
        in_maps.append(m)

    nc = build_nc()
    res = run_bass_kernel_spmd(nc, in_maps, core_ids=list(range(8)))
    out = np.empty((B, S, H), dtype=np.float32)
    for c in range(8):
        b, slc = c // 4, c % 4
        out[b, TC * slc:TC * (slc + 1), :] = np.asarray(res.results[c]["y"])
    if KDBG:
        kernel.last_results = res.results
    return out




# version 12 (final)
# speedup vs baseline: 1.0296x; 1.0065x over previous
import os
import contextlib
import numpy as np
import ml_dtypes
import concourse.bass as bass
import concourse.mybir as mybir
from concourse.bass_utils import run_bass_kernel_spmd

F32 = mybir.dt.float32
BF16 = mybir.dt.bfloat16
I32 = mybir.dt.int32
AF = mybir.ActivationFunctionType
ALU = mybir.AluOpType
AX = mybir.AxisListType

B, S, H = 2, 8192, 1024
TC = 2048
E = 2048
SD = 128
FF = 4096
HC = H // 128
NT = TC // 128
NTB = TC // 512
MCN = S // 128
EPS_LN = 1e-5
NCORE_IDS = 8

KSTAGE = int(os.environ.get("KSTAGE", "99"))
KDBG = int(os.environ.get("KDBG", "0"))
KFFN = int(os.environ.get("KFFN", "9"))

ENGINES = ("tensor", "vector", "scalar", "gpsimd", "sync")

ARENA_COLS = 49152
_BUFS = {
    "x":       (0,     16384, 1, 1),
    "xn_hi":   (16384,  8192, 1, 2),
    "xn_lo":   (24576,  8192, 1, 2),
    "xsq":     (32768,  4096, 1, 1),
    "wblk":    (0,      8192, 2, 2),
    "stage":   (8192,   2048, 2, 2),
    "base":    (10240,  2048, 2, 2),
    "kown":    (12288,  2048, 2, 2),
    "wblk2":   (36864,  8192, 2, 2),
    "q_hi":    (33792,  1024, 2, 3),
    "q_lo":    (34816,  1024, 2, 3),
    "sq_hi":   (0,      8192, 3, 3),
    "sq_lo":   (8192,   8192, 3, 3),
    "kall_hi": (16384,  4096, 3, 3),
    "kall_lo": (20480,  4096, 3, 3),
    "vbuf":    (24576,  4096, 3, 3),
    "sqevac":  (28672,  1024, 3, 3),
    "avstage": (29696,  4096, 3, 3),
    "avl":     (0,      4096, 4, 4),
    "sul":     (40960,  4096, 4, 4),
    "gsc":     (4096,   2048, 4, 4),
    "gbf":     (6144,   2048, 4, 4),
    "gT":      (8192,   4096, 4, 4),
    "outT":    (12288,  4096, 4, 4),
    "xTrel":   (16384,  4096, 4, 4),
    "osq":     (20480,  4096, 4, 4),
    "woutb":   (24576, 16384, 4, 4),
    "xi":      (0,      8192, 5, 5),
    "z":       (8192,   2048, 5, 5),
    "qscr":    (10240,  8192, 5, 5),
    "wf":      (18432, 16384, 5, 5),
    "hstage":  (34816,  2048, 5, 5),
    "zsq":     (36864,  4096, 5, 5),
    "x2str":   (40960,  4096, 5, 5),
    "w1str":   (45056,  2048, 5, 5),
    "hihalf":  (0,     16384, 6, 7),
    "hrel":    (16384,  4096, 6, 7),
    "q2scr":   (20480,  8192, 6, 7),
    "w2s":     (28672,  3072, 6, 7),
    "outp":    (32768, 16384, 6, 7),
}
for _i, (_n1, (_o1, _c1, _a1, _b1)) in enumerate(list(_BUFS.items())):
    assert _o1 + _c1 <= ARENA_COLS, _n1
    for _n2, (_o2, _c2, _a2, _b2) in list(_BUFS.items())[_i + 1:]:
        if _a1 <= _b2 and _a2 <= _b1:
            assert _o1 + _c1 <= _o2 or _o2 + _c2 <= _o1, f"overlap {_n1}/{_n2}"


class KB:

    def __init__(self, nc):
        self.nc = nc
        self.ops = {e: [] for e in ENGINES}
        self.cnt = {"cT": 0, "cV": 0, "cS": 0, "cG": 0, "dW": 0, "dG": 0,
                    "dH": 0, "dK": 0, "dB": 0, "dV0": 0, "dV1": 0, "dA0": 0,
                    "dA1": 0, "dAR": 0, "dSU": 0, "dXR": 0, "dXS": 0, "dWO": 0,
                    "dF0": 0, "dF1": 0, "dWS": 0, "dY0": 0, "dY1": 0, "dR": 0,
                    "dKL": 0, "cc": 0}
        self.eng_sem = {"tensor": "cT", "vector": "cV", "scalar": "cS", "gpsimd": "cG"}
        self.waited = {}

    def emit(self, eng, fn, inc=None, add=1):
        tok = None
        if inc is not None:
            self.cnt[inc] += add
            tok = (inc, self.cnt[inc])
        self.ops[eng].append(("emit", fn, inc, add))
        return tok

    def step(self, eng, fn):
        return self.emit(eng, fn, inc=self.eng_sem[eng])

    def dma(self, eng, fn, q):
        return self.emit(eng, fn, inc=q, add=16)

    def wait(self, eng, tok):
        if tok is None:
            return
        sem, val = tok
        key = (eng, sem)
        if self.waited.get(key, -1) >= val:
            return
        self.waited[key] = val
        self.ops[eng].append(("wait", sem, val, None))

    def tokT(self):
        return ("cT", self.cnt["cT"])

    def tok(self, s):
        return (s, self.cnt[s])

    def quant(self, src_ap, out_ap, y_ap, ci_ap, cm1_ap, d_ap, recip_ap, relu,
              entry=None, s1v=False):
        if entry is None:
            self.wait("vector", self.tok("cV"))
            self.wait("scalar", self.tok("cV"))
        else:
            vt, st0 = entry
            self.wait("vector", vt)
            self.wait("scalar", vt)
            if st0 is not None:
                self.wait("vector", st0)
                self.wait("scalar", st0)
        rc = 1.0 if recip_ap is None else recip_ap
        if s1v:
            if relu:
                t = self.step("vector", lambda h: h.tensor_scalar(
                    y_ap, src_ap, rc, 0.0, ALU.mult, ALU.max))
            else:
                t = self.step("vector", lambda h: h.tensor_scalar(
                    y_ap, src_ap, rc, None, ALU.mult))
        elif relu:
            t = self.step("scalar", lambda h: h.activation(
                y_ap, src_ap, AF.Relu, scale=rc))
        else:
            t = self.step("scalar", lambda h: h.mul(y_ap, src_ap, rc))
        self.wait("scalar", t)
        t = self.step("scalar", lambda h: h.activation(ci_ap, y_ap, AF.Copy))
        self.wait("scalar", t)
        stok = self.step("scalar", lambda h: h.activation(cm1_ap, ci_ap, AF.Copy, bias=-1.0))
        self.wait("vector", stok)
        t = self.step("vector", lambda h: h.tensor_tensor(
            d_ap, y_ap, cm1_ap, op=ALU.subtract))
        self.wait("vector", t)
        t = self.step("vector", lambda h: h.scalar_tensor_tensor(
            y_ap, d_ap, 1.5, cm1_ap, op0=ALU.is_gt, op1=ALU.add))
        self.wait("vector", t)
        vtok = self.step("vector", lambda h: h.scalar_tensor_tensor(
            out_ap, d_ap, 0.5, y_ap, op0=ALU.is_ge, op1=ALU.add))
        return (vtok, stok)

    def split(self, src_ap, hi_ap, lo_ap, eng="vector"):
        t = self.step(eng, lambda h: h.tensor_copy(hi_ap, src_ap))
        self.wait(eng, t)
        return self.step(eng, lambda h: h.tensor_tensor(lo_ap, src_ap, hi_ap,
                                                        op=ALU.subtract))

    def barrier(self, engines=("tensor", "vector", "scalar", "gpsimd", "sync")):
        snap = dict(self.cnt)
        for e in engines:
            own = self.eng_sem.get(e)
            for sem, val in snap.items():
                if sem == own or val == 0:
                    continue
                self.wait(e, (sem, val))

    def realize(self, block, sems):
        def runner(lst):
            def run(h):
                for item in lst:
                    if item[0] == "wait":
                        h.wait_ge(sems[item[1]], item[2])
                    else:
                        inst = item[1](h)
                        if item[2] is not None:
                            inst.then_inc(sems[item[2]], item[3])
            return run

        block.tensor(runner(self.ops["tensor"]))
        block.vector(runner(self.ops["vector"]))
        block.scalar(runner(self.ops["scalar"]))
        block.gpsimd(runner(self.ops["gpsimd"]))
        block.sync(runner(self.ops["sync"]))


def build_nc():
    nc = bass.Bass("TRN2", target_bir_lowering=False, debug=False)

    xT = nc.dram_tensor("xT", [H, TC], F32, kind="ExternalInput").ap()
    wuv = nc.dram_tensor("wuv", [H, 2 * E + SD], F32, kind="ExternalInput").ap()
    buv_row = nc.dram_tensor("buv_row", [1, 2 * E + SD], F32, kind="ExternalInput").ap()
    buv_base = nc.dram_tensor("buv_base", [SD, 1], F32, kind="ExternalInput").ap()
    gamT = nc.dram_tensor("gamT", [SD, 2], F32, kind="ExternalInput").ap()
    betT = nc.dram_tensor("betT", [SD, 2], F32, kind="ExternalInput").ap()
    wout = nc.dram_tensor("wout", [E, H], F32, kind="ExternalInput").ap()
    bout_pt = nc.dram_tensor("bout_pt", [128, HC], F32, kind="ExternalInput").ap()
    w1 = nc.dram_tensor("w1", [FF, H], F32, kind="ExternalInput").ap()
    w1T = nc.dram_tensor("w1T", [H, FF], F32, kind="ExternalInput").ap()
    b1_pt = nc.dram_tensor("b1_pt", [128, FF // 128], F32, kind="ExternalInput").ap()
    w2 = nc.dram_tensor("w2", [H, FF], F32, kind="ExternalInput").ap()
    w2T = nc.dram_tensor("w2T", [FF, H], F32, kind="ExternalInput").ap()
    b2_pt = nc.dram_tensor("b2_pt", [128, HC], F32, kind="ExternalInput").ap()
    y = nc.dram_tensor("y", [TC, H], F32, kind="ExternalOutput").ap()

    def scratch(name, shape, dt):
        if KDBG:
            return nc.dram_tensor(name, shape, dt, kind="ExternalOutput").ap()
        return nc.dram_tensor(name, shape, dt).ap()

    AGW = 2 * E + 2 * SD
    ag_in = nc.dram_tensor("ag_in", [AGW, TC], BF16).ap()
    ag_out = nc.dram_tensor("ag_out", [4 * AGW, TC], BF16).ap()
    dbg_ag = nc.dram_tensor("dbg_ag", [AGW, TC], BF16, kind="ExternalOutput").ap() if KDBG else None
    su_sp = scratch("su_sp", [2 * TC, E], BF16)
    av_sp = scratch("av_sp", [TC, E], F32)
    h_sp = scratch("h_sp", [FF, TC], F32)
    x2_sp = scratch("x2_sp", [H, TC], F32)
    w2i_sp = nc.dram_tensor("w2i_sp", [H, FF], BF16).ap()
    ar1_in = nc.dram_tensor("ar1_in", [1, 8], F32).ap()
    ar1_out = nc.dram_tensor("ar1_out", [1, 8], F32).ap()
    ar2_in = nc.dram_tensor("ar2_in", [1, 8], F32).ap()
    ar2_out = nc.dram_tensor("ar2_out", [1, 8], F32).ap()

    pl = KB(nc)

    with contextlib.ExitStack() as es:
        En = es.enter_context
        arena = En(nc.sbuf_tensor("arena", [128, ARENA_COLS], F32))

        def V(name, off, n, dt=F32):
            o, ctot, _, _ = _BUFS[name]
            assert off + n <= ctot, f"{name}+{off}+{n}"
            ap = arena[:, o + off:o + off + n]
            return ap if dt == F32 else ap.bitcast(dt)

        ident = En(nc.sbuf_tensor("ident", [128, 128], F32))
        ident_bf = En(nc.sbuf_tensor("ident_bf", [128, 128], BF16))
        ones1 = En(nc.sbuf_tensor("ones1", [1, 128], F32))
        onesK = En(nc.sbuf_tensor("onesK", [128, 1], F32))
        ones1_bf = En(nc.sbuf_tensor("ones1_bf", [1, 128], BF16))
        s_buvrow = En(nc.sbuf_tensor("s_buvrow", [1, 512], F32))
        s_brh = En(nc.sbuf_tensor("s_brh", [1, 512], BF16))
        s_brl = En(nc.sbuf_tensor("s_brl", [1, 512], BF16))
        s_buvb = En(nc.sbuf_tensor("s_buvb", [128, 1], F32))
        s_gb = En(nc.sbuf_tensor("s_gb", [128, 8], F32))
        s_bout = En(nc.sbuf_tensor("s_bout", [128, HC], F32))
        s_b1 = En(nc.sbuf_tensor("s_b1", [128, FF // 128], F32))
        s_b2 = En(nc.sbuf_tensor("s_b2", [128, HC], F32))
        rowM = En(nc.sbuf_tensor("rowM", [1, 512], F32))
        rowV = En(nc.sbuf_tensor("rowV", [1, 512], F32))
        rowT = En(nc.sbuf_tensor("rowT", [1, 512], F32))
        rowR = En(nc.sbuf_tensor("rowR", [1, 512], F32))
        s_sc = En(nc.sbuf_tensor("s_sc", [1, 16], F32))
        s_scp = En(nc.sbuf_tensor("s_scp", [128, 8], F32))
        s_sw1 = En(nc.sbuf_tensor("s_sw1", [128, FF // 128], F32))
        s_sw2 = En(nc.sbuf_tensor("s_sw2", [128, HC], F32))
        s_bq1 = En(nc.sbuf_tensor("s_bq1", [128, FF // 128], F32))
        s_bq2 = En(nc.sbuf_tensor("s_bq2", [128, HC], F32))
        s_sv1 = En(nc.sbuf_tensor("s_sv1", [128, 32], F32))
        s_sv2 = En(nc.sbuf_tensor("s_sv2", [128, 8], F32))
        s_tmp = En(nc.sbuf_tensor("s_tmp", [128, 160], F32))
        s_eps = En(nc.sbuf_tensor("s_eps", [1, 1], F32))
        ps = [En(nc.psum_tensor(f"ps{i}", [128, 512], F32)) for i in range(8)]

        sems = {}
        for s in ("cT", "cV", "cS", "cG", "dW", "dG", "dH", "dK", "dB", "dV0",
                  "dV1", "dA0", "dA1", "dAR", "dSU", "dXR", "dXS", "dWO",
                  "dF0", "dF1", "dWS", "dY0", "dY1", "dR", "dKL", "cc"):
            sems[s] = En(nc.semaphore(s))

        block = En(nc.Block())

        def gchain(fn):
            pl.wait("gpsimd", pl.tok("cG"))
            return pl.step("gpsimd", fn)

        gchain(lambda h: h.memset(ident[:], 0.0))
        gchain(lambda h: h.affine_select(
            out=ident[:], in_=ident[:], compare_op=ALU.not_equal,
            fill=1.0, base=0, pattern=[[-1, 128]], channel_multiplier=1))
        gchain(lambda h: h.memset(ident_bf[:], 0.0))
        gchain(lambda h: h.affine_select(
            out=ident_bf[:], in_=ident_bf[:], compare_op=ALU.not_equal,
            fill=1.0, base=0, pattern=[[-1, 128]], channel_multiplier=1))
        gchain(lambda h: h.memset(ones1[:], 1.0))
        gchain(lambda h: h.memset(onesK[:], 1.0))
        gchain(lambda h: h.memset(ones1_bf[:], 1.0))
        gchain(lambda h: h.memset(s_eps[:], EPS_LN))

        x_toks = [pl.dma("sync", lambda h, c=c: h.dma_start(
            out=V("x", 2048 * c, 2048), in_=xT[128 * c:128 * (c + 1), :]), "dW")
            for c in range(HC)]
        for dst, src in [(s_buvb[:], buv_base[:, :]),
                         (s_bout[:], bout_pt[:, :]), (s_b1[:], b1_pt[:, :]),
                         (s_b2[:], b2_pt[:, :]), (s_gb[:, 0:2], gamT[:, :]),
                         (s_gb[:, 2:4], betT[:, :])]:
            t_sm = pl.dma("sync", lambda h, d=dst, s=src: h.dma_start(out=d, in_=s), "dW")
        pl.wait("vector", t_sm)
        pl.emit("vector", lambda h: h.tensor_scalar(
            s_gb[:, 4:5], s_gb[:, 0:1], 1.0 / 8192.0, None, ALU.mult))
        t_gbs = pl.step("vector", lambda h: h.tensor_scalar(
            s_gb[:, 5:6], s_gb[:, 2:3], 1.0 / 8192.0, None, ALU.mult))

        def emit_ln(nchunk, src_of, sq_name, out_cb, tb_range):
            inv_n = 1.0 / (128 * nchunk)
            prev_vt = None
            for tb in tb_range:
                sl = slice(512 * tb, 512 * (tb + 1))
                pl.wait("vector", prev_vt)
                tv = None
                for c in range(nchunk):
                    xv = src_of(c)
                    xsq = V(sq_name, 512 * c, 512)
                    tv = pl.step("vector", lambda h, xv=xv, sl=sl, xsq=xsq:
                                 h.tensor_tensor(xsq, xv[:, sl], xv[:, sl], op=ALU.mult))
                pl.wait("tensor", tv)
                tm = None
                for c in range(nchunk):
                    xv = src_of(c)
                    xsq = V(sq_name, 512 * c, 512)
                    pl.emit("tensor", lambda h, xv=xv, sl=sl, c=c:
                            h.matmul(ps[6][0:1, :], onesK[:], xv[:, sl],
                                     start=(c == 0), stop=(c == nchunk - 1)))
                    tm = pl.step("tensor", lambda h, xsq=xsq, c=c:
                                 h.matmul(ps[7][0:1, :], onesK[:], xsq,
                                          start=(c == 0), stop=(c == nchunk - 1)))
                pl.wait("vector", tm)
                t1 = pl.step("vector", lambda h: h.tensor_scalar(
                    rowM[:], ps[6][0:1, :], inv_n, None, ALU.mult))
                pl.step("vector", lambda h: h.tensor_scalar(
                    rowV[:], ps[7][0:1, :], inv_n, None, ALU.mult))
                pl.wait("vector", t1)
                t3 = pl.step("vector", lambda h: h.tensor_tensor(
                    rowT[:], rowM[:], rowM[:], op=ALU.mult))
                pl.wait("vector", t3)
                tvv = pl.step("vector", lambda h: h.tensor_tensor(
                    rowV[:], rowV[:], rowT[:], op=ALU.subtract))
                pl.wait("scalar", tvv)
                tss = pl.step("scalar", lambda h: h.activation(
                    rowT[:], rowV[:], AF.Sqrt, bias=s_eps[:]))
                pl.wait("vector", tss)
                trr = pl.step("vector", lambda h: h.reciprocal(rowR[:], rowT[:]))
                pl.wait("tensor", trr)
                pl.emit("tensor", lambda h: h.matmul(
                    ps[6][:, :], ones1[:], rowM[:], start=True, stop=True))
                tbb = pl.step("tensor", lambda h: h.matmul(
                    ps[7][:, :], ones1[:], rowR[:], start=True, stop=True))
                pl.wait("vector", tbb)
                vt = out_cb(tb, sl)
                prev_vt = vt
                pl.wait("tensor", vt)
                pl.wait("scalar", vt)

        pl.barrier()
        pl.wait("vector", t_sm)
        pl.wait("vector", t_gbs)

        def ln1_out(tb, sl):
            t = None
            for c in range(HC):
                xv, tmp = V("x", 2048 * c, 2048), V("xsq", 512 * c, 512)
                t = pl.step("vector", lambda h, xv=xv, tmp=tmp, sl=sl: h.tensor_tensor(
                    tmp, xv[:, sl], ps[6][:, :], op=ALU.subtract))
            pl.wait("vector", t)
            for c in range(HC):
                tmp = V("xsq", 512 * c, 512)
                t = pl.step("vector", lambda h, tmp=tmp: h.tensor_tensor(
                    tmp, tmp, ps[7][:, :], op=ALU.mult))
            pl.wait("vector", t)
            for c in range(HC):
                tmp, hi = V("xsq", 512 * c, 512), V("xn_hi", 1024 * c, 1024, BF16)
                t = pl.step("vector", lambda h, hi=hi, sl=sl, tmp=tmp:
                            h.tensor_copy(hi[:, sl], tmp))
            pl.wait("vector", t)
            vt = None
            for c in range(HC):
                tmp = V("xsq", 512 * c, 512)
                hi = V("xn_hi", 1024 * c, 1024, BF16)
                lo = V("xn_lo", 1024 * c, 1024, BF16)
                vt = pl.step("vector", lambda h, hi=hi, lo=lo, sl=sl, tmp=tmp:
                             h.tensor_tensor(lo[:, sl], tmp, hi[:, sl], op=ALU.subtract))
            return vt

        emit_ln(HC, lambda c: V("x", 2048 * c, 2048), "xsq", ln1_out, range(NTB))
        t_xn = pl.tok("cV")

        pl.barrier()

        def wv_f32(c, wp=0):
            return V("wblk" if wp == 0 else "wblk2", 512 * c, 512)

        def wv_hi(c, wp=0):
            return V("wblk" if wp == 0 else "wblk2", 4096 + 256 * c, 256, BF16)

        def wv_lo(c, wp=0):
            return V("wblk" if wp == 0 else "wblk2", 6144 + 256 * c, 256, BF16)

        def load_wblk(col0, ncols, guards, wp=0, wq="dW"):
            toks = []
            for g in (guards or []):
                pl.wait("sync", g)
            for c in range(HC):
                toks.append(pl.dma("sync", lambda h, c=c, col0=col0, ncols=ncols, wp=wp:
                            h.dma_start(out=wv_f32(c, wp)[:, 0:ncols],
                                        in_=wuv[128 * c:128 * (c + 1), col0:col0 + ncols]),
                            wq))
            pl.wait("vector", toks[-1])
            for g in (guards or []):
                pl.wait("vector", g)
            t = None
            for c in range(HC):
                t = pl.step("vector", lambda h, c=c, n=ncols, wp=wp: h.tensor_copy(
                    wv_hi(c, wp)[:, 0:n], wv_f32(c, wp)[:, 0:n]))
            pl.wait("vector", t)
            last = None
            for c in range(HC):
                last = pl.step("vector", lambda h, c=c, n=ncols, wp=wp: h.tensor_tensor(
                    wv_lo(c, wp)[:, 0:n], wv_f32(c, wp)[:, 0:n], wv_hi(c, wp)[:, 0:n],
                    op=ALU.subtract))
            return last

        t_w = load_wblk(2 * E, SD, None)
        pl.wait("tensor", t_w)
        pl.wait("tensor", t_xn)
        t_base_mm = None
        for tb in range(NTB):
            sl = slice(512 * tb, 512 * (tb + 1))
            for c in range(HC):
                whi, wlo = wv_hi(c)[:, 0:128], wv_lo(c)[:, 0:128]
                xhi = V("xn_hi", 1024 * c, 1024, BF16)
                xlo = V("xn_lo", 1024 * c, 1024, BF16)
                pl.emit("tensor", lambda h, whi=whi, xhi=xhi, sl=sl, st=(c == 0):
                        h.matmul(ps[0][:, :], whi, xhi[:, sl], start=st, stop=False))
                pl.emit("tensor", lambda h, whi=whi, xlo=xlo, sl=sl:
                        h.matmul(ps[0][:, :], whi, xlo[:, sl], start=False, stop=False))
                t_base_mm = pl.step("tensor", lambda h, wlo=wlo, xhi=xhi, sl=sl, c=c:
                                    h.matmul(ps[0][:, :], wlo, xhi[:, sl],
                                             start=False, stop=(c == HC - 1)))
            pl.wait("scalar", t_base_mm)
            t_base_ev = pl.step("scalar", lambda h, sl=sl: h.activation(
                V("base", 0, 2048)[:, sl], ps[0][:, :], AF.Silu, bias=s_buvb[:]))
            pl.wait("tensor", t_base_ev)
        pl.wait("vector", t_base_ev)
        bb = V("base", 0, 2048)
        tq1 = V("stage", 0, 2048)
        tq2 = V("wblk", 0, 2048)
        qh, ql = V("q_hi", 0, 1024, BF16), V("q_lo", 0, 1024, BF16)
        kh, kl = V("kown", 0, 1024, BF16), V("kown", 1024, 1024, BF16)
        pl.emit("vector", lambda h: h.tensor_scalar(
            tq1, bb, s_gb[:, 4:5], s_gb[:, 5:6], ALU.mult, ALU.add))
        t = pl.step("vector", lambda h: h.tensor_scalar(
            tq2, bb, s_gb[:, 1:2], s_gb[:, 3:4], ALU.mult, ALU.add))
        pl.wait("vector", t)
        pl.emit("vector", lambda h: h.tensor_copy(qh, tq1))
        t = pl.step("vector", lambda h: h.tensor_copy(kh, tq2))
        pl.wait("vector", t)
        pl.emit("vector", lambda h: h.tensor_tensor(ql, tq1, qh, op=ALU.subtract))
        t_k = pl.step("vector", lambda h: h.tensor_tensor(kl, tq2, kh, op=ALU.subtract))
        pl.wait("gpsimd", t_k)
        pl.dma("gpsimd", lambda h: h.dma_start(out=ag_in[2 * E:2 * E + SD, :], in_=kh), "dK")
        pl.dma("gpsimd", lambda h: h.dma_start(out=ag_in[2 * E + SD:, :], in_=kl), "dK")

        def suv_f32(par):
            return V("stage", 1024 * par, 512)

        def suv_hi(par):
            return V("stage", 1024 * par + 512, 256, BF16)

        def suv_lo(par):
            return V("stage", 1024 * par + 768, 256, BF16)

        t_wprev = pl.tok("cV")
        stage_free = [t_k, t_k]
        t_ev_hist = [None, None]
        blk_order = list(range(4, 8)) + list(range(4))
        t_w_pref = {}
        consumed_tok = {}
        for bi, blk in enumerate(blk_order):
            is_v = blk >= 4
            col0 = 512 * blk
            wp = bi % 2
            if bi == 0:
                t_w_pref[0] = load_wblk(col0, 512, [pl.tokT(), t_k], wp=0, wq="dW")
            t_w = t_w_pref[bi]
            if bi + 1 < len(blk_order):
                g = [consumed_tok[bi - 1]] if bi >= 1 else []
                t_w_pref[bi + 1] = load_wblk(512 * blk_order[bi + 1], 512, g,
                                             wp=(bi + 1) % 2,
                                             wq="dW" if (bi + 1) % 2 == 0 else "dKL")
            if bi >= 1:
                pl.wait("sync", consumed_tok[bi - 1])
            t_brow = pl.dma("sync", lambda h, col0=col0: h.dma_start(
                out=s_buvrow[:, :], in_=buv_row[:, col0:col0 + 512]), "dB")
            pl.wait("vector", t_brow)
            pl.wait("vector", pl.tokT())
            t_brow = pl.split(s_buvrow[:, :], s_brh[:, :], s_brl[:, :])
            pl.wait("tensor", t_w)
            pl.wait("tensor", t_brow)
            for tt in range(NT):
                par = tt % 2
                bank = tt % 2
                if t_ev_hist[bank] is not None:
                    pl.wait("tensor", t_ev_hist[bank])
                for c in range(HC):
                    whi512 = wv_hi(c, wp)
                    wlo512 = wv_lo(c, wp)
                    xhi = V("xn_hi", 1024 * c, 1024, BF16)[:, 128 * tt:128 * (tt + 1)]
                    xlo = V("xn_lo", 1024 * c, 1024, BF16)[:, 128 * tt:128 * (tt + 1)]
                    pl.emit("tensor", lambda h, a=xhi, b=whi512, bank=bank, st=(c == 0):
                            h.matmul(ps[bank][:, :], a, b, start=st, stop=False))
                    pl.emit("tensor", lambda h, a=xhi, b=wlo512, bank=bank:
                            h.matmul(ps[bank][:, :], a, b, start=False, stop=False))
                    pl.emit("tensor", lambda h, a=xlo, b=whi512, bank=bank, sp=(os.environ.get("KNB2") and c == HC - 1):
                            h.matmul(ps[bank][:, :], a, b, start=False, stop=bool(sp)))
                if os.environ.get("KNB2"):
                    t_mm = pl.step("tensor", lambda h: h.nop())
                elif os.environ.get("KNOBIAS"):
                    t_mm = pl.step("tensor", lambda h, bank=bank: h.matmul(
                        ps[bank][:, :], ones1_bf[:], s_brh[:, :],
                        start=False, stop=True))
                else:
                    pl.emit("tensor", lambda h, bank=bank: h.matmul(
                        ps[bank][:, :], ones1_bf[:], s_brh[:, :],
                        start=False, stop=False))
                    t_mm = pl.step("tensor", lambda h, bank=bank: h.matmul(
                        ps[bank][:, :], ones1_bf[:], s_brl[:, :],
                        start=False, stop=True))
                pl.wait("scalar", t_mm)
                if stage_free[par] is not None:
                    pl.wait("scalar", stage_free[par])
                t_ev = pl.step("scalar", lambda h, bank=bank, par=par: h.activation(
                    suv_f32(par), ps[bank][:, :], AF.Silu))
                t_ev_hist[bank] = t_ev
                pl.wait("vector", t_ev)
                t_sp = pl.split(suv_f32(par), suv_hi(par), suv_lo(par))
                pl.wait("gpsimd", t_sp)
                r0 = 128 * tt
                dq = "dG" if par == 0 else "dH"
                if is_v:
                    cs = col0 - E
                    pl.dma("gpsimd", lambda h, r0=r0, cs=cs, par=par: h.dma_start(
                        out=ag_in[r0:r0 + 128, cs:cs + 512], in_=suv_hi(par)), dq)
                    stage_free[par] = pl.dma("gpsimd",
                        lambda h, r0=r0, cs=cs, par=par: h.dma_start(
                            out=ag_in[TC + r0:TC + r0 + 128, cs:cs + 512],
                            in_=suv_lo(par)), dq)
                else:
                    pl.dma("gpsimd", lambda h, r0=r0, cs=col0, par=par: h.dma_start(
                        out=su_sp[r0:r0 + 128, cs:cs + 512], in_=suv_hi(par)), dq)
                    stage_free[par] = pl.dma("gpsimd",
                        lambda h, r0=r0, cs=col0, par=par: h.dma_start(
                            out=su_sp[TC + r0:TC + r0 + 128, cs:cs + 512],
                            in_=suv_lo(par)), dq)
            consumed_tok[bi] = pl.tokT()
            if blk == 7:
                pl.wait("gpsimd", pl.tok("dG"))
                pl.wait("gpsimd", pl.tok("dH"))
                pl.wait("gpsimd", pl.tok("dK"))
                pl.emit("gpsimd", lambda h: h.collective_compute(
                    "AllGather", ALU.bypass,
                    replica_groups=[[0, 1, 2, 3], [4, 5, 6, 7]],
                    ins=[ag_in.opt()],
                    outs=[ag_out.opt()]), inc="cc")




        if KSTAGE >= 2:
            qh_v = V("q_hi", 0, 1024, BF16)
            ql_v = V("q_lo", 0, 1024, BF16)
            kh_v = V("kall_hi", 0, 4096, BF16)
            kl_v = V("kall_lo", 0, 4096, BF16)
            sqh_v = V("sq_hi", 0, 8192, BF16)
            sql_v = V("sq_lo", 0, 8192, BF16)

            def vb_hi(p):
                return V("vbuf", 2048 * p, 1024, BF16)

            def vb_lo(p):
                return V("vbuf", 2048 * p + 1024, 1024, BF16)

            def scr2(p):
                return V("sqevac", 512 * p, 512)

            def sq22(p):
                return V("avstage", 512 * p, 512)

            def avst(ns):
                return V("avstage", 2048 * ns, 2048)

            pl.wait("sync", pl.tok("cT"))
            pl.wait("sync", pl.tok("cc"))
            for e in ("vector", "scalar", "tensor"):
                pl.wait(e, pl.tok("dG"))
                pl.wait(e, pl.tok("dH"))
            tkl = None
            for r in range(4):
                b0 = r * AGW
                pl.dma("sync", lambda h, b0=b0, r=r: h.dma_start(
                    out=kh_v[:, 2048 * r:2048 * (r + 1)],
                    in_=ag_out[b0 + 2 * E:b0 + 2 * E + SD, :]), "dKL")
                tkl = pl.dma("sync", lambda h, b0=b0, r=r: h.dma_start(
                    out=kl_v[:, 2048 * r:2048 * (r + 1)],
                    in_=ag_out[b0 + 2 * E + SD:b0 + AGW, :]), "dKL")
            pl.wait("tensor", tkl)

            av_guard = [None, None]
            prev_evac = [pl.tok("cS"), pl.tok("cS")]
            vcons = {}

            for nb in range(8):
                nsl = slice(256 * nb, 256 * (nb + 1))
                pl.wait("tensor", prev_evac[0])
                pl.wait("tensor", prev_evac[1])
                pl.wait("vector", av_guard[0])
                pl.wait("scalar", av_guard[0])
                ev_tok = {}
                sq_tok = {}
                c_tok = {}
                sub_tok = {}

                def sq4(mp):
                    return V("avstage", 512 * (mp % 4), 512)

                def emit_cast_subs(g):
                    pl.wait("scalar", sq_tok[2 * g + 1])
                    s0 = 512 * ((2 * g) % 4)
                    c_tok[g] = pl.step("scalar", lambda h, g=g, s0=s0: h.copy(
                        sqh_v[:, 1024 * g:1024 * (g + 1)],
                        V("avstage", s0, 1024)))
                    for mp2 in (2 * g, 2 * g + 1):
                        pl.wait("vector", c_tok[g])
                        sub_tok[mp2] = pl.step(
                            "vector", lambda h, mp2=mp2: h.tensor_tensor(
                                sql_v[:, 512 * mp2:512 * (mp2 + 1)], sq4(mp2),
                                sqh_v[:, 512 * mp2:512 * (mp2 + 1)],
                                op=ALU.subtract))

                for g in range(MCN // 4):
                    for half in range(2):
                        mp = 2 * g + half
                        bank = mp % 2
                        if mp >= 2:
                            pl.wait("tensor", ev_tok[mp - 2])
                        t_mm = None
                        for sub in range(2):
                            mc = 2 * mp + sub
                            ksl = slice(128 * mc, 128 * (mc + 1))
                            osl = slice(256 * sub, 256 * (sub + 1))
                            pl.emit("tensor", lambda h, ksl=ksl, nsl=nsl, bank=bank, osl=osl:
                                    h.matmul(ps[bank][:, osl], kh_v[:, ksl], qh_v[:, nsl],
                                             start=True, stop=False))
                            pl.emit("tensor", lambda h, ksl=ksl, nsl=nsl, bank=bank, osl=osl:
                                    h.matmul(ps[bank][:, osl], kh_v[:, ksl], ql_v[:, nsl],
                                             start=False, stop=False))
                            t_mm = pl.step("tensor", lambda h, ksl=ksl, nsl=nsl, bank=bank, osl=osl:
                                           h.matmul(ps[bank][:, osl], kl_v[:, ksl],
                                                    qh_v[:, nsl], start=False, stop=True))
                        pl.wait("scalar", t_mm)
                        if mp >= 2:
                            pl.wait("scalar", sq_tok[mp - 2])
                        ev_tok[mp] = pl.step("scalar", lambda h, bank=bank:
                                             h.activation(scr2(bank), ps[bank][:, 0:512],
                                                          AF.Relu))
                        pl.wait("vector", ev_tok[mp])
                        if mp >= 4:
                            pl.wait("vector", c_tok[(mp - 4) // 2])
                            pl.wait("vector", sub_tok[mp - 4])
                        sq_tok[mp] = pl.step("vector", lambda h, mp=mp, p=bank:
                                             h.tensor_tensor(sq4(mp), scr2(p), scr2(p),
                                                             op=ALU.mult))
                    if g >= 1:
                        emit_cast_subs(g - 1)
                emit_cast_subs(MCN // 4 - 1)
                t_sq_all = sub_tok[MCN // 2 - 1]

                pl.wait("tensor", t_sq_all)
                pl.wait("tensor", ev_tok[MCN // 2 - 1])
                t_av = None
                for mc in range(MCN):
                    par = mc % 2
                    r, lt = mc // 16, mc % 16
                    b0 = r * AGW
                    if mc >= 2:
                        pl.wait("sync", vcons[(nb, mc - 2)])
                    else:
                        pl.wait("sync", vcons.get((nb - 1, MCN - 2 + mc), None))
                    pl.dma("sync", lambda h, b0=b0, lt=lt, par=par: h.dma_start(
                        out=vb_hi(par), in_=ag_out[b0 + 128 * lt:b0 + 128 * (lt + 1), :]),
                        "dV0" if par == 0 else "dV1")
                    pl.dma("sync", lambda h, b0=b0, lt=lt, par=par: h.dma_start(
                        out=vb_lo(par),
                        in_=ag_out[b0 + TC + 128 * lt:b0 + TC + 128 * (lt + 1), :]),
                        "dV0" if par == 0 else "dV1")
                    pl.wait("tensor", pl.tok("dV0" if par == 0 else "dV1"))
                    for ns in range(2):
                        s_hi = sqh_v[:, 256 * mc + 128 * ns:256 * mc + 128 * (ns + 1)]
                        s_lo = sql_v[:, 256 * mc + 128 * ns:256 * mc + 128 * (ns + 1)]
                        for eq in range(4):
                            bank = ns * 4 + eq
                            esl = slice(512 * eq, 512 * (eq + 1))
                            st = (mc == 0)
                            sp = (mc == MCN - 1)
                            pl.emit("tensor", lambda h, a=s_hi, p=par, esl=esl, bank=bank, st=st:
                                    h.matmul(ps[bank][:, :], a, vb_hi(p)[:, esl],
                                             start=st, stop=False))
                            pl.emit("tensor", lambda h, a=s_hi, p=par, esl=esl, bank=bank:
                                    h.matmul(ps[bank][:, :], a, vb_lo(p)[:, esl],
                                             start=False, stop=False))
                            t_av = pl.step("tensor",
                                           lambda h, a=s_lo, p=par, esl=esl, bank=bank, sp=sp:
                                           h.matmul(ps[bank][:, :], a, vb_hi(p)[:, esl],
                                                    start=False, stop=sp))
                    vcons[(nb, mc)] = t_av
                for ns in range(2):
                    pl.wait("scalar", t_av)
                    pl.wait("scalar", av_guard[ns])
                    tev = None
                    for eq in range(4):
                        bank = ns * 4 + eq
                        tev = pl.step("scalar", lambda h, bank=bank, ns=ns, eq=eq:
                                      h.copy(avst(ns)[:, 512 * eq:512 * (eq + 1)],
                                             ps[bank][:, :]))
                    prev_evac[ns] = tev
                    pl.wait("gpsimd", tev)
                    r0 = 256 * nb + 128 * ns
                    av_guard[ns] = pl.dma("gpsimd", lambda h, r0=r0, ns=ns: h.dma_start(
                        out=av_sp[r0:r0 + 128, :], in_=avst(ns)),
                        "dA0" if ns == 0 else "dA1")

        if KSTAGE >= 2:
            pl.barrier()
            def wo_hi(ec):
                return V("woutb", 512 * ec, 512, BF16)

            def wo_lo(ec):
                return V("woutb", 8192 + 512 * ec, 512, BF16)

            t_wo = None
            stg = V("osq", 0, 1024)
            for ec in range(16):
                if t_wo is not None:
                    pl.wait("sync", t_wo)
                tld = pl.dma("sync", lambda h, ec=ec, stg=stg: h.dma_start(
                    out=stg, in_=wout[128 * ec:128 * (ec + 1), :]), "dWO")
                pl.wait("vector", tld)
                t_wo = pl.split(stg, wo_hi(ec), wo_lo(ec))

            def avl_p(p):
                return V("avl", 2048 * p, 2048)

            def suh_p(p):
                return V("sul", 2048 * p, 1024, BF16)

            def sul_p(p):
                return V("sul", 2048 * p + 1024, 1024, BF16)
            gsc_v = V("gsc", 0, 2048)
            gbh_v = V("gbf", 0, 1024, BF16)
            gbl_v = V("gbf", 1024, 1024, BF16)

            def gT_hi(ec):
                return V("gT", 128 * ec, 128, BF16)

            def gT_lo(ec):
                return V("gT", 2048 + 128 * ec, 128, BF16)

            t_gcons = None
            gcons_hist = [None, None]
            t_x2st = None
            t_ln2v = None
            for tb in range(NTB):
                tbsl = slice(512 * tb, 512 * (tb + 1))
                pl.wait("sync", t_x2st)
                pl.wait("sync", t_ln2v)
                txr = None
                for c in range(HC):
                    txr = pl.dma("sync", lambda h, c=c, tbsl=tbsl: h.dma_start(
                        out=V("xTrel", 512 * c, 512),
                        in_=xT[128 * c:128 * (c + 1), tbsl]), "dXR")
                t_oute = None
                for htb in range(2):
                    for tl in range(2):
                        tt = 4 * tb + 2 * htb + tl
                        lp = tt % 2
                        pl.wait("sync", gcons_hist[lp])
                        if tb == 0 and htb == 0 and tl == 0:
                            pl.wait("sync", pl.tok("dA0"))
                            pl.wait("sync", pl.tok("dA1"))
                        tar = pl.dma("sync", lambda h, tt=tt, lp=lp: h.dma_start(
                            out=avl_p(lp), in_=av_sp[128 * tt:128 * (tt + 1), :]),
                            "dAR" if lp == 0 else "dKL")
                        tsr = pl.dma("sync", lambda h, tt=tt, lp=lp: h.dma_start(
                            out=suh_p(lp), in_=su_sp[128 * tt:128 * (tt + 1), :]),
                            "dSU" if lp == 0 else "dB")
                        tsr = pl.dma("sync", lambda h, tt=tt, lp=lp: h.dma_start(
                            out=sul_p(lp), in_=su_sp[TC + 128 * tt:TC + 128 * (tt + 1), :]),
                            "dSU" if lp == 0 else "dB")
                        pl.wait("vector", tar)
                        pl.wait("vector", tsr)
                        tg = pl.step("vector", lambda h, lp=lp: h.tensor_tensor(
                            gsc_v, suh_p(lp), sul_p(lp), op=ALU.add))
                        pl.wait("vector", tg)
                        t_gcons = pl.step("vector", lambda h, lp=lp: h.tensor_tensor(
                            gsc_v, gsc_v, avl_p(lp), op=ALU.mult))
                        gcons_hist[lp] = t_gcons
                        pl.wait("vector", t_gcons)
                        tsp = pl.split(gsc_v, gbh_v, gbl_v)
                        pl.wait("tensor", tsp)
                        pl.wait("tensor", t_ln2v)
                        tq = None
                        tq_hist = [None, None]
                        for i2, (ec, half) in enumerate(
                                (e, hh) for e in range(16) for hh in (0, 1)):
                            gin = gbh_v if half == 0 else gbl_v
                            gout = gT_hi if half == 0 else gT_lo
                            bank = 6 + (i2 % 2)
                            if tq_hist[i2 % 2] is not None:
                                pl.wait("tensor", tq_hist[i2 % 2])
                            tt_mm = pl.step("tensor", lambda h, gin=gin, ec=ec, bank=bank:
                                            h.transpose(ps[bank][:, 0:64].bitcast(BF16),
                                                        gin[:, 128 * ec:128 * (ec + 1)],
                                                        ident_bf[:]))
                            pl.wait("vector", tt_mm)
                            tq = pl.step("vector", lambda h, gout=gout, ec=ec, tl=tl, bank=bank:
                                         h.tensor_copy(
                                             gout(ec)[:, 128 * tl:128 * (tl + 1)],
                                             ps[bank][:, 0:64].bitcast(BF16)))
                            tq_hist[i2 % 2] = tq
                    pl.wait("tensor", tq)
                    pl.wait("tensor", t_oute)
                    if tb == 0 and htb == 0:
                        pl.wait("tensor", t_wo)
                        pl.wait("tensor", prev_evac[0])
                        pl.wait("tensor", prev_evac[1])
                    t_oute_prev = t_oute
                    t_oute = None
                    for hp in range(2):
                        pl.wait("tensor", t_oute if t_oute else t_oute_prev)
                        t_op = None
                        for ec in range(16):
                            for hq2 in range(4):
                                ht = 4 * hp + hq2
                                st = (ec == 0)
                                sp = (ec == 15)
                                hsl = slice(128 * ht, 128 * (ht + 1))
                                pl.emit("tensor", lambda h, ec=ec, hsl=hsl, hq2=hq2, st=st:
                                        h.matmul(ps[hq2][:, 0:256], wo_hi(ec)[:, hsl],
                                                 gT_hi(ec)[:, :], start=st, stop=False))
                                pl.emit("tensor", lambda h, ec=ec, hsl=hsl, hq2=hq2:
                                        h.matmul(ps[hq2][:, 0:256], wo_lo(ec)[:, hsl],
                                                 gT_hi(ec)[:, :], start=False, stop=False))
                                t_op = pl.step("tensor", lambda h, ec=ec, hsl=hsl, hq2=hq2, sp=sp:
                                               h.matmul(ps[hq2][:, 0:256], wo_hi(ec)[:, hsl],
                                                        gT_lo(ec)[:, :], start=False, stop=sp))
                        pl.wait("vector", t_op)
                        for hq2 in range(4):
                            ht = 4 * hp + hq2
                            t_oute = pl.step("vector", lambda h, ht=ht, hq2=hq2, htb=htb:
                                             h.tensor_scalar(
                                                 V("outT", 512 * ht, 512)[:, 256 * htb:256 * (htb + 1)],
                                                 ps[hq2][:, 0:256], s_bout[:, ht:ht + 1], None,
                                                 ALU.add))
                pl.wait("tensor", t_oute)
                pl.wait("vector", t_oute)
                pl.wait("vector", txr)

                def ln2_out(tb2, sl2):
                    t = None
                    for c in range(HC):
                        ov, tmp = V("outT", 512 * c, 512), V("osq", 512 * c, 512)
                        t = pl.step("vector", lambda h, ov=ov, tmp=tmp: h.tensor_tensor(
                            tmp, ov, ps[6][:, :], op=ALU.subtract))
                    pl.wait("vector", t)
                    for c in range(HC):
                        tmp = V("osq", 512 * c, 512)
                        t = pl.step("vector", lambda h, tmp=tmp: h.tensor_tensor(
                            tmp, tmp, ps[7][:, :], op=ALU.mult))
                    pl.wait("vector", t)
                    vt = None
                    for c in range(HC):
                        tmp, xr = V("osq", 512 * c, 512), V("xTrel", 512 * c, 512)
                        vt = pl.step("vector", lambda h, tmp=tmp, xr=xr: h.tensor_tensor(
                            xr, tmp, xr, op=ALU.add))
                    return vt

                emit_ln(HC, lambda c: V("outT", 512 * c, 512), "osq", ln2_out, [0])
                t_ln2v = pl.tok("cV")
                pl.wait("gpsimd", t_ln2v)
                for c in range(HC):
                    t_x2st = pl.dma("gpsimd", lambda h, c=c, tbsl=tbsl: h.dma_start(
                        out=x2_sp[128 * c:128 * (c + 1), tbsl],
                        in_=V("xTrel", 512 * c, 512)), "dXS")

        class _StopEmit(Exception):
            pass

        _ffn_stopped = False
        if KSTAGE >= 3:
          try:
            pl.barrier()

            def xi_v(dc):
                return V("xi", 1024 * dc, 1024, BF16)

            def wiT_v(dc):
                return V("wf", 2048 * dc, 2048, BF16)

            def x2s_v(c):
                return V("x2str", 512 * c, 512)

            def zsq_v(c):
                return V("zsq", 512 * c, 512)

            def m_row(tb):
                return V("z", 512 * tb, 512)[0:1, :]

            def r_row(tb):
                return V("hstage", 512 * tb, 512)[0:1, :]

            def q_y(p):
                return V("qscr", 2048 * p, 512)

            def q_ci(p):
                return V("qscr", 2048 * p + 512, 512).bitcast(I32)

            def q_cm1(p):
                return V("qscr", 2048 * p + 1024, 512)

            def q_d(p):
                return V("qscr", 2048 * p + 1536, 512)

            def w1s_v(p):
                return V("w1str", 1024 * p, 1024)

            def hst_v(p):
                return V("hstage", 512 * p, 512)

            t_x2cons = None
            rmax_cols = []
            for tb in range(NTB):
                tbsl = slice(512 * tb, 512 * (tb + 1))
                pl.wait("sync", t_x2cons)
                pl.wait("sync", pl.tok("dXS"))
                tld = None
                for c in range(HC):
                    tld = pl.dma("sync", lambda h, c=c, tbsl=tbsl: h.dma_start(
                        out=x2s_v(c), in_=x2_sp[128 * c:128 * (c + 1), tbsl]), "dXR")
                pl.wait("vector", tld)
                t = None
                for c in range(HC):
                    t = pl.step("vector", lambda h, c=c: h.tensor_tensor(
                        zsq_v(c), x2s_v(c), x2s_v(c), op=ALU.mult))
                pl.wait("tensor", t)
                tm = None
                for c in range(HC):
                    pl.emit("tensor", lambda h, c=c: h.matmul(
                        ps[6][0:1, :], onesK[:], x2s_v(c),
                        start=(c == 0), stop=(c == HC - 1)))
                    tm = pl.step("tensor", lambda h, c=c: h.matmul(
                        ps[7][0:1, :], onesK[:], zsq_v(c),
                        start=(c == 0), stop=(c == HC - 1)))
                pl.wait("vector", tm)
                t1 = pl.step("vector", lambda h: h.tensor_scalar(
                    rowM[:], ps[6][0:1, :], 1.0 / H, None, ALU.mult))
                pl.step("vector", lambda h: h.tensor_scalar(
                    rowV[:], ps[7][0:1, :], 1.0 / H, None, ALU.mult))
                pl.wait("vector", t1)
                t3 = pl.step("vector", lambda h: h.tensor_tensor(
                    rowT[:], rowM[:], rowM[:], op=ALU.mult))
                pl.wait("vector", t3)
                tvv = pl.step("vector", lambda h: h.tensor_tensor(
                    rowV[:], rowV[:], rowT[:], op=ALU.subtract))
                pl.wait("scalar", tvv)
                tss = pl.step("scalar", lambda h: h.activation(
                    rowT[:], rowV[:], AF.Sqrt, bias=s_eps[:]))
                pl.wait("vector", tss)
                trr = pl.step("vector", lambda h: h.reciprocal(rowR[:], rowT[:]))
                pl.wait("vector", trr)
                pl.emit("vector", lambda h, tb=tb: h.tensor_copy(m_row(tb), rowM[:]))
                trc = pl.step("vector", lambda h, tb=tb: h.tensor_copy(r_row(tb), rowR[:]))
                pl.wait("tensor", trc)
                pl.emit("tensor", lambda h: h.matmul(
                    ps[6][:, :], ones1[:], rowM[:], start=True, stop=True))
                tbb = pl.step("tensor", lambda h: h.matmul(
                    ps[7][:, :], ones1[:], rowR[:], start=True, stop=True))
                pl.wait("vector", tbb)
                t = None
                for c in range(HC):
                    t = pl.step("vector", lambda h, c=c: h.tensor_tensor(
                        zsq_v(c), x2s_v(c), ps[6][:, :], op=ALU.subtract))
                pl.wait("vector", t)
                for c in range(HC):
                    t = pl.step("vector", lambda h, c=c: h.tensor_tensor(
                        zsq_v(c), zsq_v(c), ps[7][:, :], op=ALU.mult))
                pl.wait("vector", t)
                for c in range(HC):
                    col = 32 + 8 * tb + c
                    rmax_cols.append(col)
                    t = pl.step("vector", lambda h, c=c, col=col: h.tensor_reduce(
                        s_tmp[:, col:col + 1], zsq_v(c), axis=AX.X, op=ALU.max))
                t_x2cons = t
                pl.wait("tensor", t)

            pl.wait("vector", t_x2cons)
            tcm = pl.step("vector", lambda h: h.reduce_max(
                s_tmp[:, 64:65], s_tmp[:, 32:64], axis=AX.X))
            pl.wait("tensor", tcm)
            ttr = pl.step("tensor", lambda h: h.transpose(
                ps[5][0:1, 0:128], s_tmp[:, 64:65], ident[:]))
            pl.wait("vector", ttr)
            tgm = pl.step("gpsimd", lambda h: h.memset(s_sc[:, :], -1e30))
            pl.wait("vector", tgm)
            tgm = pl.step("vector", lambda h: h.reduce_max(
                s_sc[0:1, 0:1], ps[5][0:1, 0:128], axis=AX.X))
            pl.wait("gpsimd", tgm)
            tdm = pl.dma("gpsimd", lambda h: h.dma_start(
                out=ar1_in[:, :], in_=s_sc[0:1, 0:8]), "dR")
            pl.wait("gpsimd", tdm)
            pl.emit("gpsimd", lambda h: h.collective_compute(
                "AllReduce", ALU.max, replica_groups=[list(range(8))],
                ins=[ar1_in.opt()], outs=[ar1_out.opt()]), inc="cc")
            t_w1cons = [None, None]
            q_entry = [None, None]
            wi_slot_guard = [None, None]
            wiT_bank_hist = [None, None]
            t_wiT = None

            def QW(p):
                return V("qscr", 4096 * p, 1024)

            def QWCI(p):
                return V("qscr", 4096 * p + 1024, 1024).bitcast(I32)

            def QWCM(p):
                return V("qscr", 4096 * p + 2048, 1024)

            def QWD(p):
                return V("qscr", 4096 * p + 3072, 1024)

            for fc in range(32):
                par = fc % 2
                pl.wait("sync", t_w1cons[par])
                tw = pl.dma("sync", lambda h, fc=fc, par=par: h.dma_start(
                    out=w1s_v(par), in_=w1[128 * fc:128 * (fc + 1), :]),
                    "dF0" if par == 0 else "dF1")
                pl.wait("vector", tw)
                trm = pl.step("vector", lambda h, fc=fc, par=par: h.tensor_reduce(
                    s_sw1[:, fc:fc + 1], w1s_v(par), axis=AX.X, op=ALU.max,
                    apply_absolute_value=True))
                pl.wait("vector", trm)
                trm = pl.step("vector", lambda h, fc=fc, par=par: h.reciprocal(
                    s_tmp[:, 66 + 2 * par:67 + 2 * par], s_sw1[:, fc:fc + 1]))
                pl.wait("vector", trm)
                trm = pl.step("vector", lambda h, par=par: h.tensor_scalar(
                    s_tmp[:, 67 + 2 * par:68 + 2 * par],
                    s_tmp[:, 66 + 2 * par:67 + 2 * par], 127.0, None, ALU.mult))
                pl.wait("vector", trm)
                pl.wait("vector", wi_slot_guard[par])
                pl.wait("scalar", trm)
                q_entry[par] = pl.quant(
                    w1s_v(par), V("zsq", 1024 * par, 512).bitcast(BF16),
                    QW(par), QWCI(par), QWCM(par), QWD(par),
                    s_tmp[:, 67 + 2 * par:68 + 2 * par], relu=False,
                    entry=q_entry[par])
                t_w1cons[par] = q_entry[par][0]
                pl.wait("tensor", q_entry[par][0])
                for tpi in range(8):
                    bank = (tpi % 2) + 2
                    if wiT_bank_hist[tpi % 2] is not None:
                        pl.wait("tensor", wiT_bank_hist[tpi % 2])
                    tmm = pl.step("tensor", lambda h, par=par, tpi=tpi, bank=bank:
                                  h.transpose(
                                      ps[bank][:, 0:64].bitcast(BF16),
                                      V("zsq", 1024 * par, 512).bitcast(BF16)[:, 128 * tpi:128 * (tpi + 1)],
                                      ident_bf[:]))
                    pl.wait("vector", tmm)
                    t_wiT = pl.step("vector", lambda h, dc=tpi, fc=fc, bank=bank:
                                    h.tensor_copy(
                                        wiT_v(dc)[:, 128 * fc:128 * (fc + 1)],
                                        ps[bank][:, 0:64].bitcast(BF16)))
                    wiT_bank_hist[tpi % 2] = t_wiT
                wi_slot_guard[par] = t_wiT

            pl.wait("gpsimd", pl.tok("cc"))
            tdm = pl.dma("gpsimd", lambda h: h.dma_start(
                out=s_sc[0:1, 8:16], in_=ar1_out[:, :]), "dR")
            pl.wait("vector", tdm)
            ts1 = pl.step("vector", lambda h: h.tensor_scalar(
                s_sc[0:1, 1:2], s_sc[0:1, 8:9], 1e-8, 1.0 / 255.0, ALU.max, ALU.mult))
            pl.wait("vector", ts1)
            ts1 = pl.step("vector", lambda h: h.reciprocal(s_sc[0:1, 2:3], s_sc[0:1, 1:2]))
            pl.wait("tensor", ts1)
            tbq = pl.step("tensor", lambda h: h.matmul(
                ps[5][:, 0:1], ones1[:], s_sc[0:1, 2:3], start=True, stop=True))
            pl.wait("vector", tbq)
            trb = pl.step("vector", lambda h: h.tensor_copy(s_scp[:, 0:1], ps[5][:, 0:1]))
            pl.wait("tensor", trb)
            tbq = pl.step("tensor", lambda h: h.matmul(
                ps[5][:, 0:1], ones1[:], s_sc[0:1, 1:2], start=True, stop=True))
            pl.wait("vector", tbq)
            trb = pl.step("vector", lambda h: h.tensor_copy(s_scp[:, 2:3], ps[5][:, 0:1]))

            pl.wait("vector", trb)
            tsv = pl.step("vector", lambda h: h.tensor_scalar(
                s_sv1[:, :], s_sw1[:, 0:32], s_scp[:, 2:3], None, ALU.mult))
            pl.wait("vector", tsv)
            tsv = pl.step("vector", lambda h: h.tensor_scalar(
                s_sv1[:, :], s_sv1[:, :], 1.0 / 127.0, None, ALU.mult))
            pl.wait("vector", tsv)
            tsv = pl.step("vector", lambda h: h.reciprocal(s_tmp[:, 68:100], s_sv1[:, :]))
            pl.wait("vector", tsv)
            tsv = pl.step("vector", lambda h: h.tensor_tensor(
                s_tmp[:, 100:132], s_b1[:, :], s_tmp[:, 68:100], op=ALU.mult))
            pl.wait("vector", tsv)
            tbqq = pl.quant(s_tmp[:, 100:132], s_tmp[:, 100:132],
                            QW(0)[:, 0:32], QWCI(0)[:, 0:32], QWCM(0)[:, 0:32],
                            QWD(0)[:, 0:32], None, relu=False)[0]
            pl.wait("vector", tbqq)
            tbqq = pl.step("vector", lambda h: h.tensor_tensor(
                s_bq1[:, :], s_tmp[:, 100:132], s_sv1[:, :], op=ALU.mult))

            pl.wait("tensor", pl.tok("cV"))
            pl.wait("tensor", pl.tok("cS"))
            tbb = None
            for tb in range(NTB):
                pl.emit("tensor", lambda h, tb=tb: h.matmul(
                    ps[tb][:, :], ones1[:], m_row(tb), start=True, stop=True))
                tbb = pl.step("tensor", lambda h, tb=tb: h.matmul(
                    ps[4 + tb][:, :], ones1[:], r_row(tb), start=True, stop=True))
            def QY(p):
                return V("qscr", 4096 * p, 1024)
            def QCI(p):
                return V("qscr", 4096 * p + 1024, 1024).bitcast(I32)
            def QCM(p):
                return V("qscr", 4096 * p + 2048, 1024)
            def QD(p):
                return V("qscr", 4096 * p + 3072, 1024)
            t_x2c2 = None
            b2_entry = [None, None]
            ld_guard = [None, None]
            for c in range(HC):
                par = c % 2
                pl.wait("sync", ld_guard[par])
                tld = pl.dma("sync", lambda h, c=c, par=par: h.dma_start(
                    out=V("x2str", 2048 * par, 2048),
                    in_=x2_sp[128 * c:128 * (c + 1), :]),
                    "dXR" if par == 0 else "dSU")
                pl.wait("vector", tld)
                pl.wait("vector", tbb)
                for hhalf in range(2):
                    hsl = slice(1024 * hhalf, 1024 * (hhalf + 1))
                    if b2_entry[par] is not None:
                        pl.wait("vector", b2_entry[par][0])
                    t = None
                    for tb in (2 * hhalf, 2 * hhalf + 1):
                        tbsl = slice(512 * tb, 512 * (tb + 1))
                        osl = slice(512 * (tb - 2 * hhalf), 512 * (tb - 2 * hhalf + 1))
                        t = pl.step("vector", lambda h, par=par, tb=tb, tbsl=tbsl, osl=osl:
                                    h.tensor_tensor(QY(par)[:, osl],
                                                    V("x2str", 2048 * par, 2048)[:, tbsl],
                                                    ps[tb][:, :], op=ALU.subtract))
                    pl.wait("vector", t)
                    for tb in (2 * hhalf, 2 * hhalf + 1):
                        osl = slice(512 * (tb - 2 * hhalf), 512 * (tb - 2 * hhalf + 1))
                        t = pl.step("vector", lambda h, par=par, tb=tb, osl=osl:
                                    h.tensor_tensor(QY(par)[:, osl], QY(par)[:, osl],
                                                    ps[4 + tb][:, :], op=ALU.mult))
                    pl.wait("vector", t)
                    b2_entry[par] = pl.quant(
                        QY(par), xi_v(c)[:, slice(1024 * hhalf, 1024 * (hhalf + 1))],
                        QY(par), QCI(par), QCM(par), QD(par),
                        s_scp[:, 0:1], relu=True, entry=b2_entry[par], s1v=True)
                ld_guard[par] = b2_entry[par][0]
                t_x2c2 = b2_entry[par][0]
            pl.wait("tensor", t_x2c2)

            _ffn_on = [KFFN >= 1]
            pl.wait("vector", tbqq)
            pl.wait("scalar", tbqq)
            pl.wait("tensor", t_wiT)
            pl.wait("vector", t_wiT)
            tmaxg = pl.step("gpsimd", lambda h: h.memset(s_tmp[:, 65:66], -1e30))
            pl.wait("vector", tmaxg)
            h_store = [None, None]
            t_hev = {}
            t_hred = {}
            t_hmax = tmaxg
            for ft in range(32 if KFFN >= 1 else 0):
                for tb in range(NTB):
                    it = ft * NTB + tb
                    par = it % 2
                    bank = it % 2
                    tbsl = slice(512 * tb, 512 * (tb + 1))
                    if it >= 2:
                        pl.wait("tensor", t_hev[it - 2])
                    tmf = None
                    for dc in range(HC):
                        tmf = pl.step("tensor", lambda h, dc=dc, ft=ft, tbsl=tbsl, bank=bank:
                                      h.matmul(ps[bank][:, :],
                                               wiT_v(dc)[:, 128 * ft:128 * (ft + 1)],
                                               xi_v(dc)[:, tbsl],
                                               start=(dc == 0), stop=(dc == HC - 1)))
                    pl.wait("scalar", tmf)
                    if h_store[par] is not None:
                        pl.wait("scalar", h_store[par])
                    if (it - 2) in t_hred:
                        pl.wait("scalar", t_hred[it - 2])
                    t_hev[it] = pl.step("scalar", lambda h, bank=bank, par=par, ft=ft:
                                        h.activation(hst_v(par), ps[bank][:, :], AF.Relu,
                                                     scale=s_sv1[:, ft:ft + 1],
                                                     bias=s_bq1[:, ft:ft + 1]))
                    pl.wait("vector", t_hev[it])
                    tr1 = pl.step("vector", lambda h, par=par: h.tensor_reduce(
                        s_tmp[:, 70 + par:71 + par], hst_v(par), axis=AX.X, op=ALU.max))
                    t_hred[it] = tr1
                    pl.wait("vector", tr1)
                    pl.wait("vector", t_hmax)
                    t_hmax = pl.step("vector", lambda h, par=par: h.tensor_tensor(
                        s_tmp[:, 65:66], s_tmp[:, 65:66], s_tmp[:, 70 + par:71 + par],
                        op=ALU.max))
                    pl.wait("gpsimd", t_hev[it])
                    h_store[par] = pl.dma("gpsimd", lambda h, ft=ft, tbsl=tbsl, par=par:
                                          h.dma_start(out=h_sp[128 * ft:128 * (ft + 1), tbsl],
                                                      in_=hst_v(par)),
                                          "dG" if par == 0 else "dH")

            if KFFN < 2:
                raise _StopEmit()
            pl.wait("vector", t_hmax)
            tcm = pl.step("vector", lambda h: h.tensor_copy(s_tmp[:, 64:65], s_tmp[:, 65:66]))
            pl.wait("tensor", tcm)
            ttr = pl.step("tensor", lambda h: h.transpose(
                ps[5][0:1, 0:128], s_tmp[:, 64:65], ident[:]))
            pl.wait("vector", ttr)
            tgm = pl.step("gpsimd", lambda h: h.memset(s_sc[:, :], -1e30))
            pl.wait("vector", tgm)
            tgm = pl.step("vector", lambda h: h.reduce_max(
                s_sc[0:1, 0:1], ps[5][0:1, 0:128], axis=AX.X))
            pl.wait("gpsimd", tgm)
            tdm = pl.dma("gpsimd", lambda h: h.dma_start(
                out=ar2_in[:, :], in_=s_sc[0:1, 0:8]), "dR")
            pl.wait("gpsimd", tdm)
            pl.emit("gpsimd", lambda h: h.collective_compute(
                "AllReduce", ALU.max, replica_groups=[list(range(8))],
                ins=[ar2_in.opt()], outs=[ar2_out.opt()]), inc="cc")
            pl.wait("gpsimd", pl.tok("cc"))
            tdm = pl.dma("gpsimd", lambda h: h.dma_start(
                out=s_sc[0:1, 8:16], in_=ar2_out[:, :]), "dR")
            pl.wait("vector", tdm)
            ts2 = pl.step("vector", lambda h: h.tensor_scalar(
                s_sc[0:1, 1:2], s_sc[0:1, 8:9], 1e-8, 1.0 / 255.0, ALU.max, ALU.mult))
            pl.wait("vector", ts2)
            ts2 = pl.step("vector", lambda h: h.reciprocal(s_sc[0:1, 2:3], s_sc[0:1, 1:2]))
            pl.wait("tensor", ts2)
            tbq = pl.step("tensor", lambda h: h.matmul(
                ps[5][:, 0:1], ones1[:], s_sc[0:1, 2:3], start=True, stop=True))
            pl.wait("vector", tbq)
            tr2b = pl.step("vector", lambda h: h.tensor_copy(s_scp[:, 1:2], ps[5][:, 0:1]))
            pl.wait("tensor", tr2b)
            tbq = pl.step("tensor", lambda h: h.matmul(
                ps[5][:, 0:1], ones1[:], s_sc[0:1, 1:2], start=True, stop=True))
            pl.wait("vector", tbq)
            tr2b = pl.step("vector", lambda h: h.tensor_copy(s_scp[:, 3:4], ps[5][:, 0:1]))

          except _StopEmit:
            _ffn_stopped = True
        if KSTAGE >= 3 and not _ffn_stopped:
          try:
            pl.barrier()

            def hi_v(fc):
                return V("hihalf", 1024 * fc, 1024, BF16)

            def hrl_v(p):
                return V("hrel", 2048 * p, 2048)

            def q2_y(p):
                return V("q2scr", 2048 * p, 512)

            def q2_ci(p):
                return V("q2scr", 2048 * p + 512, 512).bitcast(I32)

            def q2_cm1(p):
                return V("q2scr", 2048 * p + 1024, 512)

            def q2_d(p):
                return V("q2scr", 2048 * p + 1536, 512)

            def w2sv(p):
                return V("w2s", 512 * p, 512, BF16)

            def w2qstg():
                return V("q2scr", 4096, 1024)

            def w2ibf():
                return V("w2s", 1024, 1024, BF16)

            def outp_v(c):
                return V("outp", 2048 * c, 2048)

            if KFFN < 3:
                raise _StopEmit()

            def w2stg(p):
                return V("hihalf", 4096 * p, 4096)

            def w2ibf():
                return V("w2s", 0, 2048, BF16)

            def Q2Y(p):
                return V("q2scr", 4096 * p, 1024)

            def Q2CI(p):
                return V("q2scr", 4096 * p + 1024, 1024).bitcast(I32)

            def Q2CM(p):
                return V("q2scr", 4096 * p + 2048, 1024)

            def Q2D(p):
                return V("q2scr", 4096 * p + 3072, 1024)

            t_w2ld = [None, None]
            t_w2st = None
            w2_entry = [None, None]
            t_w2c = None
            for dc in range(HC):
                lpar = dc % 2
                pl.wait("sync", t_w2ld[lpar])
                tw = pl.dma("sync", lambda h, dc=dc, lpar=lpar: h.dma_start(
                    out=w2stg(lpar), in_=w2[128 * dc:128 * (dc + 1), :]), "dF0" if lpar == 0 else "dF1")
                pl.wait("vector", tw)
                trm = pl.step("vector", lambda h, dc=dc, lpar=lpar: h.tensor_reduce(
                    s_sw2[:, dc:dc + 1], w2stg(lpar), axis=AX.X, op=ALU.max,
                    apply_absolute_value=True))
                pl.wait("vector", trm)
                trm = pl.step("vector", lambda h, dc=dc, lpar=lpar: h.reciprocal(
                    s_tmp[:, 70 + 2 * lpar:71 + 2 * lpar], s_sw2[:, dc:dc + 1]))
                pl.wait("vector", trm)
                trm = pl.step("vector", lambda h, lpar=lpar: h.tensor_scalar(
                    s_tmp[:, 71 + 2 * lpar:72 + 2 * lpar],
                    s_tmp[:, 70 + 2 * lpar:71 + 2 * lpar], 127.0, None, ALU.mult))
                pl.wait("vector", trm)
                pl.wait("vector", t_w2st)
                pl.wait("scalar", trm)
                for qtr in range(4):
                    qsl = slice(1024 * qtr, 1024 * (qtr + 1))
                    qpar = qtr % 2
                    w2_entry[qpar] = pl.quant(
                        w2stg(lpar)[:, qsl], w2ibf()[:, qsl],
                        Q2Y(qpar), Q2CI(qpar), Q2CM(qpar), Q2D(qpar),
                        s_tmp[:, 71 + 2 * lpar:72 + 2 * lpar], relu=False,
                        entry=w2_entry[qpar])
                t_w2ld[lpar] = w2_entry[1][0]
                pl.wait("gpsimd", w2_entry[0][0])
                pl.wait("gpsimd", w2_entry[1][0])
                t_w2st = pl.dma("gpsimd", lambda h, dc=dc: h.dma_start(
                    out=w2i_sp[128 * dc:128 * (dc + 1), :], in_=w2ibf()), "dWS")
                t_w2c = w2_entry[1][0]

            pl.wait("vector", tr2b)
            tsv = pl.step("vector", lambda h: h.tensor_scalar(
                s_sv2[:, :], s_sw2[:, :], s_scp[:, 3:4], None, ALU.mult))
            pl.wait("vector", tsv)
            tsv = pl.step("vector", lambda h: h.tensor_scalar(
                s_sv2[:, :], s_sv2[:, :], 1.0 / 127.0, None, ALU.mult))
            pl.wait("vector", tsv)
            tsv = pl.step("vector", lambda h: h.reciprocal(s_tmp[:, 80:88], s_sv2[:, :]))
            pl.wait("vector", tsv)
            tsv = pl.step("vector", lambda h: h.tensor_tensor(
                s_tmp[:, 88:96], s_b2[:, :], s_tmp[:, 80:88], op=ALU.mult))
            pl.wait("vector", tsv)
            tbq2 = pl.quant(s_tmp[:, 88:96], s_tmp[:, 88:96],
                            Q2Y(0)[:, 0:8], Q2CI(0)[:, 0:8], Q2CM(0)[:, 0:8],
                            Q2D(0)[:, 0:8], None, relu=False)[0]
            pl.wait("vector", tbq2)
            tbq2 = pl.step("vector", lambda h: h.tensor_tensor(
                s_bq2[:, :], s_tmp[:, 88:96], s_sv2[:, :], op=ALU.mult))

            if KFFN < 4:
                raise _StopEmit()
            t_outev = None
            t_ln4 = None
            def hi_q(q, fc):
                return V("hihalf", 8192 * (q % 2) + 1024 * fc, 1024, BF16)

            hq_entry = [None, None]
            hq_ld = [None, None]

            def emit_hquant_quarter(q, gate_tok):
                tqh = gate_tok
                for fc in range(8):
                    fg = 8 * q + fc
                    par = fc % 2
                    pl.wait("sync", hq_ld[par])
                    pl.wait("sync", gate_tok)
                    th = pl.dma("sync", lambda h, fg=fg, par=par: h.dma_start(
                        out=hrl_v(par), in_=h_sp[128 * fg:128 * (fg + 1), :]),
                        "dF0" if par == 0 else "dF1")
                    pl.wait("vector", th)
                    pl.wait("scalar", th)
                    for hh in range(2):
                        hsl = slice(1024 * hh, 1024 * (hh + 1))
                        hq_entry[par] = pl.quant(
                            hrl_v(par)[:, hsl], hi_q(q, fc)[:, hsl],
                            Q2Y(par), Q2CI(par), Q2CM(par), Q2D(par),
                            s_scp[:, 1:2], relu=True, entry=hq_entry[par])
                    hq_ld[par] = hq_entry[par][0]
                    tqh = hq_entry[par][0]
                return tqh

            pl.wait("sync", pl.tok("dWS"))
            tqh = {}
            tqh[0] = emit_hquant_quarter(0, t_w2c)
            hq_cons = {}
            t_w2l = [None, None]
            for q in range(4):
                pl.wait("tensor", tqh[q])
                for tb in range(NTB):
                    tbsl = slice(512 * tb, 512 * (tb + 1))
                    if q > 0 or tb > 0:
                        pl.wait("tensor", t_outev)
                    tmf = None
                    for fc in range(8):
                        fg = 8 * q + fc
                        wpar = fc % 2
                        pl.wait("sync", t_w2l[wpar])
                        twl = pl.dma("sync", lambda h, fg=fg, wpar=wpar: h.dma_start(
                            out=w2sv(wpar)[:, :],
                            in_=w2i_sp[:, 128 * fg:128 * (fg + 1)], transpose=True),
                            "dV0" if wpar == 0 else "dV1")
                        pl.wait("tensor", twl)
                        for dt in range(HC):
                            tmf = pl.step("tensor",
                                          lambda h, q=q, fc=fc, dt=dt, tbsl=tbsl, wpar=wpar:
                                          h.matmul(ps[dt][:, :],
                                                   w2sv(wpar)[:, 128 * dt:128 * (dt + 1)],
                                                   hi_q(q, fc)[:, tbsl],
                                                   start=(fc == 0), stop=(fc == 7)))
                        t_w2l[wpar] = tmf
                    if tb == 0 and q < 3:
                        gate = hq_cons.get(q - 1, tqh[q])
                        tqh[q + 1] = emit_hquant_quarter(q + 1, gate)
                    pl.wait("vector", tmf)
                    pl.wait("vector", t_outev)
                    for dt in range(HC):
                        if q == 0:
                            t_outev = pl.step("vector", lambda h, dt=dt, tbsl=tbsl:
                                              h.tensor_copy(outp_v(dt)[:, tbsl],
                                                            ps[dt][:, :]))
                        else:
                            t_outev = pl.step("vector", lambda h, dt=dt, tbsl=tbsl:
                                              h.tensor_tensor(outp_v(dt)[:, tbsl],
                                                              outp_v(dt)[:, tbsl],
                                                              ps[dt][:, :], op=ALU.add))
                hq_cons[q] = tmf

            pl.wait("vector", tbq2)
            pl.wait("vector", t_outev)
            t = None
            for dt in range(HC):
                t = pl.step("vector", lambda h, dt=dt: h.tensor_scalar(
                    outp_v(dt), outp_v(dt), s_sv2[:, dt:dt + 1], s_bq2[:, dt:dt + 1],
                    ALU.mult, ALU.add))
            pl.wait("vector", t)
            pl.wait("tensor", t)

            def x2r_v(c):
                return V("hrel", 512 * c, 512)

            def ystg_v(p):
                return V("q2scr", 4096 + 1024 * p, 1024)

            t_yst = [None, None]
            y_bank_hist = [None, None]
            t_x2r = None
            for tb in range(NTB):
                tbsl = slice(512 * tb, 512 * (tb + 1))
                pl.wait("sync", t_x2r)
                tld = None
                for c in range(HC):
                    tld = pl.dma("sync", lambda h, c=c, tbsl=tbsl: h.dma_start(
                        out=x2r_v(c), in_=x2_sp[128 * c:128 * (c + 1), tbsl]), "dAR")
                pl.wait("vector", tld)

                def ln4_out(tb2, sl2, tbsl=tbsl):
                    t = None
                    for c in range(HC):
                        t = pl.step("vector", lambda h, c=c: h.tensor_tensor(
                            V("q2scr", 512 * c, 512), outp_v(c)[:, tbsl], ps[6][:, :],
                            op=ALU.subtract))
                    pl.wait("vector", t)
                    for c in range(HC):
                        t = pl.step("vector", lambda h, c=c: h.tensor_tensor(
                            V("q2scr", 512 * c, 512), V("q2scr", 512 * c, 512),
                            ps[7][:, :], op=ALU.mult))
                    pl.wait("vector", t)
                    vt = None
                    for c in range(HC):
                        vt = pl.step("vector", lambda h, c=c: h.tensor_tensor(
                            x2r_v(c), V("q2scr", 512 * c, 512), x2r_v(c), op=ALU.add))
                    return vt

                emit_ln(HC, lambda c, tbsl=tbsl: outp_v(c)[:, tbsl], "q2scr",
                        ln4_out, [0])
                t_ln4 = pl.tok("cV")
                for tl in range(4):
                    par = tl % 2
                    pl.wait("tensor", t_ln4)
                    pl.wait("tensor", t_yst[par])
                    tq = None
                    for c in range(HC):
                        bank = (c % 2) + 2
                        if y_bank_hist[c % 2] is not None:
                            pl.wait("tensor", y_bank_hist[c % 2])
                        tmm = pl.step("tensor", lambda h, c=c, tl=tl, bank=bank:
                                      h.transpose(ps[bank][:, 0:128],
                                                  x2r_v(c)[:, 128 * tl:128 * (tl + 1)],
                                                  ident[:]))
                        pl.wait("vector", tmm)
                        tq = pl.step("vector", lambda h, c=c, par=par, bank=bank:
                                     h.tensor_copy(ystg_v(par)[:, 128 * c:128 * (c + 1)],
                                                   ps[bank][:, 0:128]))
                        y_bank_hist[c % 2] = tq
                    pl.wait("gpsimd", tq)
                    r0 = 512 * tb + 128 * tl
                    t_yst[par] = pl.dma("gpsimd", lambda h, r0=r0, par=par: h.dma_start(
                        out=y[r0:r0 + 128, :], in_=ystg_v(par)),
                        "dY0" if par == 0 else "dY1")
                t_x2r = pl.tokT()
            pl.wait("gpsimd", pl.tok("dY0"))
            pl.wait("gpsimd", pl.tok("dY1"))
          except _StopEmit:
            _ffn_stopped = True


        if KSTAGE >= 3 and _ffn_stopped:
            pl.barrier()
            tms = pl.step("gpsimd", lambda h: h.memset(arena[:, 0:1024], 0.0))
            pl.wait("gpsimd", tms)
            for tt in range(NT):
                tst = pl.dma("gpsimd", lambda h, tt=tt: h.dma_start(
                    out=y[128 * tt:128 * (tt + 1), :], in_=arena[:, 0:1024]), "dG")
            pl.wait("gpsimd", tst)
        if KSTAGE < 3:
            pl.barrier()
            tms = pl.step("gpsimd", lambda h: h.memset(
                V("x" if KSTAGE < 2 else "avl", 0, 1024), 0.0))
            pl.wait("gpsimd", tms)
            tst = None
            for tt in range(NT):
                tst = pl.dma("gpsimd", lambda h, tt=tt: h.dma_start(
                    out=y[128 * tt:128 * (tt + 1), :],
                    in_=V("x" if KSTAGE < 2 else "avl", 0, 1024)), "dG")
            pl.wait("gpsimd", tst)
        if KDBG:
            pl.wait("gpsimd", pl.tok("cc"))
            pl.dma("gpsimd", lambda h: h.dma_start(out=dbg_ag[:, :], in_=ag_in[:, :]), "dG")
        pl.wait("gpsimd", pl.tok("cc"))
        pl.wait("gpsimd", pl.tok("dG"))
        pl.wait("gpsimd", pl.tok("dH"))
        pl.wait("gpsimd", pl.tok("dK"))
        pl.wait("gpsimd", pl.tok("dW"))

        pl.realize(block, sems)
    return nc


def kernel(L0_attn_inputs, L0_attn_W_uv, L0_attn_b_uv, L0_attn_gamma, L0_attn_beta,
           L0_attn_W_out, L0_attn_b_out, L0_ffn_w1, L0_ffn_b1, L0_ffn_w2, L0_ffn_b2):
    x = np.ascontiguousarray(np.asarray(L0_attn_inputs, dtype=np.float32))
    wuv = np.ascontiguousarray(np.asarray(L0_attn_W_uv, dtype=np.float32))
    buv = np.asarray(L0_attn_b_uv, dtype=np.float32)
    gam = np.asarray(L0_attn_gamma, dtype=np.float32)
    bet = np.asarray(L0_attn_beta, dtype=np.float32)
    wout = np.ascontiguousarray(np.asarray(L0_attn_W_out, dtype=np.float32))
    bout = np.asarray(L0_attn_b_out, dtype=np.float32)
    w1 = np.ascontiguousarray(np.asarray(L0_ffn_w1, dtype=np.float32))
    b1 = np.asarray(L0_ffn_b1, dtype=np.float32)
    w2 = np.ascontiguousarray(np.asarray(L0_ffn_w2, dtype=np.float32))
    b2 = np.asarray(L0_ffn_b2, dtype=np.float32)

    shared = dict(
        wuv=wuv,
        buv_row=buv.reshape(1, -1),
        buv_base=buv[2 * E:].reshape(SD, 1),
        gamT=np.ascontiguousarray(gam.T),
        betT=np.ascontiguousarray(bet.T),
        wout=wout,
        bout_pt=np.ascontiguousarray(bout.reshape(HC, 128).T),
        w1=w1,
        w1T=np.ascontiguousarray(w1.T),
        b1_pt=np.ascontiguousarray(b1.reshape(FF // 128, 128).T),
        w2=w2,
        w2T=np.ascontiguousarray(w2.T),
        b2_pt=np.ascontiguousarray(b2.reshape(HC, 128).T),
    )
    in_maps = []
    for c in range(NCORE_IDS):
        b, slc = c // 4, c % 4
        xs = x[b, TC * slc:TC * (slc + 1), :]
        m = dict(shared)
        m["xT"] = np.ascontiguousarray(xs.T)
        in_maps.append(m)

    nc = build_nc()
    res = run_bass_kernel_spmd(nc, in_maps, core_ids=list(range(8)))
    out = np.empty((B, S, H), dtype=np.float32)
    for c in range(8):
        b, slc = c // 4, c % 4
        out[b, TC * slc:TC * (slc + 1), :] = np.asarray(res.results[c]["y"])
    if KDBG:
        kernel.last_results = res.results
    return out


